# version 02
# baseline (speedup 1.0000x reference)
import contextlib
import math
import numpy as np
import ml_dtypes
import concourse.bass as bass
import concourse.mybir as mybir
from concourse.bass_utils import run_bass_kernel_spmd

F32 = mybir.dt.float32
BF16 = mybir.dt.bfloat16
AF = mybir.ActivationFunctionType
ALU = mybir.AluOpType
ds = bass.ds
BF = ml_dtypes.bfloat16

NCORES = 8
B, T, D = 4, 2048, 2048
TPC = 1024
NCH = D // 128
RS = 1.0 / math.sqrt(128.0)


class EB:
    def __init__(self, ctx, nc, name):
        self.h = ctx.enter_context(nc.semaphore(f"ch_{name}"))
        self.v = 0
        self.ops = []
        self.name = name

    def emit(self, fn, n=1):
        self.v += n
        h, k = self.h, n

        def op(e):
            fn(e).then_inc(h, k)
        self.ops.append(op)
        return self.v

    def raw(self, fn):
        self.ops.append(fn)

    def wait(self, other, v):
        h = other.h
        self.ops.append(lambda e: e.wait_ge(h, v))

    def dma(self, fn):
        return self.emit(fn, 16)

    def run(self, e):
        for f in self.ops:
            f(e)


class Prog:
    def __init__(self, nc, ctx):
        self.nc = nc
        self.ctx = ctx
        self.sy = EB(ctx, nc, "sy")
        self.gp = EB(ctx, nc, "gp")
        self.ve = EB(ctx, nc, "ve")
        self.sc = EB(ctx, nc, "sc")
        self.te = EB(ctx, nc, "te")

    def finish(self):
        with self.nc.Block() as block:
            block.sync(self.sy.run)
            block.gpsimd(self.gp.run)
            block.vector(self.ve.run)
            block.scalar(self.sc.run)
            block.tensor(self.te.run)


def emit_ln(p, nc, xsb, W, consts, out_bf, deps, tag="", alloc_ctx=None,
            outW=None, out_off=0):
    if outW is None:
        outW = W
    wl = []
    off = 0
    while off < W:
        ln_ = min(512, W - off)
        wl.append((off, ln_))
        off += ln_
    NW = len(wl)
    sq = alloc_ctx.enter_context(nc.sbuf_tensor(f"ln_sq{tag}", [128, W], F32))
    st = alloc_ctx.enter_context(nc.sbuf_tensor(f"ln_st{tag}", [1, 2 * W], F32))
    abc = alloc_ctx.enter_context(nc.sbuf_tensor(f"ln_ab{tag}", [128, W], F32))
    bbc = alloc_ctx.enter_context(nc.sbuf_tensor(f"ln_bb{tag}", [128, W], F32))
    pctx = alloc_ctx.enter_context(contextlib.ExitStack())
    ps_sum = [pctx.enter_context(nc.psum_tensor(f"lnS{tag}{w}", [1, wl[w][1]],
                                                F32)) for w in range(NW)]
    ps_sq = [pctx.enter_context(nc.psum_tensor(f"lnQ{tag}{w}", [1, wl[w][1]],
                                               F32)) for w in range(NW)]
    ps_bc = [pctx.enter_context(nc.psum_tensor(f"lnB{tag}{w}",
                                               [128, wl[w // 2][1]], F32))
             for w in range(2 * NW)]
    ones_c = consts["ones_col"]
    ones_r = consts["ones_row"]
    eps = consts["eps"]
    ve, sc, te = p.ve, p.sc, p.te

    for (d, v) in deps:
        sc.wait(d, v)
        te.wait(d, v)

    TE0 = te.v
    V_SQ = []
    for cc in range(NCH):
        if cc >= 1:
            sc.wait(te, TE0 + 2 * NW * cc)
        V_SQ.append(sc.emit(lambda e, cc=cc: e.activation(
            sq[:, :], xsb[:, cc * W:(cc + 1) * W], AF.Square)))
    for cc in range(NCH):
        te.wait(sc, V_SQ[cc])
        for w, (wo, wn_) in enumerate(wl):
            te.emit(lambda e, cc=cc, w=w, wo=wo, wn_=wn_: e.matmul(
                ps_sum[w][:, :], ones_c[:, 0:1],
                xsb[:, cc * W + wo: cc * W + wo + wn_],
                start=(cc == 0), stop=(cc == NCH - 1)))
            te.emit(lambda e, cc=cc, w=w, wo=wo, wn_=wn_: e.matmul(
                ps_sq[w][:, :], ones_c[:, 0:1],
                sq[:, wo: wo + wn_],
                start=(cc == 0), stop=(cc == NCH - 1)))
    V_MM = te.v
    assert V_MM == TE0 + 2 * NW * NCH

    sc.wait(te, V_MM)
    V_MU = None
    for w, (wo, wn_) in enumerate(wl):
        V_MU = sc.emit(lambda e, w=w, wo=wo, wn_=wn_: e.mul(
            st[0:1, wo: wo + wn_], ps_sum[w][:, :], 1.0 / D))
    sc.wait(sc, V_MU)
    V_M2 = sc.emit(lambda e: e.activation(st[0:1, W:2 * W], st[0:1, 0:W],
                                          AF.Square))
    ve.wait(sc, V_M2)
    V_VAR = None
    for w, (wo, wn_) in enumerate(wl):
        if w >= 1:
            ve.wait(ve, V_VAR)
        V_VAR = ve.emit(lambda e, w=w, wo=wo, wn_=wn_: e.scalar_tensor_tensor(
            st[0:1, W + wo: W + wo + wn_],
            ps_sq[w][:, :], 1.0 / D,
            st[0:1, W + wo: W + wo + wn_],
            ALU.mult, ALU.subtract))
    sc.wait(ve, V_VAR)
    V_LN1 = sc.emit(lambda e: e.activation(st[0:1, W:2 * W],
                                           st[0:1, W:2 * W], AF.Ln,
                                           bias=eps[0:1, 0:1]))
    sc.wait(sc, V_LN1)
    V_RSTD = sc.emit(lambda e: e.activation(st[0:1, W:2 * W],
                                            st[0:1, W:2 * W],
                                            AF.Exp, scale=-0.5))
    ve.wait(sc, V_RSTD)
    V_BROW = ve.emit(lambda e: e.scalar_tensor_tensor(
        st[0:1, 0:W], st[0:1, 0:W], -1.0, st[0:1, W:2 * W],
        ALU.mult, ALU.mult))
    te.wait(ve, V_BROW)
    V_BC = None
    for w, (wo, wn_) in enumerate(wl):
        V_BC = te.emit(lambda e, w=w, wo=wo, wn_=wn_: e.matmul(
            ps_bc[2 * w][:, :], ones_r[0:1, :],
            st[0:1, W + wo: W + wo + wn_],
            start=True, stop=True))
        V_BC = te.emit(lambda e, w=w, wo=wo, wn_=wn_: e.matmul(
            ps_bc[2 * w + 1][:, :], ones_r[0:1, :],
            st[0:1, wo: wo + wn_],
            start=True, stop=True))
    ve.wait(te, V_BC)
    for w, (wo, wn_) in enumerate(wl):
        ve.emit(lambda e, w=w, wo=wo, wn_=wn_: e.tensor_copy(
            abc[:, wo: wo + wn_], ps_bc[2 * w][:, :]))
        ve.emit(lambda e, w=w, wo=wo, wn_=wn_: e.tensor_copy(
            bbc[:, wo: wo + wn_], ps_bc[2 * w + 1][:, :]))
    V_CP = ve.v
    V_DONE = None
    for cc in range(NCH):
        if cc == 0:
            ve.wait(ve, V_CP)
        else:
            ve.wait(ve, V_DONE)
        ve.emit(lambda e, cc=cc: e.tensor_mul(
            sq[:, :], xsb[:, cc * W:(cc + 1) * W], abc[:, :]))
        ve.wait(ve, ve.v)
        V_DONE = ve.emit(lambda e, cc=cc: e.tensor_add(
            out_bf[:, cc * outW + out_off: cc * outW + out_off + W],
            sq[:, :], bbc[:, :]))
    pctx.close()
    return V_DONE, st




def emit_ffcols(p, nc, ctx, src_bf, srcW, w_dram, out_bf, outW, act, nfc,
                ncc=NCH, tag="f", src_off=0, L=None, out_off=0, addto=None,
                addW=None, add_off=0, multo=False):
    if L is None:
        L = srcW
    wl = []
    off = 0
    while off < L:
        wn_ = min(512, L - off)
        wl.append((off, wn_))
        off += wn_
    NT = len(wl)
    ictx = ctx.enter_context(contextlib.ExitStack())
    wcol = ictx.enter_context(nc.sbuf_tensor(f"wc{tag}", [128, 2 * ncc * 128],
                                             BF16))
    pst = [ictx.enter_context(nc.psum_tensor(f"pf{tag}{i}",
                                             [128, wl[i % NT][1]], F32))
           for i in range(2 * NT)]
    gp, te, sc, ve = p.gp, p.te, p.sc, p.ve
    ev = ve if addto is not None else sc
    GP0 = gp.v
    TE0 = te.v
    EV0 = ev.v
    gp.wait(gp, GP0)
    gp.wait(te, TE0)
    te.wait(sc, sc.v)
    te.wait(ve, ve.v)

    def te_after(fc):
        return TE0 + (fc + 1) * NT * ncc

    for fc in range(nfc):
        if fc >= 2:
            gp.wait(te, te_after(fc - 2))
        gp.dma(lambda e, fc=fc: e.dma_start(
            out=wcol.rearrange("p (s cc f) -> p s cc f", s=2, f=128)
                [:, fc % 2],
            in_=w_dram[:, fc * 128:(fc + 1) * 128].rearrange(
                "(cc p) f -> p cc f", p=128)))

    for fc in range(nfc):
        te.wait(gp, GP0 + 16 * min(fc + 2, nfc))
        if fc >= 2:
            te.wait(ev, EV0 + NT * (fc - 1))
        for tt, (wo, wn_) in enumerate(wl):
            for cc in range(ncc):
                te.emit(lambda e, fc=fc, tt=tt, cc=cc, wo=wo, wn_=wn_:
                        e.matmul(
                    pst[(fc % 2) * NT + tt][:, :],
                    wcol[:, (fc % 2) * ncc * 128 + cc * 128:
                         (fc % 2) * ncc * 128 + (cc + 1) * 128],
                    src_bf[:, cc * srcW + src_off + wo:
                           cc * srcW + src_off + wo + wn_],
                    start=(cc == 0), stop=(cc == ncc - 1)))

    for fc in range(nfc):
        for tt, (wo, wn_) in enumerate(wl):
            ev.wait(te, TE0 + fc * NT * ncc + (tt + 1) * ncc)
            if addto is not None:
                ev.wait(ve, ve.v)
                op = ALU.mult if multo else ALU.add
                ve.emit(lambda e, fc=fc, tt=tt, wo=wo, wn_=wn_, op=op:
                        e.tensor_tensor(
                    addto[:, fc * addW + add_off + wo:
                          fc * addW + add_off + wo + wn_],
                    pst[(fc % 2) * NT + tt][:, :],
                    addto[:, fc * addW + add_off + wo:
                          fc * addW + add_off + wo + wn_], op))
            else:
                sc.emit(lambda e, fc=fc, tt=tt, wo=wo, wn_=wn_: e.activation(
                    out_bf[:, fc * outW + out_off + wo:
                           fc * outW + out_off + wo + wn_],
                    pst[(fc % 2) * NT + tt][:, :], act))
    assert ev.v == EV0 + nfc * NT
    ictx.close()
    return ev.v, NT


def emit_router(p, nc, ctx, xsb, stats, consts, E, wrT_dram, wn_out, tag="r"):
    W = TPC
    wrs = ctx.enter_context(nc.sbuf_tensor(f"wrs{tag}", [128, NCH * E], F32))
    cols = ctx.enter_context(nc.sbuf_tensor(f"cols{tag}", [128, 2 * 8], F32))
    csbc = ctx.enter_context(nc.sbuf_tensor(f"csbc{tag}", [128, E], F32))
    lrow = ctx.enter_context(nc.sbuf_tensor(f"lrow{tag}", [1, E], F32))
    sml = ctx.enter_context(nc.sbuf_tensor(f"sml{tag}", [128, 2 * E + 8], F32))
    pctx = ctx.enter_context(contextlib.ExitStack())
    ps_l = pctx.enter_context(nc.psum_tensor(f"psl{tag}", [128, E], F32))
    ps_cs = pctx.enter_context(nc.psum_tensor(f"pcs{tag}", [1, E], F32))
    ps_cb = pctx.enter_context(nc.psum_tensor(f"pcb{tag}", [128, E], F32))
    ps_t = pctx.enter_context(nc.psum_tensor(f"pst{tag}", [128, 2], F32))
    gp, te, sc, ve = p.gp, p.te, p.sc, p.ve
    ones_c, ones_r = consts["ones_col"], consts["ones_row"]

    gp.wait(gp, gp.v)
    V_WR = gp.dma(lambda e: e.dma_start(
        out=wrs.rearrange("p (cc f) -> p cc f", f=E),
        in_=wrT_dram.rearrange("(cc p) f -> p cc f", p=128)))

    te.wait(gp, V_WR)
    V_CS = None
    for cc in range(NCH):
        V_CS = te.emit(lambda e, cc=cc: e.matmul(
            ps_cs[:, :], ones_c[:, 0:1], wrs[:, cc * E:(cc + 1) * E],
            start=(cc == 0), stop=(cc == NCH - 1)))
    sc.wait(te, V_CS)
    V_CSR = sc.emit(lambda e: e.activation(lrow[0:1, :], ps_cs[:, :],
                                           AF.Identity))
    te.wait(sc, V_CSR)
    V_CB = te.emit(lambda e: e.matmul(ps_cb[:, :], ones_r[0:1, :],
                                      lrow[0:1, :], start=True, stop=True))
    ve.wait(te, V_CB)
    V_CBE = ve.emit(lambda e: e.tensor_copy(csbc[:, :], ps_cb[:, :]))

    V_TC = []
    for tc in range(8):
        V1 = te.emit(lambda e, tc=tc: e.transpose(
            ps_t[:, 0:1], stats[0:1, W + tc * 128: W + (tc + 1) * 128],
            consts["one1"][0:1, 0:1]))
        V1 = te.emit(lambda e, tc=tc: e.transpose(
            ps_t[:, 1:2], stats[0:1, tc * 128:(tc + 1) * 128],
            consts["one1"][0:1, 0:1]))
        ve.wait(te, V1)
        V_TC.append(ve.emit(lambda e, tc=tc: e.tensor_copy(
            cols[:, 2 * tc:2 * tc + 2], ps_t[:, 0:2])))
        if tc < 7:
            te.wait(ve, V_TC[-1])
    V_OUT = None
    V_TS1 = None
    for tc in range(8):
        if tc >= 1:
            te.wait(ve, V_TS1)
        V_L = None
        for cc in range(NCH):
            V_L = te.emit(lambda e, tc=tc, cc=cc: e.matmul(
                ps_l[:, :],
                xsb[:, cc * W + tc * 128: cc * W + (tc + 1) * 128],
                wrs[:, cc * E:(cc + 1) * E],
                start=(cc == 0), stop=(cc == NCH - 1)))
        ve.wait(te, V_L)
        ve.wait(ve, max(V_CBE, V_TC[tc]))
        if tc >= 1:
            ve.wait(ve, V_OUT)
        V_TS1 = ve.emit(lambda e, tc=tc: e.tensor_scalar(
            sml[:, 0:E], ps_l[:, :], cols[:, 2 * tc:2 * tc + 1], None,
            ALU.mult))
        ve.wait(ve, ve.v)
        ve.emit(lambda e, tc=tc: e.scalar_tensor_tensor(
            sml[:, 0:E], csbc[:, :], cols[:, 2 * tc + 1:2 * tc + 2],
            sml[:, 0:E], ALU.mult, ALU.add))
        ve.wait(ve, ve.v)
        ve.emit(lambda e: e.tensor_reduce(
            sml[:, 2 * E:2 * E + 1], sml[:, 0:E], mybir.AxisListType.X,
            ALU.max, negate=True))
        ve.wait(ve, ve.v)
        V_NM = ve.v
        sc.wait(ve, V_NM)
        V_EX = sc.emit(lambda e: e.activation(
            sml[:, E:2 * E], sml[:, 0:E], AF.Exp,
            bias=sml[:, 2 * E:2 * E + 1]))
        ve.wait(sc, V_EX)
        ve.emit(lambda e: e.tensor_scalar(
            sml[:, 0:E], sml[:, E:2 * E], 1.0, None, ALU.is_equal))
        ve.wait(ve, ve.v)
        ve.emit(lambda e: e.scalar_tensor_tensor(
            sml[:, 0:E], sml[:, 0:E], -30000.0, sml[:, E:2 * E],
            ALU.mult, ALU.add))
        ve.wait(ve, ve.v)
        ve.emit(lambda e: e.tensor_reduce(
            sml[:, 2 * E + 1:2 * E + 2], sml[:, 0:E], mybir.AxisListType.X,
            ALU.max))
        ve.wait(ve, ve.v)
        ve.emit(lambda e: e.tensor_scalar(
            sml[:, 0:E], sml[:, E:2 * E], sml[:, 2 * E + 1:2 * E + 2], None,
            ALU.is_ge))
        ve.wait(ve, ve.v)
        ve.emit(lambda e: e.scalar_tensor_tensor(
            sml[:, 0:E], sml[:, E:2 * E], 1.0, sml[:, 0:E],
            ALU.mult, ALU.mult, accum_out=sml[:, 2 * E + 2:2 * E + 3]))
        ve.wait(ve, ve.v)
        ve.emit(lambda e: e.reciprocal(sml[:, 2 * E + 3:2 * E + 4],
                                       sml[:, 2 * E + 2:2 * E + 3]))
        ve.wait(ve, ve.v)
        V_OUT = ve.emit(lambda e, tc=tc: e.tensor_scalar(
            wn_out[:, tc * E:(tc + 1) * E], sml[:, 0:E],
            sml[:, 2 * E + 3:2 * E + 4], None, ALU.mult))
    pctx.close()
    return V_OUT


def emit_ffn0(p, nc, ctx, xsb, consts, w1T, w3T, w2T, wrT, part_out,
              deps, tag="n"):
    gp, te, sc, ve, sy = p.gp, p.te, p.sc, p.ve, p.sy
    fctx = ctx.enter_context(contextlib.ExitStack())
    wn = fctx.enter_context(nc.sbuf_tensor(f"wn{tag}", [128, 8 * 8], BF16))
    p8 = fctx.enter_context(nc.sbuf_tensor(f"p8{tag}", [8, 2048], F32))
    h2 = fctx.enter_context(nc.sbuf_tensor(f"h2{tag}", [128, NCH * TPC], BF16))
    h3t = fctx.enter_context(nc.sbuf_tensor(f"h3{tag}", [128, 2048], BF16))
    ictx = fctx.enter_context(contextlib.ExitStack())
    h1 = ictx.enter_context(nc.sbuf_tensor(f"h1{tag}", [128, NCH * TPC], BF16))
    xctx = ictx.enter_context(contextlib.ExitStack())
    xh = xctx.enter_context(nc.sbuf_tensor(f"xh{tag}", [128, NCH * TPC], BF16))
    V_LN, stats = emit_ln(p, nc, xsb, TPC, consts, xh, deps, tag=tag,
                          alloc_ctx=xctx)
    for eng in (p.te, p.sc, p.gp):
        eng.wait(ve, V_LN)
    V_RT = emit_router(p, nc, xctx, xsb, stats, consts, 8, wrT, wn, tag=tag)
    p.te.wait(ve, V_RT)
    emit_ffcols(p, nc, xctx, xh, TPC, w1T, h1, TPC, AF.Silu, NCH,
                tag=tag + "1")
    V_H1S = sc.v
    xctx.close()
    for eng in (p.te, p.gp):
        eng.wait(sc, V_H1S)
    emit_ffcols(p, nc, ictx, h1, TPC, w3T, h2, TPC, AF.Silu, NCH,
                tag=tag + "3")
    V_H2S = sc.v
    ictx.close()

    w2r = fctx.enter_context(nc.sbuf_tensor(f"w2r{tag}", [128, NCH * 2048],
                                            BF16))
    pctx = fctx.enter_context(contextlib.ExitStack())
    ps3 = [pctx.enter_context(nc.psum_tensor(f"p3{tag}{i}", [128, 512], F32))
           for i in range(2)]
    ps8 = [pctx.enter_context(nc.psum_tensor(f"p8{tag}{i}", [8, 512], F32))
           for i in range(4)]
    gp.wait(gp, gp.v)
    gp.wait(sc, V_H2S)
    for fc in range(NCH):
        gp.dma(lambda e, fc=fc: e.dma_start(
            out=w2r.rearrange("p (fc c) -> p fc c", c=2048)[:, fc],
            in_=w2T[fc * 128:(fc + 1) * 128, :]))
    V_W2 = gp.v

    te.wait(gp, V_W2)
    te.wait(sc, V_H2S)
    TE3 = te.v
    SC3 = sc.v
    VE3 = ve.v

    for tc in range(8):
        te.wait(ve, V_RT)
        for cs in range(4):
            k = tc * 4 + cs
            if k >= 2:
                te.wait(sc, SC3 + (k - 1))
            for fc in range(NCH):
                te.emit(lambda e, tc=tc, cs=cs, fc=fc, k=k: e.matmul(
                    ps3[k % 2][:, :],
                    h2[:, fc * TPC + tc * 128: fc * TPC + (tc + 1) * 128],
                    w2r[:, fc * 2048 + cs * 512: fc * 2048 + (cs + 1) * 512],
                    start=(fc == 0), stop=(fc == NCH - 1)))
            sc.wait(te, te.v)
            sc.emit(lambda e, cs=cs, k=k: e.activation(
                h3t[:, cs * 512:(cs + 1) * 512], ps3[k % 2][:, :],
                AF.Identity))
            te.wait(sc, sc.v)
            te.emit(lambda e, tc=tc, cs=cs: e.matmul(
                ps8[cs][:, :], wn[:, tc * 8:(tc + 1) * 8],
                h3t[:, cs * 512:(cs + 1) * 512],
                start=(tc == 0), stop=(tc == 7)))
    V_TE8 = te.v
    ve.wait(te, V_TE8)
    V_P8 = None
    for cs in range(4):
        V_P8 = ve.emit(lambda e, cs=cs: e.tensor_copy(
            p8[:, cs * 512:(cs + 1) * 512], ps8[cs][:, :]))
    sy.wait(ve, V_P8)
    V_PS = sy.dma(lambda e: e.dma_start(out=part_out, in_=p8[:, :]))
    pctx.close()
    fctx.close()
    return V_PS




def emit_ffn1(p, nc, ctx, xsb, consts, wr16T, sgD, suD, sdD, gwD, uwD, dwD,
              deps, tag="M"):
    gp, te, sc, ve, sy = p.gp, p.te, p.sc, p.ve, p.sy
    E = 16
    fctx = ctx.enter_context(contextlib.ExitStack())
    wn = fctx.enter_context(nc.sbuf_tensor(f"wn{tag}", [128, 8 * E], BF16))
    wTr = fctx.enter_context(nc.sbuf_tensor(f"wTr{tag}", [16, TPC], BF16))
    wrow = fctx.enter_context(nc.sbuf_tensor(f"wrw{tag}", [1, TPC], BF16))
    onebr = fctx.enter_context(nc.sbuf_tensor(f"obr{tag}", [1, 128], BF16))
    wbc = fctx.enter_context(nc.sbuf_tensor(f"wbc{tag}", [128, TPC], F32))
    xh = fctx.enter_context(nc.sbuf_tensor(f"xh{tag}", [128, NCH * TPC], BF16))
    V_LN, stats = emit_ln(p, nc, xsb, TPC, consts, xh, deps, tag=tag,
                          alloc_ctx=fctx)
    for eng in (te, sc, gp):
        eng.wait(ve, V_LN)
    V_RT = emit_router(p, nc, fctx, xsb, stats, consts, E, wr16T, wn, tag=tag)
    rctx = fctx.enter_context(contextlib.ExitStack())
    ps_tw = rctx.enter_context(nc.psum_tensor(f"ptw{tag}", [16, 128], BF16))
    te.wait(ve, V_RT)
    ve.wait(ve, V_RT)
    V_OBR = ve.emit(lambda e: e.tensor_copy(onebr[0:1, :],
                                            consts["ones_row"][0:1, :]))
    for tc in range(8):
        te.wait(ve, ve.v)
        te.emit(lambda e, tc=tc: e.transpose(
            ps_tw[:, :], wn[:, tc * E:(tc + 1) * E], consts["idb"][:, :]))
        ve.wait(te, te.v)
        ve.emit(lambda e, tc=tc: e.tensor_copy(
            wTr[:, tc * 128:(tc + 1) * 128], ps_tw[:, :]))
    V_WTR = ve.v
    rctx.close()

    sctx = fctx.enter_context(contextlib.ExitStack())
    gsh = sctx.enter_context(nc.sbuf_tensor(f"gsh{tag}", [128, NCH * TPC],
                                            BF16))
    emit_ffcols(p, nc, sctx, xh, TPC, sgD, gsh, TPC, AF.Silu, NCH,
                tag=tag + "g")
    te.wait(sc, sc.v)
    emit_ffcols(p, nc, sctx, xh, TPC, suD, None, TPC, None, NCH,
                tag=tag + "u", addto=gsh, addW=TPC, multo=True)
    V_SH = ve.v
    for eng in (te, gp):
        eng.wait(ve, V_SH)
    emit_ffcols(p, nc, sctx, gsh, TPC, sdD, None, TPC, None, NCH,
                tag=tag + "d", addto=xsb, addW=TPC)
    V_SD = ve.v
    sctx.close()

    ectx = fctx.enter_context(contextlib.ExitStack())
    gE = ectx.enter_context(nc.sbuf_tensor(f"gE{tag}", [128, 8 * TPC], BF16))
    ps_wb = ectx.enter_context(nc.psum_tensor(f"pwb{tag}", [128, 512], F32))
    for e_i in range(16):
        gp.wait(gp, gp.v)
        gp.wait(te, te.v)
        gp.wait(ve, V_WTR)
        V_WRD = gp.dma(lambda e, e_i=e_i: e.dma_start(
            out=wrow[0:1, :], in_=wTr[e_i:e_i + 1, :]))
        for w2 in range(2):
            te.wait(gp, V_WRD)
            te.wait(ve, max(V_OBR, ve.v))
            te.emit(lambda e, w2=w2: e.matmul(
                ps_wb[:, :], onebr[0:1, :],
                wrow[0:1, w2 * 512:(w2 + 1) * 512], start=True, stop=True))
            ve.wait(te, te.v)
            ve.emit(lambda e, w2=w2: e.tensor_copy(
                wbc[:, w2 * 512:(w2 + 1) * 512], ps_wb[:, :]))
        V_WBC = ve.v
        emit_ffcols(p, nc, ectx, xh, TPC,
                    gwD[e_i * D:(e_i + 1) * D, :], gE, TPC, AF.Silu, 8,
                    tag=f"{tag}G{e_i}")
        te.wait(sc, sc.v)
        emit_ffcols(p, nc, ectx, xh, TPC,
                    uwD[e_i * D:(e_i + 1) * D, :], None, TPC, None, 8,
                    tag=f"{tag}U{e_i}", addto=gE, addW=TPC, multo=True)
        ve.wait(ve, ve.v)
        for fc in range(8):
            ve.wait(ve, ve.v)
            ve.emit(lambda e, fc=fc: e.tensor_mul(
                gE[:, fc * TPC:(fc + 1) * TPC],
                gE[:, fc * TPC:(fc + 1) * TPC], wbc[:, :]))
        V_HS = ve.v
        for eng in (te, gp):
            eng.wait(ve, V_HS)
        emit_ffcols(p, nc, ectx, gE, TPC,
                    dwD[e_i * TPC:(e_i + 1) * TPC, :], None, TPC, None, NCH,
                    ncc=8, tag=f"{tag}D{e_i}", addto=xsb, addW=TPC)
    V_OUT = ve.v
    ectx.close()
    fctx.close()
    return V_OUT




def emit_A1(p, nc, ctx, t, ro=0, PFX="a0"):
    xT = t["XT_A"]
    wqkvT = t["wqkvT"]
    mask = t["maskA"]
    identb = t["identb"]
    onescol = t["onescolA"]
    onesrow = t["onesrow"]
    att = t["att2"]
    dq = t["dq_h"]
    dk = t["dk_h"]
    dv = t["dv_h"]
    gp, te, sc, ve, sy = p.gp, p.te, p.sc, p.ve, p.sy
    sy, gp, ve, sc, te = p.sy, p.gp, p.ve, p.sc, p.te
    ones_c = ctx.enter_context(nc.sbuf_tensor(PFX + "onesc", [128, 2], F32))
    ones_r = ctx.enter_context(nc.sbuf_tensor(PFX + "onesr", [1, 128], F32))
    ident_bf = ctx.enter_context(nc.sbuf_tensor(PFX + "identbf", [128, 128], BF16))
    mask_sb = ctx.enter_context(nc.sbuf_tensor(PFX + "masksb", [128, 128], F32))
    xhat = ctx.enter_context(nc.sbuf_tensor(PFX + "xhat", [128, NCH * TPC], BF16))
    consts = {"ones_col": ones_c[:, 0:1], "ones_row": ones_r,
              "eps": ones_c[:, 1:2]}

    p1 = ctx.enter_context(contextlib.ExitStack())
    xsb = p1.enter_context(nc.sbuf_tensor(PFX + "xsb", [128, NCH * TPC], F32))

    gp.dma(lambda e: e.dma_start(
        out=xsb.rearrange("p (cc t) -> p cc t", cc=NCH),
        in_=xT.rearrange("(cc p) t -> p cc t", p=128)))
    gp.dma(lambda e: e.dma_start(out=ones_c[:, :], in_=onescol))
    gp.dma(lambda e: e.dma_start(out=ones_r[:, :], in_=onesrow))
    gp.dma(lambda e: e.dma_start(out=ident_bf[:, :], in_=identb))
    V_IN = gp.dma(lambda e: e.dma_start(out=mask_sb[:, :], in_=mask))

    V_LN, _ = emit_ln(p, nc, xsb, TPC, consts, xhat, [(gp, V_IN)],
                      tag=PFX, alloc_ctx=p1)
    p1.close()

    p2 = ctx.enter_context(contextlib.ExitStack())
    wbuf = p2.enter_context(nc.sbuf_tensor(PFX + "wbuf", [128, 2 * 512], BF16))
    stage = p2.enter_context(nc.sbuf_tensor(PFX + "stage", [128, 8 * 512], BF16))
    ps_qkv = [p2.enter_context(nc.psum_tensor(f"{PFX}psq{i}", [128, 512], F32))
              for i in range(8)]
    NFG = 12
    GP0, TE0, SC0, SY0 = gp.v, te.v, sc.v, sy.v
    def te_after(fg, cc):
        return TE0 + 8 * (fg * NCH + cc + 1)

    gp.wait(ve, V_LN)
    for fg in range(NFG):
        for cc in range(NCH):
            i = fg * NCH + cc
            if i >= 2:
                fg2, cc2 = divmod(i - 2, NCH)
                gp.wait(te, te_after(fg2, cc2))
            gp.dma(lambda e, fg=fg, cc=cc, i=i: e.dma_start(
                out=wbuf[:, (i % 2) * 512:(i % 2 + 1) * 512],
                in_=wqkvT[cc * 128:(cc + 1) * 128,
                          fg * 512:(fg + 1) * 512]))

    te.wait(ve, V_LN)
    for fg in range(NFG):
        if fg >= 1:
            te.wait(sc, SC0 + 8 * fg)
        for cc in range(NCH):
            i = fg * NCH + cc
            te.wait(gp, GP0 + 16 * min(i + 2, NFG * NCH))
            for tc in range(8):
                te.emit(lambda e, i=i, cc=cc, tc=tc: e.matmul(
                    ps_qkv[tc][:, :],
                    xhat[:, cc * TPC + tc * 128: cc * TPC + (tc + 1) * 128],
                    wbuf[:, (i % 2) * 512:(i % 2 + 1) * 512],
                    start=(cc == 0), stop=(cc == NCH - 1)))
            assert te.v == te_after(fg, cc)

    for fg in range(NFG):
        sc.wait(te, te_after(fg, NCH - 1))
        sc.wait(sy, SY0 + 16 * fg)
        for tc in range(8):
            sc.emit(lambda e, tc=tc: e.activation(
                stage[:, tc * 512:(tc + 1) * 512], ps_qkv[tc][:, :],
                AF.Identity))
        assert sc.v == SC0 + 8 * (fg + 1)

    for fg in range(NFG):
        sy.wait(sc, SC0 + 8 * (fg + 1))
        sy.wait(sy, SY0 + 16 * fg)
        tgt = [dq, dk, dv][fg // 4]
        col = (fg % 4) * 512
        sy.dma(lambda e, tgt=tgt, col=col: e.dma_start(
            out=tgt.ap()[0:TPC, col:col + 512].rearrange(
                "(tc p) f -> p tc f", p=128),
            in_=stage.rearrange("p (tc f) -> p tc f", f=512)))
    V_ST = sy.v
    assert V_ST == SY0 + 16 * NFG
    p2.close()

    p3 = ctx.enter_context(contextlib.ExitStack())
    qTs = p3.enter_context(nc.sbuf_tensor(PFX + "qTs", [128, 16 * TPC], BF16))
    kTs = p3.enter_context(nc.sbuf_tensor(PFX + "kTs", [128, 16 * TPC], BF16))
    vS = p3.enter_context(nc.sbuf_tensor(PFX + "vS", [128, 16 * TPC], BF16))
    ps_sc = p3.enter_context(nc.psum_tensor(PFX + "pssc", [128, 128], F32))
    ps_tr = p3.enter_context(nc.psum_tensor(PFX + "pstr", [128, 128], BF16))
    ps_o = p3.enter_context(nc.psum_tensor(PFX + "pso", [128, 128], F32))
    scf = p3.enter_context(nc.sbuf_tensor(PFX + "scf", [128, 128], F32))
    expb = p3.enter_context(nc.sbuf_tensor(PFX + "expb", [128, 128], BF16))
    expT = p3.enter_context(nc.sbuf_tensor(PFX + "expT", [128, 128], BF16))
    den = p3.enter_context(nc.sbuf_tensor(PFX + "den", [128, 2], F32))
    ostg = p3.enter_context(nc.sbuf_tensor(PFX + "ostg", [128, 16 * 128], BF16))
    NG = 16 * TPC // 128

    sy.wait(sy, V_ST)
    sy.dma(lambda e: e.dma_start_transpose(
        qTs[:, :], dq.ap().rearrange("t (i d) -> (t i) d", d=128)))
    sy.dma(lambda e: e.dma_start_transpose(
        kTs[:, :], dk.ap().rearrange("t (i d) -> (t i) d", d=128)))
    for tp in range(8):
        sy.dma(lambda e, tp=tp: e.dma_start(
            out=vS[tp * 16:(tp + 1) * 16, :].rearrange(
                "i (g d) -> i g d", d=128),
            in_=bass.AP(dv, tp * D, [[128, 16], [8 * D, 128], [1, 128]])))
    V_TR = sy.v

    TE1, VE1, SC1, SY1 = te.v, ve.v, sc.v, sy.v
    te.wait(sy, V_TR)
    for g in range(NG):
        if g >= 1:
            te.wait(ve, VE1 + 3 * g)
        te.emit(lambda e, g=g: e.matmul(
            ps_sc[:, :], qTs[:, g * 128:(g + 1) * 128],
            kTs[:, g * 128:(g + 1) * 128], start=True, stop=True))
        te.wait(sc, SC1 + 2 * g + 1)
        te.emit(lambda e: e.transpose(ps_tr[:, :], expb[:, :],
                                      ident_bf[:, :]))
        te.wait(ve, VE1 + 3 * g + 2)
        te.emit(lambda e, g=g: e.matmul(
            ps_o[:, :], expT[:, :], vS[:, g * 128:(g + 1) * 128],
            start=True, stop=True))

    for g in range(NG):
        ve.wait(te, TE1 + 3 * g + 1)
        if g >= 1:
            ve.wait(sc, SC1 + 2 * g)
        ve.emit(lambda e: e.scalar_tensor_tensor(
            scf[:, :], ps_sc[:, :], RS, mask_sb[:, :], ALU.mult, ALU.add))
        ve.wait(te, TE1 + 3 * g + 2)
        ve.emit(lambda e: e.tensor_copy(expT[:, :], ps_tr[:, :]))
        ve.wait(ve, VE1 + 3 * g + 2)
        ve.wait(sc, SC1 + 2 * g + 1)
        ve.emit(lambda e: e.reciprocal(den[:, 1:2], den[:, 0:1]))

    for g in range(NG):
        sc.wait(ve, VE1 + 3 * g + 1)
        if g >= 1:
            sc.wait(te, TE1 + 3 * g - 1)
            sc.wait(ve, VE1 + 3 * g)
        sc.emit(lambda e: e.activation(expb[:, :], scf[:, :], AF.Exp,
                                       accum_out=den[:, 0:1]))
        sc.wait(te, TE1 + 3 * g + 3)
        sc.wait(ve, VE1 + 3 * g + 3)
        if g >= 16:
            sc.wait(sy, SY1 + 16 * (g // 16))
        sc.emit(lambda e, g=g: e.activation(
            ostg[:, (g % 16) * 128:(g % 16 + 1) * 128], ps_o[:, :],
            AF.Identity, scale=den[:, 1:2]))

    for blk in range(NG // 16):
        sy.wait(sc, SC1 + 2 * 16 * (blk + 1))
        sy.wait(sy, V_TR + 16 * blk)
        sy.dma(lambda e, blk=blk: e.dma_start(
            out=att[(ro + blk) * 2048:(ro + blk + 1) * 2048, :].rearrange(
                "(g p) d -> p g d", p=128),
            in_=ostg.rearrange("p (g d) -> p g d", d=128)))
    sy.wait(sy, sy.v)
    p3.close()



def emit_F1(p, nc, ctx, t):
    xT = t["xTme"]
    rowsT = t["rowsT"]
    wprojT = t["wprojT"]
    w1T = t["w1T"]
    w3T = t["w3T"]
    w2T = t["w2T"]
    wrT = t["wrT"]
    onescol = t["onescol"]
    onesrow = t["onesrow"]
    x1T = t["x1T"]
    part = t["part1"]
    gp, te, sc, ve, sy = p.gp, p.te, p.sc, p.ve, p.sy
    gp, te, sc, ve, sy = p.gp, p.te, p.sc, p.ve, p.sy
    ones_c = ctx.enter_context(nc.sbuf_tensor("f1onesc", [128, 4], F32))
    ones_r = ctx.enter_context(nc.sbuf_tensor("f1onesr", [1, 128], F32))
    xsb = ctx.enter_context(nc.sbuf_tensor("f1xsb", [128, NCH * TPC], F32))
    consts = {"ones_col": ones_c[:, 0:1], "ones_row": ones_r,
              "eps": ones_c[:, 1:2], "one1": ones_c,
              "ones_bf": None}

    gp.dma(lambda e: e.dma_start(out=ones_c[:, :], in_=onescol))
    gp.dma(lambda e: e.dma_start(out=ones_r[:, :], in_=onesrow))
    gp.dma(lambda e: e.dma_start(
        out=xsb.rearrange("p (cc t) -> p cc t", cc=NCH),
        in_=xT.rearrange("(cc p) t -> p cc t", p=128)))
    V_IN = gp.dma(lambda e: e.dma_start(
        out=rsb.rearrange("p (cc t) -> p cc t", cc=NCH),
        in_=rowsT.rearrange("(cc p) t -> p cc t", p=128)))

    prctx = ctx.enter_context(contextlib.ExitStack())
    rsb = prctx.enter_context(nc.sbuf_tensor("f1rsb", [128, NCH * TPC],
                                             BF16))
    wcol = prctx.enter_context(nc.sbuf_tensor("f1wcp", [128, 2 * NCH * 128],
                                              BF16))
    psp = [prctx.enter_context(nc.psum_tensor(f"f1psp{i}", [128, 512], F32))
           for i in range(4)]
    GP0, TE0, VE0 = gp.v, te.v, ve.v

    def te_after(fc):
        return TE0 + (fc + 1) * 2 * NCH

    gp.wait(gp, V_IN)
    for fc in range(NCH):
        if fc >= 2:
            gp.wait(te, te_after(fc - 2))
        gp.dma(lambda e, fc=fc: e.dma_start(
            out=wcol.rearrange("p (s cc f) -> p s cc f", s=2, f=128)
                [:, fc % 2],
            in_=wprojT[:, fc * 128:(fc + 1) * 128].rearrange(
                "(cc p) f -> p cc f", p=128)))
    te.wait(gp, V_IN)
    for fc in range(NCH):
        te.wait(gp, GP0 + 16 * min(fc + 2, NCH))
        if fc >= 2:
            te.wait(ve, VE0 + 2 * (fc - 1))
        for tt in range(2):
            for cc in range(NCH):
                te.emit(lambda e, fc=fc, tt=tt, cc=cc: e.matmul(
                    psp[(fc % 2) * 2 + tt][:, :],
                    wcol[:, (fc % 2) * NCH * 128 + cc * 128:
                         (fc % 2) * NCH * 128 + (cc + 1) * 128],
                    rsb[:, cc * TPC + tt * 512: cc * TPC + (tt + 1) * 512],
                    start=(cc == 0), stop=(cc == NCH - 1)))
    for fc in range(NCH):
        ve.wait(te, TE0 + fc * 2 * NCH + NCH)
        ve.emit(lambda e, fc=fc: e.tensor_add(
            xsb[:, fc * TPC: fc * TPC + 512], psp[(fc % 2) * 2][:, :],
            xsb[:, fc * TPC: fc * TPC + 512]))
        ve.wait(te, TE0 + (fc + 1) * 2 * NCH)
        ve.emit(lambda e, fc=fc: e.tensor_add(
            xsb[:, fc * TPC + 512: fc * TPC + 1024],
            psp[(fc % 2) * 2 + 1][:, :],
            xsb[:, fc * TPC + 512: fc * TPC + 1024]))
    V_X1 = ve.v
    assert V_X1 == VE0 + 2 * NCH
    prctx.close()

    sy.wait(ve, V_X1)
    V_X1ST = sy.dma(lambda e: e.dma_start(
        out=x1T.rearrange("(cc p) t -> p cc t", p=128),
        in_=xsb.rearrange("p (cc t) -> p cc t", cc=NCH)))

    emit_ffn0(p, nc, ctx, xsb, consts, w1T, w3T, w2T, wrT, part,
              [(ve, V_X1), (gp, V_IN)], tag="n")
    sy.wait(sy, sy.v)


def emit_B1(p, nc, ctx, t):
    xT = t["x1T"]
    scat = t["scat8a"]
    haloT = t["haloT"]
    wqTp = t["wqTp"]
    wkTp = t["wkTp"]
    wvT = t["wvT"]
    woD = t["woD"]
    tabQ = t["tabQ"]
    tabK = t["tabK"]
    maskB = t["maskB"]
    identb = t["identb"]
    w1T = t["w1T"]
    w3T = t["w3T"]
    w2T = t["w2T"]
    wrT = t["wrT"]
    onescol = t["onescol"]
    onesrow = t["onesrow"]
    x3T = t["x3T"]
    part = t["part2"]
    dxx = t["dxx"]
    datt = t["datt"]
    KL = 1280
    gp, te, sc, ve, sy = p.gp, p.te, p.sc, p.ve, p.sy
    gp, te, sc, ve, sy = p.gp, p.te, p.sc, p.ve, p.sy
    ones_c = ctx.enter_context(nc.sbuf_tensor("b1onesc", [128, 4], F32))
    ones_r = ctx.enter_context(nc.sbuf_tensor("b1onesr", [1, 128], F32))
    ones_b = ctx.enter_context(nc.sbuf_tensor("b1onesb", [128, 1], BF16))
    idb = ctx.enter_context(nc.sbuf_tensor("b1idb", [128, 128], BF16))
    consts = {"ones_col": ones_c[:, 0:1], "ones_row": ones_r,
              "eps": ones_c[:, 1:2], "one1": ones_c}
    gp.dma(lambda e: e.dma_start(out=ones_c[:, :], in_=onescol))
    gp.dma(lambda e: e.dma_start(out=ones_r[:, :], in_=onesrow))
    V_C = gp.dma(lambda e: e.dma_start(out=idb[:, :], in_=identb))
    ve.wait(gp, V_C)
    V_OB = ve.emit(lambda e: e.tensor_copy(ones_b[:, :], ones_c[:, 0:1]))

    ph1 = ctx.enter_context(contextlib.ExitStack())
    xhB = ph1.enter_context(nc.sbuf_tensor("b1xhB", [128, NCH * KL], BF16))
    ph1b = ph1.enter_context(contextlib.ExitStack())
    xsb = ph1b.enter_context(nc.sbuf_tensor("b1xsb", [128, NCH * TPC], F32))
    xhalo = ph1b.enter_context(nc.sbuf_tensor("b1xhalo", [128, NCH * 256],
                                              F32))
    ssb = ph1b.enter_context(nc.sbuf_tensor("b1ssb", [128, NCH * 8], F32))
    sred = ph1b.enter_context(nc.sbuf_tensor("b1sred", [128, 8], F32))
    gp.wait(gp, gp.v)
    gp.dma(lambda e: e.dma_start(
        out=xsb.rearrange("p (cc t) -> p cc t", cc=NCH),
        in_=xT.rearrange("(cc p) t -> p cc t", p=128)))
    gp.dma(lambda e: e.dma_start(
        out=xhalo.rearrange("p (cc t) -> p cc t", cc=NCH),
        in_=haloT.rearrange("(cc p) t -> p cc t", p=128)))
    V_XIN = gp.dma(lambda e: e.dma_start(
        out=ssb.rearrange("p (cc t) -> p cc t", cc=NCH),
        in_=scat.rearrange("(cc p) t -> p cc t", p=128)))
    ve.wait(gp, V_XIN)
    V_SC = None
    for cc in range(NCH):
        ve.wait(ve, ve.v)
        ve.emit(lambda e, cc=cc: e.tensor_copy(
            sred[:, :],
            ssb.rearrange("p (cc e) -> p cc e", cc=NCH)[:, cc]))
        ve.wait(ve, ve.v)
        V_SC = ve.emit(lambda e, cc=cc: e.tensor_add(
            xsb[:, cc * TPC: cc * TPC + 8], sred[:, :],
            xsb[:, cc * TPC: cc * TPC + 8]))
    sy.wait(ve, V_SC)
    V_X2ST = sy.dma(lambda e: e.dma_start(
        out=dxx.rearrange("(cc p) t -> p cc t", p=128),
        in_=xsb.rearrange("p (cc t) -> p cc t", cc=NCH)))
    V_LN, _ = emit_ln(p, nc, xsb, TPC, consts, xhB,
                      [(ve, V_SC), (gp, V_XIN)], tag="bo",
                      alloc_ctx=ph1b, outW=KL, out_off=256)
    V_LNH, _ = emit_ln(p, nc, xhalo, 256, consts, xhB,
                       [(gp, V_XIN)], tag="bh",
                       alloc_ctx=ph1b, outW=KL, out_off=0)
    for eng in (te, sc, gp, sy):
        eng.wait(ve, V_LN)
        eng.wait(ve, V_LNH)
    sy.wait(sy, V_X2ST)
    for eng in (te, sc, gp, ve):
        eng.wait(sy, V_X2ST)
    ph1b.close()

    att_r = ctx.enter_context(contextlib.ExitStack())
    ySB = att_r.enter_context(nc.sbuf_tensor("b1ySB", [128, 16 * TPC], BF16,
                                             side="right"))
    att_r2 = att_r.enter_context(contextlib.ExitStack())
    qT = att_r2.enter_context(nc.sbuf_tensor("b1qT", [128, 16 * TPC], BF16,
                                             side="right"))
    kT = att_r2.enter_context(nc.sbuf_tensor("b1kT", [128, 16 * KL], BF16,
                                             side="right"))
    vS = att_r2.enter_context(nc.sbuf_tensor("b1vS", [128, 160 * 128], BF16,
                                             side="right"))
    msk = att_r2.enter_context(nc.sbuf_tensor("b1msk", [128, 16 * 256], BF16,
                                              side="right"))
    gp.wait(gp, gp.v)
    V_MSK = gp.dma(lambda e: e.dma_start(out=msk[:, :], in_=maskB))

    ph2 = ctx.enter_context(contextlib.ExitStack())
    emit_ffcols(p, nc, ph2, xhB, KL, wqTp, qT, TPC, AF.Identity, NCH,
                tag="bq", src_off=256, L=TPC)
    emit_ffcols(p, nc, ph2, xhB, KL, wkTp, kT, KL, AF.Identity, NCH,
                tag="bk", L=KL)
    for h in range(16):
        hctx = ph2.enter_context(contextlib.ExitStack())
        vTh = hctx.enter_context(nc.sbuf_tensor(f"b1vTh{h}", [128, KL],
                                                BF16))
        ps_vt = hctx.enter_context(nc.psum_tensor(f"b1pvt{h}", [128, 128],
                                                  BF16))
        emit_ffcols(p, nc, hctx, xhB, KL,
                    wvT[:, h * 128:(h + 1) * 128], vTh, KL,
                    AF.Identity, 1, tag=f"bv{h}", L=KL)
        te.wait(sc, sc.v)
        te.wait(ve, V_OB)
        for tc in range(10):
            te.wait(ve, ve.v)
            te.emit(lambda e, tc=tc, ps_vt=ps_vt, vTh=vTh: e.transpose(
                ps_vt[:, :], vTh[:, tc * 128:(tc + 1) * 128], idb[:, :]))
            ve.wait(te, te.v)
            ve.emit(lambda e, h=h, tc=tc, ps_vt=ps_vt: e.tensor_copy(
                vS[:, (tc * 16 + h) * 128:(tc * 16 + h) * 128 + 128],
                ps_vt[:, :]))
        te.wait(ve, ve.v)
        hctx.close()
    V_VS = ve.v
    V_TEP2 = te.v
    ph2.close()
    ph1.close()

    ph3 = ctx.enter_context(contextlib.ExitStack())
    tq = ph3.enter_context(nc.sbuf_tensor("b1tq", [128, 2 * TPC], F32))
    tk = ph3.enter_context(nc.sbuf_tensor("b1tk", [128, 2 * KL], F32))
    scr = ph3.enter_context(nc.sbuf_tensor("b1scr", [64, 3 * KL], F32))
    gp.wait(gp, gp.v)
    gp.wait(te, V_TEP2)
    gp.wait(ve, V_VS)
    gp.wait(sc, sc.v)
    gp.dma(lambda e: e.dma_start(out=tq[:, :], in_=tabQ))
    V_TAB = gp.dma(lambda e: e.dma_start(out=tk[:, :], in_=tabK))
    ve.wait(gp, V_TAB)
    ve.wait(sc, sc.v)

    def rope(buf, tab, W):
        cos0 = tab[0:64, 0:W]
        cos1 = tab[64:128, 0:W]
        sin0 = tab[0:64, W:2 * W]
        sin1 = tab[64:128, W:2 * W]
        for h in range(16):
            qe = buf[:, h * W:(h + 1) * W][0:64, :]
            qo = buf[:, h * W:(h + 1) * W][64:128, :]
            s1 = scr[0:64, 0:W]
            s2 = scr[0:64, KL:KL + W]
            s3 = scr[0:64, 2 * KL:2 * KL + W]
            ve.wait(ve, ve.v)
            ve.emit(lambda e, qe=qe, c=cos0, s1=s1: e.tensor_mul(
                s1, qe, c))
            ve.emit(lambda e, qo=qo, sn=sin1, s2=s2: e.tensor_mul(
                s2, qo, sn))
            ve.emit(lambda e, qe=qe, sn=sin0, s3=s3: e.tensor_mul(
                s3, qe, sn))
            ve.wait(ve, ve.v)
            ve.emit(lambda e, qe=qe, s1=s1, s2=s2: e.tensor_sub(
                qe, s1, s2))
            ve.wait(ve, ve.v)
            ve.emit(lambda e, qo=qo, c=cos1, s2=s2: e.tensor_mul(
                s2, qo, c))
            ve.wait(ve, ve.v)
            ve.emit(lambda e, qo=qo, s2=s2, s3=s3: e.tensor_add(
                qo, s3, s2))
    rope(qT, tq, TPC)
    rope(kT, tk, KL)
    V_ROPE = ve.v
    ph3.close()

    ph4 = ctx.enter_context(contextlib.ExitStack())
    exB = ph4.enter_context(nc.sbuf_tensor("b1exB", [128, 4 * 256], BF16))
    rbc = ph4.enter_context(nc.sbuf_tensor("b1rbc", [128, 256], F32))
    rro = ph4.enter_context(nc.sbuf_tensor("b1rro", [1, 256], F32))
    ps_s = ph4.enter_context(nc.psum_tensor("b1psbs", [128, 256], F32))
    ps_d = ph4.enter_context(nc.psum_tensor("b1psbd", [1, 256], F32))
    ps_y = ph4.enter_context(nc.psum_tensor("b1psby", [128, 256], F32))
    ps_rb = ph4.enter_context(nc.psum_tensor("b1psrb", [128, 256], F32))
    te.wait(ve, V_ROPE)
    te.wait(gp, V_MSK)
    for ci in range(4):
        for h in range(16):
            for kc in range(4):
                te.wait(ve, ve.v)
                te.emit(lambda e, ci=ci, h=h, kc=kc: e.matmul(
                    ps_s[:, :],
                    kT[:, h * KL + ci * 256 + kc * 128:
                       h * KL + ci * 256 + (kc + 1) * 128],
                    qT[:, h * TPC + ci * 256: h * TPC + (ci + 1) * 256],
                    start=True, stop=True))
                ve.wait(te, te.v)
                ve.emit(lambda e, ci=ci, kc=kc: e.scalar_tensor_tensor(
                    exB[:, kc * 256:(kc + 1) * 256], ps_s[:, :], RS,
                    msk[:, (ci * 4 + kc) * 256:(ci * 4 + kc + 1) * 256],
                    ALU.mult, ALU.add))
                sc.wait(ve, ve.v)
                sc.emit(lambda e, kc=kc: e.activation(
                    exB[:, kc * 256:(kc + 1) * 256],
                    exB[:, kc * 256:(kc + 1) * 256], AF.Exp))
            te.wait(sc, sc.v)
            for kc in range(4):
                te.emit(lambda e, kc=kc: e.matmul(
                    ps_d[:, :], ones_b[:, 0:1],
                    exB[:, kc * 256:(kc + 1) * 256],
                    start=(kc == 0), stop=(kc == 3)))
            for kc in range(4):
                te.emit(lambda e, ci=ci, h=h, kc=kc: e.matmul(
                    ps_y[:, :],
                    vS[:, ((ci * 2 + kc) * 16 + h) * 128:
                       ((ci * 2 + kc) * 16 + h) * 128 + 128],
                    exB[:, kc * 256:(kc + 1) * 256],
                    start=(kc == 0), stop=(kc == 3)))
            V_YMM = te.v
            ve.wait(te, V_YMM)
            ve.emit(lambda e: e.reciprocal(rro[0:1, :], ps_d[:, :]))
            te.wait(ve, ve.v)
            V_RB = te.emit(lambda e: e.matmul(
                ps_rb[:, :], ones_r[0:1, :], rro[0:1, :],
                start=True, stop=True))
            ve.wait(te, V_RB)
            ve.emit(lambda e: e.tensor_copy(rbc[:, :], ps_rb[:, :]))
            ve.wait(ve, ve.v)
            ve.emit(lambda e, ci=ci, h=h: e.tensor_mul(
                ySB[:, h * TPC + ci * 256: h * TPC + (ci + 1) * 256],
                ps_y[:, :], rbc[:, :]))
    V_ATT = ve.v
    ph4.close()

    ph5a = ctx.enter_context(contextlib.ExitStack())
    atout = ph5a.enter_context(nc.sbuf_tensor("b1atout", [128, NCH * TPC],
                                              BF16))
    for eng in (te, sc, gp):
        eng.wait(ve, V_ATT)
    emit_ffcols(p, nc, ph5a, ySB, TPC, woD, atout, TPC, AF.Identity, NCH,
                tag="bw")
    V_WOS = sc.v
    sy.wait(sc, V_WOS)
    sy.wait(sy, sy.v)
    V_ATST = sy.dma(lambda e: e.dma_start(
        out=datt.rearrange("(cc p) t -> p cc t", p=128),
        in_=atout.rearrange("p (cc t) -> p cc t", cc=NCH)))
    ph5a.close()
    att_r2.close()
    att_r.close()

    ph5 = ctx.enter_context(contextlib.ExitStack())
    xsb2 = ph5.enter_context(nc.sbuf_tensor("b1xsb2", [128, NCH * TPC], F32))
    ph5c = ph5.enter_context(contextlib.ExitStack())
    attin = ph5c.enter_context(nc.sbuf_tensor("b1attin", [128, NCH * TPC],
                                              BF16))
    gp.wait(gp, gp.v)
    gp.wait(sy, V_ATST)
    gp.wait(sc, sc.v)
    gp.wait(te, te.v)
    gp.dma(lambda e: e.dma_start(
        out=xsb2.rearrange("p (cc t) -> p cc t", cc=NCH),
        in_=dxx.rearrange("(cc p) t -> p cc t", p=128)))
    V_RL = gp.dma(lambda e: e.dma_start(
        out=attin.rearrange("p (cc t) -> p cc t", cc=NCH),
        in_=datt.rearrange("(cc p) t -> p cc t", p=128)))
    ve.wait(gp, V_RL)
    V_X3 = None
    for cc in range(NCH):
        ve.wait(ve, ve.v)
        V_X3 = ve.emit(lambda e, cc=cc: e.tensor_add(
            xsb2[:, cc * TPC:(cc + 1) * TPC],
            attin[:, cc * TPC:(cc + 1) * TPC],
            xsb2[:, cc * TPC:(cc + 1) * TPC]))
    ph5c.close()
    sy.wait(ve, V_X3)
    sy.dma(lambda e: e.dma_start(
        out=x3T.rearrange("(cc p) t -> p cc t", p=128),
        in_=xsb2.rearrange("p (cc t) -> p cc t", cc=NCH)))
    emit_ffn0(p, nc, ph5, xsb2, consts, w1T, w3T, w2T, wrT, part,
              [(ve, V_X3)], tag="c")
    sy.wait(sy, sy.v)
    ph5.close()


def emit_M1(p, nc, ctx, t):
    xT = t["x3T"]
    scat = t["scat8b"]
    onescol = t["onescol"]
    onesrow = t["onesrow"]
    x4T = t["x4T"]
    xhT = t["xhT"]
    gp, te, sc, ve, sy = p.gp, p.te, p.sc, p.ve, p.sy
    gp, te, sc, ve, sy = p.gp, p.te, p.sc, p.ve, p.sy
    ones_c = ctx.enter_context(nc.sbuf_tensor("m1onesc", [128, 4], F32))
    ones_r = ctx.enter_context(nc.sbuf_tensor("m1onesr", [1, 128], F32))
    consts = {"ones_col": ones_c[:, 0:1], "ones_row": ones_r,
              "eps": ones_c[:, 1:2], "one1": ones_c}
    xsb = ctx.enter_context(nc.sbuf_tensor("m1xsb", [128, NCH * TPC], F32))
    xh = ctx.enter_context(nc.sbuf_tensor("m1xh", [128, NCH * TPC], F32))
    ssb = ctx.enter_context(nc.sbuf_tensor("m1ssb", [128, NCH * 8], F32))
    sred = ctx.enter_context(nc.sbuf_tensor("m1sred", [128, 8], F32))
    gp.dma(lambda e: e.dma_start(out=ones_c[:, :], in_=onescol))
    gp.dma(lambda e: e.dma_start(out=ones_r[:, :], in_=onesrow))
    gp.dma(lambda e: e.dma_start(
        out=xsb.rearrange("p (cc t) -> p cc t", cc=NCH),
        in_=xT.rearrange("(cc p) t -> p cc t", p=128)))
    V_XIN = gp.dma(lambda e: e.dma_start(
        out=ssb.rearrange("p (cc t) -> p cc t", cc=NCH),
        in_=scat.rearrange("(cc p) t -> p cc t", p=128)))
    ve.wait(gp, V_XIN)
    V_SC = None
    for cc in range(NCH):
        ve.wait(ve, ve.v)
        ve.emit(lambda e, cc=cc: e.tensor_copy(
            sred[:, :],
            ssb.rearrange("p (cc e) -> p cc e", cc=NCH)[:, cc]))
        ve.wait(ve, ve.v)
        V_SC = ve.emit(lambda e, cc=cc: e.tensor_add(
            xsb[:, cc * TPC: cc * TPC + 8], sred[:, :],
            xsb[:, cc * TPC: cc * TPC + 8]))
    sy.wait(ve, V_SC)
    sy.dma(lambda e: e.dma_start(
        out=x4T.rearrange("(cc p) t -> p cc t", p=128),
        in_=xsb.rearrange("p (cc t) -> p cc t", cc=NCH)))
    V_LN, _ = emit_ln(p, nc, xsb, TPC, consts, xh,
                      [(ve, V_SC), (gp, V_XIN)], tag="m", alloc_ctx=ctx)
    sy.wait(ve, V_LN)
    sy.dma(lambda e: e.dma_start(
        out=xhT.rearrange("(cc p) t -> p cc t", p=128),
        in_=xh.rearrange("p (cc t) -> p cc t", cc=NCH)))
    sy.wait(sy, sy.v)


def emit_M2(p, nc, ctx, t):
    xh = t["xhP"]
    cw = t["cw"]
    cb = t["cbI"]
    xpT = t["xpT"]
    dtwT = t["dtwT"]
    dtb = t["dtb"]
    Ain = t["Ain"]
    Dvin = t["Dvin"]
    onescol = t["onescol"]
    onesrow = t["onesrow"]
    yout = t["yout"]
    TH = 1024
    gp, te, sc, ve, sy = p.gp, p.te, p.sc, p.ve, p.sy
    gp, te, sc, ve, sy = p.gp, p.te, p.sc, p.ve, p.sy
    ones_c = ctx.enter_context(nc.sbuf_tensor("m2onesc", [128, 4], F32))
    ones_r = ctx.enter_context(nc.sbuf_tensor("m2onesr", [1, 128], F32))
    bst = ctx.enter_context(nc.sbuf_tensor("m2bst", [1, 2 * TH], F32))
    cwsb = ctx.enter_context(nc.sbuf_tensor("m2cwsb", [128, NCH * 4], F32))
    cbsb = ctx.enter_context(nc.sbuf_tensor("m2cbsb", [128, NCH], F32))
    xps = ctx.enter_context(nc.sbuf_tensor("m2xps", [128, NCH * 96], BF16))
    dtw = ctx.enter_context(nc.sbuf_tensor("m2dtw", [64, TPC], BF16))
    dtbs = ctx.enter_context(nc.sbuf_tensor("m2dtbs", [128, 8], F32))
    Acol = ctx.enter_context(nc.sbuf_tensor("m2Acol", [128, 8 * 16], F32))
    Dcol = ctx.enter_context(nc.sbuf_tensor("m2Dcol", [128, 8], F32))
    carry = ctx.enter_context(nc.sbuf_tensor("m2carry", [128, 128], F32))
    xcN = ctx.enter_context(nc.sbuf_tensor("m2xcN", [128, 8 * TH], F32))
    dltN = ctx.enter_context(nc.sbuf_tensor("m2dltN", [128, 8 * TH], F32))
    duN = ctx.enter_context(nc.sbuf_tensor("m2duN", [128, 8 * TH], F32))
    yac = ctx.enter_context(nc.sbuf_tensor("m2yac", [128, 8 * TH], F32))
    Bbc = ctx.enter_context(nc.sbuf_tensor("m2Bbc", [128, TH], F32))
    Cbc = ctx.enter_context(nc.sbuf_tensor("m2Cbc", [128, TH], F32))
    htl = ctx.enter_context(nc.sbuf_tensor("m2htl", [128, TH], F32))
    ttl = ctx.enter_context(nc.sbuf_tensor("m2ttl", [128, TH], F32))
    atl = ctx.enter_context(nc.sbuf_tensor("m2atl", [128, TH], F32))
    xdb = ctx.enter_context(nc.sbuf_tensor("m2xdb", [96, TH], F32))
    xdbb = ctx.enter_context(nc.sbuf_tensor("m2xdbb", [64, TH], BF16))
    pad = ctx.enter_context(nc.sbuf_tensor("m2pad", [128, TH + 3], F32))
    xct = ctx.enter_context(nc.sbuf_tensor("m2xct", [128, TH], F32))
    xcb = ctx.enter_context(nc.sbuf_tensor("m2xcb", [128, TH], BF16))
    ps_xd = [ctx.enter_context(nc.psum_tensor(f"m2pxd{i}", [96, 512], F32))
             for i in range(2)]
    ps_dl = [ctx.enter_context(nc.psum_tensor(f"m2pdl{i}", [128, 512], F32))
             for i in range(2)]
    ps_bc = [ctx.enter_context(nc.psum_tensor(f"m2pbc{i}", [128, 512], F32))
             for i in range(4)]
    spl = ctx.enter_context(nc.sbuf_tensor("m2spl", [128, 2 * 512], F32))

    gp.dma(lambda e: e.dma_start(out=ones_c[:, :], in_=onescol))
    gp.dma(lambda e: e.dma_start(out=ones_r[:, :], in_=onesrow))
    gp.dma(lambda e: e.dma_start(
        out=cwsb.rearrange("p (cc k) -> p cc k", k=4),
        in_=cw.rearrange("(cc p) k -> p cc k", p=128)))
    gp.dma(lambda e: e.dma_start(out=cbsb[:, :], in_=cb))
    gp.dma(lambda e: e.dma_start(
        out=xps.rearrange("p (cc f) -> p cc f", f=96),
        in_=xpT.rearrange("(cc p) f -> p cc f", p=128)))
    gp.dma(lambda e: e.dma_start(out=dtw[:, :], in_=dtwT))
    gp.dma(lambda e: e.dma_start(out=dtbs[:, :], in_=dtb))
    gp.dma(lambda e: e.dma_start(out=Acol[:, :], in_=Ain))
    V_W = gp.dma(lambda e: e.dma_start(out=Dcol[:, :], in_=Dvin))
    ve.wait(gp, V_W)
    V_CZ = ve.emit(lambda e: e.memset(carry[:, :], 0.0))
    sc.wait(gp, V_W)
    V_AE = sc.emit(lambda e: e.activation(Acol[:, :], Acol[:, :], AF.Exp))
    ve.wait(sc, V_AE)
    V_AN = ve.emit(lambda e: e.tensor_scalar(Acol[:, :], Acol[:, :], -1.0,
                                             None, ALU.mult))
    sc.wait(ve, V_AN)

    for th in range(2):
        GP_TH = gp.v
        for cc in range(NCH):
            tgt = xcN[:, cc * TH:(cc + 1) * TH] if cc < 8 else xct[:, :]
            gp.wait(ve, ve.v)
            gp.wait(sc, sc.v)
            gp.wait(gp, gp.v)
            if th == 0:
                ve.wait(gp, gp.v)
                ve.emit(lambda e: e.memset(pad[:, 0:3], 0.0))
                V_PD = gp.dma(lambda e, cc=cc: e.dma_start(
                    out=pad[:, 3:3 + TH],
                    in_=xh[cc * 128:(cc + 1) * 128, 0:TH]))
            else:
                V_PD = gp.dma(lambda e, cc=cc: e.dma_start(
                    out=pad[:, 0:3 + TH],
                    in_=xh[cc * 128:(cc + 1) * 128, TH - 3:T]))
            ve.wait(gp, V_PD)
            ve.wait(ve, ve.v)
            ve.wait(sc, sc.v)
            ve.wait(te, te.v)
            ve.emit(lambda e, cc=cc: e.tensor_scalar(
                ttl[:, :], pad[:, 0:TH], cwsb[:, 4 * cc:4 * cc + 1], None,
                ALU.mult))
            for k in range(1, 4):
                ve.wait(ve, ve.v)
                ve.emit(lambda e, cc=cc, k=k: e.scalar_tensor_tensor(
                    ttl[:, :], pad[:, k:k + TH],
                    cwsb[:, 4 * cc + k:4 * cc + k + 1],
                    ttl[:, :], ALU.mult, ALU.add))
            sc.wait(ve, ve.v)
            sc.wait(te, te.v)
            V_SIL = sc.emit(lambda e, cc=cc, tgt=tgt: e.activation(
                tgt, ttl[:, :], AF.Silu, bias=cbsb[:, cc:cc + 1]))
            sc.wait(sc, V_SIL)
            V_XCB = sc.emit(lambda e, cc=cc, tgt=tgt: e.activation(
                xcb[:, :], tgt, AF.Identity))
            te.wait(sc, V_XCB)
            for w2 in range(2):
                te.emit(lambda e, cc=cc, w2=w2: e.matmul(
                    ps_xd[w2][:, :], xps[:, cc * 96:(cc + 1) * 96],
                    xcb[:, w2 * 512:(w2 + 1) * 512],
                    start=(cc == 0), stop=(cc == NCH - 1)))
        V_XD = te.v
        ve.wait(te, V_XD)
        for w2 in range(2):
            ve.wait(ve, ve.v)
            ve.emit(lambda e, w2=w2: e.tensor_copy(
                xdb[:, w2 * 512:(w2 + 1) * 512], ps_xd[w2][:, :]))
        V_XDB = ve.v
        ve.wait(ve, V_XDB)
        V_XBB = ve.emit(lambda e: e.tensor_copy(xdbb[:, :], xdb[0:64, :]))
        te.wait(ve, V_XBB)
        for ec in range(8):
            if ec >= 1:
                te.wait(sc, sc.v)
            for w2 in range(2):
                te.emit(lambda e, ec=ec, w2=w2: e.matmul(
                    ps_dl[w2][:, :], dtw[:, ec * 128:(ec + 1) * 128],
                    xdbb[:, w2 * 512:(w2 + 1) * 512],
                    start=True, stop=True))
            sc.wait(te, te.v)
            for w2 in range(2):
                sc.wait(sc, sc.v)
                sc.wait(ve, ve.v)
                sc.emit(lambda e, ec=ec, w2=w2: e.activation(
                    spl[:, 0:512], ps_dl[w2][:, :], AF.Abs,
                    bias=dtbs[:, ec:ec + 1]))
                sc.wait(sc, sc.v)
                sc.emit(lambda e: e.activation(
                    spl[:, 0:512], spl[:, 0:512], AF.Exp, scale=-1.0))
                sc.wait(sc, sc.v)
                sc.emit(lambda e: e.activation(
                    spl[:, 0:512], spl[:, 0:512], AF.Ln,
                    bias=ones_c[:, 0:1]))
                sc.wait(sc, sc.v)
                sc.emit(lambda e, ec=ec, w2=w2: e.activation(
                    spl[:, 512:1024], ps_dl[w2][:, :], AF.Relu,
                    bias=dtbs[:, ec:ec + 1]))
                ve.wait(sc, sc.v)
                ve.wait(ve, ve.v)
                ve.emit(lambda e, ec=ec, w2=w2: e.tensor_add(
                    dltN[:, ec * TH + w2 * 512: ec * TH + (w2 + 1) * 512],
                    spl[:, 0:512], spl[:, 512:1024]))
        V_DLT = ve.v
        for ec in range(8):
            ve.wait(ve, ve.v)
            ve.emit(lambda e, ec=ec: e.tensor_mul(
                duN[:, ec * TH:(ec + 1) * TH],
                dltN[:, ec * TH:(ec + 1) * TH],
                xcN[:, ec * TH:(ec + 1) * TH]))
        V_DU = ve.v

        for n in range(16):
            gp.wait(gp, gp.v)
            gp.wait(te, te.v)
            gp.wait(ve, V_XDB)
            gp.dma(lambda e, n=n: e.dma_start(
                out=bst[0:1, 0:TH], in_=xdb[64 + n:65 + n, 0:TH]))
            V_BST = gp.dma(lambda e, n=n: e.dma_start(
                out=bst[0:1, TH:2 * TH], in_=xdb[80 + n:81 + n, 0:TH]))
            te.wait(ve, ve.v)
            te.wait(gp, V_BST)
            for w2 in range(2):
                te.emit(lambda e, w2=w2: e.matmul(
                    ps_bc[w2][:, :], ones_r[0:1, :],
                    bst[0:1, w2 * 512:(w2 + 1) * 512],
                    start=True, stop=True))
                te.emit(lambda e, w2=w2: e.matmul(
                    ps_bc[2 + w2][:, :], ones_r[0:1, :],
                    bst[0:1, TH + w2 * 512: TH + (w2 + 1) * 512],
                    start=True, stop=True))
            ve.wait(te, te.v)
            for w2 in range(2):
                ve.wait(ve, ve.v)
                ve.emit(lambda e, w2=w2: e.tensor_copy(
                    Bbc[:, w2 * 512:(w2 + 1) * 512], ps_bc[w2][:, :]))
                ve.wait(ve, ve.v)
                ve.emit(lambda e, w2=w2: e.tensor_copy(
                    Cbc[:, w2 * 512:(w2 + 1) * 512], ps_bc[2 + w2][:, :]))
            V_BCC = ve.v
            for ec in range(8):
                idx = ec * 16 + n
                sc.wait(ve, ve.v)
                sc.wait(sc, sc.v)
                V_A = sc.emit(lambda e, ec=ec, n=n: e.activation(
                    atl[:, :], dltN[:, ec * TH:(ec + 1) * TH], AF.Exp,
                    scale=Acol[:, ec * 16 + n: ec * 16 + n + 1]))
                ve.wait(sc, V_A)
                ve.wait(ve, ve.v)
                ve.emit(lambda e, ec=ec: e.tensor_mul(
                    ttl[:, :], duN[:, ec * TH:(ec + 1) * TH], Bbc[:, :]))
                ve.wait(ve, ve.v)
                ve.emit(lambda e, idx=idx: e.tensor_tensor_scan(
                    htl[:, :], atl[:, :], ttl[:, :],
                    carry[:, idx:idx + 1], ALU.mult, ALU.add))
                ve.wait(ve, ve.v)
                ve.emit(lambda e, idx=idx: e.tensor_copy(
                    carry[:, idx:idx + 1], htl[:, TH - 1:TH]))
                ve.wait(ve, ve.v)
                ve.emit(lambda e: e.tensor_mul(ttl[:, :], htl[:, :],
                                               Cbc[:, :]))
                ve.wait(ve, ve.v)
                if n == 0:
                    ve.emit(lambda e, ec=ec: e.tensor_copy(
                        yac[:, ec * TH:(ec + 1) * TH], ttl[:, :]))
                else:
                    ve.emit(lambda e, ec=ec: e.tensor_add(
                        yac[:, ec * TH:(ec + 1) * TH], ttl[:, :],
                        yac[:, ec * TH:(ec + 1) * TH]))
        for ec in range(8):
            ve.wait(ve, ve.v)
            ve.emit(lambda e, ec=ec: e.scalar_tensor_tensor(
                yac[:, ec * TH:(ec + 1) * TH],
                xcN[:, ec * TH:(ec + 1) * TH], Dcol[:, ec:ec + 1],
                yac[:, ec * TH:(ec + 1) * TH], ALU.mult, ALU.add))
        V_Y = ve.v
        sy.wait(ve, V_Y)
        sy.wait(sy, sy.v)
        sy.dma(lambda e, th=th: e.dma_start(
            out=yout.rearrange("(ec p) t -> p ec t", p=128)
                [:, :, th * TH:(th + 1) * TH],
            in_=yac.rearrange("p (ec t) -> p ec t", t=TH)))
        ve.wait(sy, sy.v)
    sy.wait(sy, sy.v)


def emit_F2(p, nc, ctx, t):
    xT = t["x4T"]
    yT = t["yT"]
    wr16T = t["wr16T"]
    sgD = t["sgD"]
    suD = t["suD"]
    sdD = t["sdD"]
    gwD = t["gwD"]
    uwD = t["uwD"]
    dwD = t["dwD"]
    wqD = t["wqD"]
    wkD = t["wkD"]
    wvD = t["wvD"]
    identb = t["identb"]
    onescol = t["onescol"]
    onesrow = t["onesrow"]
    x6T = t["x6T"]
    qTo = t["qTo"]
    kTo = t["kTo"]
    vO = t["vO"]
    gp, te, sc, ve, sy = p.gp, p.te, p.sc, p.ve, p.sy
    gp, te, sc, ve, sy = p.gp, p.te, p.sc, p.ve, p.sy
    ones_c = ctx.enter_context(nc.sbuf_tensor("f2onesc", [128, 4], F32))
    ones_r = ctx.enter_context(nc.sbuf_tensor("f2onesr", [1, 128], F32))
    idb = ctx.enter_context(nc.sbuf_tensor("f2idb", [128, 128], BF16))
    consts = {"ones_col": ones_c[:, 0:1], "ones_row": ones_r,
              "eps": ones_c[:, 1:2], "one1": ones_c, "idb": idb}
    xsb = ctx.enter_context(nc.sbuf_tensor("f2xsb", [128, NCH * TPC], F32))
    gp.dma(lambda e: e.dma_start(out=ones_c[:, :], in_=onescol))
    gp.dma(lambda e: e.dma_start(out=ones_r[:, :], in_=onesrow))
    gp.dma(lambda e: e.dma_start(out=idb[:, :], in_=identb))
    gp.dma(lambda e: e.dma_start(
        out=xsb.rearrange("p (cc t) -> p cc t", cc=NCH),
        in_=xT.rearrange("(cc p) t -> p cc t", p=128)))
    ph0 = ctx.enter_context(contextlib.ExitStack())
    ysb = ph0.enter_context(nc.sbuf_tensor("f2ysb", [128, NCH * TPC], F32))
    V_IN = gp.dma(lambda e: e.dma_start(
        out=ysb.rearrange("p (cc t) -> p cc t", cc=NCH),
        in_=yT.rearrange("(cc p) t -> p cc t", p=128)))
    ve.wait(gp, V_IN)
    V_X5 = None
    for cc in range(NCH):
        ve.wait(ve, ve.v)
        V_X5 = ve.emit(lambda e, cc=cc: e.tensor_add(
            xsb[:, cc * TPC:(cc + 1) * TPC],
            ysb[:, cc * TPC:(cc + 1) * TPC],
            xsb[:, cc * TPC:(cc + 1) * TPC]))
    ph0.close()

    V_X6 = emit_ffn1(p, nc, ctx, xsb, consts, wr16T, sgD, suD, sdD,
                     gwD, uwD, dwD, [(ve, V_X5), (gp, V_IN)], tag="M")
    sy.wait(ve, V_X6)
    sy.dma(lambda e: e.dma_start(
        out=x6T.rearrange("(cc p) t -> p cc t", p=128),
        in_=xsb.rearrange("p (cc t) -> p cc t", cc=NCH)))

    ph2 = ctx.enter_context(contextlib.ExitStack())
    xh2 = ph2.enter_context(nc.sbuf_tensor("f2xh2", [128, NCH * TPC], BF16))
    V_LN2, _ = emit_ln(p, nc, xsb, TPC, consts, xh2, [(ve, V_X6)],
                       tag="q", alloc_ctx=ph2)
    for eng in (te, sc, gp):
        eng.wait(ve, V_LN2)
    qTt = ph2.enter_context(nc.sbuf_tensor("f2qTt", [128, NCH * TPC], BF16))
    emit_ffcols(p, nc, ph2, xh2, TPC, wqD, qTt, TPC, AF.Identity, NCH,
                tag="q2")
    V_QS = sc.v
    kTt = ph2.enter_context(nc.sbuf_tensor("f2kTt", [128, TPC], BF16))
    emit_ffcols(p, nc, ph2, xh2, TPC, wkD, kTt, TPC, AF.Identity, 1,
                tag="k2")
    V_KS = sc.v
    wvs = ph2.enter_context(nc.sbuf_tensor("f2wvs", [128, NCH * 128], BF16))
    vOt = ph2.enter_context(nc.sbuf_tensor("f2vOt", [128, 8 * 128], BF16))
    ps_vv = [ph2.enter_context(nc.psum_tensor(f"f2pvv{i}", [128, 128], F32))
             for i in range(2)]
    gp.wait(gp, gp.v)
    gp.wait(te, te.v)
    gp.wait(sc, sc.v)
    V_WV = gp.dma(lambda e: e.dma_start(
        out=wvs.rearrange("p (cc f) -> p cc f", f=128),
        in_=wvD.rearrange("(cc p) f -> p cc f", p=128)))
    te.wait(gp, V_WV)
    te.wait(sc, V_KS)
    for tc in range(8):
        if tc >= 2:
            te.wait(sc, sc.v)
        for cc in range(NCH):
            te.emit(lambda e, tc=tc, cc=cc: e.matmul(
                ps_vv[tc % 2][:, :],
                xh2[:, cc * TPC + tc * 128: cc * TPC + (tc + 1) * 128],
                wvs[:, cc * 128:(cc + 1) * 128],
                start=(cc == 0), stop=(cc == NCH - 1)))
        sc.wait(te, te.v)
        sc.emit(lambda e, tc=tc: e.activation(
            vOt[:, tc * 128:(tc + 1) * 128], ps_vv[tc % 2][:, :],
            AF.Identity))
    V_VO = sc.v
    sy.wait(sc, V_QS)
    sy.wait(sy, sy.v)
    sy.dma(lambda e: e.dma_start(out=qTo, in_=qTt[:, :]))
    sy.wait(sc, V_KS)
    sy.wait(sy, sy.v)
    sy.dma(lambda e: e.dma_start(out=kTo, in_=kTt[:, :]))
    sy.wait(sc, V_VO)
    sy.wait(sy, sy.v)
    sy.dma(lambda e: e.dma_start(
        out=vO.rearrange("(tc p) d -> p tc d", p=128),
        in_=vOt.rearrange("p (tc d) -> p tc d", d=128)))
    sy.wait(sy, sy.v)
    ph2.close()


def emit_Z(p, nc, ctx, t):
    xT = t["x6T"]
    qTd = t["qTo"]
    kTf = t["kTf"]
    vf = t["vf"]
    woD2 = t["woD2"]
    wr16T = t["wr16T"]
    sgD = t["sgD"]
    suD = t["suD"]
    sdD = t["sdD"]
    gwD = t["gwD"]
    uwD = t["uwD"]
    dwD = t["dwD"]
    identb = t["identb"]
    onescol = t["onescol"]
    onesrow = t["onesrow"]
    x8T = t["x8T"]
    gp, te, sc, ve, sy = p.gp, p.te, p.sc, p.ve, p.sy
    gp, te, sc, ve, sy = p.gp, p.te, p.sc, p.ve, p.sy
    ones_c = ctx.enter_context(nc.sbuf_tensor("zonesc", [128, 4], F32))
    ones_r = ctx.enter_context(nc.sbuf_tensor("zonesr", [1, 128], F32))
    ones_b = ctx.enter_context(nc.sbuf_tensor("zonesb", [128, 1], BF16))
    idb = ctx.enter_context(nc.sbuf_tensor("zidb", [128, 128], BF16))
    consts = {"ones_col": ones_c[:, 0:1], "ones_row": ones_r,
              "eps": ones_c[:, 1:2], "one1": ones_c, "idb": idb}
    xsb = ctx.enter_context(nc.sbuf_tensor("zxsb", [128, NCH * TPC], F32))
    gp.dma(lambda e: e.dma_start(out=ones_c[:, :], in_=onescol))
    gp.dma(lambda e: e.dma_start(out=ones_r[:, :], in_=onesrow))
    gp.dma(lambda e: e.dma_start(out=idb[:, :], in_=identb))
    V_C = gp.dma(lambda e: e.dma_start(
        out=xsb.rearrange("p (cc t) -> p cc t", cc=NCH),
        in_=xT.rearrange("(cc p) t -> p cc t", p=128)))
    ve.wait(gp, V_C)
    V_OB = ve.emit(lambda e: e.tensor_copy(ones_b[:, :], ones_c[:, 0:1]))

    pha = ctx.enter_context(contextlib.ExitStack())
    ySB = pha.enter_context(nc.sbuf_tensor("zySB", [128, NCH * TPC], BF16))
    pha2 = pha.enter_context(contextlib.ExitStack())
    qTs = pha2.enter_context(nc.sbuf_tensor("zqTs", [128, 16 * TPC], BF16))
    kTs = pha2.enter_context(nc.sbuf_tensor("zkTs", [128, T], BF16))
    vSf = pha2.enter_context(nc.sbuf_tensor("zvSf", [128, 16 * 128], BF16))
    exQ = pha2.enter_context(nc.sbuf_tensor("zexQ", [128, 16 * 512], BF16))
    rbc = pha2.enter_context(nc.sbuf_tensor("zrbc", [128, 512], F32))
    rro = pha2.enter_context(nc.sbuf_tensor("zrro", [1, 512], F32))
    ps_s = pha2.enter_context(nc.psum_tensor("zpzs", [128, 512], F32))
    ps_d = pha2.enter_context(nc.psum_tensor("zpzd", [1, 512], F32))
    ps_y = pha2.enter_context(nc.psum_tensor("zpzy", [128, 512], F32))
    ps_rb = pha2.enter_context(nc.psum_tensor("zpzrb", [128, 512], F32))
    gp.wait(gp, gp.v)
    gp.dma(lambda e: e.dma_start(out=qTs[:, :], in_=qTd))
    gp.dma(lambda e: e.dma_start(out=kTs[:, :], in_=kTf))
    V_KV = gp.dma(lambda e: e.dma_start(
        out=vSf.rearrange("p (g d) -> p g d", d=128),
        in_=vf.rearrange("(g p) d -> p g d", p=128)))
    te.wait(gp, V_KV)
    te.wait(ve, V_OB)
    for h in range(16):
        for qh in range(2):
            for kc in range(16):
                te.wait(sc, sc.v)
                te.emit(lambda e, h=h, qh=qh, kc=kc: e.matmul(
                    ps_s[:, :], kTs[:, kc * 128:(kc + 1) * 128],
                    qTs[:, h * TPC + qh * 512: h * TPC + qh * 512 + 512],
                    start=True, stop=True))
                sc.wait(te, te.v)
                if kc == 0:
                    sc.wait(ve, ve.v)
                sc.emit(lambda e, kc=kc: e.activation(
                    exQ[:, kc * 512:(kc + 1) * 512], ps_s[:, :], AF.Exp,
                    scale=RS, bias=ones_c[:, 2:3]))
            te.wait(sc, sc.v)
            for kc in range(16):
                te.emit(lambda e, kc=kc: e.matmul(
                    ps_d[:, :], ones_b[:, 0:1],
                    exQ[:, kc * 512:(kc + 1) * 512],
                    start=(kc == 0), stop=(kc == 15)))
            for kc in range(16):
                te.emit(lambda e, kc=kc: e.matmul(
                    ps_y[:, :], vSf[:, kc * 128:(kc + 1) * 128],
                    exQ[:, kc * 512:(kc + 1) * 512],
                    start=(kc == 0), stop=(kc == 15)))
            V_YMM = te.v
            ve.wait(te, V_YMM)
            ve.wait(ve, ve.v)
            ve.emit(lambda e: e.reciprocal(rro[0:1, :], ps_d[:, :]))
            te.wait(ve, ve.v)
            V_RB = te.emit(lambda e: e.matmul(
                ps_rb[:, :], ones_r[0:1, :], rro[0:1, :],
                start=True, stop=True))
            ve.wait(te, V_RB)
            ve.emit(lambda e: e.tensor_copy(rbc[:, :], ps_rb[:, :]))
            ve.wait(ve, ve.v)
            ve.emit(lambda e, h=h, qh=qh: e.tensor_mul(
                ySB[:, h * TPC + qh * 512: h * TPC + qh * 512 + 512],
                ps_y[:, :], rbc[:, :]))
    V_ATT = ve.v
    pha2.close()

    for eng in (te, sc, gp):
        eng.wait(ve, V_ATT)
    emit_ffcols(p, nc, pha, ySB, TPC, woD2, None, TPC, None, NCH,
                tag="zw", addto=xsb, addW=TPC)
    V_X7 = ve.v
    pha.close()

    V_X8 = emit_ffn1(p, nc, ctx, xsb, consts, wr16T, sgD, suD, sdD,
                     gwD, uwD, dwD, [(ve, V_X7)], tag="Z")
    sy.wait(ve, V_X8)
    sy.dma(lambda e: e.dma_start(
        out=x8T.rearrange("(cc p) t -> p cc t", p=128),
        in_=xsb.rearrange("p (cc t) -> p cc t", cc=NCH)))
    sy.wait(sy, sy.v)



PAIRS = [[0, 1], [2, 3], [4, 5], [6, 7]]
ALL8 = [[0, 1, 2, 3, 4, 5, 6, 7]]


def emit_barrier(p):
    es = [p.sy, p.gp, p.ve, p.sc, p.te]
    vs = [e.v for e in es]
    for e in es:
        for o, v in zip(es, vs):
            if o is not e and v > 0:
                e.wait(o, v)


def emit_rows_glue(p, nc, ctx, t, flg, idgb):
    gp, te, sc, ve, sy = p.gp, p.te, p.sc, p.ve, p.sy
    ah = t["att2_h"]
    c0 = ctx.enter_context(nc.sbuf_tensor("rg_c0", [128, 2 * 512], BF16))
    c1 = ctx.enter_context(nc.sbuf_tensor("rg_c1", [128, 2 * 512], BF16))
    bt = ctx.enter_context(nc.sbuf_tensor("rg_bt", [128, 2 * 512], BF16))
    rstg = [ctx.enter_context(nc.sbuf_tensor(f"rg_s{i}", [128, TPC], BF16))
            for i in range(4)]
    pctx = ctx.enter_context(contextlib.ExitStack())
    ps_tr = [pctx.enter_context(nc.psum_tensor(f"rg_pt{i}", [128, 128], BF16))
             for i in range(2)]

    def gath(j0, head):
        return bass.AP(ah, (16 * j0 + head) * 128,
                       [[256 * 128, 128], [16 * 128, 4], [1, 128]])

    for j4 in range(4):
        for hl in range(8):
            it = j4 * 8 + hl
            b = it % 2
            gp.wait(ve, ve.v)
            sy.wait(ve, ve.v)
            V0 = gp.dma(lambda e, j4=j4, hl=hl, b=b: e.dma_start(
                out=c0[:, b * 512:(b + 1) * 512].rearrange(
                    "p (j d) -> p j d", d=128),
                in_=gath(j4 * 4, hl)))
            V1 = sy.dma(lambda e, j4=j4, hl=hl, b=b: e.dma_start(
                out=c1[:, b * 512:(b + 1) * 512].rearrange(
                    "p (j d) -> p j d", d=128),
                in_=gath(j4 * 4, 8 + hl)))
            ve.wait(gp, V0)
            ve.wait(sy, V1)
            ve.wait(ve, ve.v)
            ve.emit(lambda e, b=b: e.tensor_scalar(
                bt[:, b * 512:(b + 1) * 512], c0[:, b * 512:(b + 1) * 512],
                flg[:, 1:2], None, ALU.mult))
            ve.wait(ve, ve.v)
            ve.emit(lambda e, b=b: e.scalar_tensor_tensor(
                bt[:, b * 512:(b + 1) * 512], c1[:, b * 512:(b + 1) * 512],
                flg[:, 2:3], bt[:, b * 512:(b + 1) * 512],
                ALU.mult, ALU.add))
            for jq in range(4):
                te.wait(ve, ve.v)
                te.emit(lambda e, b=b, jq=jq: e.transpose(
                    ps_tr[jq % 2][:, :],
                    bt[:, b * 512 + jq * 128: b * 512 + (jq + 1) * 128],
                    idgb[:, :]))
                ve.wait(te, te.v)
                ve.emit(lambda e, jq=jq, hl=hl: e.tensor_copy(
                    rstg[jq][:, hl * 128:(hl + 1) * 128], ps_tr[jq % 2][:, :]))
        sy.wait(ve, ve.v)
        for jq in range(4):
            sy.dma(lambda e, j4=j4, jq=jq: e.dma_start(
                out=t["rowsT"][(j4 * 4 + jq) * 128:(j4 * 4 + jq + 1) * 128, :],
                in_=rstg[jq][:, :]))
    V_RST = sy.v
    for j4 in range(4):
        for hb in range(2):
            b = hb
            gp.wait(ve, ve.v)
            gp.wait(sy, V_RST)
            V0 = gp.dma(lambda e, j4=j4, hb=hb, b=b: e.dma_start(
                out=c0[:, b * 512:(b + 1) * 512].rearrange(
                    "p (j d) -> p j d", d=128),
                in_=gath(j4 * 4, 6 + hb)))
            for jq in range(4):
                te.wait(gp, V0)
                te.wait(ve, ve.v)
                te.emit(lambda e, b=b, jq=jq: e.transpose(
                    ps_tr[jq % 2][:, :],
                    c0[:, b * 512 + jq * 128: b * 512 + (jq + 1) * 128],
                    idgb[:, :]))
                ve.wait(te, te.v)
                ve.emit(lambda e, jq=jq, hb=hb: e.tensor_copy(
                    rstg[jq][:, hb * 128:(hb + 1) * 128], ps_tr[jq % 2][:, :]))
        sy.wait(ve, ve.v)
        for jq in range(4):
            sy.dma(lambda e, j4=j4, jq=jq: e.dma_start(
                out=t["rowsHT"][(j4 * 4 + jq) * 128:
                                (j4 * 4 + jq + 1) * 128, :],
                in_=rstg[jq][:, 0:256]))
    sy.wait(sy, sy.v)
    pctx.close()


def emit_halo(p, nc, ctx, t):
    gp, te, sc, ve, sy = p.gp, p.te, p.sc, p.ve, p.sy
    xhalo = ctx.enter_context(nc.sbuf_tensor("hp_x", [128, NCH * 256], F32))
    rh = ctx.enter_context(nc.sbuf_tensor("hp_r", [128, NCH * 256], BF16))
    gp.wait(gp, gp.v)
    gp.dma(lambda e: e.dma_start(
        out=xhalo.rearrange("p (cc t) -> p cc t", cc=NCH),
        in_=t["xg0"][:, 768:1024].rearrange("(cc p) t -> p cc t", p=128)))
    V_IN = gp.dma(lambda e: e.dma_start(
        out=rh.rearrange("p (cc t) -> p cc t", cc=NCH),
        in_=t["rowsHT"].rearrange("(cc p) t -> p cc t", p=128)))
    for eng in (te, sc, ve):
        eng.wait(gp, V_IN)
    emit_ffcols(p, nc, ctx, rh, 256, t["wprojT"], None, 256, None, NCH,
                tag="hp", addto=xhalo, addW=256)
    sy.wait(ve, ve.v)
    sy.dma(lambda e: e.dma_start(
        out=t["haloT"].rearrange("(cc p) t -> p cc t", p=128),
        in_=xhalo.rearrange("p (cc t) -> p cc t", cc=NCH)))
    sy.wait(sy, sy.v)


def emit_scat_glue(p, nc, ctx, t, inkey, outkey, flg, idg, tag):
    gp, te, sc, ve, sy = p.gp, p.te, p.sc, p.ve, p.sy
    srB = ctx.enter_context(nc.sbuf_tensor(f"sg_r{tag}", [8, 2048], F32))
    sc8 = ctx.enter_context(nc.sbuf_tensor(f"sg_o{tag}", [128, NCH * 8], F32))
    pctx = ctx.enter_context(contextlib.ExitStack())
    ps8 = pctx.enter_context(nc.psum_tensor(f"sg_p{tag}", [128, 8], F32))
    gp.wait(gp, gp.v)
    V = gp.dma(lambda e: e.dma_start(out=srB[:, :], in_=t[inkey]))
    te.wait(gp, V)
    for cc in range(NCH):
        te.wait(ve, ve.v)
        te.emit(lambda e, cc=cc: e.transpose(
            ps8[:, :], srB[0:8, cc * 128:(cc + 1) * 128], idg[0:8, 0:8]))
        ve.wait(te, te.v)
        ve.emit(lambda e, cc=cc: e.tensor_scalar(
            sc8[:, cc * 8:(cc + 1) * 8], ps8[:, :], flg[:, 0:1], None,
            ALU.mult))
    sy.wait(ve, ve.v)
    sy.dma(lambda e: e.dma_start(
        out=t[outkey].rearrange("(cc p) e -> p cc e", p=128),
        in_=sc8.rearrange("p (cc e) -> p cc e", e=8)))
    sy.wait(sy, sy.v)
    pctx.close()


def emit_cvt(p, nc, ctx, t, inkey, outkey, nrb, C, tag):
    gp, te, sc, ve, sy = p.gp, p.te, p.sc, p.ve, p.sy
    af = ctx.enter_context(nc.sbuf_tensor(f"cv_f{tag}", [128, 2 * C], F32))
    ab = ctx.enter_context(nc.sbuf_tensor(f"cv_b{tag}", [128, 2 * C], BF16))
    for rb in range(nrb):
        b = rb % 2
        gp.wait(sc, sc.v)
        V = gp.dma(lambda e, rb=rb, b=b: e.dma_start(
            out=af[:, b * C:(b + 1) * C],
            in_=t[inkey][rb * 128:(rb + 1) * 128, :]))
        sc.wait(gp, V)
        sc.wait(sy, sy.v)
        sc.emit(lambda e, b=b: e.activation(
            ab[:, b * C:(b + 1) * C], af[:, b * C:(b + 1) * C], AF.Identity))
        sy.wait(sc, sc.v)
        sy.dma(lambda e, rb=rb, b=b: e.dma_start(
            out=t[outkey][rb * 128:(rb + 1) * 128, :],
            in_=ab[:, b * C:(b + 1) * C]))
    sy.wait(sy, sy.v)


def emit_xhP_glue(p, nc, ctx, t, flg):
    gp, te, sc, ve, sy = p.gp, p.te, p.sc, p.ve, p.sy
    c0 = ctx.enter_context(nc.sbuf_tensor("xp_c0", [128, 2 * TPC], BF16))
    c1 = ctx.enter_context(nc.sbuf_tensor("xp_c1", [128, 2 * TPC], BF16))
    xf = ctx.enter_context(nc.sbuf_tensor("xp_f", [128, 2 * TPC], F32))
    for pb in range(16):
        for th in range(2):
            it = pb * 2 + th
            b = it % 2
            gp.wait(ve, ve.v)
            sy.wait(ve, ve.v)
            V0 = gp.dma(lambda e, pb=pb, th=th, b=b: e.dma_start(
                out=c0[:, b * TPC:(b + 1) * TPC],
                in_=t["Gxh"][th * D + pb * 128: th * D + (pb + 1) * 128, :]))
            nb = (pb + 8) % 16
            V1 = sy.dma(lambda e, nb=nb, th=th, b=b: e.dma_start(
                out=c1[:, b * TPC:(b + 1) * TPC],
                in_=t["Gxh"][th * D + nb * 128: th * D + (nb + 1) * 128, :]))
            ve.wait(gp, V0)
            ve.wait(sy, V1)
            ve.wait(ve, ve.v)
            ve.emit(lambda e, b=b: e.tensor_scalar(
                xf[:, b * TPC:(b + 1) * TPC], c0[:, b * TPC:(b + 1) * TPC],
                flg[:, 1:2], None, ALU.mult))
            ve.wait(ve, ve.v)
            ve.emit(lambda e, b=b: e.scalar_tensor_tensor(
                xf[:, b * TPC:(b + 1) * TPC], c1[:, b * TPC:(b + 1) * TPC],
                flg[:, 2:3], xf[:, b * TPC:(b + 1) * TPC], ALU.mult, ALU.add))
            sy.wait(ve, ve.v)
            sy.dma(lambda e, pb=pb, th=th, b=b: e.dma_start(
                out=t["xhP"][pb * 128:(pb + 1) * 128,
                             th * TPC:(th + 1) * TPC],
                in_=xf[:, b * TPC:(b + 1) * TPC]))
    sy.wait(sy, sy.v)


def emit_yT_glue(p, nc, ctx, t, flg):
    gp, te, sc, ve, sy = p.gp, p.te, p.sc, p.ve, p.sy
    c0 = ctx.enter_context(nc.sbuf_tensor("yt_c0", [128, 2 * TPC], BF16))
    c1 = ctx.enter_context(nc.sbuf_tensor("yt_c1", [128, 2 * TPC], BF16))
    yf = ctx.enter_context(nc.sbuf_tensor("yt_f", [128, 2 * TPC], F32))
    for gb in range(16):
        b = gb % 2
        gp.wait(ve, ve.v)
        sy.wait(ve, ve.v)
        V0 = gp.dma(lambda e, gb=gb, b=b: e.dma_start(
            out=c0[:, b * TPC:(b + 1) * TPC],
            in_=t["Gy"][gb * 128:(gb + 1) * 128, 0:TPC]))
        V1 = sy.dma(lambda e, gb=gb, b=b: e.dma_start(
            out=c1[:, b * TPC:(b + 1) * TPC],
            in_=t["Gy"][gb * 128:(gb + 1) * 128, TPC:2 * TPC]))
        ve.wait(gp, V0)
        ve.wait(sy, V1)
        ve.wait(ve, ve.v)
        ve.emit(lambda e, b=b: e.tensor_scalar(
            yf[:, b * TPC:(b + 1) * TPC], c0[:, b * TPC:(b + 1) * TPC],
            flg[:, 1:2], None, ALU.mult))
        ve.wait(ve, ve.v)
        ve.emit(lambda e, b=b: e.scalar_tensor_tensor(
            yf[:, b * TPC:(b + 1) * TPC], c1[:, b * TPC:(b + 1) * TPC],
            flg[:, 2:3], yf[:, b * TPC:(b + 1) * TPC], ALU.mult, ALU.add))
        sy.wait(ve, ve.v)
        sy.dma(lambda e, gb=gb, b=b: e.dma_start(
            out=t["yT"][gb * 128:(gb + 1) * 128, :],
            in_=yf[:, b * TPC:(b + 1) * TPC]))
    sy.wait(sy, sy.v)


def emit_kTf_copy(p, nc, ctx, t):
    gp, te, sc, ve, sy = p.gp, p.te, p.sc, p.ve, p.sy
    kb = ctx.enter_context(nc.sbuf_tensor("kc_b", [128, 2 * TPC], BF16))
    gp.wait(gp, gp.v)
    for r in range(2):
        gp.dma(lambda e, r=r: e.dma_start(
            out=kb[:, r * TPC:(r + 1) * TPC],
            in_=t["Gk"][r * 128:(r + 1) * 128, :]))
    V = gp.v
    sy.wait(gp, V)
    sy.dma(lambda e: e.dma_start(out=t["kTf"], in_=kb[:, :]))
    sy.wait(sy, sy.v)


def build_fused():
    nc = bass.Bass("TRN2", target_bir_lowering=False)
    t = {}

    def inp(name, shape, dt=F32):
        t[name] = nc.dram_tensor(name, shape, dt, kind="ExternalInput").ap()

    def intl(name, shape, dt=F32, handle=False):
        h = nc.dram_tensor(name, shape, dt)
        t[name] = h.ap()
        if handle:
            t[name + "_h"] = h

    inp("xg0", [D, TPC]); inp("xg1", [D, TPC]); inp("xTme", [D, TPC])
    inp("wqkvT", [D, 3 * D]); inp("maskA", [128, 128])
    inp("identb", [128, 128]); inp("onescolA", [128, 2])
    inp("onescol", [128, 4]); inp("onesrow", [1, 128]); inp("flags", [128, 4])
    inp("wprojT", [D, D]); inp("w1T", [D, D]); inp("w3T", [D, D])
    inp("w2T", [D, D]); inp("wrT", [D, 8])
    inp("wqTp", [D, D]); inp("wkTp", [D, D]); inp("wvT", [D, D])
    inp("woD", [D, D]); inp("tabQ", [128, 2 * TPC]); inp("tabK", [128, 2 * 1280])
    inp("maskB", [128, 16 * 256])
    inp("cw", [D, 4]); inp("cbI", [128, NCH]); inp("xpT", [D, 96])
    inp("dtwT", [64, TPC]); inp("dtb", [128, 8]); inp("Ain", [128, 128])
    inp("Dvin", [128, 8])
    inp("wr16T", [D, 16]); inp("sgD", [D, D]); inp("suD", [D, D])
    inp("sdD", [D, D]); inp("gwD", [16 * D, TPC]); inp("uwD", [16 * D, TPC])
    inp("dwD", [16 * TPC, D]); inp("wqD", [D, D]); inp("wkD", [D, 128])
    inp("wvD", [D, 128]); inp("woD2", [D, D])

    intl("att2", [2 * 16 * TPC, 128], BF16, handle=True)
    intl("dq", [TPC, D], BF16, handle=True)
    intl("dk", [TPC, D], BF16, handle=True)
    intl("dv", [TPC, D], BF16, handle=True)
    intl("rowsT", [D, TPC], BF16); intl("rowsHT", [D, 256], BF16)
    intl("x1T", [D, TPC]); intl("haloT", [D, 256])
    intl("part1", [8, 2048]); intl("scatR1", [8, 2048]); intl("scat8a", [D, 8])
    intl("dxx", [D, TPC]); intl("datt", [D, TPC], BF16)
    intl("x3T", [D, TPC]); intl("part2", [8, 2048])
    intl("scatR2", [8, 2048]); intl("scat8b", [D, 8])
    intl("x4T", [D, TPC]); intl("xhT", [D, TPC]); intl("xhbf", [D, TPC], BF16)
    intl("Gxh", [2 * D, TPC], BF16); intl("xhP", [D, T])
    intl("yout", [TPC, T]); intl("ybf", [TPC, T], BF16)
    intl("Gy", [2 * TPC, T], BF16); intl("yT", [D, TPC])
    intl("x6T", [D, TPC]); intl("qTo", [128, 16 * TPC], BF16)
    intl("kTo", [128, TPC], BF16); intl("vO", [TPC, 128], BF16)
    intl("Gk", [256, TPC], BF16); intl("vf", [T, 128], BF16)
    intl("kTf", [128, T], BF16)
    t["x8T"] = nc.dram_tensor("x8T", [D, TPC], F32, kind="ExternalOutput").ap()

    with contextlib.ExitStack() as ctx:
        p = Prog(nc, ctx)
        gp, te, sc, ve, sy = p.gp, p.te, p.sc, p.ve, p.sy
        flg = ctx.enter_context(nc.sbuf_tensor("flgT", [128, 4], F32))
        idg = ctx.enter_context(nc.sbuf_tensor("idgT", [128, 128], F32))
        idgb = ctx.enter_context(nc.sbuf_tensor("idgbT", [128, 128], BF16))
        gp.dma(lambda e: e.dma_start(out=flg[:, :], in_=t["flags"]))
        V_ID = gp.dma(lambda e: e.dma_start(out=idg[:, :], in_=t["identb"]))
        ve.wait(gp, V_ID)
        ve.emit(lambda e: e.tensor_copy(idgb[:, :], idg[:, :]))

        def phase(fn, *args):
            c = contextlib.ExitStack()
            fn(p, nc, c, *args)
            c.close()
            emit_barrier(p)

        for r in (0, 1):
            phase(emit_A1, {**t, "XT_A": t[f"xg{r}"]}, r * 8, f"a{r}")
        phase(emit_rows_glue, t, flg, idgb)
        phase(emit_F1, t)
        phase(emit_halo, t)
        V = gp.emit(lambda e: e.collective_compute(
            "AllReduce", ALU.add, ins=[t["part1"]], outs=[t["scatR1"]],
            replica_groups=ALL8), 1)
        gp.wait(gp, V)
        phase(emit_scat_glue, t, "scatR1", "scat8a", flg, idg, "a")
        phase(emit_B1, t)
        V = gp.emit(lambda e: e.collective_compute(
            "AllReduce", ALU.add, ins=[t["part2"]], outs=[t["scatR2"]],
            replica_groups=ALL8), 1)
        gp.wait(gp, V)
        phase(emit_scat_glue, t, "scatR2", "scat8b", flg, idg, "b")
        phase(emit_M1, t)
        phase(emit_cvt, t, "xhT", "xhbf", 16, TPC, "x")
        V = gp.emit(lambda e: e.collective_compute(
            "AllGather", ALU.bypass, ins=[t["xhbf"]], outs=[t["Gxh"]],
            replica_groups=PAIRS), 1)
        gp.wait(gp, V)
        sy.wait(gp, V)
        phase(emit_xhP_glue, t, flg)
        phase(emit_M2, t)
        phase(emit_cvt, t, "yout", "ybf", 8, T, "y")
        V = gp.emit(lambda e: e.collective_compute(
            "AllGather", ALU.bypass, ins=[t["ybf"]], outs=[t["Gy"]],
            replica_groups=PAIRS), 1)
        gp.wait(gp, V)
        sy.wait(gp, V)
        phase(emit_yT_glue, t, flg)
        phase(emit_F2, t)
        V = gp.emit(lambda e: e.collective_compute(
            "AllGather", ALU.bypass, ins=[t["kTo"]], outs=[t["Gk"]],
            replica_groups=PAIRS), 1)
        V = gp.emit(lambda e: e.collective_compute(
            "AllGather", ALU.bypass, ins=[t["vO"]], outs=[t["vf"]],
            replica_groups=PAIRS), 1)
        gp.wait(gp, V)
        phase(emit_kTf_copy, t)
        c = contextlib.ExitStack()
        emit_Z(p, nc, c, t)
        c.close()
        p.finish()
    return nc



def _rope_tab(positions):
    inv = 1.0 / 10000.0 ** (np.arange(0, 128, 2, dtype=np.float32) / 128.0)
    fr = np.outer(positions.astype(np.float32), inv)
    W = len(positions)
    tab = np.zeros((128, 2 * W), np.float32)
    tab[0:64, 0:W] = np.cos(fr).T
    tab[64:128, 0:W] = np.cos(fr).T
    tab[0:64, W:] = np.sin(fr).T
    tab[64:128, W:] = np.sin(fr).T
    return tab


def _perm_headcols(Wm):
    WT = np.ascontiguousarray(np.asarray(Wm, np.float32).T)
    idx = np.zeros(2048, np.int64)
    for h in range(16):
        idx[h * 128:h * 128 + 64] = h * 128 + np.arange(0, 128, 2)
        idx[h * 128 + 64:(h + 1) * 128] = h * 128 + np.arange(1, 128, 2)
    return np.ascontiguousarray(WT[:, idx])


def _build_maskB(h):
    m = np.zeros((16, 128, 256), np.float32)
    for ci in range(4):
        gc = h * 4 + ci
        for kc in range(4):
            for jj in range(128):
                ww = kc * 128 + jj
                if ww < 256:
                    m[ci * 4 + kc, jj, :] = 0.0 if gc > 0 else -30000.0
                else:
                    qq = np.arange(256)
                    m[ci * 4 + kc, jj, :] = np.where(ww - 256 <= qq, 0.0,
                                                     -30000.0)
    return np.ascontiguousarray(m.transpose(1, 0, 2).reshape(128, 16 * 256))


def kernel(**inputs):
    inputs = {k: np.asarray(v) for k, v in inputs.items()}
    T4 = lambda a: np.ascontiguousarray(np.asarray(a, np.float32).T)
    DC = lambda a: np.ascontiguousarray(np.asarray(a, np.float32))
    onescol = np.ones((128, 4), np.float32)
    onescol[:, 1] = 1e-5
    onescol[:, 2] = -12.0
    onescolA = np.ones((128, 2), np.float32)
    onescolA[:, 1] = 1e-5
    onesrow = np.ones((1, 128), np.float32)
    ident = np.eye(128, dtype=np.float32)
    maskA = np.full((128, 128), -30000.0, np.float32)
    for tt in range(8):
        maskA[tt * 16:(tt + 1) * 16, tt * 16:(tt + 1) * 16] = 0.0

    x = np.asarray(inputs["L0_attn_a_x"], np.float32)
    shared = {
        "wqkvT": T4(inputs["L0_attn_a_w_qkv"]),
        "maskA": maskA, "identb": ident, "onescolA": onescolA,
        "onescol": onescol, "onesrow": onesrow,
        "wprojT": T4(inputs["L0_attn_a_w_proj"]),
        "w1T": T4(inputs["L0_ffn_w1"]), "w3T": T4(inputs["L0_ffn_w3"]),
        "w2T": T4(inputs["L0_ffn_w2"]), "wrT": T4(inputs["L0_ffn_wr"]),
        "wqTp": _perm_headcols(inputs["L0_attn_b_Wq"]),
        "wkTp": _perm_headcols(inputs["L0_attn_b_Wk"]),
        "wvT": T4(inputs["L0_attn_b_Wv"]),
        "woD": T4(inputs["L0_attn_b_Wo"]),
        "wr16T": T4(inputs["L1_ffn_router_w"]),
        "sgD": DC(inputs["L1_ffn_shared_gate_w"]),
        "suD": DC(inputs["L1_ffn_shared_up_w"]),
        "sdD": DC(inputs["L1_ffn_shared_down_w"]),
        "gwD": np.ascontiguousarray(np.asarray(
            inputs["L1_ffn_gate_w"], np.float32).reshape(16 * D, TPC)),
        "uwD": np.ascontiguousarray(np.asarray(
            inputs["L1_ffn_up_w"], np.float32).reshape(16 * D, TPC)),
        "dwD": np.ascontiguousarray(np.asarray(
            inputs["L1_ffn_down_w"], np.float32).reshape(16 * TPC, D)),
        "wqD": DC(inputs["L1_attn_b_Wq"]),
        "wkD": DC(inputs["L1_attn_b_Wk"]),
        "wvD": DC(inputs["L1_attn_b_Wv"]),
        "woD2": DC(inputs["L1_attn_b_Wo"]),
    }
    maps = []
    for c in range(NCORES):
        b, h = c // 2, c % 2
        xg0 = np.ascontiguousarray(x[b, 0:TPC, :].T)
        xg1 = np.ascontiguousarray(x[b, TPC:T, :].T)
        flags = np.zeros((128, 4), np.float32)
        flags[:, 0] = 1.0 if c == 0 else 0.0
        flags[:, 1] = 1.0 if h == 0 else 0.0
        flags[:, 2] = 1.0 if h == 1 else 0.0
        if h == 1:
            kpos = np.arange(768, 2048)
        else:
            kpos = np.concatenate([np.zeros(256, np.int64),
                                   np.arange(0, 1024)])
        eh = h
        permv = np.concatenate([np.arange(eh * 1024, (eh + 1) * 1024),
                                np.arange((1 - eh) * 1024, (2 - eh) * 1024)])
        own = permv[:1024]
        cwp = np.asarray(inputs["L1_attn_a_conv_w"], np.float32)[permv]
        cbp = np.asarray(inputs["L1_attn_a_conv_b"], np.float32)[permv]
        xp = np.asarray(inputs["L1_attn_a_x_proj_w"], np.float32)
        dtw = np.asarray(inputs["L1_attn_a_dt_proj_w"], np.float32)
        dtb = np.asarray(inputs["L1_attn_a_dt_proj_b"], np.float32)[own]
        Al = np.asarray(inputs["L1_attn_a_A_log"], np.float32)[own]
        Dv = np.asarray(inputs["L1_attn_a_D"], np.float32)[own]
        maps.append({
            **shared,
            "xg0": xg0, "xg1": xg1, "xTme": xg0 if h == 0 else xg1,
            "flags": flags,
            "tabQ": _rope_tab(np.arange(h * 1024, (h + 1) * 1024)),
            "tabK": _rope_tab(kpos), "maskB": _build_maskB(h),
            "cw": np.ascontiguousarray(cwp),
            "cbI": np.ascontiguousarray(cbp.reshape(16, 128).T),
            "xpT": np.ascontiguousarray(xp[:, permv].T),
            "dtwT": np.ascontiguousarray(dtw[own].T),
            "dtb": np.ascontiguousarray(dtb.reshape(8, 128).T),
            "Ain": np.ascontiguousarray(
                Al.reshape(8, 128, 16).transpose(1, 0, 2).reshape(128, 128)),
            "Dvin": np.ascontiguousarray(Dv.reshape(8, 128).T),
        })
    nc = build_fused()
    res = run_bass_kernel_spmd(nc, maps, core_ids=list(range(NCORES)))
    out = np.zeros((B, T, D), np.float32)
    for c in range(NCORES):
        b, h = c // 2, c % 2
        out[b, h * TPC:(h + 1) * TPC, :] = np.asarray(
            res.results[c]["x8T"]).T
    return out


# version 03
# speedup vs baseline: 1.4977x; 1.4977x over previous
import contextlib
import math
import numpy as np
import ml_dtypes
import concourse.bass as bass
import concourse.mybir as mybir
from concourse.bass_utils import run_bass_kernel_spmd

F32 = mybir.dt.float32
BF16 = mybir.dt.bfloat16
AF = mybir.ActivationFunctionType
ALU = mybir.AluOpType
ds = bass.ds
BF = ml_dtypes.bfloat16

NCORES = 8
B, T, D = 4, 2048, 2048
TPC = 1024
NCH = D // 128
RS = 1.0 / math.sqrt(128.0)


class EB:
    def __init__(self, ctx, nc, name):
        self.h = ctx.enter_context(nc.semaphore(f"ch_{name}"))
        self.v = 0
        self.ops = []
        self.name = name

    def emit(self, fn, n=1):
        self.v += n
        h, k = self.h, n

        def op(e):
            fn(e).then_inc(h, k)
        self.ops.append(op)
        return self.v

    def raw(self, fn):
        self.ops.append(fn)

    def wait(self, other, v):
        h = other.h
        self.ops.append(lambda e: e.wait_ge(h, v))

    def dma(self, fn):
        return self.emit(fn, 16)

    def run(self, e):
        for f in self.ops:
            f(e)


class Prog:
    def __init__(self, nc, ctx):
        self.nc = nc
        self.ctx = ctx
        self.sy = EB(ctx, nc, "sy")
        self.gp = EB(ctx, nc, "gp")
        self.ve = EB(ctx, nc, "ve")
        self.sc = EB(ctx, nc, "sc")
        self.te = EB(ctx, nc, "te")

    def finish(self):
        with self.nc.Block() as block:
            block.sync(self.sy.run)
            block.gpsimd(self.gp.run)
            block.vector(self.ve.run)
            block.scalar(self.sc.run)
            block.tensor(self.te.run)


def emit_ln(p, nc, xsb, W, consts, out_bf, deps, tag="", alloc_ctx=None,
            outW=None, out_off=0):
    if outW is None:
        outW = W
    wl = []
    off = 0
    while off < W:
        ln_ = min(512, W - off)
        wl.append((off, ln_))
        off += ln_
    NW = len(wl)
    sq = alloc_ctx.enter_context(nc.sbuf_tensor(f"ln_sq{tag}", [128, W], F32))
    st = alloc_ctx.enter_context(nc.sbuf_tensor(f"ln_st{tag}", [1, 2 * W], F32))
    abc = alloc_ctx.enter_context(nc.sbuf_tensor(f"ln_ab{tag}", [128, W], F32))
    bbc = alloc_ctx.enter_context(nc.sbuf_tensor(f"ln_bb{tag}", [128, W], F32))
    pctx = alloc_ctx.enter_context(contextlib.ExitStack())
    ps_sum = [pctx.enter_context(nc.psum_tensor(f"lnS{tag}{w}", [1, wl[w][1]],
                                                F32)) for w in range(NW)]
    ps_sq = [pctx.enter_context(nc.psum_tensor(f"lnQ{tag}{w}", [1, wl[w][1]],
                                               F32)) for w in range(NW)]
    ps_bc = [pctx.enter_context(nc.psum_tensor(f"lnB{tag}{w}",
                                               [128, wl[w // 2][1]], F32))
             for w in range(2 * NW)]
    ones_c = consts["ones_col"]
    ones_r = consts["ones_row"]
    eps = consts["eps"]
    ve, sc, te = p.ve, p.sc, p.te

    for (d, v) in deps:
        sc.wait(d, v)
        te.wait(d, v)

    TE0 = te.v
    V_SQ = []
    for cc in range(NCH):
        if cc >= 1:
            sc.wait(te, TE0 + 2 * NW * cc)
        V_SQ.append(sc.emit(lambda e, cc=cc: e.activation(
            sq[:, :], xsb[:, cc * W:(cc + 1) * W], AF.Square)))
    for cc in range(NCH):
        te.wait(sc, V_SQ[cc])
        for w, (wo, wn_) in enumerate(wl):
            te.emit(lambda e, cc=cc, w=w, wo=wo, wn_=wn_: e.matmul(
                ps_sum[w][:, :], ones_c[:, 0:1],
                xsb[:, cc * W + wo: cc * W + wo + wn_],
                start=(cc == 0), stop=(cc == NCH - 1)))
            te.emit(lambda e, cc=cc, w=w, wo=wo, wn_=wn_: e.matmul(
                ps_sq[w][:, :], ones_c[:, 0:1],
                sq[:, wo: wo + wn_],
                start=(cc == 0), stop=(cc == NCH - 1)))
    V_MM = te.v
    assert V_MM == TE0 + 2 * NW * NCH

    sc.wait(te, V_MM)
    V_MU = None
    for w, (wo, wn_) in enumerate(wl):
        V_MU = sc.emit(lambda e, w=w, wo=wo, wn_=wn_: e.mul(
            st[0:1, wo: wo + wn_], ps_sum[w][:, :], 1.0 / D))
    sc.wait(sc, V_MU)
    V_M2 = sc.emit(lambda e: e.activation(st[0:1, W:2 * W], st[0:1, 0:W],
                                          AF.Square))
    ve.wait(sc, V_M2)
    V_VAR = None
    for w, (wo, wn_) in enumerate(wl):
        if w >= 1:
            ve.wait(ve, V_VAR)
        V_VAR = ve.emit(lambda e, w=w, wo=wo, wn_=wn_: e.scalar_tensor_tensor(
            st[0:1, W + wo: W + wo + wn_],
            ps_sq[w][:, :], 1.0 / D,
            st[0:1, W + wo: W + wo + wn_],
            ALU.mult, ALU.subtract))
    sc.wait(ve, V_VAR)
    V_LN1 = sc.emit(lambda e: e.activation(st[0:1, W:2 * W],
                                           st[0:1, W:2 * W], AF.Ln,
                                           bias=eps[0:1, 0:1]))
    sc.wait(sc, V_LN1)
    V_RSTD = sc.emit(lambda e: e.activation(st[0:1, W:2 * W],
                                            st[0:1, W:2 * W],
                                            AF.Exp, scale=-0.5))
    ve.wait(sc, V_RSTD)
    V_BROW = ve.emit(lambda e: e.scalar_tensor_tensor(
        st[0:1, 0:W], st[0:1, 0:W], -1.0, st[0:1, W:2 * W],
        ALU.mult, ALU.mult))
    te.wait(ve, V_BROW)
    V_BC = None
    for w, (wo, wn_) in enumerate(wl):
        V_BC = te.emit(lambda e, w=w, wo=wo, wn_=wn_: e.matmul(
            ps_bc[2 * w][:, :], ones_r[0:1, :],
            st[0:1, W + wo: W + wo + wn_],
            start=True, stop=True))
        V_BC = te.emit(lambda e, w=w, wo=wo, wn_=wn_: e.matmul(
            ps_bc[2 * w + 1][:, :], ones_r[0:1, :],
            st[0:1, wo: wo + wn_],
            start=True, stop=True))
    ve.wait(te, V_BC)
    for w, (wo, wn_) in enumerate(wl):
        ve.emit(lambda e, w=w, wo=wo, wn_=wn_: e.tensor_copy(
            abc[:, wo: wo + wn_], ps_bc[2 * w][:, :]))
        ve.emit(lambda e, w=w, wo=wo, wn_=wn_: e.tensor_copy(
            bbc[:, wo: wo + wn_], ps_bc[2 * w + 1][:, :]))
    V_CP = ve.v
    V_DONE = None
    for cc in range(NCH):
        if cc == 0:
            ve.wait(ve, V_CP)
        else:
            ve.wait(ve, V_DONE)
        ve.emit(lambda e, cc=cc: e.tensor_mul(
            sq[:, :], xsb[:, cc * W:(cc + 1) * W], abc[:, :]))
        ve.wait(ve, ve.v)
        V_DONE = ve.emit(lambda e, cc=cc: e.tensor_add(
            out_bf[:, cc * outW + out_off: cc * outW + out_off + W],
            sq[:, :], bbc[:, :]))
    pctx.close()
    return V_DONE, st




def emit_ffcols(p, nc, ctx, src_bf, srcW, w_dram, out_bf, outW, act, nfc,
                ncc=NCH, tag="f", src_off=0, L=None, out_off=0, addto=None,
                addW=None, add_off=0, multo=False):
    if L is None:
        L = srcW
    wl = []
    off = 0
    while off < L:
        wn_ = min(512, L - off)
        wl.append((off, wn_))
        off += wn_
    NT = len(wl)
    NS = max(1, min(3 if NT <= 3 else 2, 7 // NT, nfc))
    NW = max(1, min(3, nfc))
    ictx = ctx.enter_context(contextlib.ExitStack())
    wcol = ictx.enter_context(nc.sbuf_tensor(f"wc{tag}", [128, NW * ncc * 128],
                                             BF16))
    pst = [ictx.enter_context(nc.psum_tensor(f"pf{tag}{i}",
                                             [128, wl[i % NT][1]], F32))
           for i in range(NS * NT)]
    gp, te, sc, ve = p.gp, p.te, p.sc, p.ve
    ev = ve if addto is not None else sc
    GP0 = gp.v
    TE0 = te.v
    EV0 = ev.v
    gp.wait(gp, GP0)
    gp.wait(te, TE0)
    te.wait(sc, sc.v)
    te.wait(ve, ve.v)

    def te_after(fc):
        return TE0 + (fc + 1) * NT * ncc

    for fc in range(nfc):
        if fc >= NW:
            gp.wait(te, te_after(fc - NW))
        gp.dma(lambda e, fc=fc: e.dma_start(
            out=wcol.rearrange("p (s cc f) -> p s cc f", s=NW, f=128)
                [:, fc % NW],
            in_=w_dram[:, fc * 128:(fc + 1) * 128].rearrange(
                "(cc p) f -> p cc f", p=128)))

    for fc in range(nfc):
        te.wait(gp, GP0 + 16 * min(fc + 2, nfc))
        if fc >= NS:
            te.wait(ev, EV0 + NT * (fc - NS + 1))
        for tt, (wo, wn_) in enumerate(wl):
            for cc in range(ncc):
                te.emit(lambda e, fc=fc, tt=tt, cc=cc, wo=wo, wn_=wn_:
                        e.matmul(
                    pst[(fc % NS) * NT + tt][:, :],
                    wcol[:, (fc % NW) * ncc * 128 + cc * 128:
                         (fc % NW) * ncc * 128 + (cc + 1) * 128],
                    src_bf[:, cc * srcW + src_off + wo:
                           cc * srcW + src_off + wo + wn_],
                    start=(cc == 0), stop=(cc == ncc - 1)))

    for fc in range(nfc):
        for tt, (wo, wn_) in enumerate(wl):
            ev.wait(te, TE0 + fc * NT * ncc + (tt + 1) * ncc)
            if addto is not None:
                ev.wait(ve, ve.v)
                op = ALU.mult if multo else ALU.add
                ve.emit(lambda e, fc=fc, tt=tt, wo=wo, wn_=wn_, op=op:
                        e.tensor_tensor(
                    addto[:, fc * addW + add_off + wo:
                          fc * addW + add_off + wo + wn_],
                    pst[(fc % NS) * NT + tt][:, :],
                    addto[:, fc * addW + add_off + wo:
                          fc * addW + add_off + wo + wn_], op))
            else:
                sc.emit(lambda e, fc=fc, tt=tt, wo=wo, wn_=wn_: e.activation(
                    out_bf[:, fc * outW + out_off + wo:
                           fc * outW + out_off + wo + wn_],
                    pst[(fc % NS) * NT + tt][:, :], act))
    assert ev.v == EV0 + nfc * NT
    ictx.close()
    return ev.v, NT


def emit_router(p, nc, ctx, xsb, stats, consts, E, wrT_dram, wn_out, tag="r"):
    W = TPC
    wrs = ctx.enter_context(nc.sbuf_tensor(f"wrs{tag}", [128, NCH * E], F32))
    cols = ctx.enter_context(nc.sbuf_tensor(f"cols{tag}", [128, 2 * 8], F32))
    csbc = ctx.enter_context(nc.sbuf_tensor(f"csbc{tag}", [128, E], F32))
    lrow = ctx.enter_context(nc.sbuf_tensor(f"lrow{tag}", [1, E], F32))
    sml = ctx.enter_context(nc.sbuf_tensor(f"sml{tag}", [128, 2 * E + 8], F32))
    pctx = ctx.enter_context(contextlib.ExitStack())
    ps_l = pctx.enter_context(nc.psum_tensor(f"psl{tag}", [128, E], F32))
    ps_cs = pctx.enter_context(nc.psum_tensor(f"pcs{tag}", [1, E], F32))
    ps_cb = pctx.enter_context(nc.psum_tensor(f"pcb{tag}", [128, E], F32))
    ps_t = pctx.enter_context(nc.psum_tensor(f"pst{tag}", [128, 2], F32))
    gp, te, sc, ve = p.gp, p.te, p.sc, p.ve
    ones_c, ones_r = consts["ones_col"], consts["ones_row"]

    gp.wait(gp, gp.v)
    V_WR = gp.dma(lambda e: e.dma_start(
        out=wrs.rearrange("p (cc f) -> p cc f", f=E),
        in_=wrT_dram.rearrange("(cc p) f -> p cc f", p=128)))

    te.wait(gp, V_WR)
    V_CS = None
    for cc in range(NCH):
        V_CS = te.emit(lambda e, cc=cc: e.matmul(
            ps_cs[:, :], ones_c[:, 0:1], wrs[:, cc * E:(cc + 1) * E],
            start=(cc == 0), stop=(cc == NCH - 1)))
    sc.wait(te, V_CS)
    V_CSR = sc.emit(lambda e: e.activation(lrow[0:1, :], ps_cs[:, :],
                                           AF.Identity))
    te.wait(sc, V_CSR)
    V_CB = te.emit(lambda e: e.matmul(ps_cb[:, :], ones_r[0:1, :],
                                      lrow[0:1, :], start=True, stop=True))
    ve.wait(te, V_CB)
    V_CBE = ve.emit(lambda e: e.tensor_copy(csbc[:, :], ps_cb[:, :]))

    V_TC = []
    for tc in range(8):
        V1 = te.emit(lambda e, tc=tc: e.transpose(
            ps_t[:, 0:1], stats[0:1, W + tc * 128: W + (tc + 1) * 128],
            consts["one1"][0:1, 0:1]))
        V1 = te.emit(lambda e, tc=tc: e.transpose(
            ps_t[:, 1:2], stats[0:1, tc * 128:(tc + 1) * 128],
            consts["one1"][0:1, 0:1]))
        ve.wait(te, V1)
        V_TC.append(ve.emit(lambda e, tc=tc: e.tensor_copy(
            cols[:, 2 * tc:2 * tc + 2], ps_t[:, 0:2])))
        if tc < 7:
            te.wait(ve, V_TC[-1])
    V_OUT = None
    V_TS1 = None
    for tc in range(8):
        if tc >= 1:
            te.wait(ve, V_TS1)
        V_L = None
        for cc in range(NCH):
            V_L = te.emit(lambda e, tc=tc, cc=cc: e.matmul(
                ps_l[:, :],
                xsb[:, cc * W + tc * 128: cc * W + (tc + 1) * 128],
                wrs[:, cc * E:(cc + 1) * E],
                start=(cc == 0), stop=(cc == NCH - 1)))
        ve.wait(te, V_L)
        ve.wait(ve, max(V_CBE, V_TC[tc]))
        if tc >= 1:
            ve.wait(ve, V_OUT)
        V_TS1 = ve.emit(lambda e, tc=tc: e.tensor_scalar(
            sml[:, 0:E], ps_l[:, :], cols[:, 2 * tc:2 * tc + 1], None,
            ALU.mult))
        ve.wait(ve, ve.v)
        ve.emit(lambda e, tc=tc: e.scalar_tensor_tensor(
            sml[:, 0:E], csbc[:, :], cols[:, 2 * tc + 1:2 * tc + 2],
            sml[:, 0:E], ALU.mult, ALU.add))
        ve.wait(ve, ve.v)
        ve.emit(lambda e: e.tensor_reduce(
            sml[:, 2 * E:2 * E + 1], sml[:, 0:E], mybir.AxisListType.X,
            ALU.max, negate=True))
        ve.wait(ve, ve.v)
        V_NM = ve.v
        sc.wait(ve, V_NM)
        V_EX = sc.emit(lambda e: e.activation(
            sml[:, E:2 * E], sml[:, 0:E], AF.Exp,
            bias=sml[:, 2 * E:2 * E + 1]))
        ve.wait(sc, V_EX)
        ve.emit(lambda e: e.tensor_scalar(
            sml[:, 0:E], sml[:, E:2 * E], 1.0, None, ALU.is_equal))
        ve.wait(ve, ve.v)
        ve.emit(lambda e: e.scalar_tensor_tensor(
            sml[:, 0:E], sml[:, 0:E], -30000.0, sml[:, E:2 * E],
            ALU.mult, ALU.add))
        ve.wait(ve, ve.v)
        ve.emit(lambda e: e.tensor_reduce(
            sml[:, 2 * E + 1:2 * E + 2], sml[:, 0:E], mybir.AxisListType.X,
            ALU.max))
        ve.wait(ve, ve.v)
        ve.emit(lambda e: e.tensor_scalar(
            sml[:, 0:E], sml[:, E:2 * E], sml[:, 2 * E + 1:2 * E + 2], None,
            ALU.is_ge))
        ve.wait(ve, ve.v)
        ve.emit(lambda e: e.scalar_tensor_tensor(
            sml[:, 0:E], sml[:, E:2 * E], 1.0, sml[:, 0:E],
            ALU.mult, ALU.mult, accum_out=sml[:, 2 * E + 2:2 * E + 3]))
        ve.wait(ve, ve.v)
        ve.emit(lambda e: e.reciprocal(sml[:, 2 * E + 3:2 * E + 4],
                                       sml[:, 2 * E + 2:2 * E + 3]))
        ve.wait(ve, ve.v)
        V_OUT = ve.emit(lambda e, tc=tc: e.tensor_scalar(
            wn_out[:, tc * E:(tc + 1) * E], sml[:, 0:E],
            sml[:, 2 * E + 3:2 * E + 4], None, ALU.mult))
    pctx.close()
    return V_OUT


def emit_ffn0(p, nc, ctx, xsb, consts, w1T, w3T, w2T, wrT, part_out,
              deps, tag="n"):
    gp, te, sc, ve, sy = p.gp, p.te, p.sc, p.ve, p.sy
    fctx = ctx.enter_context(contextlib.ExitStack())
    wn = fctx.enter_context(nc.sbuf_tensor(f"wn{tag}", [128, 8 * 8], BF16))
    p8 = fctx.enter_context(nc.sbuf_tensor(f"p8{tag}", [8, 2048], F32))
    h2 = fctx.enter_context(nc.sbuf_tensor(f"h2{tag}", [128, NCH * TPC], BF16))
    h3t = fctx.enter_context(nc.sbuf_tensor(f"h3{tag}", [128, 2048], BF16))
    ictx = fctx.enter_context(contextlib.ExitStack())
    h1 = ictx.enter_context(nc.sbuf_tensor(f"h1{tag}", [128, NCH * TPC], BF16))
    xctx = ictx.enter_context(contextlib.ExitStack())
    xh = xctx.enter_context(nc.sbuf_tensor(f"xh{tag}", [128, NCH * TPC], BF16))
    V_LN, stats = emit_ln(p, nc, xsb, TPC, consts, xh, deps, tag=tag,
                          alloc_ctx=xctx)
    for eng in (p.te, p.sc, p.gp):
        eng.wait(ve, V_LN)
    V_RT = emit_router(p, nc, xctx, xsb, stats, consts, 8, wrT, wn, tag=tag)
    p.te.wait(ve, V_RT)
    emit_ffcols(p, nc, xctx, xh, TPC, w1T, h1, TPC, AF.Silu, NCH,
                tag=tag + "1")
    V_H1S = sc.v
    xctx.close()
    for eng in (p.te, p.gp):
        eng.wait(sc, V_H1S)
    emit_ffcols(p, nc, ictx, h1, TPC, w3T, h2, TPC, AF.Silu, NCH,
                tag=tag + "3")
    V_H2S = sc.v
    ictx.close()

    w2r = fctx.enter_context(nc.sbuf_tensor(f"w2r{tag}", [128, NCH * 2048],
                                            BF16))
    pctx = fctx.enter_context(contextlib.ExitStack())
    ps3 = [pctx.enter_context(nc.psum_tensor(f"p3{tag}{i}", [128, 512], F32))
           for i in range(2)]
    ps8 = [pctx.enter_context(nc.psum_tensor(f"p8{tag}{i}", [8, 512], F32))
           for i in range(4)]
    gp.wait(gp, gp.v)
    gp.wait(sc, V_H2S)
    for fc in range(NCH):
        gp.dma(lambda e, fc=fc: e.dma_start(
            out=w2r.rearrange("p (fc c) -> p fc c", c=2048)[:, fc],
            in_=w2T[fc * 128:(fc + 1) * 128, :]))
    V_W2 = gp.v

    te.wait(gp, V_W2)
    te.wait(sc, V_H2S)
    TE3 = te.v
    SC3 = sc.v
    VE3 = ve.v

    for tc in range(8):
        te.wait(ve, V_RT)
        for cs in range(4):
            k = tc * 4 + cs
            if k >= 2:
                te.wait(sc, SC3 + (k - 1))
            for fc in range(NCH):
                te.emit(lambda e, tc=tc, cs=cs, fc=fc, k=k: e.matmul(
                    ps3[k % 2][:, :],
                    h2[:, fc * TPC + tc * 128: fc * TPC + (tc + 1) * 128],
                    w2r[:, fc * 2048 + cs * 512: fc * 2048 + (cs + 1) * 512],
                    start=(fc == 0), stop=(fc == NCH - 1)))
            sc.wait(te, te.v)
            sc.emit(lambda e, cs=cs, k=k: e.activation(
                h3t[:, cs * 512:(cs + 1) * 512], ps3[k % 2][:, :],
                AF.Identity))
            te.wait(sc, sc.v)
            te.emit(lambda e, tc=tc, cs=cs: e.matmul(
                ps8[cs][:, :], wn[:, tc * 8:(tc + 1) * 8],
                h3t[:, cs * 512:(cs + 1) * 512],
                start=(tc == 0), stop=(tc == 7)))
    V_TE8 = te.v
    ve.wait(te, V_TE8)
    V_P8 = None
    for cs in range(4):
        V_P8 = ve.emit(lambda e, cs=cs: e.tensor_copy(
            p8[:, cs * 512:(cs + 1) * 512], ps8[cs][:, :]))
    sy.wait(ve, V_P8)
    V_PS = sy.dma(lambda e: e.dma_start(out=part_out, in_=p8[:, :]))
    pctx.close()
    fctx.close()
    return V_PS




def emit_ffn1(p, nc, ctx, xsb, consts, wr16T, sgD, suD, sdD, gwD, uwD, dwD,
              deps, tag="M"):
    gp, te, sc, ve, sy = p.gp, p.te, p.sc, p.ve, p.sy
    E = 16
    fctx = ctx.enter_context(contextlib.ExitStack())
    wn = fctx.enter_context(nc.sbuf_tensor(f"wn{tag}", [128, 8 * E], BF16))
    wTr = fctx.enter_context(nc.sbuf_tensor(f"wTr{tag}", [16, TPC], BF16))
    wrow = fctx.enter_context(nc.sbuf_tensor(f"wrw{tag}", [1, TPC], BF16))
    onebr = fctx.enter_context(nc.sbuf_tensor(f"obr{tag}", [1, 128], BF16))
    wbc = fctx.enter_context(nc.sbuf_tensor(f"wbc{tag}", [128, TPC], F32))
    xh = fctx.enter_context(nc.sbuf_tensor(f"xh{tag}", [128, NCH * TPC], BF16))
    V_LN, stats = emit_ln(p, nc, xsb, TPC, consts, xh, deps, tag=tag,
                          alloc_ctx=fctx)
    for eng in (te, sc, gp):
        eng.wait(ve, V_LN)
    V_RT = emit_router(p, nc, fctx, xsb, stats, consts, E, wr16T, wn, tag=tag)
    rctx = fctx.enter_context(contextlib.ExitStack())
    ps_tw = rctx.enter_context(nc.psum_tensor(f"ptw{tag}", [16, 128], BF16))
    te.wait(ve, V_RT)
    ve.wait(ve, V_RT)
    V_OBR = ve.emit(lambda e: e.tensor_copy(onebr[0:1, :],
                                            consts["ones_row"][0:1, :]))
    for tc in range(8):
        te.wait(ve, ve.v)
        te.emit(lambda e, tc=tc: e.transpose(
            ps_tw[:, :], wn[:, tc * E:(tc + 1) * E], consts["idb"][:, :]))
        ve.wait(te, te.v)
        ve.emit(lambda e, tc=tc: e.tensor_copy(
            wTr[:, tc * 128:(tc + 1) * 128], ps_tw[:, :]))
    V_WTR = ve.v
    rctx.close()

    sctx = fctx.enter_context(contextlib.ExitStack())
    gsh = sctx.enter_context(nc.sbuf_tensor(f"gsh{tag}", [128, NCH * TPC],
                                            BF16))
    emit_ffcols(p, nc, sctx, xh, TPC, sgD, gsh, TPC, AF.Silu, NCH,
                tag=tag + "g")
    te.wait(sc, sc.v)
    emit_ffcols(p, nc, sctx, xh, TPC, suD, None, TPC, None, NCH,
                tag=tag + "u", addto=gsh, addW=TPC, multo=True)
    V_SH = ve.v
    for eng in (te, gp):
        eng.wait(ve, V_SH)
    emit_ffcols(p, nc, sctx, gsh, TPC, sdD, None, TPC, None, NCH,
                tag=tag + "d", addto=xsb, addW=TPC)
    V_SD = ve.v
    sctx.close()

    ectx = fctx.enter_context(contextlib.ExitStack())
    gE = ectx.enter_context(nc.sbuf_tensor(f"gE{tag}", [128, 8 * TPC], BF16))
    ps_wb = ectx.enter_context(nc.psum_tensor(f"pwb{tag}", [128, 512], F32))
    for e_i in range(16):
        gp.wait(gp, gp.v)
        gp.wait(te, te.v)
        gp.wait(ve, V_WTR)
        V_WRD = gp.dma(lambda e, e_i=e_i: e.dma_start(
            out=wrow[0:1, :], in_=wTr[e_i:e_i + 1, :]))
        for w2 in range(2):
            te.wait(gp, V_WRD)
            te.wait(ve, max(V_OBR, ve.v))
            te.emit(lambda e, w2=w2: e.matmul(
                ps_wb[:, :], onebr[0:1, :],
                wrow[0:1, w2 * 512:(w2 + 1) * 512], start=True, stop=True))
            ve.wait(te, te.v)
            ve.emit(lambda e, w2=w2: e.tensor_copy(
                wbc[:, w2 * 512:(w2 + 1) * 512], ps_wb[:, :]))
        V_WBC = ve.v
        emit_ffcols(p, nc, ectx, xh, TPC,
                    gwD[e_i * D:(e_i + 1) * D, :], gE, TPC, AF.Silu, 8,
                    tag=f"{tag}G{e_i}")
        te.wait(sc, sc.v)
        emit_ffcols(p, nc, ectx, xh, TPC,
                    uwD[e_i * D:(e_i + 1) * D, :], None, TPC, None, 8,
                    tag=f"{tag}U{e_i}", addto=gE, addW=TPC, multo=True)
        ve.wait(ve, ve.v)
        for fc in range(8):
            ve.wait(ve, ve.v)
            ve.emit(lambda e, fc=fc: e.tensor_mul(
                gE[:, fc * TPC:(fc + 1) * TPC],
                gE[:, fc * TPC:(fc + 1) * TPC], wbc[:, :]))
        V_HS = ve.v
        for eng in (te, gp):
            eng.wait(ve, V_HS)
        emit_ffcols(p, nc, ectx, gE, TPC,
                    dwD[e_i * TPC:(e_i + 1) * TPC, :], None, TPC, None, NCH,
                    ncc=8, tag=f"{tag}D{e_i}", addto=xsb, addW=TPC)
    V_OUT = ve.v
    ectx.close()
    fctx.close()
    return V_OUT




def emit_A1(p, nc, ctx, t, ro=0, PFX="a0"):
    xT = t["XT_A"]
    wqkvT = t["wqkvT"]
    mask = t["maskA"]
    identb = t["identb"]
    onescol = t["onescolA"]
    onesrow = t["onesrow"]
    att = t["att2"]
    dq = t["dq_h"]
    dk = t["dk_h"]
    dv = t["dv_h"]
    gp, te, sc, ve, sy = p.gp, p.te, p.sc, p.ve, p.sy
    sy, gp, ve, sc, te = p.sy, p.gp, p.ve, p.sc, p.te
    ones_c = ctx.enter_context(nc.sbuf_tensor(PFX + "onesc", [128, 2], F32))
    ones_r = ctx.enter_context(nc.sbuf_tensor(PFX + "onesr", [1, 128], F32))
    ident_bf = ctx.enter_context(nc.sbuf_tensor(PFX + "identbf", [128, 128], BF16))
    mask_sb = ctx.enter_context(nc.sbuf_tensor(PFX + "masksb", [128, 128], F32))
    xhat = ctx.enter_context(nc.sbuf_tensor(PFX + "xhat", [128, NCH * TPC], BF16))
    consts = {"ones_col": ones_c[:, 0:1], "ones_row": ones_r,
              "eps": ones_c[:, 1:2]}

    p1 = ctx.enter_context(contextlib.ExitStack())
    xsb = p1.enter_context(nc.sbuf_tensor(PFX + "xsb", [128, NCH * TPC], F32))

    gp.dma(lambda e: e.dma_start(
        out=xsb.rearrange("p (cc t) -> p cc t", cc=NCH),
        in_=xT.rearrange("(cc p) t -> p cc t", p=128)))
    gp.dma(lambda e: e.dma_start(out=ones_c[:, :], in_=onescol))
    gp.dma(lambda e: e.dma_start(out=ones_r[:, :], in_=onesrow))
    gp.dma(lambda e: e.dma_start(out=ident_bf[:, :], in_=identb))
    V_IN = gp.dma(lambda e: e.dma_start(out=mask_sb[:, :], in_=mask))

    V_LN, _ = emit_ln(p, nc, xsb, TPC, consts, xhat, [(gp, V_IN)],
                      tag=PFX, alloc_ctx=p1)
    p1.close()

    p2 = ctx.enter_context(contextlib.ExitStack())
    wbuf = p2.enter_context(nc.sbuf_tensor(PFX + "wbuf", [128, 2 * 512], BF16))
    stage = p2.enter_context(nc.sbuf_tensor(PFX + "stage", [128, 8 * 512], BF16))
    ps_qkv = [p2.enter_context(nc.psum_tensor(f"{PFX}psq{i}", [128, 512], F32))
              for i in range(8)]
    NFG = 12
    GP0, TE0, SC0, SY0 = gp.v, te.v, sc.v, sy.v
    def te_after(fg, cc):
        return TE0 + 8 * (fg * NCH + cc + 1)

    gp.wait(ve, V_LN)
    for fg in range(NFG):
        for cc in range(NCH):
            i = fg * NCH + cc
            if i >= 2:
                fg2, cc2 = divmod(i - 2, NCH)
                gp.wait(te, te_after(fg2, cc2))
            gp.dma(lambda e, fg=fg, cc=cc, i=i: e.dma_start(
                out=wbuf[:, (i % 2) * 512:(i % 2 + 1) * 512],
                in_=wqkvT[cc * 128:(cc + 1) * 128,
                          fg * 512:(fg + 1) * 512]))

    te.wait(ve, V_LN)
    for fg in range(NFG):
        if fg >= 1:
            te.wait(sc, SC0 + 8 * fg)
        for cc in range(NCH):
            i = fg * NCH + cc
            te.wait(gp, GP0 + 16 * min(i + 2, NFG * NCH))
            for tc in range(8):
                te.emit(lambda e, i=i, cc=cc, tc=tc: e.matmul(
                    ps_qkv[tc][:, :],
                    xhat[:, cc * TPC + tc * 128: cc * TPC + (tc + 1) * 128],
                    wbuf[:, (i % 2) * 512:(i % 2 + 1) * 512],
                    start=(cc == 0), stop=(cc == NCH - 1)))
            assert te.v == te_after(fg, cc)

    for fg in range(NFG):
        sc.wait(te, te_after(fg, NCH - 1))
        sc.wait(sy, SY0 + 16 * fg)
        for tc in range(8):
            sc.emit(lambda e, tc=tc: e.activation(
                stage[:, tc * 512:(tc + 1) * 512], ps_qkv[tc][:, :],
                AF.Identity))
        assert sc.v == SC0 + 8 * (fg + 1)

    for fg in range(NFG):
        sy.wait(sc, SC0 + 8 * (fg + 1))
        sy.wait(sy, SY0 + 16 * fg)
        tgt = [dq, dk, dv][fg // 4]
        col = (fg % 4) * 512
        sy.dma(lambda e, tgt=tgt, col=col: e.dma_start(
            out=tgt.ap()[0:TPC, col:col + 512].rearrange(
                "(tc p) f -> p tc f", p=128),
            in_=stage.rearrange("p (tc f) -> p tc f", f=512)))
    V_ST = sy.v
    assert V_ST == SY0 + 16 * NFG
    p2.close()

    p3 = ctx.enter_context(contextlib.ExitStack())
    qTs = p3.enter_context(nc.sbuf_tensor(PFX + "qTs", [128, 16 * TPC], BF16))
    kTs = p3.enter_context(nc.sbuf_tensor(PFX + "kTs", [128, 16 * TPC], BF16))
    vS = p3.enter_context(nc.sbuf_tensor(PFX + "vS", [128, 16 * TPC], BF16))
    ps_sc = p3.enter_context(nc.psum_tensor(PFX + "pssc", [128, 128], F32))
    ps_tr = p3.enter_context(nc.psum_tensor(PFX + "pstr", [128, 128], BF16))
    ps_o = p3.enter_context(nc.psum_tensor(PFX + "pso", [128, 128], F32))
    scf = p3.enter_context(nc.sbuf_tensor(PFX + "scf", [128, 128], F32))
    expb = p3.enter_context(nc.sbuf_tensor(PFX + "expb", [128, 128], BF16))
    expT = p3.enter_context(nc.sbuf_tensor(PFX + "expT", [128, 128], BF16))
    den = p3.enter_context(nc.sbuf_tensor(PFX + "den", [128, 2], F32))
    ostg = p3.enter_context(nc.sbuf_tensor(PFX + "ostg", [128, 16 * 128], BF16))
    NG = 16 * TPC // 128

    sy.wait(sy, V_ST)
    sy.dma(lambda e: e.dma_start_transpose(
        qTs[:, :], dq.ap().rearrange("t (i d) -> (t i) d", d=128)))
    sy.dma(lambda e: e.dma_start_transpose(
        kTs[:, :], dk.ap().rearrange("t (i d) -> (t i) d", d=128)))
    for tp in range(8):
        sy.dma(lambda e, tp=tp: e.dma_start(
            out=vS[tp * 16:(tp + 1) * 16, :].rearrange(
                "i (g d) -> i g d", d=128),
            in_=bass.AP(dv, tp * D, [[128, 16], [8 * D, 128], [1, 128]])))
    V_TR = sy.v

    TE1, VE1, SC1, SY1 = te.v, ve.v, sc.v, sy.v
    te.wait(sy, V_TR)
    for g in range(NG):
        if g >= 1:
            te.wait(ve, VE1 + 3 * g)
        te.emit(lambda e, g=g: e.matmul(
            ps_sc[:, :], qTs[:, g * 128:(g + 1) * 128],
            kTs[:, g * 128:(g + 1) * 128], start=True, stop=True))
        te.wait(sc, SC1 + 2 * g + 1)
        te.emit(lambda e: e.transpose(ps_tr[:, :], expb[:, :],
                                      ident_bf[:, :]))
        te.wait(ve, VE1 + 3 * g + 2)
        te.emit(lambda e, g=g: e.matmul(
            ps_o[:, :], expT[:, :], vS[:, g * 128:(g + 1) * 128],
            start=True, stop=True))

    for g in range(NG):
        ve.wait(te, TE1 + 3 * g + 1)
        if g >= 1:
            ve.wait(sc, SC1 + 2 * g)
        ve.emit(lambda e: e.scalar_tensor_tensor(
            scf[:, :], ps_sc[:, :], RS, mask_sb[:, :], ALU.mult, ALU.add))
        ve.wait(te, TE1 + 3 * g + 2)
        ve.emit(lambda e: e.tensor_copy(expT[:, :], ps_tr[:, :]))
        ve.wait(ve, VE1 + 3 * g + 2)
        ve.wait(sc, SC1 + 2 * g + 1)
        ve.emit(lambda e: e.reciprocal(den[:, 1:2], den[:, 0:1]))

    for g in range(NG):
        sc.wait(ve, VE1 + 3 * g + 1)
        if g >= 1:
            sc.wait(te, TE1 + 3 * g - 1)
            sc.wait(ve, VE1 + 3 * g)
        sc.emit(lambda e: e.activation(expb[:, :], scf[:, :], AF.Exp,
                                       accum_out=den[:, 0:1]))
        sc.wait(te, TE1 + 3 * g + 3)
        sc.wait(ve, VE1 + 3 * g + 3)
        if g >= 16:
            sc.wait(sy, SY1 + 16 * (g // 16))
        sc.emit(lambda e, g=g: e.activation(
            ostg[:, (g % 16) * 128:(g % 16 + 1) * 128], ps_o[:, :],
            AF.Identity, scale=den[:, 1:2]))

    for blk in range(NG // 16):
        sy.wait(sc, SC1 + 2 * 16 * (blk + 1))
        sy.wait(sy, V_TR + 16 * blk)
        sy.dma(lambda e, blk=blk: e.dma_start(
            out=att[(ro + blk) * 2048:(ro + blk + 1) * 2048, :].rearrange(
                "(g p) d -> p g d", p=128),
            in_=ostg.rearrange("p (g d) -> p g d", d=128)))
    sy.wait(sy, sy.v)
    p3.close()



def emit_F1(p, nc, ctx, t):
    xT = t["xTme"]
    rowsT = t["rowsT"]
    wprojT = t["wprojT"]
    w1T = t["w1T"]
    w3T = t["w3T"]
    w2T = t["w2T"]
    wrT = t["wrT"]
    onescol = t["onescol"]
    onesrow = t["onesrow"]
    x1T = t["x1T"]
    part = t["part1"]
    gp, te, sc, ve, sy = p.gp, p.te, p.sc, p.ve, p.sy
    gp, te, sc, ve, sy = p.gp, p.te, p.sc, p.ve, p.sy
    ones_c = ctx.enter_context(nc.sbuf_tensor("f1onesc", [128, 4], F32))
    ones_r = ctx.enter_context(nc.sbuf_tensor("f1onesr", [1, 128], F32))
    xsb = ctx.enter_context(nc.sbuf_tensor("f1xsb", [128, NCH * TPC], F32))
    consts = {"ones_col": ones_c[:, 0:1], "ones_row": ones_r,
              "eps": ones_c[:, 1:2], "one1": ones_c,
              "ones_bf": None}

    gp.dma(lambda e: e.dma_start(out=ones_c[:, :], in_=onescol))
    gp.dma(lambda e: e.dma_start(out=ones_r[:, :], in_=onesrow))
    gp.dma(lambda e: e.dma_start(
        out=xsb.rearrange("p (cc t) -> p cc t", cc=NCH),
        in_=xT.rearrange("(cc p) t -> p cc t", p=128)))
    V_IN = gp.dma(lambda e: e.dma_start(
        out=rsb.rearrange("p (cc t) -> p cc t", cc=NCH),
        in_=rowsT.rearrange("(cc p) t -> p cc t", p=128)))

    prctx = ctx.enter_context(contextlib.ExitStack())
    rsb = prctx.enter_context(nc.sbuf_tensor("f1rsb", [128, NCH * TPC],
                                             BF16))
    wcol = prctx.enter_context(nc.sbuf_tensor("f1wcp", [128, 2 * NCH * 128],
                                              BF16))
    psp = [prctx.enter_context(nc.psum_tensor(f"f1psp{i}", [128, 512], F32))
           for i in range(4)]
    GP0, TE0, VE0 = gp.v, te.v, ve.v

    def te_after(fc):
        return TE0 + (fc + 1) * 2 * NCH

    gp.wait(gp, V_IN)
    for fc in range(NCH):
        if fc >= 2:
            gp.wait(te, te_after(fc - 2))
        gp.dma(lambda e, fc=fc: e.dma_start(
            out=wcol.rearrange("p (s cc f) -> p s cc f", s=2, f=128)
                [:, fc % 2],
            in_=wprojT[:, fc * 128:(fc + 1) * 128].rearrange(
                "(cc p) f -> p cc f", p=128)))
    te.wait(gp, V_IN)
    for fc in range(NCH):
        te.wait(gp, GP0 + 16 * min(fc + 2, NCH))
        if fc >= 2:
            te.wait(ve, VE0 + 2 * (fc - 1))
        for tt in range(2):
            for cc in range(NCH):
                te.emit(lambda e, fc=fc, tt=tt, cc=cc: e.matmul(
                    psp[(fc % 2) * 2 + tt][:, :],
                    wcol[:, (fc % 2) * NCH * 128 + cc * 128:
                         (fc % 2) * NCH * 128 + (cc + 1) * 128],
                    rsb[:, cc * TPC + tt * 512: cc * TPC + (tt + 1) * 512],
                    start=(cc == 0), stop=(cc == NCH - 1)))
    for fc in range(NCH):
        ve.wait(te, TE0 + fc * 2 * NCH + NCH)
        ve.emit(lambda e, fc=fc: e.tensor_add(
            xsb[:, fc * TPC: fc * TPC + 512], psp[(fc % 2) * 2][:, :],
            xsb[:, fc * TPC: fc * TPC + 512]))
        ve.wait(te, TE0 + (fc + 1) * 2 * NCH)
        ve.emit(lambda e, fc=fc: e.tensor_add(
            xsb[:, fc * TPC + 512: fc * TPC + 1024],
            psp[(fc % 2) * 2 + 1][:, :],
            xsb[:, fc * TPC + 512: fc * TPC + 1024]))
    V_X1 = ve.v
    assert V_X1 == VE0 + 2 * NCH
    prctx.close()

    sy.wait(ve, V_X1)
    V_X1ST = sy.dma(lambda e: e.dma_start(
        out=x1T.rearrange("(cc p) t -> p cc t", p=128),
        in_=xsb.rearrange("p (cc t) -> p cc t", cc=NCH)))

    emit_ffn0(p, nc, ctx, xsb, consts, w1T, w3T, w2T, wrT, part,
              [(ve, V_X1), (gp, V_IN)], tag="n")
    sy.wait(sy, sy.v)


def emit_B1(p, nc, ctx, t):
    xT = t["x1T"]
    scat = t["scat8a"]
    haloT = t["haloT"]
    wqTp = t["wqTp"]
    wkTp = t["wkTp"]
    wvT = t["wvT"]
    woD = t["woD"]
    tabQ = t["tabQ"]
    tabK = t["tabK"]
    maskB = t["maskB"]
    identb = t["identb"]
    w1T = t["w1T"]
    w3T = t["w3T"]
    w2T = t["w2T"]
    wrT = t["wrT"]
    onescol = t["onescol"]
    onesrow = t["onesrow"]
    x3T = t["x3T"]
    part = t["part2"]
    dxx = t["dxx"]
    datt = t["datt"]
    KL = 1280
    gp, te, sc, ve, sy = p.gp, p.te, p.sc, p.ve, p.sy
    gp, te, sc, ve, sy = p.gp, p.te, p.sc, p.ve, p.sy
    ones_c = ctx.enter_context(nc.sbuf_tensor("b1onesc", [128, 4], F32))
    ones_r = ctx.enter_context(nc.sbuf_tensor("b1onesr", [1, 128], F32))
    ones_b = ctx.enter_context(nc.sbuf_tensor("b1onesb", [128, 1], BF16))
    idb = ctx.enter_context(nc.sbuf_tensor("b1idb", [128, 128], BF16))
    consts = {"ones_col": ones_c[:, 0:1], "ones_row": ones_r,
              "eps": ones_c[:, 1:2], "one1": ones_c}
    gp.dma(lambda e: e.dma_start(out=ones_c[:, :], in_=onescol))
    gp.dma(lambda e: e.dma_start(out=ones_r[:, :], in_=onesrow))
    V_C = gp.dma(lambda e: e.dma_start(out=idb[:, :], in_=identb))
    ve.wait(gp, V_C)
    V_OB = ve.emit(lambda e: e.tensor_copy(ones_b[:, :], ones_c[:, 0:1]))

    ph1 = ctx.enter_context(contextlib.ExitStack())
    xhB = ph1.enter_context(nc.sbuf_tensor("b1xhB", [128, NCH * KL], BF16))
    ph1b = ph1.enter_context(contextlib.ExitStack())
    xsb = ph1b.enter_context(nc.sbuf_tensor("b1xsb", [128, NCH * TPC], F32))
    xhalo = ph1b.enter_context(nc.sbuf_tensor("b1xhalo", [128, NCH * 256],
                                              F32))
    ssb = ph1b.enter_context(nc.sbuf_tensor("b1ssb", [128, NCH * 8], F32))
    sred = ph1b.enter_context(nc.sbuf_tensor("b1sred", [128, 8], F32))
    gp.wait(gp, gp.v)
    gp.dma(lambda e: e.dma_start(
        out=xsb.rearrange("p (cc t) -> p cc t", cc=NCH),
        in_=xT.rearrange("(cc p) t -> p cc t", p=128)))
    gp.dma(lambda e: e.dma_start(
        out=xhalo.rearrange("p (cc t) -> p cc t", cc=NCH),
        in_=haloT.rearrange("(cc p) t -> p cc t", p=128)))
    V_XIN = gp.dma(lambda e: e.dma_start(
        out=ssb.rearrange("p (cc t) -> p cc t", cc=NCH),
        in_=scat.rearrange("(cc p) t -> p cc t", p=128)))
    ve.wait(gp, V_XIN)
    V_SC = None
    for cc in range(NCH):
        ve.wait(ve, ve.v)
        ve.emit(lambda e, cc=cc: e.tensor_copy(
            sred[:, :],
            ssb.rearrange("p (cc e) -> p cc e", cc=NCH)[:, cc]))
        ve.wait(ve, ve.v)
        V_SC = ve.emit(lambda e, cc=cc: e.tensor_add(
            xsb[:, cc * TPC: cc * TPC + 8], sred[:, :],
            xsb[:, cc * TPC: cc * TPC + 8]))
    sy.wait(ve, V_SC)
    V_X2ST = sy.dma(lambda e: e.dma_start(
        out=dxx.rearrange("(cc p) t -> p cc t", p=128),
        in_=xsb.rearrange("p (cc t) -> p cc t", cc=NCH)))
    V_LN, _ = emit_ln(p, nc, xsb, TPC, consts, xhB,
                      [(ve, V_SC), (gp, V_XIN)], tag="bo",
                      alloc_ctx=ph1b, outW=KL, out_off=256)
    V_LNH, _ = emit_ln(p, nc, xhalo, 256, consts, xhB,
                       [(gp, V_XIN)], tag="bh",
                       alloc_ctx=ph1b, outW=KL, out_off=0)
    for eng in (te, sc, gp, sy):
        eng.wait(ve, V_LN)
        eng.wait(ve, V_LNH)
    sy.wait(sy, V_X2ST)
    for eng in (te, sc, gp, ve):
        eng.wait(sy, V_X2ST)
    ph1b.close()

    att_r = ctx.enter_context(contextlib.ExitStack())
    ySB = att_r.enter_context(nc.sbuf_tensor("b1ySB", [128, 16 * TPC], BF16,
                                             side="right"))
    att_r2 = att_r.enter_context(contextlib.ExitStack())
    qT = att_r2.enter_context(nc.sbuf_tensor("b1qT", [128, 16 * TPC], BF16,
                                             side="right"))
    kT = att_r2.enter_context(nc.sbuf_tensor("b1kT", [128, 16 * KL], BF16,
                                             side="right"))
    vS = att_r2.enter_context(nc.sbuf_tensor("b1vS", [128, 160 * 128], BF16,
                                             side="right"))
    msk = att_r2.enter_context(nc.sbuf_tensor("b1msk", [128, 16 * 256], BF16,
                                              side="right"))
    gp.wait(gp, gp.v)
    V_MSK = gp.dma(lambda e: e.dma_start(out=msk[:, :], in_=maskB))

    ph2 = ctx.enter_context(contextlib.ExitStack())
    emit_ffcols(p, nc, ph2, xhB, KL, wqTp, qT, TPC, AF.Identity, NCH,
                tag="bq", src_off=256, L=TPC)
    emit_ffcols(p, nc, ph2, xhB, KL, wkTp, kT, KL, AF.Identity, NCH,
                tag="bk", L=KL)
    for h in range(16):
        hctx = ph2.enter_context(contextlib.ExitStack())
        vTh = hctx.enter_context(nc.sbuf_tensor(f"b1vTh{h}", [128, KL],
                                                BF16))
        ps_vt = hctx.enter_context(nc.psum_tensor(f"b1pvt{h}", [128, 128],
                                                  BF16))
        emit_ffcols(p, nc, hctx, xhB, KL,
                    wvT[:, h * 128:(h + 1) * 128], vTh, KL,
                    AF.Identity, 1, tag=f"bv{h}", L=KL)
        te.wait(sc, sc.v)
        te.wait(ve, V_OB)
        for tc in range(10):
            te.wait(ve, ve.v)
            te.emit(lambda e, tc=tc, ps_vt=ps_vt, vTh=vTh: e.transpose(
                ps_vt[:, :], vTh[:, tc * 128:(tc + 1) * 128], idb[:, :]))
            ve.wait(te, te.v)
            ve.emit(lambda e, h=h, tc=tc, ps_vt=ps_vt: e.tensor_copy(
                vS[:, (tc * 16 + h) * 128:(tc * 16 + h) * 128 + 128],
                ps_vt[:, :]))
        te.wait(ve, ve.v)
        hctx.close()
    V_VS = ve.v
    V_TEP2 = te.v
    ph2.close()
    ph1.close()

    ph3 = ctx.enter_context(contextlib.ExitStack())
    tq = ph3.enter_context(nc.sbuf_tensor("b1tq", [128, 2 * TPC], F32))
    tk = ph3.enter_context(nc.sbuf_tensor("b1tk", [128, 2 * KL], F32))
    scr = ph3.enter_context(nc.sbuf_tensor("b1scr", [64, 3 * KL], F32))
    gp.wait(gp, gp.v)
    gp.wait(te, V_TEP2)
    gp.wait(ve, V_VS)
    gp.wait(sc, sc.v)
    gp.dma(lambda e: e.dma_start(out=tq[:, :], in_=tabQ))
    V_TAB = gp.dma(lambda e: e.dma_start(out=tk[:, :], in_=tabK))
    ve.wait(gp, V_TAB)
    ve.wait(sc, sc.v)

    def rope(buf, tab, W):
        cos0 = tab[0:64, 0:W]
        cos1 = tab[64:128, 0:W]
        sin0 = tab[0:64, W:2 * W]
        sin1 = tab[64:128, W:2 * W]
        for h in range(16):
            qe = buf[:, h * W:(h + 1) * W][0:64, :]
            qo = buf[:, h * W:(h + 1) * W][64:128, :]
            s1 = scr[0:64, 0:W]
            s2 = scr[0:64, KL:KL + W]
            s3 = scr[0:64, 2 * KL:2 * KL + W]
            ve.wait(ve, ve.v)
            ve.emit(lambda e, qe=qe, c=cos0, s1=s1: e.tensor_mul(
                s1, qe, c))
            ve.emit(lambda e, qo=qo, sn=sin1, s2=s2: e.tensor_mul(
                s2, qo, sn))
            ve.emit(lambda e, qe=qe, sn=sin0, s3=s3: e.tensor_mul(
                s3, qe, sn))
            ve.wait(ve, ve.v)
            ve.emit(lambda e, qe=qe, s1=s1, s2=s2: e.tensor_sub(
                qe, s1, s2))
            ve.wait(ve, ve.v)
            ve.emit(lambda e, qo=qo, c=cos1, s2=s2: e.tensor_mul(
                s2, qo, c))
            ve.wait(ve, ve.v)
            ve.emit(lambda e, qo=qo, s2=s2, s3=s3: e.tensor_add(
                qo, s3, s2))
    rope(qT, tq, TPC)
    rope(kT, tk, KL)
    V_ROPE = ve.v
    ph3.close()

    ph4 = ctx.enter_context(contextlib.ExitStack())
    exB = ph4.enter_context(nc.sbuf_tensor("b1exB", [128, 4 * 256], BF16))
    rbc = ph4.enter_context(nc.sbuf_tensor("b1rbc", [128, 256], F32))
    rro = ph4.enter_context(nc.sbuf_tensor("b1rro", [1, 256], F32))
    ps_s = ph4.enter_context(nc.psum_tensor("b1psbs", [128, 256], F32))
    ps_d = ph4.enter_context(nc.psum_tensor("b1psbd", [1, 256], F32))
    ps_y = ph4.enter_context(nc.psum_tensor("b1psby", [128, 256], F32))
    ps_rb = ph4.enter_context(nc.psum_tensor("b1psrb", [128, 256], F32))
    te.wait(ve, V_ROPE)
    te.wait(gp, V_MSK)
    for ci in range(4):
        for h in range(16):
            for kc in range(4):
                te.wait(ve, ve.v)
                te.emit(lambda e, ci=ci, h=h, kc=kc: e.matmul(
                    ps_s[:, :],
                    kT[:, h * KL + ci * 256 + kc * 128:
                       h * KL + ci * 256 + (kc + 1) * 128],
                    qT[:, h * TPC + ci * 256: h * TPC + (ci + 1) * 256],
                    start=True, stop=True))
                ve.wait(te, te.v)
                ve.emit(lambda e, ci=ci, kc=kc: e.scalar_tensor_tensor(
                    exB[:, kc * 256:(kc + 1) * 256], ps_s[:, :], RS,
                    msk[:, (ci * 4 + kc) * 256:(ci * 4 + kc + 1) * 256],
                    ALU.mult, ALU.add))
                sc.wait(ve, ve.v)
                sc.emit(lambda e, kc=kc: e.activation(
                    exB[:, kc * 256:(kc + 1) * 256],
                    exB[:, kc * 256:(kc + 1) * 256], AF.Exp))
            te.wait(sc, sc.v)
            for kc in range(4):
                te.emit(lambda e, kc=kc: e.matmul(
                    ps_d[:, :], ones_b[:, 0:1],
                    exB[:, kc * 256:(kc + 1) * 256],
                    start=(kc == 0), stop=(kc == 3)))
            for kc in range(4):
                te.emit(lambda e, ci=ci, h=h, kc=kc: e.matmul(
                    ps_y[:, :],
                    vS[:, ((ci * 2 + kc) * 16 + h) * 128:
                       ((ci * 2 + kc) * 16 + h) * 128 + 128],
                    exB[:, kc * 256:(kc + 1) * 256],
                    start=(kc == 0), stop=(kc == 3)))
            V_YMM = te.v
            ve.wait(te, V_YMM)
            ve.emit(lambda e: e.reciprocal(rro[0:1, :], ps_d[:, :]))
            te.wait(ve, ve.v)
            V_RB = te.emit(lambda e: e.matmul(
                ps_rb[:, :], ones_r[0:1, :], rro[0:1, :],
                start=True, stop=True))
            ve.wait(te, V_RB)
            ve.emit(lambda e: e.tensor_copy(rbc[:, :], ps_rb[:, :]))
            ve.wait(ve, ve.v)
            ve.emit(lambda e, ci=ci, h=h: e.tensor_mul(
                ySB[:, h * TPC + ci * 256: h * TPC + (ci + 1) * 256],
                ps_y[:, :], rbc[:, :]))
    V_ATT = ve.v
    ph4.close()

    ph5a = ctx.enter_context(contextlib.ExitStack())
    atout = ph5a.enter_context(nc.sbuf_tensor("b1atout", [128, NCH * TPC],
                                              BF16))
    for eng in (te, sc, gp):
        eng.wait(ve, V_ATT)
    emit_ffcols(p, nc, ph5a, ySB, TPC, woD, atout, TPC, AF.Identity, NCH,
                tag="bw")
    V_WOS = sc.v
    sy.wait(sc, V_WOS)
    sy.wait(sy, sy.v)
    V_ATST = sy.dma(lambda e: e.dma_start(
        out=datt.rearrange("(cc p) t -> p cc t", p=128),
        in_=atout.rearrange("p (cc t) -> p cc t", cc=NCH)))
    ph5a.close()
    att_r2.close()
    att_r.close()

    ph5 = ctx.enter_context(contextlib.ExitStack())
    xsb2 = ph5.enter_context(nc.sbuf_tensor("b1xsb2", [128, NCH * TPC], F32))
    ph5c = ph5.enter_context(contextlib.ExitStack())
    attin = ph5c.enter_context(nc.sbuf_tensor("b1attin", [128, NCH * TPC],
                                              BF16))
    gp.wait(gp, gp.v)
    gp.wait(sy, V_ATST)
    gp.wait(sc, sc.v)
    gp.wait(te, te.v)
    gp.dma(lambda e: e.dma_start(
        out=xsb2.rearrange("p (cc t) -> p cc t", cc=NCH),
        in_=dxx.rearrange("(cc p) t -> p cc t", p=128)))
    V_RL = gp.dma(lambda e: e.dma_start(
        out=attin.rearrange("p (cc t) -> p cc t", cc=NCH),
        in_=datt.rearrange("(cc p) t -> p cc t", p=128)))
    ve.wait(gp, V_RL)
    V_X3 = None
    for cc in range(NCH):
        ve.wait(ve, ve.v)
        V_X3 = ve.emit(lambda e, cc=cc: e.tensor_add(
            xsb2[:, cc * TPC:(cc + 1) * TPC],
            attin[:, cc * TPC:(cc + 1) * TPC],
            xsb2[:, cc * TPC:(cc + 1) * TPC]))
    ph5c.close()
    sy.wait(ve, V_X3)
    sy.dma(lambda e: e.dma_start(
        out=x3T.rearrange("(cc p) t -> p cc t", p=128),
        in_=xsb2.rearrange("p (cc t) -> p cc t", cc=NCH)))
    emit_ffn0(p, nc, ph5, xsb2, consts, w1T, w3T, w2T, wrT, part,
              [(ve, V_X3)], tag="c")
    sy.wait(sy, sy.v)
    ph5.close()


def emit_M1(p, nc, ctx, t):
    xT = t["x3T"]
    scat = t["scat8b"]
    onescol = t["onescol"]
    onesrow = t["onesrow"]
    x4T = t["x4T"]
    xhT = t["xhT"]
    gp, te, sc, ve, sy = p.gp, p.te, p.sc, p.ve, p.sy
    gp, te, sc, ve, sy = p.gp, p.te, p.sc, p.ve, p.sy
    ones_c = ctx.enter_context(nc.sbuf_tensor("m1onesc", [128, 4], F32))
    ones_r = ctx.enter_context(nc.sbuf_tensor("m1onesr", [1, 128], F32))
    consts = {"ones_col": ones_c[:, 0:1], "ones_row": ones_r,
              "eps": ones_c[:, 1:2], "one1": ones_c}
    xsb = ctx.enter_context(nc.sbuf_tensor("m1xsb", [128, NCH * TPC], F32))
    xh = ctx.enter_context(nc.sbuf_tensor("m1xh", [128, NCH * TPC], F32))
    ssb = ctx.enter_context(nc.sbuf_tensor("m1ssb", [128, NCH * 8], F32))
    sred = ctx.enter_context(nc.sbuf_tensor("m1sred", [128, 8], F32))
    gp.dma(lambda e: e.dma_start(out=ones_c[:, :], in_=onescol))
    gp.dma(lambda e: e.dma_start(out=ones_r[:, :], in_=onesrow))
    gp.dma(lambda e: e.dma_start(
        out=xsb.rearrange("p (cc t) -> p cc t", cc=NCH),
        in_=xT.rearrange("(cc p) t -> p cc t", p=128)))
    V_XIN = gp.dma(lambda e: e.dma_start(
        out=ssb.rearrange("p (cc t) -> p cc t", cc=NCH),
        in_=scat.rearrange("(cc p) t -> p cc t", p=128)))
    ve.wait(gp, V_XIN)
    V_SC = None
    for cc in range(NCH):
        ve.wait(ve, ve.v)
        ve.emit(lambda e, cc=cc: e.tensor_copy(
            sred[:, :],
            ssb.rearrange("p (cc e) -> p cc e", cc=NCH)[:, cc]))
        ve.wait(ve, ve.v)
        V_SC = ve.emit(lambda e, cc=cc: e.tensor_add(
            xsb[:, cc * TPC: cc * TPC + 8], sred[:, :],
            xsb[:, cc * TPC: cc * TPC + 8]))
    sy.wait(ve, V_SC)
    sy.dma(lambda e: e.dma_start(
        out=x4T.rearrange("(cc p) t -> p cc t", p=128),
        in_=xsb.rearrange("p (cc t) -> p cc t", cc=NCH)))
    V_LN, _ = emit_ln(p, nc, xsb, TPC, consts, xh,
                      [(ve, V_SC), (gp, V_XIN)], tag="m", alloc_ctx=ctx)
    sy.wait(ve, V_LN)
    sy.dma(lambda e: e.dma_start(
        out=xhT.rearrange("(cc p) t -> p cc t", p=128),
        in_=xh.rearrange("p (cc t) -> p cc t", cc=NCH)))
    sy.wait(sy, sy.v)


def emit_M2(p, nc, ctx, t):
    xh = t["xhP"]
    cw = t["cw"]
    cb = t["cbI"]
    xpT = t["xpT"]
    dtwT = t["dtwT"]
    dtb = t["dtb"]
    Ain = t["Ain"]
    Dvin = t["Dvin"]
    onescol = t["onescol"]
    onesrow = t["onesrow"]
    yout = t["yout"]
    TH = 1024
    gp, te, sc, ve, sy = p.gp, p.te, p.sc, p.ve, p.sy
    gp, te, sc, ve, sy = p.gp, p.te, p.sc, p.ve, p.sy
    ones_c = ctx.enter_context(nc.sbuf_tensor("m2onesc", [128, 4], F32))
    ones_r = ctx.enter_context(nc.sbuf_tensor("m2onesr", [1, 128], F32))
    bst = ctx.enter_context(nc.sbuf_tensor("m2bst", [1, 2 * TH], F32))
    cwsb = ctx.enter_context(nc.sbuf_tensor("m2cwsb", [128, NCH * 4], F32))
    cbsb = ctx.enter_context(nc.sbuf_tensor("m2cbsb", [128, NCH], F32))
    xps = ctx.enter_context(nc.sbuf_tensor("m2xps", [128, NCH * 96], BF16))
    dtw = ctx.enter_context(nc.sbuf_tensor("m2dtw", [64, TPC], BF16))
    dtbs = ctx.enter_context(nc.sbuf_tensor("m2dtbs", [128, 8], F32))
    Acol = ctx.enter_context(nc.sbuf_tensor("m2Acol", [128, 8 * 16], F32))
    Dcol = ctx.enter_context(nc.sbuf_tensor("m2Dcol", [128, 8], F32))
    carry = ctx.enter_context(nc.sbuf_tensor("m2carry", [128, 128], F32))
    xcN = ctx.enter_context(nc.sbuf_tensor("m2xcN", [128, 8 * TH], F32))
    dltN = ctx.enter_context(nc.sbuf_tensor("m2dltN", [128, 8 * TH], F32))
    duN = ctx.enter_context(nc.sbuf_tensor("m2duN", [128, 8 * TH], F32))
    yac = ctx.enter_context(nc.sbuf_tensor("m2yac", [128, 8 * TH], F32))
    Bbc = ctx.enter_context(nc.sbuf_tensor("m2Bbc", [128, TH], F32))
    Cbc = ctx.enter_context(nc.sbuf_tensor("m2Cbc", [128, TH], F32))
    htl = ctx.enter_context(nc.sbuf_tensor("m2htl", [128, TH], F32))
    ttl = ctx.enter_context(nc.sbuf_tensor("m2ttl", [128, TH], F32))
    atl = ctx.enter_context(nc.sbuf_tensor("m2atl", [128, TH], F32))
    xdb = ctx.enter_context(nc.sbuf_tensor("m2xdb", [96, TH], F32))
    xdbb = ctx.enter_context(nc.sbuf_tensor("m2xdbb", [64, TH], BF16))
    pad = ctx.enter_context(nc.sbuf_tensor("m2pad", [128, TH + 3], F32))
    xct = ctx.enter_context(nc.sbuf_tensor("m2xct", [128, TH], F32))
    xcb = ctx.enter_context(nc.sbuf_tensor("m2xcb", [128, TH], BF16))
    ps_xd = [ctx.enter_context(nc.psum_tensor(f"m2pxd{i}", [96, 512], F32))
             for i in range(2)]
    ps_dl = [ctx.enter_context(nc.psum_tensor(f"m2pdl{i}", [128, 512], F32))
             for i in range(2)]
    ps_bc = [ctx.enter_context(nc.psum_tensor(f"m2pbc{i}", [128, 512], F32))
             for i in range(4)]
    spl = ctx.enter_context(nc.sbuf_tensor("m2spl", [128, 2 * 512], F32))

    gp.dma(lambda e: e.dma_start(out=ones_c[:, :], in_=onescol))
    gp.dma(lambda e: e.dma_start(out=ones_r[:, :], in_=onesrow))
    gp.dma(lambda e: e.dma_start(
        out=cwsb.rearrange("p (cc k) -> p cc k", k=4),
        in_=cw.rearrange("(cc p) k -> p cc k", p=128)))
    gp.dma(lambda e: e.dma_start(out=cbsb[:, :], in_=cb))
    gp.dma(lambda e: e.dma_start(
        out=xps.rearrange("p (cc f) -> p cc f", f=96),
        in_=xpT.rearrange("(cc p) f -> p cc f", p=128)))
    gp.dma(lambda e: e.dma_start(out=dtw[:, :], in_=dtwT))
    gp.dma(lambda e: e.dma_start(out=dtbs[:, :], in_=dtb))
    gp.dma(lambda e: e.dma_start(out=Acol[:, :], in_=Ain))
    V_W = gp.dma(lambda e: e.dma_start(out=Dcol[:, :], in_=Dvin))
    ve.wait(gp, V_W)
    V_CZ = ve.emit(lambda e: e.memset(carry[:, :], 0.0))
    sc.wait(gp, V_W)
    V_AE = sc.emit(lambda e: e.activation(Acol[:, :], Acol[:, :], AF.Exp))
    ve.wait(sc, V_AE)
    V_AN = ve.emit(lambda e: e.tensor_scalar(Acol[:, :], Acol[:, :], -1.0,
                                             None, ALU.mult))
    sc.wait(ve, V_AN)

    for th in range(2):
        GP_TH = gp.v
        for cc in range(NCH):
            tgt = xcN[:, cc * TH:(cc + 1) * TH] if cc < 8 else xct[:, :]
            gp.wait(ve, ve.v)
            gp.wait(sc, sc.v)
            gp.wait(gp, gp.v)
            if th == 0:
                ve.wait(gp, gp.v)
                ve.emit(lambda e: e.memset(pad[:, 0:3], 0.0))
                V_PD = gp.dma(lambda e, cc=cc: e.dma_start(
                    out=pad[:, 3:3 + TH],
                    in_=xh[cc * 128:(cc + 1) * 128, 0:TH]))
            else:
                V_PD = gp.dma(lambda e, cc=cc: e.dma_start(
                    out=pad[:, 0:3 + TH],
                    in_=xh[cc * 128:(cc + 1) * 128, TH - 3:T]))
            ve.wait(gp, V_PD)
            ve.wait(ve, ve.v)
            ve.wait(sc, sc.v)
            ve.wait(te, te.v)
            ve.emit(lambda e, cc=cc: e.tensor_scalar(
                ttl[:, :], pad[:, 0:TH], cwsb[:, 4 * cc:4 * cc + 1], None,
                ALU.mult))
            for k in range(1, 4):
                ve.wait(ve, ve.v)
                ve.emit(lambda e, cc=cc, k=k: e.scalar_tensor_tensor(
                    ttl[:, :], pad[:, k:k + TH],
                    cwsb[:, 4 * cc + k:4 * cc + k + 1],
                    ttl[:, :], ALU.mult, ALU.add))
            sc.wait(ve, ve.v)
            sc.wait(te, te.v)
            V_SIL = sc.emit(lambda e, cc=cc, tgt=tgt: e.activation(
                tgt, ttl[:, :], AF.Silu, bias=cbsb[:, cc:cc + 1]))
            sc.wait(sc, V_SIL)
            V_XCB = sc.emit(lambda e, cc=cc, tgt=tgt: e.activation(
                xcb[:, :], tgt, AF.Identity))
            te.wait(sc, V_XCB)
            for w2 in range(2):
                te.emit(lambda e, cc=cc, w2=w2: e.matmul(
                    ps_xd[w2][:, :], xps[:, cc * 96:(cc + 1) * 96],
                    xcb[:, w2 * 512:(w2 + 1) * 512],
                    start=(cc == 0), stop=(cc == NCH - 1)))
        V_XD = te.v
        ve.wait(te, V_XD)
        for w2 in range(2):
            ve.wait(ve, ve.v)
            ve.emit(lambda e, w2=w2: e.tensor_copy(
                xdb[:, w2 * 512:(w2 + 1) * 512], ps_xd[w2][:, :]))
        V_XDB = ve.v
        ve.wait(ve, V_XDB)
        V_XBB = ve.emit(lambda e: e.tensor_copy(xdbb[:, :], xdb[0:64, :]))
        te.wait(ve, V_XBB)
        for ec in range(8):
            if ec >= 1:
                te.wait(sc, sc.v)
            for w2 in range(2):
                te.emit(lambda e, ec=ec, w2=w2: e.matmul(
                    ps_dl[w2][:, :], dtw[:, ec * 128:(ec + 1) * 128],
                    xdbb[:, w2 * 512:(w2 + 1) * 512],
                    start=True, stop=True))
            sc.wait(te, te.v)
            for w2 in range(2):
                sc.wait(sc, sc.v)
                sc.wait(ve, ve.v)
                sc.emit(lambda e, ec=ec, w2=w2: e.activation(
                    spl[:, 0:512], ps_dl[w2][:, :], AF.Abs,
                    bias=dtbs[:, ec:ec + 1]))
                sc.wait(sc, sc.v)
                sc.emit(lambda e: e.activation(
                    spl[:, 0:512], spl[:, 0:512], AF.Exp, scale=-1.0))
                sc.wait(sc, sc.v)
                sc.emit(lambda e: e.activation(
                    spl[:, 0:512], spl[:, 0:512], AF.Ln,
                    bias=ones_c[:, 0:1]))
                sc.wait(sc, sc.v)
                sc.emit(lambda e, ec=ec, w2=w2: e.activation(
                    spl[:, 512:1024], ps_dl[w2][:, :], AF.Relu,
                    bias=dtbs[:, ec:ec + 1]))
                ve.wait(sc, sc.v)
                ve.wait(ve, ve.v)
                ve.emit(lambda e, ec=ec, w2=w2: e.tensor_add(
                    dltN[:, ec * TH + w2 * 512: ec * TH + (w2 + 1) * 512],
                    spl[:, 0:512], spl[:, 512:1024]))
        V_DLT = ve.v
        for ec in range(8):
            ve.wait(ve, ve.v)
            ve.emit(lambda e, ec=ec: e.tensor_mul(
                duN[:, ec * TH:(ec + 1) * TH],
                dltN[:, ec * TH:(ec + 1) * TH],
                xcN[:, ec * TH:(ec + 1) * TH]))
        V_DU = ve.v

        for n in range(16):
            gp.wait(gp, gp.v)
            gp.wait(te, te.v)
            gp.wait(ve, V_XDB)
            gp.dma(lambda e, n=n: e.dma_start(
                out=bst[0:1, 0:TH], in_=xdb[64 + n:65 + n, 0:TH]))
            V_BST = gp.dma(lambda e, n=n: e.dma_start(
                out=bst[0:1, TH:2 * TH], in_=xdb[80 + n:81 + n, 0:TH]))
            te.wait(ve, ve.v)
            te.wait(gp, V_BST)
            for w2 in range(2):
                te.emit(lambda e, w2=w2: e.matmul(
                    ps_bc[w2][:, :], ones_r[0:1, :],
                    bst[0:1, w2 * 512:(w2 + 1) * 512],
                    start=True, stop=True))
                te.emit(lambda e, w2=w2: e.matmul(
                    ps_bc[2 + w2][:, :], ones_r[0:1, :],
                    bst[0:1, TH + w2 * 512: TH + (w2 + 1) * 512],
                    start=True, stop=True))
            ve.wait(te, te.v)
            for w2 in range(2):
                ve.wait(ve, ve.v)
                ve.emit(lambda e, w2=w2: e.tensor_copy(
                    Bbc[:, w2 * 512:(w2 + 1) * 512], ps_bc[w2][:, :]))
                ve.wait(ve, ve.v)
                ve.emit(lambda e, w2=w2: e.tensor_copy(
                    Cbc[:, w2 * 512:(w2 + 1) * 512], ps_bc[2 + w2][:, :]))
            V_BCC = ve.v
            for ec in range(8):
                idx = ec * 16 + n
                sc.wait(ve, ve.v)
                sc.wait(sc, sc.v)
                V_A = sc.emit(lambda e, ec=ec, n=n: e.activation(
                    atl[:, :], dltN[:, ec * TH:(ec + 1) * TH], AF.Exp,
                    scale=Acol[:, ec * 16 + n: ec * 16 + n + 1]))
                ve.wait(sc, V_A)
                ve.wait(ve, ve.v)
                ve.emit(lambda e, ec=ec: e.tensor_mul(
                    ttl[:, :], duN[:, ec * TH:(ec + 1) * TH], Bbc[:, :]))
                ve.wait(ve, ve.v)
                ve.emit(lambda e, idx=idx: e.tensor_tensor_scan(
                    htl[:, :], atl[:, :], ttl[:, :],
                    carry[:, idx:idx + 1], ALU.mult, ALU.add))
                ve.wait(ve, ve.v)
                ve.emit(lambda e, idx=idx: e.tensor_copy(
                    carry[:, idx:idx + 1], htl[:, TH - 1:TH]))
                ve.wait(ve, ve.v)
                ve.emit(lambda e: e.tensor_mul(ttl[:, :], htl[:, :],
                                               Cbc[:, :]))
                ve.wait(ve, ve.v)
                if n == 0:
                    ve.emit(lambda e, ec=ec: e.tensor_copy(
                        yac[:, ec * TH:(ec + 1) * TH], ttl[:, :]))
                else:
                    ve.emit(lambda e, ec=ec: e.tensor_add(
                        yac[:, ec * TH:(ec + 1) * TH], ttl[:, :],
                        yac[:, ec * TH:(ec + 1) * TH]))
        for ec in range(8):
            ve.wait(ve, ve.v)
            ve.emit(lambda e, ec=ec: e.scalar_tensor_tensor(
                yac[:, ec * TH:(ec + 1) * TH],
                xcN[:, ec * TH:(ec + 1) * TH], Dcol[:, ec:ec + 1],
                yac[:, ec * TH:(ec + 1) * TH], ALU.mult, ALU.add))
        V_Y = ve.v
        sy.wait(ve, V_Y)
        sy.wait(sy, sy.v)
        sy.dma(lambda e, th=th: e.dma_start(
            out=yout.rearrange("(ec p) t -> p ec t", p=128)
                [:, :, th * TH:(th + 1) * TH],
            in_=yac.rearrange("p (ec t) -> p ec t", t=TH)))
        ve.wait(sy, sy.v)
    sy.wait(sy, sy.v)


def emit_F2(p, nc, ctx, t):
    xT = t["x4T"]
    yT = t["yT"]
    wr16T = t["wr16T"]
    sgD = t["sgD"]
    suD = t["suD"]
    sdD = t["sdD"]
    gwD = t["gwD"]
    uwD = t["uwD"]
    dwD = t["dwD"]
    wqD = t["wqD"]
    wkD = t["wkD"]
    wvD = t["wvD"]
    identb = t["identb"]
    onescol = t["onescol"]
    onesrow = t["onesrow"]
    x6T = t["x6T"]
    qTo = t["qTo"]
    kTo = t["kTo"]
    vO = t["vO"]
    gp, te, sc, ve, sy = p.gp, p.te, p.sc, p.ve, p.sy
    gp, te, sc, ve, sy = p.gp, p.te, p.sc, p.ve, p.sy
    ones_c = ctx.enter_context(nc.sbuf_tensor("f2onesc", [128, 4], F32))
    ones_r = ctx.enter_context(nc.sbuf_tensor("f2onesr", [1, 128], F32))
    idb = ctx.enter_context(nc.sbuf_tensor("f2idb", [128, 128], BF16))
    consts = {"ones_col": ones_c[:, 0:1], "ones_row": ones_r,
              "eps": ones_c[:, 1:2], "one1": ones_c, "idb": idb}
    xsb = ctx.enter_context(nc.sbuf_tensor("f2xsb", [128, NCH * TPC], F32))
    gp.dma(lambda e: e.dma_start(out=ones_c[:, :], in_=onescol))
    gp.dma(lambda e: e.dma_start(out=ones_r[:, :], in_=onesrow))
    gp.dma(lambda e: e.dma_start(out=idb[:, :], in_=identb))
    gp.dma(lambda e: e.dma_start(
        out=xsb.rearrange("p (cc t) -> p cc t", cc=NCH),
        in_=xT.rearrange("(cc p) t -> p cc t", p=128)))
    ph0 = ctx.enter_context(contextlib.ExitStack())
    ysb = ph0.enter_context(nc.sbuf_tensor("f2ysb", [128, NCH * TPC], F32))
    V_IN = gp.dma(lambda e: e.dma_start(
        out=ysb.rearrange("p (cc t) -> p cc t", cc=NCH),
        in_=yT.rearrange("(cc p) t -> p cc t", p=128)))
    ve.wait(gp, V_IN)
    V_X5 = None
    for cc in range(NCH):
        ve.wait(ve, ve.v)
        V_X5 = ve.emit(lambda e, cc=cc: e.tensor_add(
            xsb[:, cc * TPC:(cc + 1) * TPC],
            ysb[:, cc * TPC:(cc + 1) * TPC],
            xsb[:, cc * TPC:(cc + 1) * TPC]))
    ph0.close()

    V_X6 = emit_ffn1(p, nc, ctx, xsb, consts, wr16T, sgD, suD, sdD,
                     gwD, uwD, dwD, [(ve, V_X5), (gp, V_IN)], tag="M")
    sy.wait(ve, V_X6)
    sy.dma(lambda e: e.dma_start(
        out=x6T.rearrange("(cc p) t -> p cc t", p=128),
        in_=xsb.rearrange("p (cc t) -> p cc t", cc=NCH)))

    ph2 = ctx.enter_context(contextlib.ExitStack())
    xh2 = ph2.enter_context(nc.sbuf_tensor("f2xh2", [128, NCH * TPC], BF16))
    V_LN2, _ = emit_ln(p, nc, xsb, TPC, consts, xh2, [(ve, V_X6)],
                       tag="q", alloc_ctx=ph2)
    for eng in (te, sc, gp):
        eng.wait(ve, V_LN2)
    qTt = ph2.enter_context(nc.sbuf_tensor("f2qTt", [128, NCH * TPC], BF16))
    emit_ffcols(p, nc, ph2, xh2, TPC, wqD, qTt, TPC, AF.Identity, NCH,
                tag="q2")
    V_QS = sc.v
    kTt = ph2.enter_context(nc.sbuf_tensor("f2kTt", [128, TPC], BF16))
    emit_ffcols(p, nc, ph2, xh2, TPC, wkD, kTt, TPC, AF.Identity, 1,
                tag="k2")
    V_KS = sc.v
    wvs = ph2.enter_context(nc.sbuf_tensor("f2wvs", [128, NCH * 128], BF16))
    vOt = ph2.enter_context(nc.sbuf_tensor("f2vOt", [128, 8 * 128], BF16))
    ps_vv = [ph2.enter_context(nc.psum_tensor(f"f2pvv{i}", [128, 128], F32))
             for i in range(2)]
    gp.wait(gp, gp.v)
    gp.wait(te, te.v)
    gp.wait(sc, sc.v)
    V_WV = gp.dma(lambda e: e.dma_start(
        out=wvs.rearrange("p (cc f) -> p cc f", f=128),
        in_=wvD.rearrange("(cc p) f -> p cc f", p=128)))
    te.wait(gp, V_WV)
    te.wait(sc, V_KS)
    for tc in range(8):
        if tc >= 2:
            te.wait(sc, sc.v)
        for cc in range(NCH):
            te.emit(lambda e, tc=tc, cc=cc: e.matmul(
                ps_vv[tc % 2][:, :],
                xh2[:, cc * TPC + tc * 128: cc * TPC + (tc + 1) * 128],
                wvs[:, cc * 128:(cc + 1) * 128],
                start=(cc == 0), stop=(cc == NCH - 1)))
        sc.wait(te, te.v)
        sc.emit(lambda e, tc=tc: e.activation(
            vOt[:, tc * 128:(tc + 1) * 128], ps_vv[tc % 2][:, :],
            AF.Identity))
    V_VO = sc.v
    sy.wait(sc, V_QS)
    sy.wait(sy, sy.v)
    sy.dma(lambda e: e.dma_start(out=qTo, in_=qTt[:, :]))
    sy.wait(sc, V_KS)
    sy.wait(sy, sy.v)
    sy.dma(lambda e: e.dma_start(out=kTo, in_=kTt[:, :]))
    sy.wait(sc, V_VO)
    sy.wait(sy, sy.v)
    sy.dma(lambda e: e.dma_start(
        out=vO.rearrange("(tc p) d -> p tc d", p=128),
        in_=vOt.rearrange("p (tc d) -> p tc d", d=128)))
    sy.wait(sy, sy.v)
    ph2.close()


def emit_Z(p, nc, ctx, t):
    xT = t["x6T"]
    qTd = t["qTo"]
    kTf = t["kTf"]
    vf = t["vf"]
    woD2 = t["woD2"]
    wr16T = t["wr16T"]
    sgD = t["sgD"]
    suD = t["suD"]
    sdD = t["sdD"]
    gwD = t["gwD"]
    uwD = t["uwD"]
    dwD = t["dwD"]
    identb = t["identb"]
    onescol = t["onescol"]
    onesrow = t["onesrow"]
    x8T = t["x8T"]
    gp, te, sc, ve, sy = p.gp, p.te, p.sc, p.ve, p.sy
    gp, te, sc, ve, sy = p.gp, p.te, p.sc, p.ve, p.sy
    ones_c = ctx.enter_context(nc.sbuf_tensor("zonesc", [128, 4], F32))
    ones_r = ctx.enter_context(nc.sbuf_tensor("zonesr", [1, 128], F32))
    ones_b = ctx.enter_context(nc.sbuf_tensor("zonesb", [128, 1], BF16))
    idb = ctx.enter_context(nc.sbuf_tensor("zidb", [128, 128], BF16))
    consts = {"ones_col": ones_c[:, 0:1], "ones_row": ones_r,
              "eps": ones_c[:, 1:2], "one1": ones_c, "idb": idb}
    xsb = ctx.enter_context(nc.sbuf_tensor("zxsb", [128, NCH * TPC], F32))
    gp.dma(lambda e: e.dma_start(out=ones_c[:, :], in_=onescol))
    gp.dma(lambda e: e.dma_start(out=ones_r[:, :], in_=onesrow))
    gp.dma(lambda e: e.dma_start(out=idb[:, :], in_=identb))
    V_C = gp.dma(lambda e: e.dma_start(
        out=xsb.rearrange("p (cc t) -> p cc t", cc=NCH),
        in_=xT.rearrange("(cc p) t -> p cc t", p=128)))
    ve.wait(gp, V_C)
    V_OB = ve.emit(lambda e: e.tensor_copy(ones_b[:, :], ones_c[:, 0:1]))

    pha = ctx.enter_context(contextlib.ExitStack())
    ySB = pha.enter_context(nc.sbuf_tensor("zySB", [128, NCH * TPC], BF16))
    pha2 = pha.enter_context(contextlib.ExitStack())
    qTs = pha2.enter_context(nc.sbuf_tensor("zqTs", [128, 16 * TPC], BF16))
    kTs = pha2.enter_context(nc.sbuf_tensor("zkTs", [128, T], BF16))
    vSf = pha2.enter_context(nc.sbuf_tensor("zvSf", [128, 16 * 128], BF16))
    exQ = pha2.enter_context(nc.sbuf_tensor("zexQ", [128, 16 * 512], BF16))
    rbc = pha2.enter_context(nc.sbuf_tensor("zrbc", [128, 512], F32))
    rro = pha2.enter_context(nc.sbuf_tensor("zrro", [1, 512], F32))
    ps_s = pha2.enter_context(nc.psum_tensor("zpzs", [128, 512], F32))
    ps_d = pha2.enter_context(nc.psum_tensor("zpzd", [1, 512], F32))
    ps_y = pha2.enter_context(nc.psum_tensor("zpzy", [128, 512], F32))
    ps_rb = pha2.enter_context(nc.psum_tensor("zpzrb", [128, 512], F32))
    gp.wait(gp, gp.v)
    gp.dma(lambda e: e.dma_start(out=qTs[:, :], in_=qTd))
    gp.dma(lambda e: e.dma_start(out=kTs[:, :], in_=kTf))
    V_KV = gp.dma(lambda e: e.dma_start(
        out=vSf.rearrange("p (g d) -> p g d", d=128),
        in_=vf.rearrange("(g p) d -> p g d", p=128)))
    te.wait(gp, V_KV)
    te.wait(ve, V_OB)
    for h in range(16):
        for qh in range(2):
            for kc in range(16):
                te.wait(sc, sc.v)
                te.emit(lambda e, h=h, qh=qh, kc=kc: e.matmul(
                    ps_s[:, :], kTs[:, kc * 128:(kc + 1) * 128],
                    qTs[:, h * TPC + qh * 512: h * TPC + qh * 512 + 512],
                    start=True, stop=True))
                sc.wait(te, te.v)
                if kc == 0:
                    sc.wait(ve, ve.v)
                sc.emit(lambda e, kc=kc: e.activation(
                    exQ[:, kc * 512:(kc + 1) * 512], ps_s[:, :], AF.Exp,
                    scale=RS, bias=ones_c[:, 2:3]))
            te.wait(sc, sc.v)
            for kc in range(16):
                te.emit(lambda e, kc=kc: e.matmul(
                    ps_d[:, :], ones_b[:, 0:1],
                    exQ[:, kc * 512:(kc + 1) * 512],
                    start=(kc == 0), stop=(kc == 15)))
            for kc in range(16):
                te.emit(lambda e, kc=kc: e.matmul(
                    ps_y[:, :], vSf[:, kc * 128:(kc + 1) * 128],
                    exQ[:, kc * 512:(kc + 1) * 512],
                    start=(kc == 0), stop=(kc == 15)))
            V_YMM = te.v
            ve.wait(te, V_YMM)
            ve.wait(ve, ve.v)
            ve.emit(lambda e: e.reciprocal(rro[0:1, :], ps_d[:, :]))
            te.wait(ve, ve.v)
            V_RB = te.emit(lambda e: e.matmul(
                ps_rb[:, :], ones_r[0:1, :], rro[0:1, :],
                start=True, stop=True))
            ve.wait(te, V_RB)
            ve.emit(lambda e: e.tensor_copy(rbc[:, :], ps_rb[:, :]))
            ve.wait(ve, ve.v)
            ve.emit(lambda e, h=h, qh=qh: e.tensor_mul(
                ySB[:, h * TPC + qh * 512: h * TPC + qh * 512 + 512],
                ps_y[:, :], rbc[:, :]))
    V_ATT = ve.v
    pha2.close()

    for eng in (te, sc, gp):
        eng.wait(ve, V_ATT)
    emit_ffcols(p, nc, pha, ySB, TPC, woD2, None, TPC, None, NCH,
                tag="zw", addto=xsb, addW=TPC)
    V_X7 = ve.v
    pha.close()

    V_X8 = emit_ffn1(p, nc, ctx, xsb, consts, wr16T, sgD, suD, sdD,
                     gwD, uwD, dwD, [(ve, V_X7)], tag="Z")
    sy.wait(ve, V_X8)
    sy.dma(lambda e: e.dma_start(
        out=x8T.rearrange("(cc p) t -> p cc t", p=128),
        in_=xsb.rearrange("p (cc t) -> p cc t", cc=NCH)))
    sy.wait(sy, sy.v)



PAIRS = [[0, 1], [2, 3], [4, 5], [6, 7]]
ALL8 = [[0, 1, 2, 3, 4, 5, 6, 7]]


def emit_barrier(p):
    es = [p.sy, p.gp, p.ve, p.sc, p.te]
    vs = [e.v for e in es]
    for e in es:
        for o, v in zip(es, vs):
            if o is not e and v > 0:
                e.wait(o, v)


def emit_rows_glue(p, nc, ctx, t, flg, idgb):
    gp, te, sc, ve, sy = p.gp, p.te, p.sc, p.ve, p.sy
    ah = t["att2_h"]
    c0 = ctx.enter_context(nc.sbuf_tensor("rg_c0", [128, 2 * 512], BF16))
    c1 = ctx.enter_context(nc.sbuf_tensor("rg_c1", [128, 2 * 512], BF16))
    bt = ctx.enter_context(nc.sbuf_tensor("rg_bt", [128, 2 * 512], BF16))
    rstg = [ctx.enter_context(nc.sbuf_tensor(f"rg_s{i}", [128, TPC], BF16))
            for i in range(4)]
    pctx = ctx.enter_context(contextlib.ExitStack())
    ps_tr = [pctx.enter_context(nc.psum_tensor(f"rg_pt{i}", [128, 128], BF16))
             for i in range(2)]

    def gath(j0, head):
        return bass.AP(ah, (16 * j0 + head) * 128,
                       [[256 * 128, 128], [16 * 128, 4], [1, 128]])

    for j4 in range(4):
        for hl in range(8):
            it = j4 * 8 + hl
            b = it % 2
            gp.wait(ve, ve.v)
            sy.wait(ve, ve.v)
            V0 = gp.dma(lambda e, j4=j4, hl=hl, b=b: e.dma_start(
                out=c0[:, b * 512:(b + 1) * 512].rearrange(
                    "p (j d) -> p j d", d=128),
                in_=gath(j4 * 4, hl)))
            V1 = sy.dma(lambda e, j4=j4, hl=hl, b=b: e.dma_start(
                out=c1[:, b * 512:(b + 1) * 512].rearrange(
                    "p (j d) -> p j d", d=128),
                in_=gath(j4 * 4, 8 + hl)))
            ve.wait(gp, V0)
            ve.wait(sy, V1)
            ve.wait(ve, ve.v)
            ve.emit(lambda e, b=b: e.tensor_scalar(
                bt[:, b * 512:(b + 1) * 512], c0[:, b * 512:(b + 1) * 512],
                flg[:, 1:2], None, ALU.mult))
            ve.wait(ve, ve.v)
            ve.emit(lambda e, b=b: e.scalar_tensor_tensor(
                bt[:, b * 512:(b + 1) * 512], c1[:, b * 512:(b + 1) * 512],
                flg[:, 2:3], bt[:, b * 512:(b + 1) * 512],
                ALU.mult, ALU.add))
            for jq in range(4):
                te.wait(ve, ve.v)
                te.emit(lambda e, b=b, jq=jq: e.transpose(
                    ps_tr[jq % 2][:, :],
                    bt[:, b * 512 + jq * 128: b * 512 + (jq + 1) * 128],
                    idgb[:, :]))
                ve.wait(te, te.v)
                ve.emit(lambda e, jq=jq, hl=hl: e.tensor_copy(
                    rstg[jq][:, hl * 128:(hl + 1) * 128], ps_tr[jq % 2][:, :]))
        sy.wait(ve, ve.v)
        for jq in range(4):
            sy.dma(lambda e, j4=j4, jq=jq: e.dma_start(
                out=t["rowsT"][(j4 * 4 + jq) * 128:(j4 * 4 + jq + 1) * 128, :],
                in_=rstg[jq][:, :]))
    V_RST = sy.v
    for j4 in range(4):
        for hb in range(2):
            b = hb
            gp.wait(ve, ve.v)
            gp.wait(sy, V_RST)
            V0 = gp.dma(lambda e, j4=j4, hb=hb, b=b: e.dma_start(
                out=c0[:, b * 512:(b + 1) * 512].rearrange(
                    "p (j d) -> p j d", d=128),
                in_=gath(j4 * 4, 6 + hb)))
            for jq in range(4):
                te.wait(gp, V0)
                te.wait(ve, ve.v)
                te.emit(lambda e, b=b, jq=jq: e.transpose(
                    ps_tr[jq % 2][:, :],
                    c0[:, b * 512 + jq * 128: b * 512 + (jq + 1) * 128],
                    idgb[:, :]))
                ve.wait(te, te.v)
                ve.emit(lambda e, jq=jq, hb=hb: e.tensor_copy(
                    rstg[jq][:, hb * 128:(hb + 1) * 128], ps_tr[jq % 2][:, :]))
        sy.wait(ve, ve.v)
        for jq in range(4):
            sy.dma(lambda e, j4=j4, jq=jq: e.dma_start(
                out=t["rowsHT"][(j4 * 4 + jq) * 128:
                                (j4 * 4 + jq + 1) * 128, :],
                in_=rstg[jq][:, 0:256]))
    sy.wait(sy, sy.v)
    pctx.close()


def emit_halo(p, nc, ctx, t):
    gp, te, sc, ve, sy = p.gp, p.te, p.sc, p.ve, p.sy
    xhalo = ctx.enter_context(nc.sbuf_tensor("hp_x", [128, NCH * 256], F32))
    rh = ctx.enter_context(nc.sbuf_tensor("hp_r", [128, NCH * 256], BF16))
    gp.wait(gp, gp.v)
    gp.dma(lambda e: e.dma_start(
        out=xhalo.rearrange("p (cc t) -> p cc t", cc=NCH),
        in_=t["xg0"][:, 768:1024].rearrange("(cc p) t -> p cc t", p=128)))
    V_IN = gp.dma(lambda e: e.dma_start(
        out=rh.rearrange("p (cc t) -> p cc t", cc=NCH),
        in_=t["rowsHT"].rearrange("(cc p) t -> p cc t", p=128)))
    for eng in (te, sc, ve):
        eng.wait(gp, V_IN)
    emit_ffcols(p, nc, ctx, rh, 256, t["wprojT"], None, 256, None, NCH,
                tag="hp", addto=xhalo, addW=256)
    sy.wait(ve, ve.v)
    sy.dma(lambda e: e.dma_start(
        out=t["haloT"].rearrange("(cc p) t -> p cc t", p=128),
        in_=xhalo.rearrange("p (cc t) -> p cc t", cc=NCH)))
    sy.wait(sy, sy.v)


def emit_scat_glue(p, nc, ctx, t, inkey, outkey, flg, idg, tag):
    gp, te, sc, ve, sy = p.gp, p.te, p.sc, p.ve, p.sy
    srB = ctx.enter_context(nc.sbuf_tensor(f"sg_r{tag}", [8, 2048], F32))
    sc8 = ctx.enter_context(nc.sbuf_tensor(f"sg_o{tag}", [128, NCH * 8], F32))
    pctx = ctx.enter_context(contextlib.ExitStack())
    ps8 = pctx.enter_context(nc.psum_tensor(f"sg_p{tag}", [128, 8], F32))
    gp.wait(gp, gp.v)
    V = gp.dma(lambda e: e.dma_start(out=srB[:, :], in_=t[inkey]))
    te.wait(gp, V)
    for cc in range(NCH):
        te.wait(ve, ve.v)
        te.emit(lambda e, cc=cc: e.transpose(
            ps8[:, :], srB[0:8, cc * 128:(cc + 1) * 128], idg[0:8, 0:8]))
        ve.wait(te, te.v)
        ve.emit(lambda e, cc=cc: e.tensor_scalar(
            sc8[:, cc * 8:(cc + 1) * 8], ps8[:, :], flg[:, 0:1], None,
            ALU.mult))
    sy.wait(ve, ve.v)
    sy.dma(lambda e: e.dma_start(
        out=t[outkey].rearrange("(cc p) e -> p cc e", p=128),
        in_=sc8.rearrange("p (cc e) -> p cc e", e=8)))
    sy.wait(sy, sy.v)
    pctx.close()


def emit_cvt(p, nc, ctx, t, inkey, outkey, nrb, C, tag):
    gp, te, sc, ve, sy = p.gp, p.te, p.sc, p.ve, p.sy
    af = ctx.enter_context(nc.sbuf_tensor(f"cv_f{tag}", [128, 2 * C], F32))
    ab = ctx.enter_context(nc.sbuf_tensor(f"cv_b{tag}", [128, 2 * C], BF16))
    for rb in range(nrb):
        b = rb % 2
        gp.wait(sc, sc.v)
        V = gp.dma(lambda e, rb=rb, b=b: e.dma_start(
            out=af[:, b * C:(b + 1) * C],
            in_=t[inkey][rb * 128:(rb + 1) * 128, :]))
        sc.wait(gp, V)
        sc.wait(sy, sy.v)
        sc.emit(lambda e, b=b: e.activation(
            ab[:, b * C:(b + 1) * C], af[:, b * C:(b + 1) * C], AF.Identity))
        sy.wait(sc, sc.v)
        sy.dma(lambda e, rb=rb, b=b: e.dma_start(
            out=t[outkey][rb * 128:(rb + 1) * 128, :],
            in_=ab[:, b * C:(b + 1) * C]))
    sy.wait(sy, sy.v)


def emit_xhP_glue(p, nc, ctx, t, flg):
    gp, te, sc, ve, sy = p.gp, p.te, p.sc, p.ve, p.sy
    c0 = ctx.enter_context(nc.sbuf_tensor("xp_c0", [128, 2 * TPC], BF16))
    c1 = ctx.enter_context(nc.sbuf_tensor("xp_c1", [128, 2 * TPC], BF16))
    xf = ctx.enter_context(nc.sbuf_tensor("xp_f", [128, 2 * TPC], F32))
    for pb in range(16):
        for th in range(2):
            it = pb * 2 + th
            b = it % 2
            gp.wait(ve, ve.v)
            sy.wait(ve, ve.v)
            V0 = gp.dma(lambda e, pb=pb, th=th, b=b: e.dma_start(
                out=c0[:, b * TPC:(b + 1) * TPC],
                in_=t["Gxh"][th * D + pb * 128: th * D + (pb + 1) * 128, :]))
            nb = (pb + 8) % 16
            V1 = sy.dma(lambda e, nb=nb, th=th, b=b: e.dma_start(
                out=c1[:, b * TPC:(b + 1) * TPC],
                in_=t["Gxh"][th * D + nb * 128: th * D + (nb + 1) * 128, :]))
            ve.wait(gp, V0)
            ve.wait(sy, V1)
            ve.wait(ve, ve.v)
            ve.emit(lambda e, b=b: e.tensor_scalar(
                xf[:, b * TPC:(b + 1) * TPC], c0[:, b * TPC:(b + 1) * TPC],
                flg[:, 1:2], None, ALU.mult))
            ve.wait(ve, ve.v)
            ve.emit(lambda e, b=b: e.scalar_tensor_tensor(
                xf[:, b * TPC:(b + 1) * TPC], c1[:, b * TPC:(b + 1) * TPC],
                flg[:, 2:3], xf[:, b * TPC:(b + 1) * TPC], ALU.mult, ALU.add))
            sy.wait(ve, ve.v)
            sy.dma(lambda e, pb=pb, th=th, b=b: e.dma_start(
                out=t["xhP"][pb * 128:(pb + 1) * 128,
                             th * TPC:(th + 1) * TPC],
                in_=xf[:, b * TPC:(b + 1) * TPC]))
    sy.wait(sy, sy.v)


def emit_yT_glue(p, nc, ctx, t, flg):
    gp, te, sc, ve, sy = p.gp, p.te, p.sc, p.ve, p.sy
    c0 = ctx.enter_context(nc.sbuf_tensor("yt_c0", [128, 2 * TPC], BF16))
    c1 = ctx.enter_context(nc.sbuf_tensor("yt_c1", [128, 2 * TPC], BF16))
    yf = ctx.enter_context(nc.sbuf_tensor("yt_f", [128, 2 * TPC], F32))
    for gb in range(16):
        b = gb % 2
        gp.wait(ve, ve.v)
        sy.wait(ve, ve.v)
        V0 = gp.dma(lambda e, gb=gb, b=b: e.dma_start(
            out=c0[:, b * TPC:(b + 1) * TPC],
            in_=t["Gy"][gb * 128:(gb + 1) * 128, 0:TPC]))
        V1 = sy.dma(lambda e, gb=gb, b=b: e.dma_start(
            out=c1[:, b * TPC:(b + 1) * TPC],
            in_=t["Gy"][gb * 128:(gb + 1) * 128, TPC:2 * TPC]))
        ve.wait(gp, V0)
        ve.wait(sy, V1)
        ve.wait(ve, ve.v)
        ve.emit(lambda e, b=b: e.tensor_scalar(
            yf[:, b * TPC:(b + 1) * TPC], c0[:, b * TPC:(b + 1) * TPC],
            flg[:, 1:2], None, ALU.mult))
        ve.wait(ve, ve.v)
        ve.emit(lambda e, b=b: e.scalar_tensor_tensor(
            yf[:, b * TPC:(b + 1) * TPC], c1[:, b * TPC:(b + 1) * TPC],
            flg[:, 2:3], yf[:, b * TPC:(b + 1) * TPC], ALU.mult, ALU.add))
        sy.wait(ve, ve.v)
        sy.dma(lambda e, gb=gb, b=b: e.dma_start(
            out=t["yT"][gb * 128:(gb + 1) * 128, :],
            in_=yf[:, b * TPC:(b + 1) * TPC]))
    sy.wait(sy, sy.v)


def emit_kTf_copy(p, nc, ctx, t):
    gp, te, sc, ve, sy = p.gp, p.te, p.sc, p.ve, p.sy
    kb = ctx.enter_context(nc.sbuf_tensor("kc_b", [128, 2 * TPC], BF16))
    gp.wait(gp, gp.v)
    for r in range(2):
        gp.dma(lambda e, r=r: e.dma_start(
            out=kb[:, r * TPC:(r + 1) * TPC],
            in_=t["Gk"][r * 128:(r + 1) * 128, :]))
    V = gp.v
    sy.wait(gp, V)
    sy.dma(lambda e: e.dma_start(out=t["kTf"], in_=kb[:, :]))
    sy.wait(sy, sy.v)


def build_fused():
    nc = bass.Bass("TRN2", target_bir_lowering=False)
    t = {}

    def inp(name, shape, dt=F32):
        t[name] = nc.dram_tensor(name, shape, dt, kind="ExternalInput").ap()

    def intl(name, shape, dt=F32, handle=False):
        h = nc.dram_tensor(name, shape, dt)
        t[name] = h.ap()
        if handle:
            t[name + "_h"] = h

    inp("xg0", [D, TPC]); inp("xg1", [D, TPC]); inp("xTme", [D, TPC])
    inp("wqkvT", [D, 3 * D]); inp("maskA", [128, 128])
    inp("identb", [128, 128]); inp("onescolA", [128, 2])
    inp("onescol", [128, 4]); inp("onesrow", [1, 128]); inp("flags", [128, 4])
    inp("wprojT", [D, D]); inp("w1T", [D, D]); inp("w3T", [D, D])
    inp("w2T", [D, D]); inp("wrT", [D, 8])
    inp("wqTp", [D, D]); inp("wkTp", [D, D]); inp("wvT", [D, D])
    inp("woD", [D, D]); inp("tabQ", [128, 2 * TPC]); inp("tabK", [128, 2 * 1280])
    inp("maskB", [128, 16 * 256])
    inp("cw", [D, 4]); inp("cbI", [128, NCH]); inp("xpT", [D, 96])
    inp("dtwT", [64, TPC]); inp("dtb", [128, 8]); inp("Ain", [128, 128])
    inp("Dvin", [128, 8])
    inp("wr16T", [D, 16]); inp("sgD", [D, D]); inp("suD", [D, D])
    inp("sdD", [D, D]); inp("gwD", [16 * D, TPC]); inp("uwD", [16 * D, TPC])
    inp("dwD", [16 * TPC, D]); inp("wqD", [D, D]); inp("wkD", [D, 128])
    inp("wvD", [D, 128]); inp("woD2", [D, D])

    intl("att2", [2 * 16 * TPC, 128], BF16, handle=True)
    intl("dq", [TPC, D], BF16, handle=True)
    intl("dk", [TPC, D], BF16, handle=True)
    intl("dv", [TPC, D], BF16, handle=True)
    intl("rowsT", [D, TPC], BF16); intl("rowsHT", [D, 256], BF16)
    intl("x1T", [D, TPC]); intl("haloT", [D, 256])
    intl("part1", [8, 2048]); intl("scatR1", [8, 2048]); intl("scat8a", [D, 8])
    intl("dxx", [D, TPC]); intl("datt", [D, TPC], BF16)
    intl("x3T", [D, TPC]); intl("part2", [8, 2048])
    intl("scatR2", [8, 2048]); intl("scat8b", [D, 8])
    intl("x4T", [D, TPC]); intl("xhT", [D, TPC]); intl("xhbf", [D, TPC], BF16)
    intl("Gxh", [2 * D, TPC], BF16); intl("xhP", [D, T])
    intl("yout", [TPC, T]); intl("ybf", [TPC, T], BF16)
    intl("Gy", [2 * TPC, T], BF16); intl("yT", [D, TPC])
    intl("x6T", [D, TPC]); intl("qTo", [128, 16 * TPC], BF16)
    intl("kTo", [128, TPC], BF16); intl("vO", [TPC, 128], BF16)
    intl("Gk", [256, TPC], BF16); intl("vf", [T, 128], BF16)
    intl("kTf", [128, T], BF16)
    t["x8T"] = nc.dram_tensor("x8T", [D, TPC], F32, kind="ExternalOutput").ap()

    with contextlib.ExitStack() as ctx:
        p = Prog(nc, ctx)
        gp, te, sc, ve, sy = p.gp, p.te, p.sc, p.ve, p.sy
        flg = ctx.enter_context(nc.sbuf_tensor("flgT", [128, 4], F32))
        idg = ctx.enter_context(nc.sbuf_tensor("idgT", [128, 128], F32))
        idgb = ctx.enter_context(nc.sbuf_tensor("idgbT", [128, 128], BF16))
        gp.dma(lambda e: e.dma_start(out=flg[:, :], in_=t["flags"]))
        V_ID = gp.dma(lambda e: e.dma_start(out=idg[:, :], in_=t["identb"]))
        ve.wait(gp, V_ID)
        ve.emit(lambda e: e.tensor_copy(idgb[:, :], idg[:, :]))

        def phase(fn, *args):
            c = contextlib.ExitStack()
            fn(p, nc, c, *args)
            c.close()
            emit_barrier(p)

        for r in (0, 1):
            phase(emit_A1, {**t, "XT_A": t[f"xg{r}"]}, r * 8, f"a{r}")
        phase(emit_rows_glue, t, flg, idgb)
        phase(emit_F1, t)
        phase(emit_halo, t)
        V = gp.emit(lambda e: e.collective_compute(
            "AllReduce", ALU.add, ins=[t["part1"]], outs=[t["scatR1"]],
            replica_groups=ALL8), 1)
        gp.wait(gp, V)
        phase(emit_scat_glue, t, "scatR1", "scat8a", flg, idg, "a")
        phase(emit_B1, t)
        V = gp.emit(lambda e: e.collective_compute(
            "AllReduce", ALU.add, ins=[t["part2"]], outs=[t["scatR2"]],
            replica_groups=ALL8), 1)
        gp.wait(gp, V)
        phase(emit_scat_glue, t, "scatR2", "scat8b", flg, idg, "b")
        phase(emit_M1, t)
        phase(emit_cvt, t, "xhT", "xhbf", 16, TPC, "x")
        V = gp.emit(lambda e: e.collective_compute(
            "AllGather", ALU.bypass, ins=[t["xhbf"]], outs=[t["Gxh"]],
            replica_groups=PAIRS), 1)
        gp.wait(gp, V)
        sy.wait(gp, V)
        phase(emit_xhP_glue, t, flg)
        phase(emit_M2, t)
        phase(emit_cvt, t, "yout", "ybf", 8, T, "y")
        V = gp.emit(lambda e: e.collective_compute(
            "AllGather", ALU.bypass, ins=[t["ybf"]], outs=[t["Gy"]],
            replica_groups=PAIRS), 1)
        gp.wait(gp, V)
        sy.wait(gp, V)
        phase(emit_yT_glue, t, flg)
        phase(emit_F2, t)
        V = gp.emit(lambda e: e.collective_compute(
            "AllGather", ALU.bypass, ins=[t["kTo"]], outs=[t["Gk"]],
            replica_groups=PAIRS), 1)
        V = gp.emit(lambda e: e.collective_compute(
            "AllGather", ALU.bypass, ins=[t["vO"]], outs=[t["vf"]],
            replica_groups=PAIRS), 1)
        gp.wait(gp, V)
        phase(emit_kTf_copy, t)
        c = contextlib.ExitStack()
        emit_Z(p, nc, c, t)
        c.close()
        p.finish()
    return nc



def _rope_tab(positions):
    inv = 1.0 / 10000.0 ** (np.arange(0, 128, 2, dtype=np.float32) / 128.0)
    fr = np.outer(positions.astype(np.float32), inv)
    W = len(positions)
    tab = np.zeros((128, 2 * W), np.float32)
    tab[0:64, 0:W] = np.cos(fr).T
    tab[64:128, 0:W] = np.cos(fr).T
    tab[0:64, W:] = np.sin(fr).T
    tab[64:128, W:] = np.sin(fr).T
    return tab


def _perm_headcols(Wm):
    WT = np.ascontiguousarray(np.asarray(Wm, np.float32).T)
    idx = np.zeros(2048, np.int64)
    for h in range(16):
        idx[h * 128:h * 128 + 64] = h * 128 + np.arange(0, 128, 2)
        idx[h * 128 + 64:(h + 1) * 128] = h * 128 + np.arange(1, 128, 2)
    return np.ascontiguousarray(WT[:, idx])


def _build_maskB(h):
    m = np.zeros((16, 128, 256), np.float32)
    for ci in range(4):
        gc = h * 4 + ci
        for kc in range(4):
            for jj in range(128):
                ww = kc * 128 + jj
                if ww < 256:
                    m[ci * 4 + kc, jj, :] = 0.0 if gc > 0 else -30000.0
                else:
                    qq = np.arange(256)
                    m[ci * 4 + kc, jj, :] = np.where(ww - 256 <= qq, 0.0,
                                                     -30000.0)
    return np.ascontiguousarray(m.transpose(1, 0, 2).reshape(128, 16 * 256))


def kernel(**inputs):
    inputs = {k: np.asarray(v) for k, v in inputs.items()}
    T4 = lambda a: np.ascontiguousarray(np.asarray(a, np.float32).T)
    DC = lambda a: np.ascontiguousarray(np.asarray(a, np.float32))
    onescol = np.ones((128, 4), np.float32)
    onescol[:, 1] = 1e-5
    onescol[:, 2] = -12.0
    onescolA = np.ones((128, 2), np.float32)
    onescolA[:, 1] = 1e-5
    onesrow = np.ones((1, 128), np.float32)
    ident = np.eye(128, dtype=np.float32)
    maskA = np.full((128, 128), -30000.0, np.float32)
    for tt in range(8):
        maskA[tt * 16:(tt + 1) * 16, tt * 16:(tt + 1) * 16] = 0.0

    x = np.asarray(inputs["L0_attn_a_x"], np.float32)
    shared = {
        "wqkvT": T4(inputs["L0_attn_a_w_qkv"]),
        "maskA": maskA, "identb": ident, "onescolA": onescolA,
        "onescol": onescol, "onesrow": onesrow,
        "wprojT": T4(inputs["L0_attn_a_w_proj"]),
        "w1T": T4(inputs["L0_ffn_w1"]), "w3T": T4(inputs["L0_ffn_w3"]),
        "w2T": T4(inputs["L0_ffn_w2"]), "wrT": T4(inputs["L0_ffn_wr"]),
        "wqTp": _perm_headcols(inputs["L0_attn_b_Wq"]),
        "wkTp": _perm_headcols(inputs["L0_attn_b_Wk"]),
        "wvT": T4(inputs["L0_attn_b_Wv"]),
        "woD": T4(inputs["L0_attn_b_Wo"]),
        "wr16T": T4(inputs["L1_ffn_router_w"]),
        "sgD": DC(inputs["L1_ffn_shared_gate_w"]),
        "suD": DC(inputs["L1_ffn_shared_up_w"]),
        "sdD": DC(inputs["L1_ffn_shared_down_w"]),
        "gwD": np.ascontiguousarray(np.asarray(
            inputs["L1_ffn_gate_w"], np.float32).reshape(16 * D, TPC)),
        "uwD": np.ascontiguousarray(np.asarray(
            inputs["L1_ffn_up_w"], np.float32).reshape(16 * D, TPC)),
        "dwD": np.ascontiguousarray(np.asarray(
            inputs["L1_ffn_down_w"], np.float32).reshape(16 * TPC, D)),
        "wqD": DC(inputs["L1_attn_b_Wq"]),
        "wkD": DC(inputs["L1_attn_b_Wk"]),
        "wvD": DC(inputs["L1_attn_b_Wv"]),
        "woD2": DC(inputs["L1_attn_b_Wo"]),
    }
    maps = []
    for c in range(NCORES):
        b, h = c // 2, c % 2
        xg0 = np.ascontiguousarray(x[b, 0:TPC, :].T)
        xg1 = np.ascontiguousarray(x[b, TPC:T, :].T)
        flags = np.zeros((128, 4), np.float32)
        flags[:, 0] = 1.0 if c == 0 else 0.0
        flags[:, 1] = 1.0 if h == 0 else 0.0
        flags[:, 2] = 1.0 if h == 1 else 0.0
        if h == 1:
            kpos = np.arange(768, 2048)
        else:
            kpos = np.concatenate([np.zeros(256, np.int64),
                                   np.arange(0, 1024)])
        eh = h
        permv = np.concatenate([np.arange(eh * 1024, (eh + 1) * 1024),
                                np.arange((1 - eh) * 1024, (2 - eh) * 1024)])
        own = permv[:1024]
        cwp = np.asarray(inputs["L1_attn_a_conv_w"], np.float32)[permv]
        cbp = np.asarray(inputs["L1_attn_a_conv_b"], np.float32)[permv]
        xp = np.asarray(inputs["L1_attn_a_x_proj_w"], np.float32)
        dtw = np.asarray(inputs["L1_attn_a_dt_proj_w"], np.float32)
        dtb = np.asarray(inputs["L1_attn_a_dt_proj_b"], np.float32)[own]
        Al = np.asarray(inputs["L1_attn_a_A_log"], np.float32)[own]
        Dv = np.asarray(inputs["L1_attn_a_D"], np.float32)[own]
        maps.append({
            **shared,
            "xg0": xg0, "xg1": xg1, "xTme": xg0 if h == 0 else xg1,
            "flags": flags,
            "tabQ": _rope_tab(np.arange(h * 1024, (h + 1) * 1024)),
            "tabK": _rope_tab(kpos), "maskB": _build_maskB(h),
            "cw": np.ascontiguousarray(cwp),
            "cbI": np.ascontiguousarray(cbp.reshape(16, 128).T),
            "xpT": np.ascontiguousarray(xp[:, permv].T),
            "dtwT": np.ascontiguousarray(dtw[own].T),
            "dtb": np.ascontiguousarray(dtb.reshape(8, 128).T),
            "Ain": np.ascontiguousarray(
                Al.reshape(8, 128, 16).transpose(1, 0, 2).reshape(128, 128)),
            "Dvin": np.ascontiguousarray(Dv.reshape(8, 128).T),
        })
    nc = build_fused()
    res = run_bass_kernel_spmd(nc, maps, core_ids=list(range(NCORES)))
    out = np.zeros((B, T, D), np.float32)
    for c in range(NCORES):
        b, h = c // 2, c % 2
        out[b, h * TPC:(h + 1) * TPC, :] = np.asarray(
            res.results[c]["x8T"]).T
    return out


# version 04
# speedup vs baseline: 1.5151x; 1.0117x over previous
import contextlib
import math
import numpy as np
import ml_dtypes
import concourse.bass as bass
import concourse.mybir as mybir
from concourse.bass_utils import run_bass_kernel_spmd

F32 = mybir.dt.float32
BF16 = mybir.dt.bfloat16
AF = mybir.ActivationFunctionType
ALU = mybir.AluOpType
ds = bass.ds
BF = ml_dtypes.bfloat16

NCORES = 8
B, T, D = 4, 2048, 2048
TPC = 1024
NCH = D // 128
RS = 1.0 / math.sqrt(128.0)


class EB:
    def __init__(self, ctx, nc, name):
        self.h = ctx.enter_context(nc.semaphore(f"ch_{name}"))
        self.v = 0
        self.ops = []
        self.name = name

    def emit(self, fn, n=1):
        self.v += n
        h, k = self.h, n

        def op(e):
            fn(e).then_inc(h, k)
        self.ops.append(op)
        return self.v

    def raw(self, fn):
        self.ops.append(fn)

    def wait(self, other, v):
        h = other.h
        self.ops.append(lambda e: e.wait_ge(h, v))

    def dma(self, fn):
        return self.emit(fn, 16)

    def run(self, e):
        for f in self.ops:
            f(e)


class Prog:
    def __init__(self, nc, ctx):
        self.nc = nc
        self.ctx = ctx
        self.sy = EB(ctx, nc, "sy")
        self.gp = EB(ctx, nc, "gp")
        self.ve = EB(ctx, nc, "ve")
        self.sc = EB(ctx, nc, "sc")
        self.te = EB(ctx, nc, "te")

    def finish(self):
        with self.nc.Block() as block:
            block.sync(self.sy.run)
            block.gpsimd(self.gp.run)
            block.vector(self.ve.run)
            block.scalar(self.sc.run)
            block.tensor(self.te.run)


def emit_ln(p, nc, xsb, W, consts, out_bf, deps, tag="", alloc_ctx=None,
            outW=None, out_off=0):
    if outW is None:
        outW = W
    wl = []
    off = 0
    while off < W:
        ln_ = min(512, W - off)
        wl.append((off, ln_))
        off += ln_
    NW = len(wl)
    sq = alloc_ctx.enter_context(nc.sbuf_tensor(f"ln_sq{tag}", [128, W], F32))
    st = alloc_ctx.enter_context(nc.sbuf_tensor(f"ln_st{tag}", [1, 2 * W], F32))
    abc = alloc_ctx.enter_context(nc.sbuf_tensor(f"ln_ab{tag}", [128, W], F32))
    bbc = alloc_ctx.enter_context(nc.sbuf_tensor(f"ln_bb{tag}", [128, W], F32))
    pctx = alloc_ctx.enter_context(contextlib.ExitStack())
    ps_sum = [pctx.enter_context(nc.psum_tensor(f"lnS{tag}{w}", [1, wl[w][1]],
                                                F32)) for w in range(NW)]
    ps_sq = [pctx.enter_context(nc.psum_tensor(f"lnQ{tag}{w}", [1, wl[w][1]],
                                               F32)) for w in range(NW)]
    ps_bc = [pctx.enter_context(nc.psum_tensor(f"lnB{tag}{w}",
                                               [128, wl[w // 2][1]], F32))
             for w in range(2 * NW)]
    ones_c = consts["ones_col"]
    ones_r = consts["ones_row"]
    eps = consts["eps"]
    ve, sc, te = p.ve, p.sc, p.te

    for (d, v) in deps:
        sc.wait(d, v)
        te.wait(d, v)

    TE0 = te.v
    V_SQ = []
    for cc in range(NCH):
        if cc >= 1:
            sc.wait(te, TE0 + 2 * NW * cc)
        V_SQ.append(sc.emit(lambda e, cc=cc: e.activation(
            sq[:, :], xsb[:, cc * W:(cc + 1) * W], AF.Square)))
    for cc in range(NCH):
        te.wait(sc, V_SQ[cc])
        for w, (wo, wn_) in enumerate(wl):
            te.emit(lambda e, cc=cc, w=w, wo=wo, wn_=wn_: e.matmul(
                ps_sum[w][:, :], ones_c[:, 0:1],
                xsb[:, cc * W + wo: cc * W + wo + wn_],
                start=(cc == 0), stop=(cc == NCH - 1)))
            te.emit(lambda e, cc=cc, w=w, wo=wo, wn_=wn_: e.matmul(
                ps_sq[w][:, :], ones_c[:, 0:1],
                sq[:, wo: wo + wn_],
                start=(cc == 0), stop=(cc == NCH - 1)))
    V_MM = te.v
    assert V_MM == TE0 + 2 * NW * NCH

    sc.wait(te, V_MM)
    V_MU = None
    for w, (wo, wn_) in enumerate(wl):
        V_MU = sc.emit(lambda e, w=w, wo=wo, wn_=wn_: e.mul(
            st[0:1, wo: wo + wn_], ps_sum[w][:, :], 1.0 / D))
    sc.wait(sc, V_MU)
    V_M2 = sc.emit(lambda e: e.activation(st[0:1, W:2 * W], st[0:1, 0:W],
                                          AF.Square))
    ve.wait(sc, V_M2)
    V_VAR = None
    for w, (wo, wn_) in enumerate(wl):
        if w >= 1:
            ve.wait(ve, V_VAR)
        V_VAR = ve.emit(lambda e, w=w, wo=wo, wn_=wn_: e.scalar_tensor_tensor(
            st[0:1, W + wo: W + wo + wn_],
            ps_sq[w][:, :], 1.0 / D,
            st[0:1, W + wo: W + wo + wn_],
            ALU.mult, ALU.subtract))
    sc.wait(ve, V_VAR)
    V_LN1 = sc.emit(lambda e: e.activation(st[0:1, W:2 * W],
                                           st[0:1, W:2 * W], AF.Ln,
                                           bias=eps[0:1, 0:1]))
    sc.wait(sc, V_LN1)
    V_RSTD = sc.emit(lambda e: e.activation(st[0:1, W:2 * W],
                                            st[0:1, W:2 * W],
                                            AF.Exp, scale=-0.5))
    ve.wait(sc, V_RSTD)
    V_BROW = ve.emit(lambda e: e.scalar_tensor_tensor(
        st[0:1, 0:W], st[0:1, 0:W], -1.0, st[0:1, W:2 * W],
        ALU.mult, ALU.mult))
    te.wait(ve, V_BROW)
    V_BC = None
    for w, (wo, wn_) in enumerate(wl):
        V_BC = te.emit(lambda e, w=w, wo=wo, wn_=wn_: e.matmul(
            ps_bc[2 * w][:, :], ones_r[0:1, :],
            st[0:1, W + wo: W + wo + wn_],
            start=True, stop=True))
        V_BC = te.emit(lambda e, w=w, wo=wo, wn_=wn_: e.matmul(
            ps_bc[2 * w + 1][:, :], ones_r[0:1, :],
            st[0:1, wo: wo + wn_],
            start=True, stop=True))
    ve.wait(te, V_BC)
    for w, (wo, wn_) in enumerate(wl):
        ve.emit(lambda e, w=w, wo=wo, wn_=wn_: e.tensor_copy(
            abc[:, wo: wo + wn_], ps_bc[2 * w][:, :]))
        ve.emit(lambda e, w=w, wo=wo, wn_=wn_: e.tensor_copy(
            bbc[:, wo: wo + wn_], ps_bc[2 * w + 1][:, :]))
    V_CP = ve.v
    V_DONE = None
    for cc in range(NCH):
        if cc == 0:
            ve.wait(ve, V_CP)
        else:
            ve.wait(ve, V_DONE)
        ve.emit(lambda e, cc=cc: e.tensor_mul(
            sq[:, :], xsb[:, cc * W:(cc + 1) * W], abc[:, :]))
        ve.wait(ve, ve.v)
        V_DONE = ve.emit(lambda e, cc=cc: e.tensor_add(
            out_bf[:, cc * outW + out_off: cc * outW + out_off + W],
            sq[:, :], bbc[:, :]))
    pctx.close()
    return V_DONE, st




def emit_ffcols(p, nc, ctx, src_bf, srcW, w_dram, out_bf, outW, act, nfc,
                ncc=NCH, tag="f", src_off=0, L=None, out_off=0, addto=None,
                addW=None, add_off=0, multo=False):
    if L is None:
        L = srcW
    wl = []
    off = 0
    while off < L:
        wn_ = min(512, L - off)
        wl.append((off, wn_))
        off += wn_
    NT = len(wl)
    NS = max(1, min(3 if NT <= 3 else 2, 7 // NT, nfc))
    NW = max(1, min(3, nfc))
    ictx = ctx.enter_context(contextlib.ExitStack())
    wcol = ictx.enter_context(nc.sbuf_tensor(f"wc{tag}", [128, NW * ncc * 128],
                                             BF16))
    pst = [ictx.enter_context(nc.psum_tensor(f"pf{tag}{i}",
                                             [128, wl[i % NT][1]], F32))
           for i in range(NS * NT)]
    gp, te, sc, ve = p.gp, p.te, p.sc, p.ve
    ev = ve if addto is not None else sc
    GP0 = gp.v
    TE0 = te.v
    EV0 = ev.v
    gp.wait(gp, GP0)
    gp.wait(te, TE0)
    te.wait(sc, sc.v)
    te.wait(ve, ve.v)

    def te_after(fc):
        return TE0 + (fc + 1) * NT * ncc

    for fc in range(nfc):
        if fc >= NW:
            gp.wait(te, te_after(fc - NW))
        gp.dma(lambda e, fc=fc: e.dma_start(
            out=wcol.rearrange("p (s cc f) -> p s cc f", s=NW, f=128)
                [:, fc % NW],
            in_=w_dram[:, fc * 128:(fc + 1) * 128].rearrange(
                "(cc p) f -> p cc f", p=128)))

    for fc in range(nfc):
        te.wait(gp, GP0 + 16 * min(fc + 2, nfc))
        if fc >= NS:
            te.wait(ev, EV0 + NT * (fc - NS + 1))
        for tt, (wo, wn_) in enumerate(wl):
            for cc in range(ncc):
                te.emit(lambda e, fc=fc, tt=tt, cc=cc, wo=wo, wn_=wn_:
                        e.matmul(
                    pst[(fc % NS) * NT + tt][:, :],
                    wcol[:, (fc % NW) * ncc * 128 + cc * 128:
                         (fc % NW) * ncc * 128 + (cc + 1) * 128],
                    src_bf[:, cc * srcW + src_off + wo:
                           cc * srcW + src_off + wo + wn_],
                    start=(cc == 0), stop=(cc == ncc - 1)))

    for fc in range(nfc):
        for tt, (wo, wn_) in enumerate(wl):
            ev.wait(te, TE0 + fc * NT * ncc + (tt + 1) * ncc)
            if addto is not None:
                ev.wait(ve, ve.v)
                op = ALU.mult if multo else ALU.add
                ve.emit(lambda e, fc=fc, tt=tt, wo=wo, wn_=wn_, op=op:
                        e.tensor_tensor(
                    addto[:, fc * addW + add_off + wo:
                          fc * addW + add_off + wo + wn_],
                    pst[(fc % NS) * NT + tt][:, :],
                    addto[:, fc * addW + add_off + wo:
                          fc * addW + add_off + wo + wn_], op))
            else:
                sc.emit(lambda e, fc=fc, tt=tt, wo=wo, wn_=wn_: e.activation(
                    out_bf[:, fc * outW + out_off + wo:
                           fc * outW + out_off + wo + wn_],
                    pst[(fc % NS) * NT + tt][:, :], act))
    assert ev.v == EV0 + nfc * NT
    ictx.close()
    return ev.v, NT


def emit_router(p, nc, ctx, xsb, stats, consts, E, wrT_dram, wn_out, tag="r"):
    W = TPC
    wrs = ctx.enter_context(nc.sbuf_tensor(f"wrs{tag}", [128, NCH * E], F32))
    cols = ctx.enter_context(nc.sbuf_tensor(f"cols{tag}", [128, 2 * 8], F32))
    csbc = ctx.enter_context(nc.sbuf_tensor(f"csbc{tag}", [128, E], F32))
    lrow = ctx.enter_context(nc.sbuf_tensor(f"lrow{tag}", [1, E], F32))
    sml = ctx.enter_context(nc.sbuf_tensor(f"sml{tag}", [128, 2 * E + 8], F32))
    pctx = ctx.enter_context(contextlib.ExitStack())
    ps_l = pctx.enter_context(nc.psum_tensor(f"psl{tag}", [128, E], F32))
    ps_cs = pctx.enter_context(nc.psum_tensor(f"pcs{tag}", [1, E], F32))
    ps_cb = pctx.enter_context(nc.psum_tensor(f"pcb{tag}", [128, E], F32))
    ps_t = pctx.enter_context(nc.psum_tensor(f"pst{tag}", [128, 2], F32))
    gp, te, sc, ve = p.gp, p.te, p.sc, p.ve
    ones_c, ones_r = consts["ones_col"], consts["ones_row"]

    gp.wait(gp, gp.v)
    V_WR = gp.dma(lambda e: e.dma_start(
        out=wrs.rearrange("p (cc f) -> p cc f", f=E),
        in_=wrT_dram.rearrange("(cc p) f -> p cc f", p=128)))

    te.wait(gp, V_WR)
    V_CS = None
    for cc in range(NCH):
        V_CS = te.emit(lambda e, cc=cc: e.matmul(
            ps_cs[:, :], ones_c[:, 0:1], wrs[:, cc * E:(cc + 1) * E],
            start=(cc == 0), stop=(cc == NCH - 1)))
    sc.wait(te, V_CS)
    V_CSR = sc.emit(lambda e: e.activation(lrow[0:1, :], ps_cs[:, :],
                                           AF.Identity))
    te.wait(sc, V_CSR)
    V_CB = te.emit(lambda e: e.matmul(ps_cb[:, :], ones_r[0:1, :],
                                      lrow[0:1, :], start=True, stop=True))
    ve.wait(te, V_CB)
    V_CBE = ve.emit(lambda e: e.tensor_copy(csbc[:, :], ps_cb[:, :]))

    V_TC = []
    for tc in range(8):
        V1 = te.emit(lambda e, tc=tc: e.transpose(
            ps_t[:, 0:1], stats[0:1, W + tc * 128: W + (tc + 1) * 128],
            consts["one1"][0:1, 0:1]))
        V1 = te.emit(lambda e, tc=tc: e.transpose(
            ps_t[:, 1:2], stats[0:1, tc * 128:(tc + 1) * 128],
            consts["one1"][0:1, 0:1]))
        ve.wait(te, V1)
        V_TC.append(ve.emit(lambda e, tc=tc: e.tensor_copy(
            cols[:, 2 * tc:2 * tc + 2], ps_t[:, 0:2])))
        if tc < 7:
            te.wait(ve, V_TC[-1])
    V_OUT = None
    V_TS1 = None
    for tc in range(8):
        if tc >= 1:
            te.wait(ve, V_TS1)
        V_L = None
        for cc in range(NCH):
            V_L = te.emit(lambda e, tc=tc, cc=cc: e.matmul(
                ps_l[:, :],
                xsb[:, cc * W + tc * 128: cc * W + (tc + 1) * 128],
                wrs[:, cc * E:(cc + 1) * E],
                start=(cc == 0), stop=(cc == NCH - 1)))
        ve.wait(te, V_L)
        ve.wait(ve, max(V_CBE, V_TC[tc]))
        if tc >= 1:
            ve.wait(ve, V_OUT)
        V_TS1 = ve.emit(lambda e, tc=tc: e.tensor_scalar(
            sml[:, 0:E], ps_l[:, :], cols[:, 2 * tc:2 * tc + 1], None,
            ALU.mult))
        ve.wait(ve, ve.v)
        ve.emit(lambda e, tc=tc: e.scalar_tensor_tensor(
            sml[:, 0:E], csbc[:, :], cols[:, 2 * tc + 1:2 * tc + 2],
            sml[:, 0:E], ALU.mult, ALU.add))
        ve.wait(ve, ve.v)
        ve.emit(lambda e: e.tensor_reduce(
            sml[:, 2 * E:2 * E + 1], sml[:, 0:E], mybir.AxisListType.X,
            ALU.max, negate=True))
        ve.wait(ve, ve.v)
        V_NM = ve.v
        sc.wait(ve, V_NM)
        V_EX = sc.emit(lambda e: e.activation(
            sml[:, E:2 * E], sml[:, 0:E], AF.Exp,
            bias=sml[:, 2 * E:2 * E + 1]))
        ve.wait(sc, V_EX)
        ve.emit(lambda e: e.tensor_scalar(
            sml[:, 0:E], sml[:, E:2 * E], 1.0, None, ALU.is_equal))
        ve.wait(ve, ve.v)
        ve.emit(lambda e: e.scalar_tensor_tensor(
            sml[:, 0:E], sml[:, 0:E], -30000.0, sml[:, E:2 * E],
            ALU.mult, ALU.add))
        ve.wait(ve, ve.v)
        ve.emit(lambda e: e.tensor_reduce(
            sml[:, 2 * E + 1:2 * E + 2], sml[:, 0:E], mybir.AxisListType.X,
            ALU.max))
        ve.wait(ve, ve.v)
        ve.emit(lambda e: e.tensor_scalar(
            sml[:, 0:E], sml[:, E:2 * E], sml[:, 2 * E + 1:2 * E + 2], None,
            ALU.is_ge))
        ve.wait(ve, ve.v)
        ve.emit(lambda e: e.scalar_tensor_tensor(
            sml[:, 0:E], sml[:, E:2 * E], 1.0, sml[:, 0:E],
            ALU.mult, ALU.mult, accum_out=sml[:, 2 * E + 2:2 * E + 3]))
        ve.wait(ve, ve.v)
        ve.emit(lambda e: e.reciprocal(sml[:, 2 * E + 3:2 * E + 4],
                                       sml[:, 2 * E + 2:2 * E + 3]))
        ve.wait(ve, ve.v)
        V_OUT = ve.emit(lambda e, tc=tc: e.tensor_scalar(
            wn_out[:, tc * E:(tc + 1) * E], sml[:, 0:E],
            sml[:, 2 * E + 3:2 * E + 4], None, ALU.mult))
    pctx.close()
    return V_OUT


def emit_ffn0(p, nc, ctx, xsb, consts, w1T, w3T, w2T, wrT, part_out,
              deps, tag="n"):
    gp, te, sc, ve, sy = p.gp, p.te, p.sc, p.ve, p.sy
    fctx = ctx.enter_context(contextlib.ExitStack())
    wn = fctx.enter_context(nc.sbuf_tensor(f"wn{tag}", [128, 8 * 8], BF16))
    p8 = fctx.enter_context(nc.sbuf_tensor(f"p8{tag}", [8, 2048], F32))
    h2 = fctx.enter_context(nc.sbuf_tensor(f"h2{tag}", [128, NCH * TPC], BF16))
    h3t = fctx.enter_context(nc.sbuf_tensor(f"h3{tag}", [128, 2048], BF16))
    ictx = fctx.enter_context(contextlib.ExitStack())
    h1 = ictx.enter_context(nc.sbuf_tensor(f"h1{tag}", [128, NCH * TPC], BF16))
    xctx = ictx.enter_context(contextlib.ExitStack())
    xh = xctx.enter_context(nc.sbuf_tensor(f"xh{tag}", [128, NCH * TPC], BF16))
    V_LN, stats = emit_ln(p, nc, xsb, TPC, consts, xh, deps, tag=tag,
                          alloc_ctx=xctx)
    for eng in (p.te, p.sc, p.gp):
        eng.wait(ve, V_LN)
    V_RT = emit_router(p, nc, xctx, xsb, stats, consts, 8, wrT, wn, tag=tag)
    p.te.wait(ve, V_RT)
    emit_ffcols(p, nc, xctx, xh, TPC, w1T, h1, TPC, AF.Silu, NCH,
                tag=tag + "1")
    V_H1S = sc.v
    xctx.close()
    for eng in (p.te, p.gp):
        eng.wait(sc, V_H1S)
    emit_ffcols(p, nc, ictx, h1, TPC, w3T, h2, TPC, AF.Silu, NCH,
                tag=tag + "3")
    V_H2S = sc.v
    ictx.close()

    w2r = fctx.enter_context(nc.sbuf_tensor(f"w2r{tag}", [128, NCH * 2048],
                                            BF16))
    pctx = fctx.enter_context(contextlib.ExitStack())
    ps3 = [pctx.enter_context(nc.psum_tensor(f"p3{tag}{i}", [128, 512], F32))
           for i in range(2)]
    ps8 = [pctx.enter_context(nc.psum_tensor(f"p8{tag}{i}", [8, 512], F32))
           for i in range(4)]
    gp.wait(gp, gp.v)
    gp.wait(sc, V_H2S)
    for fc in range(NCH):
        gp.dma(lambda e, fc=fc: e.dma_start(
            out=w2r.rearrange("p (fc c) -> p fc c", c=2048)[:, fc],
            in_=w2T[fc * 128:(fc + 1) * 128, :]))
    V_W2 = gp.v

    te.wait(gp, V_W2)
    te.wait(sc, V_H2S)
    TE3 = te.v
    SC3 = sc.v
    VE3 = ve.v

    for tc in range(8):
        te.wait(ve, V_RT)
        for cs in range(4):
            k = tc * 4 + cs
            if k >= 2:
                te.wait(sc, SC3 + (k - 1))
            for fc in range(NCH):
                te.emit(lambda e, tc=tc, cs=cs, fc=fc, k=k: e.matmul(
                    ps3[k % 2][:, :],
                    h2[:, fc * TPC + tc * 128: fc * TPC + (tc + 1) * 128],
                    w2r[:, fc * 2048 + cs * 512: fc * 2048 + (cs + 1) * 512],
                    start=(fc == 0), stop=(fc == NCH - 1)))
            sc.wait(te, te.v)
            sc.emit(lambda e, cs=cs, k=k: e.activation(
                h3t[:, cs * 512:(cs + 1) * 512], ps3[k % 2][:, :],
                AF.Identity))
            te.wait(sc, sc.v)
            te.emit(lambda e, tc=tc, cs=cs: e.matmul(
                ps8[cs][:, :], wn[:, tc * 8:(tc + 1) * 8],
                h3t[:, cs * 512:(cs + 1) * 512],
                start=(tc == 0), stop=(tc == 7)))
    V_TE8 = te.v
    ve.wait(te, V_TE8)
    V_P8 = None
    for cs in range(4):
        V_P8 = ve.emit(lambda e, cs=cs: e.tensor_copy(
            p8[:, cs * 512:(cs + 1) * 512], ps8[cs][:, :]))
    sy.wait(ve, V_P8)
    V_PS = sy.dma(lambda e: e.dma_start(out=part_out, in_=p8[:, :]))
    pctx.close()
    fctx.close()
    return V_PS




def emit_ffn1(p, nc, ctx, xsb, consts, wr16T, sgD, suD, sdD, gwD, uwD, dwD,
              deps, tag="M"):
    gp, te, sc, ve, sy = p.gp, p.te, p.sc, p.ve, p.sy
    E = 16
    fctx = ctx.enter_context(contextlib.ExitStack())
    wn = fctx.enter_context(nc.sbuf_tensor(f"wn{tag}", [128, 8 * E], BF16))
    wTr = fctx.enter_context(nc.sbuf_tensor(f"wTr{tag}", [16, TPC], BF16))
    wrow = fctx.enter_context(nc.sbuf_tensor(f"wrw{tag}", [1, TPC], BF16))
    onebr = fctx.enter_context(nc.sbuf_tensor(f"obr{tag}", [1, 128], BF16))
    wbc = fctx.enter_context(nc.sbuf_tensor(f"wbc{tag}", [128, TPC], F32))
    xh = fctx.enter_context(nc.sbuf_tensor(f"xh{tag}", [128, NCH * TPC], BF16))
    V_LN, stats = emit_ln(p, nc, xsb, TPC, consts, xh, deps, tag=tag,
                          alloc_ctx=fctx)
    for eng in (te, sc, gp):
        eng.wait(ve, V_LN)
    V_RT = emit_router(p, nc, fctx, xsb, stats, consts, E, wr16T, wn, tag=tag)
    rctx = fctx.enter_context(contextlib.ExitStack())
    ps_tw = rctx.enter_context(nc.psum_tensor(f"ptw{tag}", [16, 128], BF16))
    te.wait(ve, V_RT)
    ve.wait(ve, V_RT)
    V_OBR = ve.emit(lambda e: e.tensor_copy(onebr[0:1, :],
                                            consts["ones_row"][0:1, :]))
    for tc in range(8):
        te.wait(ve, ve.v)
        te.emit(lambda e, tc=tc: e.transpose(
            ps_tw[:, :], wn[:, tc * E:(tc + 1) * E], consts["idb"][:, :]))
        ve.wait(te, te.v)
        ve.emit(lambda e, tc=tc: e.tensor_copy(
            wTr[:, tc * 128:(tc + 1) * 128], ps_tw[:, :]))
    V_WTR = ve.v
    rctx.close()

    sctx = fctx.enter_context(contextlib.ExitStack())
    gsh = sctx.enter_context(nc.sbuf_tensor(f"gsh{tag}", [128, NCH * TPC],
                                            BF16))
    emit_ffcols(p, nc, sctx, xh, TPC, sgD, gsh, TPC, AF.Silu, NCH,
                tag=tag + "g")
    te.wait(sc, sc.v)
    emit_ffcols(p, nc, sctx, xh, TPC, suD, None, TPC, None, NCH,
                tag=tag + "u", addto=gsh, addW=TPC, multo=True)
    V_SH = ve.v
    for eng in (te, gp):
        eng.wait(ve, V_SH)
    emit_ffcols(p, nc, sctx, gsh, TPC, sdD, None, TPC, None, NCH,
                tag=tag + "d", addto=xsb, addW=TPC)
    V_SD = ve.v
    sctx.close()

    ectx = fctx.enter_context(contextlib.ExitStack())
    gE = ectx.enter_context(nc.sbuf_tensor(f"gE{tag}", [128, 8 * TPC], BF16))
    ps_wb = ectx.enter_context(nc.psum_tensor(f"pwb{tag}", [128, 512], F32))
    for e_i in range(16):
        gp.wait(gp, gp.v)
        gp.wait(te, te.v)
        gp.wait(ve, V_WTR)
        V_WRD = gp.dma(lambda e, e_i=e_i: e.dma_start(
            out=wrow[0:1, :], in_=wTr[e_i:e_i + 1, :]))
        for w2 in range(2):
            te.wait(gp, V_WRD)
            te.wait(ve, max(V_OBR, ve.v))
            te.emit(lambda e, w2=w2: e.matmul(
                ps_wb[:, :], onebr[0:1, :],
                wrow[0:1, w2 * 512:(w2 + 1) * 512], start=True, stop=True))
            ve.wait(te, te.v)
            ve.emit(lambda e, w2=w2: e.tensor_copy(
                wbc[:, w2 * 512:(w2 + 1) * 512], ps_wb[:, :]))
        V_WBC = ve.v
        emit_ffcols(p, nc, ectx, xh, TPC,
                    gwD[e_i * D:(e_i + 1) * D, :], gE, TPC, AF.Silu, 8,
                    tag=f"{tag}G{e_i}")
        te.wait(sc, sc.v)
        emit_ffcols(p, nc, ectx, xh, TPC,
                    uwD[e_i * D:(e_i + 1) * D, :], None, TPC, None, 8,
                    tag=f"{tag}U{e_i}", addto=gE, addW=TPC, multo=True)
        ve.wait(ve, ve.v)
        for fc in range(8):
            ve.wait(ve, ve.v)
            ve.emit(lambda e, fc=fc: e.tensor_mul(
                gE[:, fc * TPC:(fc + 1) * TPC],
                gE[:, fc * TPC:(fc + 1) * TPC], wbc[:, :]))
        V_HS = ve.v
        for eng in (te, gp):
            eng.wait(ve, V_HS)
        emit_ffcols(p, nc, ectx, gE, TPC,
                    dwD[e_i * TPC:(e_i + 1) * TPC, :], None, TPC, None, NCH,
                    ncc=8, tag=f"{tag}D{e_i}", addto=xsb, addW=TPC)
    V_OUT = ve.v
    ectx.close()
    fctx.close()
    return V_OUT




def emit_A1(p, nc, ctx, t, ro=0, PFX="a0"):
    xT = t["XT_A"]
    wqkvT = t["wqkvT"]
    mask = t["maskA"]
    identb = t["identb"]
    onescol = t["onescolA"]
    onesrow = t["onesrow"]
    att = t["att2"]
    dq = t["dq_h"]
    dk = t["dk_h"]
    dv = t["dv_h"]
    gp, te, sc, ve, sy = p.gp, p.te, p.sc, p.ve, p.sy
    sy, gp, ve, sc, te = p.sy, p.gp, p.ve, p.sc, p.te
    ones_c = ctx.enter_context(nc.sbuf_tensor(PFX + "onesc", [128, 2], F32))
    ones_r = ctx.enter_context(nc.sbuf_tensor(PFX + "onesr", [1, 128], F32))
    ident_bf = ctx.enter_context(nc.sbuf_tensor(PFX + "identbf", [128, 128], BF16))
    mask_sb = ctx.enter_context(nc.sbuf_tensor(PFX + "masksb", [128, 128], F32))
    xhat = ctx.enter_context(nc.sbuf_tensor(PFX + "xhat", [128, NCH * TPC], BF16))
    consts = {"ones_col": ones_c[:, 0:1], "ones_row": ones_r,
              "eps": ones_c[:, 1:2]}

    p1 = ctx.enter_context(contextlib.ExitStack())
    xsb = p1.enter_context(nc.sbuf_tensor(PFX + "xsb", [128, NCH * TPC], F32))

    gp.dma(lambda e: e.dma_start(
        out=xsb.rearrange("p (cc t) -> p cc t", cc=NCH),
        in_=xT.rearrange("(cc p) t -> p cc t", p=128)))
    gp.dma(lambda e: e.dma_start(out=ones_c[:, :], in_=onescol))
    gp.dma(lambda e: e.dma_start(out=ones_r[:, :], in_=onesrow))
    gp.dma(lambda e: e.dma_start(out=ident_bf[:, :], in_=identb))
    V_IN = gp.dma(lambda e: e.dma_start(out=mask_sb[:, :], in_=mask))

    V_LN, _ = emit_ln(p, nc, xsb, TPC, consts, xhat, [(gp, V_IN)],
                      tag=PFX, alloc_ctx=p1)
    p1.close()

    p2 = ctx.enter_context(contextlib.ExitStack())
    wbuf = p2.enter_context(nc.sbuf_tensor(PFX + "wbuf", [128, 2 * 512], BF16))
    stage = p2.enter_context(nc.sbuf_tensor(PFX + "stage", [128, 8 * 512], BF16))
    ps_qkv = [p2.enter_context(nc.psum_tensor(f"{PFX}psq{i}", [128, 512], F32))
              for i in range(8)]
    NFG = 12
    GP0, TE0, SC0, SY0 = gp.v, te.v, sc.v, sy.v
    def te_after(fg, cc):
        return TE0 + 8 * (fg * NCH + cc + 1)

    gp.wait(ve, V_LN)
    for fg in range(NFG):
        for cc in range(NCH):
            i = fg * NCH + cc
            if i >= 2:
                fg2, cc2 = divmod(i - 2, NCH)
                gp.wait(te, te_after(fg2, cc2))
            gp.dma(lambda e, fg=fg, cc=cc, i=i: e.dma_start(
                out=wbuf[:, (i % 2) * 512:(i % 2 + 1) * 512],
                in_=wqkvT[cc * 128:(cc + 1) * 128,
                          fg * 512:(fg + 1) * 512]))

    te.wait(ve, V_LN)
    for fg in range(NFG):
        if fg >= 1:
            te.wait(sc, SC0 + 8 * fg)
        for cc in range(NCH):
            i = fg * NCH + cc
            te.wait(gp, GP0 + 16 * min(i + 2, NFG * NCH))
            for tc in range(8):
                te.emit(lambda e, i=i, cc=cc, tc=tc: e.matmul(
                    ps_qkv[tc][:, :],
                    xhat[:, cc * TPC + tc * 128: cc * TPC + (tc + 1) * 128],
                    wbuf[:, (i % 2) * 512:(i % 2 + 1) * 512],
                    start=(cc == 0), stop=(cc == NCH - 1)))
            assert te.v == te_after(fg, cc)

    for fg in range(NFG):
        sc.wait(te, te_after(fg, NCH - 1))
        sc.wait(sy, SY0 + 16 * fg)
        for tc in range(8):
            sc.emit(lambda e, tc=tc: e.activation(
                stage[:, tc * 512:(tc + 1) * 512], ps_qkv[tc][:, :],
                AF.Identity))
        assert sc.v == SC0 + 8 * (fg + 1)

    for fg in range(NFG):
        sy.wait(sc, SC0 + 8 * (fg + 1))
        sy.wait(sy, SY0 + 16 * fg)
        tgt = [dq, dk, dv][fg // 4]
        col = (fg % 4) * 512
        sy.dma(lambda e, tgt=tgt, col=col: e.dma_start(
            out=tgt.ap()[0:TPC, col:col + 512].rearrange(
                "(tc p) f -> p tc f", p=128),
            in_=stage.rearrange("p (tc f) -> p tc f", f=512)))
    V_ST = sy.v
    assert V_ST == SY0 + 16 * NFG
    p2.close()

    p3 = ctx.enter_context(contextlib.ExitStack())
    qTs = p3.enter_context(nc.sbuf_tensor(PFX + "qTs", [128, 16 * TPC], BF16))
    kTs = p3.enter_context(nc.sbuf_tensor(PFX + "kTs", [128, 16 * TPC], BF16))
    vS = p3.enter_context(nc.sbuf_tensor(PFX + "vS", [128, 16 * TPC], BF16))
    ps_sc = p3.enter_context(nc.psum_tensor(PFX + "pssc", [128, 128], F32))
    ps_tr = p3.enter_context(nc.psum_tensor(PFX + "pstr", [128, 128], BF16))
    ps_o = p3.enter_context(nc.psum_tensor(PFX + "pso", [128, 128], F32))
    scf = p3.enter_context(nc.sbuf_tensor(PFX + "scf", [128, 128], F32))
    expb = p3.enter_context(nc.sbuf_tensor(PFX + "expb", [128, 128], BF16))
    expT = p3.enter_context(nc.sbuf_tensor(PFX + "expT", [128, 128], BF16))
    den = p3.enter_context(nc.sbuf_tensor(PFX + "den", [128, 2], F32))
    ostg = p3.enter_context(nc.sbuf_tensor(PFX + "ostg", [128, 16 * 128], BF16))
    NG = 16 * TPC // 128

    sy.wait(sy, V_ST)
    sy.dma(lambda e: e.dma_start_transpose(
        qTs[:, :], dq.ap().rearrange("t (i d) -> (t i) d", d=128)))
    sy.dma(lambda e: e.dma_start_transpose(
        kTs[:, :], dk.ap().rearrange("t (i d) -> (t i) d", d=128)))
    for tp in range(8):
        sy.dma(lambda e, tp=tp: e.dma_start(
            out=vS[tp * 16:(tp + 1) * 16, :].rearrange(
                "i (g d) -> i g d", d=128),
            in_=bass.AP(dv, tp * D, [[128, 16], [8 * D, 128], [1, 128]])))
    V_TR = sy.v

    TE1, VE1, SC1, SY1 = te.v, ve.v, sc.v, sy.v
    te.wait(sy, V_TR)
    for g in range(NG):
        if g >= 1:
            te.wait(ve, VE1 + 3 * g)
        te.emit(lambda e, g=g: e.matmul(
            ps_sc[:, :], qTs[:, g * 128:(g + 1) * 128],
            kTs[:, g * 128:(g + 1) * 128], start=True, stop=True))
        te.wait(sc, SC1 + 2 * g + 1)
        te.emit(lambda e: e.transpose(ps_tr[:, :], expb[:, :],
                                      ident_bf[:, :]))
        te.wait(ve, VE1 + 3 * g + 2)
        te.emit(lambda e, g=g: e.matmul(
            ps_o[:, :], expT[:, :], vS[:, g * 128:(g + 1) * 128],
            start=True, stop=True))

    for g in range(NG):
        ve.wait(te, TE1 + 3 * g + 1)
        if g >= 1:
            ve.wait(sc, SC1 + 2 * g)
        ve.emit(lambda e: e.scalar_tensor_tensor(
            scf[:, :], ps_sc[:, :], RS, mask_sb[:, :], ALU.mult, ALU.add))
        ve.wait(te, TE1 + 3 * g + 2)
        ve.emit(lambda e: e.tensor_copy(expT[:, :], ps_tr[:, :]))
        ve.wait(ve, VE1 + 3 * g + 2)
        ve.wait(sc, SC1 + 2 * g + 1)
        ve.emit(lambda e: e.reciprocal(den[:, 1:2], den[:, 0:1]))

    for g in range(NG):
        sc.wait(ve, VE1 + 3 * g + 1)
        if g >= 1:
            sc.wait(te, TE1 + 3 * g - 1)
            sc.wait(ve, VE1 + 3 * g)
        sc.emit(lambda e: e.activation(expb[:, :], scf[:, :], AF.Exp,
                                       accum_out=den[:, 0:1]))
        sc.wait(te, TE1 + 3 * g + 3)
        sc.wait(ve, VE1 + 3 * g + 3)
        if g >= 16:
            sc.wait(sy, SY1 + 16 * (g // 16))
        sc.emit(lambda e, g=g: e.activation(
            ostg[:, (g % 16) * 128:(g % 16 + 1) * 128], ps_o[:, :],
            AF.Identity, scale=den[:, 1:2]))

    for blk in range(NG // 16):
        sy.wait(sc, SC1 + 2 * 16 * (blk + 1))
        sy.wait(sy, V_TR + 16 * blk)
        sy.dma(lambda e, blk=blk: e.dma_start(
            out=att[(ro + blk) * 2048:(ro + blk + 1) * 2048, :].rearrange(
                "(g p) d -> p g d", p=128),
            in_=ostg.rearrange("p (g d) -> p g d", d=128)))
    sy.wait(sy, sy.v)
    p3.close()



def emit_F1(p, nc, ctx, t):
    xT = t["xTme"]
    rowsT = t["rowsT"]
    wprojT = t["wprojT"]
    w1T = t["w1T"]
    w3T = t["w3T"]
    w2T = t["w2T"]
    wrT = t["wrT"]
    onescol = t["onescol"]
    onesrow = t["onesrow"]
    x1T = t["x1T"]
    part = t["part1"]
    gp, te, sc, ve, sy = p.gp, p.te, p.sc, p.ve, p.sy
    gp, te, sc, ve, sy = p.gp, p.te, p.sc, p.ve, p.sy
    ones_c = ctx.enter_context(nc.sbuf_tensor("f1onesc", [128, 4], F32))
    ones_r = ctx.enter_context(nc.sbuf_tensor("f1onesr", [1, 128], F32))
    xsb = ctx.enter_context(nc.sbuf_tensor("f1xsb", [128, NCH * TPC], F32))
    consts = {"ones_col": ones_c[:, 0:1], "ones_row": ones_r,
              "eps": ones_c[:, 1:2], "one1": ones_c,
              "ones_bf": None}

    gp.dma(lambda e: e.dma_start(out=ones_c[:, :], in_=onescol))
    gp.dma(lambda e: e.dma_start(out=ones_r[:, :], in_=onesrow))
    gp.dma(lambda e: e.dma_start(
        out=xsb.rearrange("p (cc t) -> p cc t", cc=NCH),
        in_=xT.rearrange("(cc p) t -> p cc t", p=128)))
    V_IN = gp.dma(lambda e: e.dma_start(
        out=rsb.rearrange("p (cc t) -> p cc t", cc=NCH),
        in_=rowsT.rearrange("(cc p) t -> p cc t", p=128)))

    prctx = ctx.enter_context(contextlib.ExitStack())
    rsb = prctx.enter_context(nc.sbuf_tensor("f1rsb", [128, NCH * TPC],
                                             BF16))
    wcol = prctx.enter_context(nc.sbuf_tensor("f1wcp", [128, 2 * NCH * 128],
                                              BF16))
    psp = [prctx.enter_context(nc.psum_tensor(f"f1psp{i}", [128, 512], F32))
           for i in range(8)]
    GP0, TE0, VE0 = gp.v, te.v, ve.v

    def te_after(fc):
        return TE0 + (fc + 1) * 2 * NCH

    gp.wait(gp, V_IN)
    for fc in range(NCH):
        if fc >= 2:
            gp.wait(te, te_after(fc - 2))
        gp.dma(lambda e, fc=fc: e.dma_start(
            out=wcol.rearrange("p (s cc f) -> p s cc f", s=2, f=128)
                [:, fc % 2],
            in_=wprojT[:, fc * 128:(fc + 1) * 128].rearrange(
                "(cc p) f -> p cc f", p=128)))
    te.wait(gp, V_IN)
    for fc in range(NCH):
        te.wait(gp, GP0 + 16 * min(fc + 2, NCH))
        if fc >= 4:
            te.wait(ve, VE0 + 2 * (fc - 3))
        for tt in range(2):
            for cc in range(NCH):
                te.emit(lambda e, fc=fc, tt=tt, cc=cc: e.matmul(
                    psp[(fc % 4) * 2 + tt][:, :],
                    wcol[:, (fc % 2) * NCH * 128 + cc * 128:
                         (fc % 2) * NCH * 128 + (cc + 1) * 128],
                    rsb[:, cc * TPC + tt * 512: cc * TPC + (tt + 1) * 512],
                    start=(cc == 0), stop=(cc == NCH - 1)))
    for fc in range(NCH):
        ve.wait(te, TE0 + fc * 2 * NCH + NCH)
        ve.emit(lambda e, fc=fc: e.tensor_add(
            xsb[:, fc * TPC: fc * TPC + 512], psp[(fc % 4) * 2][:, :],
            xsb[:, fc * TPC: fc * TPC + 512]))
        ve.wait(te, TE0 + (fc + 1) * 2 * NCH)
        ve.emit(lambda e, fc=fc: e.tensor_add(
            xsb[:, fc * TPC + 512: fc * TPC + 1024],
            psp[(fc % 4) * 2 + 1][:, :],
            xsb[:, fc * TPC + 512: fc * TPC + 1024]))
    V_X1 = ve.v
    assert V_X1 == VE0 + 2 * NCH
    prctx.close()

    sy.wait(ve, V_X1)
    V_X1ST = sy.dma(lambda e: e.dma_start(
        out=x1T.rearrange("(cc p) t -> p cc t", p=128),
        in_=xsb.rearrange("p (cc t) -> p cc t", cc=NCH)))

    emit_ffn0(p, nc, ctx, xsb, consts, w1T, w3T, w2T, wrT, part,
              [(ve, V_X1), (gp, V_IN)], tag="n")
    sy.wait(sy, sy.v)


def emit_B1(p, nc, ctx, t):
    xT = t["x1T"]
    scat = t["scat8a"]
    haloT = t["haloT"]
    wqTp = t["wqTp"]
    wkTp = t["wkTp"]
    wvT = t["wvT"]
    woD = t["woD"]
    tabQ = t["tabQ"]
    tabK = t["tabK"]
    maskB = t["maskB"]
    identb = t["identb"]
    w1T = t["w1T"]
    w3T = t["w3T"]
    w2T = t["w2T"]
    wrT = t["wrT"]
    onescol = t["onescol"]
    onesrow = t["onesrow"]
    x3T = t["x3T"]
    part = t["part2"]
    dxx = t["dxx"]
    datt = t["datt"]
    KL = 1280
    gp, te, sc, ve, sy = p.gp, p.te, p.sc, p.ve, p.sy
    gp, te, sc, ve, sy = p.gp, p.te, p.sc, p.ve, p.sy
    ones_c = ctx.enter_context(nc.sbuf_tensor("b1onesc", [128, 4], F32))
    ones_r = ctx.enter_context(nc.sbuf_tensor("b1onesr", [1, 128], F32))
    ones_b = ctx.enter_context(nc.sbuf_tensor("b1onesb", [128, 1], BF16))
    idb = ctx.enter_context(nc.sbuf_tensor("b1idb", [128, 128], BF16))
    consts = {"ones_col": ones_c[:, 0:1], "ones_row": ones_r,
              "eps": ones_c[:, 1:2], "one1": ones_c}
    gp.dma(lambda e: e.dma_start(out=ones_c[:, :], in_=onescol))
    gp.dma(lambda e: e.dma_start(out=ones_r[:, :], in_=onesrow))
    V_C = gp.dma(lambda e: e.dma_start(out=idb[:, :], in_=identb))
    ve.wait(gp, V_C)
    V_OB = ve.emit(lambda e: e.tensor_copy(ones_b[:, :], ones_c[:, 0:1]))

    ph1 = ctx.enter_context(contextlib.ExitStack())
    xhB = ph1.enter_context(nc.sbuf_tensor("b1xhB", [128, NCH * KL], BF16))
    ph1b = ph1.enter_context(contextlib.ExitStack())
    xsb = ph1b.enter_context(nc.sbuf_tensor("b1xsb", [128, NCH * TPC], F32))
    xhalo = ph1b.enter_context(nc.sbuf_tensor("b1xhalo", [128, NCH * 256],
                                              F32))
    ssb = ph1b.enter_context(nc.sbuf_tensor("b1ssb", [128, NCH * 8], F32))
    sred = ph1b.enter_context(nc.sbuf_tensor("b1sred", [128, 8], F32))
    gp.wait(gp, gp.v)
    gp.dma(lambda e: e.dma_start(
        out=xsb.rearrange("p (cc t) -> p cc t", cc=NCH),
        in_=xT.rearrange("(cc p) t -> p cc t", p=128)))
    gp.dma(lambda e: e.dma_start(
        out=xhalo.rearrange("p (cc t) -> p cc t", cc=NCH),
        in_=haloT.rearrange("(cc p) t -> p cc t", p=128)))
    V_XIN = gp.dma(lambda e: e.dma_start(
        out=ssb.rearrange("p (cc t) -> p cc t", cc=NCH),
        in_=scat.rearrange("(cc p) t -> p cc t", p=128)))
    ve.wait(gp, V_XIN)
    V_SC = None
    for cc in range(NCH):
        ve.wait(ve, ve.v)
        ve.emit(lambda e, cc=cc: e.tensor_copy(
            sred[:, :],
            ssb.rearrange("p (cc e) -> p cc e", cc=NCH)[:, cc]))
        ve.wait(ve, ve.v)
        V_SC = ve.emit(lambda e, cc=cc: e.tensor_add(
            xsb[:, cc * TPC: cc * TPC + 8], sred[:, :],
            xsb[:, cc * TPC: cc * TPC + 8]))
    sy.wait(ve, V_SC)
    V_X2ST = sy.dma(lambda e: e.dma_start(
        out=dxx.rearrange("(cc p) t -> p cc t", p=128),
        in_=xsb.rearrange("p (cc t) -> p cc t", cc=NCH)))
    V_LN, _ = emit_ln(p, nc, xsb, TPC, consts, xhB,
                      [(ve, V_SC), (gp, V_XIN)], tag="bo",
                      alloc_ctx=ph1b, outW=KL, out_off=256)
    V_LNH, _ = emit_ln(p, nc, xhalo, 256, consts, xhB,
                       [(gp, V_XIN)], tag="bh",
                       alloc_ctx=ph1b, outW=KL, out_off=0)
    for eng in (te, sc, gp, sy):
        eng.wait(ve, V_LN)
        eng.wait(ve, V_LNH)
    sy.wait(sy, V_X2ST)
    for eng in (te, sc, gp, ve):
        eng.wait(sy, V_X2ST)
    ph1b.close()

    att_r = ctx.enter_context(contextlib.ExitStack())
    ySB = att_r.enter_context(nc.sbuf_tensor("b1ySB", [128, 16 * TPC], BF16,
                                             side="right"))
    att_r2 = att_r.enter_context(contextlib.ExitStack())
    qT = att_r2.enter_context(nc.sbuf_tensor("b1qT", [128, 16 * TPC], BF16,
                                             side="right"))
    kT = att_r2.enter_context(nc.sbuf_tensor("b1kT", [128, 16 * KL], BF16,
                                             side="right"))
    vS = att_r2.enter_context(nc.sbuf_tensor("b1vS", [128, 160 * 128], BF16,
                                             side="right"))
    msk = att_r2.enter_context(nc.sbuf_tensor("b1msk", [128, 16 * 256], BF16,
                                              side="right"))
    gp.wait(gp, gp.v)
    V_MSK = gp.dma(lambda e: e.dma_start(out=msk[:, :], in_=maskB))

    ph2 = ctx.enter_context(contextlib.ExitStack())
    emit_ffcols(p, nc, ph2, xhB, KL, wqTp, qT, TPC, AF.Identity, NCH,
                tag="bq", src_off=256, L=TPC)
    emit_ffcols(p, nc, ph2, xhB, KL, wkTp, kT, KL, AF.Identity, NCH,
                tag="bk", L=KL)
    for h in range(16):
        hctx = ph2.enter_context(contextlib.ExitStack())
        vTh = hctx.enter_context(nc.sbuf_tensor(f"b1vTh{h}", [128, KL],
                                                BF16))
        ps_vt = hctx.enter_context(nc.psum_tensor(f"b1pvt{h}", [128, 128],
                                                  BF16))
        emit_ffcols(p, nc, hctx, xhB, KL,
                    wvT[:, h * 128:(h + 1) * 128], vTh, KL,
                    AF.Identity, 1, tag=f"bv{h}", L=KL)
        te.wait(sc, sc.v)
        te.wait(ve, V_OB)
        for tc in range(10):
            te.wait(ve, ve.v)
            te.emit(lambda e, tc=tc, ps_vt=ps_vt, vTh=vTh: e.transpose(
                ps_vt[:, :], vTh[:, tc * 128:(tc + 1) * 128], idb[:, :]))
            ve.wait(te, te.v)
            ve.emit(lambda e, h=h, tc=tc, ps_vt=ps_vt: e.tensor_copy(
                vS[:, (tc * 16 + h) * 128:(tc * 16 + h) * 128 + 128],
                ps_vt[:, :]))
        te.wait(ve, ve.v)
        hctx.close()
    V_VS = ve.v
    V_TEP2 = te.v
    ph2.close()
    ph1.close()

    ph3 = ctx.enter_context(contextlib.ExitStack())
    tq = ph3.enter_context(nc.sbuf_tensor("b1tq", [128, 2 * TPC], F32))
    tk = ph3.enter_context(nc.sbuf_tensor("b1tk", [128, 2 * KL], F32))
    scr = ph3.enter_context(nc.sbuf_tensor("b1scr", [64, 3 * KL], F32))
    gp.wait(gp, gp.v)
    gp.wait(te, V_TEP2)
    gp.wait(ve, V_VS)
    gp.wait(sc, sc.v)
    gp.dma(lambda e: e.dma_start(out=tq[:, :], in_=tabQ))
    V_TAB = gp.dma(lambda e: e.dma_start(out=tk[:, :], in_=tabK))
    ve.wait(gp, V_TAB)
    ve.wait(sc, sc.v)

    def rope(buf, tab, W):
        cos0 = tab[0:64, 0:W]
        cos1 = tab[64:128, 0:W]
        sin0 = tab[0:64, W:2 * W]
        sin1 = tab[64:128, W:2 * W]
        for h in range(16):
            qe = buf[:, h * W:(h + 1) * W][0:64, :]
            qo = buf[:, h * W:(h + 1) * W][64:128, :]
            s1 = scr[0:64, 0:W]
            s2 = scr[0:64, KL:KL + W]
            s3 = scr[0:64, 2 * KL:2 * KL + W]
            ve.wait(ve, ve.v)
            ve.emit(lambda e, qe=qe, c=cos0, s1=s1: e.tensor_mul(
                s1, qe, c))
            ve.emit(lambda e, qo=qo, sn=sin1, s2=s2: e.tensor_mul(
                s2, qo, sn))
            ve.emit(lambda e, qe=qe, sn=sin0, s3=s3: e.tensor_mul(
                s3, qe, sn))
            ve.wait(ve, ve.v)
            ve.emit(lambda e, qe=qe, s1=s1, s2=s2: e.tensor_sub(
                qe, s1, s2))
            ve.wait(ve, ve.v)
            ve.emit(lambda e, qo=qo, c=cos1, s2=s2: e.tensor_mul(
                s2, qo, c))
            ve.wait(ve, ve.v)
            ve.emit(lambda e, qo=qo, s2=s2, s3=s3: e.tensor_add(
                qo, s3, s2))
    rope(qT, tq, TPC)
    rope(kT, tk, KL)
    V_ROPE = ve.v
    ph3.close()

    ph4 = ctx.enter_context(contextlib.ExitStack())
    exB = ph4.enter_context(nc.sbuf_tensor("b1exB", [128, 4 * 256], BF16))
    rbc = ph4.enter_context(nc.sbuf_tensor("b1rbc", [128, 256], F32))
    rro = ph4.enter_context(nc.sbuf_tensor("b1rro", [1, 256], F32))
    ps_s = ph4.enter_context(nc.psum_tensor("b1psbs", [128, 256], F32))
    ps_d = ph4.enter_context(nc.psum_tensor("b1psbd", [1, 256], F32))
    ps_y = ph4.enter_context(nc.psum_tensor("b1psby", [128, 256], F32))
    ps_rb = ph4.enter_context(nc.psum_tensor("b1psrb", [128, 256], F32))
    te.wait(ve, V_ROPE)
    te.wait(gp, V_MSK)
    for ci in range(4):
        for h in range(16):
            for kc in range(4):
                te.wait(ve, ve.v)
                te.emit(lambda e, ci=ci, h=h, kc=kc: e.matmul(
                    ps_s[:, :],
                    kT[:, h * KL + ci * 256 + kc * 128:
                       h * KL + ci * 256 + (kc + 1) * 128],
                    qT[:, h * TPC + ci * 256: h * TPC + (ci + 1) * 256],
                    start=True, stop=True))
                ve.wait(te, te.v)
                ve.emit(lambda e, ci=ci, kc=kc: e.scalar_tensor_tensor(
                    exB[:, kc * 256:(kc + 1) * 256], ps_s[:, :], RS,
                    msk[:, (ci * 4 + kc) * 256:(ci * 4 + kc + 1) * 256],
                    ALU.mult, ALU.add))
                sc.wait(ve, ve.v)
                sc.emit(lambda e, kc=kc: e.activation(
                    exB[:, kc * 256:(kc + 1) * 256],
                    exB[:, kc * 256:(kc + 1) * 256], AF.Exp))
            te.wait(sc, sc.v)
            for kc in range(4):
                te.emit(lambda e, kc=kc: e.matmul(
                    ps_d[:, :], ones_b[:, 0:1],
                    exB[:, kc * 256:(kc + 1) * 256],
                    start=(kc == 0), stop=(kc == 3)))
            for kc in range(4):
                te.emit(lambda e, ci=ci, h=h, kc=kc: e.matmul(
                    ps_y[:, :],
                    vS[:, ((ci * 2 + kc) * 16 + h) * 128:
                       ((ci * 2 + kc) * 16 + h) * 128 + 128],
                    exB[:, kc * 256:(kc + 1) * 256],
                    start=(kc == 0), stop=(kc == 3)))
            V_YMM = te.v
            ve.wait(te, V_YMM)
            ve.emit(lambda e: e.reciprocal(rro[0:1, :], ps_d[:, :]))
            te.wait(ve, ve.v)
            V_RB = te.emit(lambda e: e.matmul(
                ps_rb[:, :], ones_r[0:1, :], rro[0:1, :],
                start=True, stop=True))
            ve.wait(te, V_RB)
            ve.emit(lambda e: e.tensor_copy(rbc[:, :], ps_rb[:, :]))
            ve.wait(ve, ve.v)
            ve.emit(lambda e, ci=ci, h=h: e.tensor_mul(
                ySB[:, h * TPC + ci * 256: h * TPC + (ci + 1) * 256],
                ps_y[:, :], rbc[:, :]))
    V_ATT = ve.v
    ph4.close()

    ph5a = ctx.enter_context(contextlib.ExitStack())
    atout = ph5a.enter_context(nc.sbuf_tensor("b1atout", [128, NCH * TPC],
                                              BF16))
    for eng in (te, sc, gp):
        eng.wait(ve, V_ATT)
    emit_ffcols(p, nc, ph5a, ySB, TPC, woD, atout, TPC, AF.Identity, NCH,
                tag="bw")
    V_WOS = sc.v
    sy.wait(sc, V_WOS)
    sy.wait(sy, sy.v)
    V_ATST = sy.dma(lambda e: e.dma_start(
        out=datt.rearrange("(cc p) t -> p cc t", p=128),
        in_=atout.rearrange("p (cc t) -> p cc t", cc=NCH)))
    ph5a.close()
    att_r2.close()
    att_r.close()

    ph5 = ctx.enter_context(contextlib.ExitStack())
    xsb2 = ph5.enter_context(nc.sbuf_tensor("b1xsb2", [128, NCH * TPC], F32))
    ph5c = ph5.enter_context(contextlib.ExitStack())
    attin = ph5c.enter_context(nc.sbuf_tensor("b1attin", [128, NCH * TPC],
                                              BF16))
    gp.wait(gp, gp.v)
    gp.wait(sy, V_ATST)
    gp.wait(sc, sc.v)
    gp.wait(te, te.v)
    gp.dma(lambda e: e.dma_start(
        out=xsb2.rearrange("p (cc t) -> p cc t", cc=NCH),
        in_=dxx.rearrange("(cc p) t -> p cc t", p=128)))
    V_RL = gp.dma(lambda e: e.dma_start(
        out=attin.rearrange("p (cc t) -> p cc t", cc=NCH),
        in_=datt.rearrange("(cc p) t -> p cc t", p=128)))
    ve.wait(gp, V_RL)
    V_X3 = None
    for cc in range(NCH):
        ve.wait(ve, ve.v)
        V_X3 = ve.emit(lambda e, cc=cc: e.tensor_add(
            xsb2[:, cc * TPC:(cc + 1) * TPC],
            attin[:, cc * TPC:(cc + 1) * TPC],
            xsb2[:, cc * TPC:(cc + 1) * TPC]))
    ph5c.close()
    sy.wait(ve, V_X3)
    sy.dma(lambda e: e.dma_start(
        out=x3T.rearrange("(cc p) t -> p cc t", p=128),
        in_=xsb2.rearrange("p (cc t) -> p cc t", cc=NCH)))
    emit_ffn0(p, nc, ph5, xsb2, consts, w1T, w3T, w2T, wrT, part,
              [(ve, V_X3)], tag="c")
    sy.wait(sy, sy.v)
    ph5.close()


def emit_M1(p, nc, ctx, t):
    xT = t["x3T"]
    scat = t["scat8b"]
    onescol = t["onescol"]
    onesrow = t["onesrow"]
    x4T = t["x4T"]
    xhT = t["xhT"]
    gp, te, sc, ve, sy = p.gp, p.te, p.sc, p.ve, p.sy
    gp, te, sc, ve, sy = p.gp, p.te, p.sc, p.ve, p.sy
    ones_c = ctx.enter_context(nc.sbuf_tensor("m1onesc", [128, 4], F32))
    ones_r = ctx.enter_context(nc.sbuf_tensor("m1onesr", [1, 128], F32))
    consts = {"ones_col": ones_c[:, 0:1], "ones_row": ones_r,
              "eps": ones_c[:, 1:2], "one1": ones_c}
    xsb = ctx.enter_context(nc.sbuf_tensor("m1xsb", [128, NCH * TPC], F32))
    xh = ctx.enter_context(nc.sbuf_tensor("m1xh", [128, NCH * TPC], F32))
    ssb = ctx.enter_context(nc.sbuf_tensor("m1ssb", [128, NCH * 8], F32))
    sred = ctx.enter_context(nc.sbuf_tensor("m1sred", [128, 8], F32))
    gp.dma(lambda e: e.dma_start(out=ones_c[:, :], in_=onescol))
    gp.dma(lambda e: e.dma_start(out=ones_r[:, :], in_=onesrow))
    gp.dma(lambda e: e.dma_start(
        out=xsb.rearrange("p (cc t) -> p cc t", cc=NCH),
        in_=xT.rearrange("(cc p) t -> p cc t", p=128)))
    V_XIN = gp.dma(lambda e: e.dma_start(
        out=ssb.rearrange("p (cc t) -> p cc t", cc=NCH),
        in_=scat.rearrange("(cc p) t -> p cc t", p=128)))
    ve.wait(gp, V_XIN)
    V_SC = None
    for cc in range(NCH):
        ve.wait(ve, ve.v)
        ve.emit(lambda e, cc=cc: e.tensor_copy(
            sred[:, :],
            ssb.rearrange("p (cc e) -> p cc e", cc=NCH)[:, cc]))
        ve.wait(ve, ve.v)
        V_SC = ve.emit(lambda e, cc=cc: e.tensor_add(
            xsb[:, cc * TPC: cc * TPC + 8], sred[:, :],
            xsb[:, cc * TPC: cc * TPC + 8]))
    sy.wait(ve, V_SC)
    sy.dma(lambda e: e.dma_start(
        out=x4T.rearrange("(cc p) t -> p cc t", p=128),
        in_=xsb.rearrange("p (cc t) -> p cc t", cc=NCH)))
    V_LN, _ = emit_ln(p, nc, xsb, TPC, consts, xh,
                      [(ve, V_SC), (gp, V_XIN)], tag="m", alloc_ctx=ctx)
    sy.wait(ve, V_LN)
    sy.dma(lambda e: e.dma_start(
        out=xhT.rearrange("(cc p) t -> p cc t", p=128),
        in_=xh.rearrange("p (cc t) -> p cc t", cc=NCH)))
    sy.wait(sy, sy.v)


def emit_M2(p, nc, ctx, t):
    xh = t["xhP"]
    cw = t["cw"]
    cb = t["cbI"]
    xpT = t["xpT"]
    dtwT = t["dtwT"]
    dtb = t["dtb"]
    Ain = t["Ain"]
    Dvin = t["Dvin"]
    onescol = t["onescol"]
    onesrow = t["onesrow"]
    yout = t["yout"]
    TH = 1024
    gp, te, sc, ve, sy = p.gp, p.te, p.sc, p.ve, p.sy
    gp, te, sc, ve, sy = p.gp, p.te, p.sc, p.ve, p.sy
    ones_c = ctx.enter_context(nc.sbuf_tensor("m2onesc", [128, 4], F32))
    ones_r = ctx.enter_context(nc.sbuf_tensor("m2onesr", [1, 128], F32))
    bst = ctx.enter_context(nc.sbuf_tensor("m2bst", [1, 2 * TH], F32))
    cwsb = ctx.enter_context(nc.sbuf_tensor("m2cwsb", [128, NCH * 4], F32))
    cbsb = ctx.enter_context(nc.sbuf_tensor("m2cbsb", [128, NCH], F32))
    xps = ctx.enter_context(nc.sbuf_tensor("m2xps", [128, NCH * 96], BF16))
    dtw = ctx.enter_context(nc.sbuf_tensor("m2dtw", [64, TPC], BF16))
    dtbs = ctx.enter_context(nc.sbuf_tensor("m2dtbs", [128, 8], F32))
    Acol = ctx.enter_context(nc.sbuf_tensor("m2Acol", [128, 8 * 16], F32))
    Dcol = ctx.enter_context(nc.sbuf_tensor("m2Dcol", [128, 8], F32))
    carry = ctx.enter_context(nc.sbuf_tensor("m2carry", [128, 128], F32))
    xcN = ctx.enter_context(nc.sbuf_tensor("m2xcN", [128, 8 * TH], F32))
    dltN = ctx.enter_context(nc.sbuf_tensor("m2dltN", [128, 8 * TH], F32))
    duN = ctx.enter_context(nc.sbuf_tensor("m2duN", [128, 8 * TH], F32))
    yac = ctx.enter_context(nc.sbuf_tensor("m2yac", [128, 8 * TH], F32))
    Bbc = ctx.enter_context(nc.sbuf_tensor("m2Bbc", [128, TH], F32))
    Cbc = ctx.enter_context(nc.sbuf_tensor("m2Cbc", [128, TH], F32))
    htl = ctx.enter_context(nc.sbuf_tensor("m2htl", [128, TH], F32))
    ttl = ctx.enter_context(nc.sbuf_tensor("m2ttl", [128, TH], F32))
    atl = ctx.enter_context(nc.sbuf_tensor("m2atl", [128, TH], F32))
    atl2 = ctx.enter_context(nc.sbuf_tensor("m2atl2", [128, 2 * TH], F32))
    t2b = ctx.enter_context(nc.sbuf_tensor("m2t2b", [128, 2 * TH], F32))
    xdb = ctx.enter_context(nc.sbuf_tensor("m2xdb", [96, TH], F32))
    xdbb = ctx.enter_context(nc.sbuf_tensor("m2xdbb", [64, TH], BF16))
    pad = ctx.enter_context(nc.sbuf_tensor("m2pad", [128, TH + 3], F32))
    xct = ctx.enter_context(nc.sbuf_tensor("m2xct", [128, TH], F32))
    xcb = ctx.enter_context(nc.sbuf_tensor("m2xcb", [128, TH], BF16))
    ps_xd = [ctx.enter_context(nc.psum_tensor(f"m2pxd{i}", [96, 512], F32))
             for i in range(2)]
    ps_dl = [ctx.enter_context(nc.psum_tensor(f"m2pdl{i}", [128, 512], F32))
             for i in range(2)]
    ps_bc = [ctx.enter_context(nc.psum_tensor(f"m2pbc{i}", [128, 512], F32))
             for i in range(4)]
    spl = ctx.enter_context(nc.sbuf_tensor("m2spl", [128, 2 * 512], F32))

    gp.dma(lambda e: e.dma_start(out=ones_c[:, :], in_=onescol))
    gp.dma(lambda e: e.dma_start(out=ones_r[:, :], in_=onesrow))
    gp.dma(lambda e: e.dma_start(
        out=cwsb.rearrange("p (cc k) -> p cc k", k=4),
        in_=cw.rearrange("(cc p) k -> p cc k", p=128)))
    gp.dma(lambda e: e.dma_start(out=cbsb[:, :], in_=cb))
    gp.dma(lambda e: e.dma_start(
        out=xps.rearrange("p (cc f) -> p cc f", f=96),
        in_=xpT.rearrange("(cc p) f -> p cc f", p=128)))
    gp.dma(lambda e: e.dma_start(out=dtw[:, :], in_=dtwT))
    gp.dma(lambda e: e.dma_start(out=dtbs[:, :], in_=dtb))
    gp.dma(lambda e: e.dma_start(out=Acol[:, :], in_=Ain))
    V_W = gp.dma(lambda e: e.dma_start(out=Dcol[:, :], in_=Dvin))
    ve.wait(gp, V_W)
    V_CZ = ve.emit(lambda e: e.memset(carry[:, :], 0.0))
    sc.wait(gp, V_W)
    V_AE = sc.emit(lambda e: e.activation(Acol[:, :], Acol[:, :], AF.Exp))
    ve.wait(sc, V_AE)
    V_AN = ve.emit(lambda e: e.tensor_scalar(Acol[:, :], Acol[:, :], -1.0,
                                             None, ALU.mult))
    sc.wait(ve, V_AN)

    for th in range(2):
        GP_TH = gp.v
        for cc in range(NCH):
            tgt = xcN[:, cc * TH:(cc + 1) * TH] if cc < 8 else xct[:, :]
            gp.wait(ve, ve.v)
            gp.wait(sc, sc.v)
            gp.wait(gp, gp.v)
            if th == 0:
                ve.wait(gp, gp.v)
                ve.emit(lambda e: e.memset(pad[:, 0:3], 0.0))
                V_PD = gp.dma(lambda e, cc=cc: e.dma_start(
                    out=pad[:, 3:3 + TH],
                    in_=xh[cc * 128:(cc + 1) * 128, 0:TH]))
            else:
                V_PD = gp.dma(lambda e, cc=cc: e.dma_start(
                    out=pad[:, 0:3 + TH],
                    in_=xh[cc * 128:(cc + 1) * 128, TH - 3:T]))
            ve.wait(gp, V_PD)
            ve.wait(ve, ve.v)
            ve.wait(sc, sc.v)
            ve.wait(te, te.v)
            ve.emit(lambda e, cc=cc: e.tensor_scalar(
                ttl[:, :], pad[:, 0:TH], cwsb[:, 4 * cc:4 * cc + 1], None,
                ALU.mult))
            for k in range(1, 4):
                ve.wait(ve, ve.v)
                ve.emit(lambda e, cc=cc, k=k: e.scalar_tensor_tensor(
                    ttl[:, :], pad[:, k:k + TH],
                    cwsb[:, 4 * cc + k:4 * cc + k + 1],
                    ttl[:, :], ALU.mult, ALU.add))
            sc.wait(ve, ve.v)
            sc.wait(te, te.v)
            V_SIL = sc.emit(lambda e, cc=cc, tgt=tgt: e.activation(
                tgt, ttl[:, :], AF.Silu, bias=cbsb[:, cc:cc + 1]))
            sc.wait(sc, V_SIL)
            V_XCB = sc.emit(lambda e, cc=cc, tgt=tgt: e.activation(
                xcb[:, :], tgt, AF.Identity))
            te.wait(sc, V_XCB)
            for w2 in range(2):
                te.emit(lambda e, cc=cc, w2=w2: e.matmul(
                    ps_xd[w2][:, :], xps[:, cc * 96:(cc + 1) * 96],
                    xcb[:, w2 * 512:(w2 + 1) * 512],
                    start=(cc == 0), stop=(cc == NCH - 1)))
        V_XD = te.v
        ve.wait(te, V_XD)
        for w2 in range(2):
            ve.wait(ve, ve.v)
            ve.emit(lambda e, w2=w2: e.tensor_copy(
                xdb[:, w2 * 512:(w2 + 1) * 512], ps_xd[w2][:, :]))
        V_XDB = ve.v
        ve.wait(ve, V_XDB)
        V_XBB = ve.emit(lambda e: e.tensor_copy(xdbb[:, :], xdb[0:64, :]))
        te.wait(ve, V_XBB)
        for ec in range(8):
            if ec >= 1:
                te.wait(sc, sc.v)
            for w2 in range(2):
                te.emit(lambda e, ec=ec, w2=w2: e.matmul(
                    ps_dl[w2][:, :], dtw[:, ec * 128:(ec + 1) * 128],
                    xdbb[:, w2 * 512:(w2 + 1) * 512],
                    start=True, stop=True))
            sc.wait(te, te.v)
            for w2 in range(2):
                sc.wait(sc, sc.v)
                sc.wait(ve, ve.v)
                sc.emit(lambda e, ec=ec, w2=w2: e.activation(
                    spl[:, 0:512], ps_dl[w2][:, :], AF.Abs,
                    bias=dtbs[:, ec:ec + 1]))
                sc.wait(sc, sc.v)
                sc.emit(lambda e: e.activation(
                    spl[:, 0:512], spl[:, 0:512], AF.Exp, scale=-1.0))
                sc.wait(sc, sc.v)
                sc.emit(lambda e: e.activation(
                    spl[:, 0:512], spl[:, 0:512], AF.Ln,
                    bias=ones_c[:, 0:1]))
                sc.wait(sc, sc.v)
                sc.emit(lambda e, ec=ec, w2=w2: e.activation(
                    spl[:, 512:1024], ps_dl[w2][:, :], AF.Relu,
                    bias=dtbs[:, ec:ec + 1]))
                ve.wait(sc, sc.v)
                ve.wait(ve, ve.v)
                ve.emit(lambda e, ec=ec, w2=w2: e.tensor_add(
                    dltN[:, ec * TH + w2 * 512: ec * TH + (w2 + 1) * 512],
                    spl[:, 0:512], spl[:, 512:1024]))
        V_DLT = ve.v
        for ec in range(8):
            ve.wait(ve, ve.v)
            ve.emit(lambda e, ec=ec: e.tensor_mul(
                duN[:, ec * TH:(ec + 1) * TH],
                dltN[:, ec * TH:(ec + 1) * TH],
                xcN[:, ec * TH:(ec + 1) * TH]))
        V_DU = ve.v

        V_scb = [0, 0]
        V_sclast = 0
        V_t2gp = [0, 0]
        V_cw = [0] * 128
        for n in range(16):
            gp.wait(gp, gp.v)
            gp.wait(te, te.v)
            gp.wait(ve, V_XDB)
            gp.dma(lambda e, n=n: e.dma_start(
                out=bst[0:1, 0:TH], in_=xdb[64 + n:65 + n, 0:TH]))
            V_BST = gp.dma(lambda e, n=n: e.dma_start(
                out=bst[0:1, TH:2 * TH], in_=xdb[80 + n:81 + n, 0:TH]))
            te.wait(ve, ve.v)
            te.wait(gp, V_BST)
            for w2 in range(2):
                te.emit(lambda e, w2=w2: e.matmul(
                    ps_bc[w2][:, :], ones_r[0:1, :],
                    bst[0:1, w2 * 512:(w2 + 1) * 512],
                    start=True, stop=True))
                te.emit(lambda e, w2=w2: e.matmul(
                    ps_bc[2 + w2][:, :], ones_r[0:1, :],
                    bst[0:1, TH + w2 * 512: TH + (w2 + 1) * 512],
                    start=True, stop=True))
            ve.wait(te, te.v)
            for w2 in range(2):
                ve.wait(ve, ve.v)
                ve.emit(lambda e, w2=w2: e.tensor_copy(
                    Bbc[:, w2 * 512:(w2 + 1) * 512], ps_bc[w2][:, :]))
                ve.wait(ve, ve.v)
                ve.emit(lambda e, w2=w2: e.tensor_copy(
                    Cbc[:, w2 * 512:(w2 + 1) * 512], ps_bc[2 + w2][:, :]))
            V_BCC = ve.v
            for ec in range(8):
                b = ec % 2
                idx = ec * 16 + n
                sc.wait(ve, V_scb[b])
                sc.wait(sc, sc.v)
                V_A = sc.emit(lambda e, ec=ec, n=n, b=b: e.activation(
                    atl2[:, b * TH:(b + 1) * TH],
                    dltN[:, ec * TH:(ec + 1) * TH], AF.Exp,
                    scale=Acol[:, ec * 16 + n: ec * 16 + n + 1]))
                ve.wait(ve, V_sclast)
                V_T1 = ve.emit(lambda e, ec=ec: e.tensor_mul(
                    ttl[:, :], duN[:, ec * TH:(ec + 1) * TH], Bbc[:, :]))
                ve.wait(sc, V_A)
                ve.wait(ve, V_T1)
                ve.wait(ve, V_cw[idx])
                V_SCN = ve.emit(lambda e, idx=idx, b=b: e.tensor_tensor_scan(
                    htl[:, :], atl2[:, b * TH:(b + 1) * TH], ttl[:, :],
                    carry[:, idx:idx + 1], ALU.mult, ALU.add))
                V_scb[b] = V_SCN
                V_sclast = V_SCN
                ve.wait(ve, V_SCN)
                V_cw[idx] = ve.emit(lambda e, idx=idx: e.tensor_copy(
                    carry[:, idx:idx + 1], htl[:, TH - 1:TH]))
                ve.wait(ve, ve.v)
                ve.wait(gp, V_t2gp[b])
                V_T2 = ve.emit(lambda e, b=b: e.tensor_mul(
                    t2b[:, b * TH:(b + 1) * TH], htl[:, :], Cbc[:, :]))
                gp.wait(ve, V_T2)
                gp.wait(gp, gp.v)
                if n == 0:
                    V_t2gp[b] = gp.emit(lambda e, ec=ec, b=b: e.tensor_copy(
                        yac[:, ec * TH:(ec + 1) * TH],
                        t2b[:, b * TH:(b + 1) * TH]))
                else:
                    V_t2gp[b] = gp.emit(lambda e, ec=ec, b=b: e.tensor_add(
                        yac[:, ec * TH:(ec + 1) * TH],
                        t2b[:, b * TH:(b + 1) * TH],
                        yac[:, ec * TH:(ec + 1) * TH]))
        ve.wait(gp, gp.v)
        for ec in range(8):
            ve.wait(ve, ve.v)
            ve.emit(lambda e, ec=ec: e.scalar_tensor_tensor(
                yac[:, ec * TH:(ec + 1) * TH],
                xcN[:, ec * TH:(ec + 1) * TH], Dcol[:, ec:ec + 1],
                yac[:, ec * TH:(ec + 1) * TH], ALU.mult, ALU.add))
        V_Y = ve.v
        sy.wait(ve, V_Y)
        sy.wait(sy, sy.v)
        sy.dma(lambda e, th=th: e.dma_start(
            out=yout.rearrange("(ec p) t -> p ec t", p=128)
                [:, :, th * TH:(th + 1) * TH],
            in_=yac.rearrange("p (ec t) -> p ec t", t=TH)))
        ve.wait(sy, sy.v)
        gp.wait(sy, sy.v)
    sy.wait(sy, sy.v)


def emit_F2(p, nc, ctx, t):
    xT = t["x4T"]
    yT = t["yT"]
    wr16T = t["wr16T"]
    sgD = t["sgD"]
    suD = t["suD"]
    sdD = t["sdD"]
    gwD = t["gwD"]
    uwD = t["uwD"]
    dwD = t["dwD"]
    wqD = t["wqD"]
    wkD = t["wkD"]
    wvD = t["wvD"]
    identb = t["identb"]
    onescol = t["onescol"]
    onesrow = t["onesrow"]
    x6T = t["x6T"]
    qTo = t["qTo"]
    kTo = t["kTo"]
    vO = t["vO"]
    gp, te, sc, ve, sy = p.gp, p.te, p.sc, p.ve, p.sy
    gp, te, sc, ve, sy = p.gp, p.te, p.sc, p.ve, p.sy
    ones_c = ctx.enter_context(nc.sbuf_tensor("f2onesc", [128, 4], F32))
    ones_r = ctx.enter_context(nc.sbuf_tensor("f2onesr", [1, 128], F32))
    idb = ctx.enter_context(nc.sbuf_tensor("f2idb", [128, 128], BF16))
    consts = {"ones_col": ones_c[:, 0:1], "ones_row": ones_r,
              "eps": ones_c[:, 1:2], "one1": ones_c, "idb": idb}
    xsb = ctx.enter_context(nc.sbuf_tensor("f2xsb", [128, NCH * TPC], F32))
    gp.dma(lambda e: e.dma_start(out=ones_c[:, :], in_=onescol))
    gp.dma(lambda e: e.dma_start(out=ones_r[:, :], in_=onesrow))
    gp.dma(lambda e: e.dma_start(out=idb[:, :], in_=identb))
    gp.dma(lambda e: e.dma_start(
        out=xsb.rearrange("p (cc t) -> p cc t", cc=NCH),
        in_=xT.rearrange("(cc p) t -> p cc t", p=128)))
    ph0 = ctx.enter_context(contextlib.ExitStack())
    ysb = ph0.enter_context(nc.sbuf_tensor("f2ysb", [128, NCH * TPC], F32))
    V_IN = gp.dma(lambda e: e.dma_start(
        out=ysb.rearrange("p (cc t) -> p cc t", cc=NCH),
        in_=yT.rearrange("(cc p) t -> p cc t", p=128)))
    ve.wait(gp, V_IN)
    V_X5 = None
    for cc in range(NCH):
        ve.wait(ve, ve.v)
        V_X5 = ve.emit(lambda e, cc=cc: e.tensor_add(
            xsb[:, cc * TPC:(cc + 1) * TPC],
            ysb[:, cc * TPC:(cc + 1) * TPC],
            xsb[:, cc * TPC:(cc + 1) * TPC]))
    ph0.close()

    V_X6 = emit_ffn1(p, nc, ctx, xsb, consts, wr16T, sgD, suD, sdD,
                     gwD, uwD, dwD, [(ve, V_X5), (gp, V_IN)], tag="M")
    sy.wait(ve, V_X6)
    sy.dma(lambda e: e.dma_start(
        out=x6T.rearrange("(cc p) t -> p cc t", p=128),
        in_=xsb.rearrange("p (cc t) -> p cc t", cc=NCH)))

    ph2 = ctx.enter_context(contextlib.ExitStack())
    xh2 = ph2.enter_context(nc.sbuf_tensor("f2xh2", [128, NCH * TPC], BF16))
    V_LN2, _ = emit_ln(p, nc, xsb, TPC, consts, xh2, [(ve, V_X6)],
                       tag="q", alloc_ctx=ph2)
    for eng in (te, sc, gp):
        eng.wait(ve, V_LN2)
    qTt = ph2.enter_context(nc.sbuf_tensor("f2qTt", [128, NCH * TPC], BF16))
    emit_ffcols(p, nc, ph2, xh2, TPC, wqD, qTt, TPC, AF.Identity, NCH,
                tag="q2")
    V_QS = sc.v
    kTt = ph2.enter_context(nc.sbuf_tensor("f2kTt", [128, TPC], BF16))
    emit_ffcols(p, nc, ph2, xh2, TPC, wkD, kTt, TPC, AF.Identity, 1,
                tag="k2")
    V_KS = sc.v
    wvs = ph2.enter_context(nc.sbuf_tensor("f2wvs", [128, NCH * 128], BF16))
    vOt = ph2.enter_context(nc.sbuf_tensor("f2vOt", [128, 8 * 128], BF16))
    ps_vv = [ph2.enter_context(nc.psum_tensor(f"f2pvv{i}", [128, 128], F32))
             for i in range(2)]
    gp.wait(gp, gp.v)
    gp.wait(te, te.v)
    gp.wait(sc, sc.v)
    V_WV = gp.dma(lambda e: e.dma_start(
        out=wvs.rearrange("p (cc f) -> p cc f", f=128),
        in_=wvD.rearrange("(cc p) f -> p cc f", p=128)))
    te.wait(gp, V_WV)
    te.wait(sc, V_KS)
    for tc in range(8):
        if tc >= 2:
            te.wait(sc, sc.v)
        for cc in range(NCH):
            te.emit(lambda e, tc=tc, cc=cc: e.matmul(
                ps_vv[tc % 2][:, :],
                xh2[:, cc * TPC + tc * 128: cc * TPC + (tc + 1) * 128],
                wvs[:, cc * 128:(cc + 1) * 128],
                start=(cc == 0), stop=(cc == NCH - 1)))
        sc.wait(te, te.v)
        sc.emit(lambda e, tc=tc: e.activation(
            vOt[:, tc * 128:(tc + 1) * 128], ps_vv[tc % 2][:, :],
            AF.Identity))
    V_VO = sc.v
    sy.wait(sc, V_QS)
    sy.wait(sy, sy.v)
    sy.dma(lambda e: e.dma_start(out=qTo, in_=qTt[:, :]))
    sy.wait(sc, V_KS)
    sy.wait(sy, sy.v)
    sy.dma(lambda e: e.dma_start(out=kTo, in_=kTt[:, :]))
    sy.wait(sc, V_VO)
    sy.wait(sy, sy.v)
    sy.dma(lambda e: e.dma_start(
        out=vO.rearrange("(tc p) d -> p tc d", p=128),
        in_=vOt.rearrange("p (tc d) -> p tc d", d=128)))
    sy.wait(sy, sy.v)
    ph2.close()


def emit_Z(p, nc, ctx, t):
    xT = t["x6T"]
    qTd = t["qTo"]
    kTf = t["kTf"]
    vf = t["vf"]
    woD2 = t["woD2"]
    wr16T = t["wr16T"]
    sgD = t["sgD"]
    suD = t["suD"]
    sdD = t["sdD"]
    gwD = t["gwD"]
    uwD = t["uwD"]
    dwD = t["dwD"]
    identb = t["identb"]
    onescol = t["onescol"]
    onesrow = t["onesrow"]
    x8T = t["x8T"]
    gp, te, sc, ve, sy = p.gp, p.te, p.sc, p.ve, p.sy
    gp, te, sc, ve, sy = p.gp, p.te, p.sc, p.ve, p.sy
    ones_c = ctx.enter_context(nc.sbuf_tensor("zonesc", [128, 4], F32))
    ones_r = ctx.enter_context(nc.sbuf_tensor("zonesr", [1, 128], F32))
    ones_b = ctx.enter_context(nc.sbuf_tensor("zonesb", [128, 1], BF16))
    idb = ctx.enter_context(nc.sbuf_tensor("zidb", [128, 128], BF16))
    consts = {"ones_col": ones_c[:, 0:1], "ones_row": ones_r,
              "eps": ones_c[:, 1:2], "one1": ones_c, "idb": idb}
    xsb = ctx.enter_context(nc.sbuf_tensor("zxsb", [128, NCH * TPC], F32))
    gp.dma(lambda e: e.dma_start(out=ones_c[:, :], in_=onescol))
    gp.dma(lambda e: e.dma_start(out=ones_r[:, :], in_=onesrow))
    gp.dma(lambda e: e.dma_start(out=idb[:, :], in_=identb))
    V_C = gp.dma(lambda e: e.dma_start(
        out=xsb.rearrange("p (cc t) -> p cc t", cc=NCH),
        in_=xT.rearrange("(cc p) t -> p cc t", p=128)))
    ve.wait(gp, V_C)
    V_OB = ve.emit(lambda e: e.tensor_copy(ones_b[:, :], ones_c[:, 0:1]))

    pha = ctx.enter_context(contextlib.ExitStack())
    ySB = pha.enter_context(nc.sbuf_tensor("zySB", [128, NCH * TPC], BF16))
    pha2 = pha.enter_context(contextlib.ExitStack())
    qTs = pha2.enter_context(nc.sbuf_tensor("zqTs", [128, 16 * TPC], BF16))
    kTs = pha2.enter_context(nc.sbuf_tensor("zkTs", [128, T], BF16))
    vSf = pha2.enter_context(nc.sbuf_tensor("zvSf", [128, 16 * 128], BF16))
    exQ = pha2.enter_context(nc.sbuf_tensor("zexQ", [128, 16 * 512], BF16))
    rbc = pha2.enter_context(nc.sbuf_tensor("zrbc", [128, 512], F32))
    rro = pha2.enter_context(nc.sbuf_tensor("zrro", [1, 512], F32))
    ps_s = pha2.enter_context(nc.psum_tensor("zpzs", [128, 512], F32))
    ps_d = pha2.enter_context(nc.psum_tensor("zpzd", [1, 512], F32))
    ps_y = pha2.enter_context(nc.psum_tensor("zpzy", [128, 512], F32))
    ps_rb = pha2.enter_context(nc.psum_tensor("zpzrb", [128, 512], F32))
    gp.wait(gp, gp.v)
    gp.dma(lambda e: e.dma_start(out=qTs[:, :], in_=qTd))
    gp.dma(lambda e: e.dma_start(out=kTs[:, :], in_=kTf))
    V_KV = gp.dma(lambda e: e.dma_start(
        out=vSf.rearrange("p (g d) -> p g d", d=128),
        in_=vf.rearrange("(g p) d -> p g d", p=128)))
    te.wait(gp, V_KV)
    te.wait(ve, V_OB)
    for h in range(16):
        for qh in range(2):
            for kc in range(16):
                te.wait(sc, sc.v)
                te.emit(lambda e, h=h, qh=qh, kc=kc: e.matmul(
                    ps_s[:, :], kTs[:, kc * 128:(kc + 1) * 128],
                    qTs[:, h * TPC + qh * 512: h * TPC + qh * 512 + 512],
                    start=True, stop=True))
                sc.wait(te, te.v)
                if kc == 0:
                    sc.wait(ve, ve.v)
                sc.emit(lambda e, kc=kc: e.activation(
                    exQ[:, kc * 512:(kc + 1) * 512], ps_s[:, :], AF.Exp,
                    scale=RS, bias=ones_c[:, 2:3]))
            te.wait(sc, sc.v)
            for kc in range(16):
                te.emit(lambda e, kc=kc: e.matmul(
                    ps_d[:, :], ones_b[:, 0:1],
                    exQ[:, kc * 512:(kc + 1) * 512],
                    start=(kc == 0), stop=(kc == 15)))
            for kc in range(16):
                te.emit(lambda e, kc=kc: e.matmul(
                    ps_y[:, :], vSf[:, kc * 128:(kc + 1) * 128],
                    exQ[:, kc * 512:(kc + 1) * 512],
                    start=(kc == 0), stop=(kc == 15)))
            V_YMM = te.v
            ve.wait(te, V_YMM)
            ve.wait(ve, ve.v)
            ve.emit(lambda e: e.reciprocal(rro[0:1, :], ps_d[:, :]))
            te.wait(ve, ve.v)
            V_RB = te.emit(lambda e: e.matmul(
                ps_rb[:, :], ones_r[0:1, :], rro[0:1, :],
                start=True, stop=True))
            ve.wait(te, V_RB)
            ve.emit(lambda e: e.tensor_copy(rbc[:, :], ps_rb[:, :]))
            ve.wait(ve, ve.v)
            ve.emit(lambda e, h=h, qh=qh: e.tensor_mul(
                ySB[:, h * TPC + qh * 512: h * TPC + qh * 512 + 512],
                ps_y[:, :], rbc[:, :]))
    V_ATT = ve.v
    pha2.close()

    for eng in (te, sc, gp):
        eng.wait(ve, V_ATT)
    emit_ffcols(p, nc, pha, ySB, TPC, woD2, None, TPC, None, NCH,
                tag="zw", addto=xsb, addW=TPC)
    V_X7 = ve.v
    pha.close()

    V_X8 = emit_ffn1(p, nc, ctx, xsb, consts, wr16T, sgD, suD, sdD,
                     gwD, uwD, dwD, [(ve, V_X7)], tag="Z")
    sy.wait(ve, V_X8)
    sy.dma(lambda e: e.dma_start(
        out=x8T.rearrange("(cc p) t -> p cc t", p=128),
        in_=xsb.rearrange("p (cc t) -> p cc t", cc=NCH)))
    sy.wait(sy, sy.v)



PAIRS = [[0, 1], [2, 3], [4, 5], [6, 7]]
ALL8 = [[0, 1, 2, 3, 4, 5, 6, 7]]


def emit_barrier(p):
    es = [p.sy, p.gp, p.ve, p.sc, p.te]
    vs = [e.v for e in es]
    for e in es:
        for o, v in zip(es, vs):
            if o is not e and v > 0:
                e.wait(o, v)


def emit_rows_glue(p, nc, ctx, t, flg, idgb):
    gp, te, sc, ve, sy = p.gp, p.te, p.sc, p.ve, p.sy
    ah = t["att2_h"]
    c0 = ctx.enter_context(nc.sbuf_tensor("rg_c0", [128, 2 * 512], BF16))
    c1 = ctx.enter_context(nc.sbuf_tensor("rg_c1", [128, 2 * 512], BF16))
    bt = ctx.enter_context(nc.sbuf_tensor("rg_bt", [128, 2 * 512], BF16))
    rstg = [ctx.enter_context(nc.sbuf_tensor(f"rg_s{i}", [128, TPC], BF16))
            for i in range(4)]
    pctx = ctx.enter_context(contextlib.ExitStack())
    ps_tr = [pctx.enter_context(nc.psum_tensor(f"rg_pt{i}", [128, 128], BF16))
             for i in range(2)]

    def gath(j0, head):
        return bass.AP(ah, (16 * j0 + head) * 128,
                       [[256 * 128, 128], [16 * 128, 4], [1, 128]])

    for j4 in range(4):
        for hl in range(8):
            it = j4 * 8 + hl
            b = it % 2
            gp.wait(ve, ve.v)
            sy.wait(ve, ve.v)
            V0 = gp.dma(lambda e, j4=j4, hl=hl, b=b: e.dma_start(
                out=c0[:, b * 512:(b + 1) * 512].rearrange(
                    "p (j d) -> p j d", d=128),
                in_=gath(j4 * 4, hl)))
            V1 = sy.dma(lambda e, j4=j4, hl=hl, b=b: e.dma_start(
                out=c1[:, b * 512:(b + 1) * 512].rearrange(
                    "p (j d) -> p j d", d=128),
                in_=gath(j4 * 4, 8 + hl)))
            ve.wait(gp, V0)
            ve.wait(sy, V1)
            ve.wait(ve, ve.v)
            ve.emit(lambda e, b=b: e.tensor_scalar(
                bt[:, b * 512:(b + 1) * 512], c0[:, b * 512:(b + 1) * 512],
                flg[:, 1:2], None, ALU.mult))
            ve.wait(ve, ve.v)
            ve.emit(lambda e, b=b: e.scalar_tensor_tensor(
                bt[:, b * 512:(b + 1) * 512], c1[:, b * 512:(b + 1) * 512],
                flg[:, 2:3], bt[:, b * 512:(b + 1) * 512],
                ALU.mult, ALU.add))
            for jq in range(4):
                te.wait(ve, ve.v)
                te.emit(lambda e, b=b, jq=jq: e.transpose(
                    ps_tr[jq % 2][:, :],
                    bt[:, b * 512 + jq * 128: b * 512 + (jq + 1) * 128],
                    idgb[:, :]))
                ve.wait(te, te.v)
                ve.emit(lambda e, jq=jq, hl=hl: e.tensor_copy(
                    rstg[jq][:, hl * 128:(hl + 1) * 128], ps_tr[jq % 2][:, :]))
        sy.wait(ve, ve.v)
        for jq in range(4):
            sy.dma(lambda e, j4=j4, jq=jq: e.dma_start(
                out=t["rowsT"][(j4 * 4 + jq) * 128:(j4 * 4 + jq + 1) * 128, :],
                in_=rstg[jq][:, :]))
    V_RST = sy.v
    for j4 in range(4):
        for hb in range(2):
            b = hb
            gp.wait(ve, ve.v)
            gp.wait(sy, V_RST)
            V0 = gp.dma(lambda e, j4=j4, hb=hb, b=b: e.dma_start(
                out=c0[:, b * 512:(b + 1) * 512].rearrange(
                    "p (j d) -> p j d", d=128),
                in_=gath(j4 * 4, 6 + hb)))
            for jq in range(4):
                te.wait(gp, V0)
                te.wait(ve, ve.v)
                te.emit(lambda e, b=b, jq=jq: e.transpose(
                    ps_tr[jq % 2][:, :],
                    c0[:, b * 512 + jq * 128: b * 512 + (jq + 1) * 128],
                    idgb[:, :]))
                ve.wait(te, te.v)
                ve.emit(lambda e, jq=jq, hb=hb: e.tensor_copy(
                    rstg[jq][:, hb * 128:(hb + 1) * 128], ps_tr[jq % 2][:, :]))
        sy.wait(ve, ve.v)
        for jq in range(4):
            sy.dma(lambda e, j4=j4, jq=jq: e.dma_start(
                out=t["rowsHT"][(j4 * 4 + jq) * 128:
                                (j4 * 4 + jq + 1) * 128, :],
                in_=rstg[jq][:, 0:256]))
    sy.wait(sy, sy.v)
    pctx.close()


def emit_halo(p, nc, ctx, t):
    gp, te, sc, ve, sy = p.gp, p.te, p.sc, p.ve, p.sy
    xhalo = ctx.enter_context(nc.sbuf_tensor("hp_x", [128, NCH * 256], F32))
    rh = ctx.enter_context(nc.sbuf_tensor("hp_r", [128, NCH * 256], BF16))
    gp.wait(gp, gp.v)
    gp.dma(lambda e: e.dma_start(
        out=xhalo.rearrange("p (cc t) -> p cc t", cc=NCH),
        in_=t["xg0"][:, 768:1024].rearrange("(cc p) t -> p cc t", p=128)))
    V_IN = gp.dma(lambda e: e.dma_start(
        out=rh.rearrange("p (cc t) -> p cc t", cc=NCH),
        in_=t["rowsHT"].rearrange("(cc p) t -> p cc t", p=128)))
    for eng in (te, sc, ve):
        eng.wait(gp, V_IN)
    emit_ffcols(p, nc, ctx, rh, 256, t["wprojT"], None, 256, None, NCH,
                tag="hp", addto=xhalo, addW=256)
    sy.wait(ve, ve.v)
    sy.dma(lambda e: e.dma_start(
        out=t["haloT"].rearrange("(cc p) t -> p cc t", p=128),
        in_=xhalo.rearrange("p (cc t) -> p cc t", cc=NCH)))
    sy.wait(sy, sy.v)


def emit_scat_glue(p, nc, ctx, t, inkey, outkey, flg, idg, tag):
    gp, te, sc, ve, sy = p.gp, p.te, p.sc, p.ve, p.sy
    srB = ctx.enter_context(nc.sbuf_tensor(f"sg_r{tag}", [8, 2048], F32))
    sc8 = ctx.enter_context(nc.sbuf_tensor(f"sg_o{tag}", [128, NCH * 8], F32))
    pctx = ctx.enter_context(contextlib.ExitStack())
    ps8 = pctx.enter_context(nc.psum_tensor(f"sg_p{tag}", [128, 8], F32))
    gp.wait(gp, gp.v)
    V = gp.dma(lambda e: e.dma_start(out=srB[:, :], in_=t[inkey]))
    te.wait(gp, V)
    for cc in range(NCH):
        te.wait(ve, ve.v)
        te.emit(lambda e, cc=cc: e.transpose(
            ps8[:, :], srB[0:8, cc * 128:(cc + 1) * 128], idg[0:8, 0:8]))
        ve.wait(te, te.v)
        ve.emit(lambda e, cc=cc: e.tensor_scalar(
            sc8[:, cc * 8:(cc + 1) * 8], ps8[:, :], flg[:, 0:1], None,
            ALU.mult))
    sy.wait(ve, ve.v)
    sy.dma(lambda e: e.dma_start(
        out=t[outkey].rearrange("(cc p) e -> p cc e", p=128),
        in_=sc8.rearrange("p (cc e) -> p cc e", e=8)))
    sy.wait(sy, sy.v)
    pctx.close()


def emit_cvt(p, nc, ctx, t, inkey, outkey, nrb, C, tag):
    gp, te, sc, ve, sy = p.gp, p.te, p.sc, p.ve, p.sy
    af = ctx.enter_context(nc.sbuf_tensor(f"cv_f{tag}", [128, 2 * C], F32))
    ab = ctx.enter_context(nc.sbuf_tensor(f"cv_b{tag}", [128, 2 * C], BF16))
    for rb in range(nrb):
        b = rb % 2
        gp.wait(sc, sc.v)
        V = gp.dma(lambda e, rb=rb, b=b: e.dma_start(
            out=af[:, b * C:(b + 1) * C],
            in_=t[inkey][rb * 128:(rb + 1) * 128, :]))
        sc.wait(gp, V)
        sc.wait(sy, sy.v)
        sc.emit(lambda e, b=b: e.activation(
            ab[:, b * C:(b + 1) * C], af[:, b * C:(b + 1) * C], AF.Identity))
        sy.wait(sc, sc.v)
        sy.dma(lambda e, rb=rb, b=b: e.dma_start(
            out=t[outkey][rb * 128:(rb + 1) * 128, :],
            in_=ab[:, b * C:(b + 1) * C]))
    sy.wait(sy, sy.v)


def emit_xhP_glue(p, nc, ctx, t, flg):
    gp, te, sc, ve, sy = p.gp, p.te, p.sc, p.ve, p.sy
    c0 = ctx.enter_context(nc.sbuf_tensor("xp_c0", [128, 2 * TPC], F32))
    c1 = ctx.enter_context(nc.sbuf_tensor("xp_c1", [128, 2 * TPC], F32))
    xf = ctx.enter_context(nc.sbuf_tensor("xp_f", [128, 2 * TPC], F32))
    for pb in range(16):
        for th in range(2):
            it = pb * 2 + th
            b = it % 2
            gp.wait(ve, ve.v)
            sy.wait(ve, ve.v)
            V0 = gp.dma(lambda e, pb=pb, th=th, b=b: e.dma_start(
                out=c0[:, b * TPC:(b + 1) * TPC],
                in_=t["Gxh"][th * D + pb * 128: th * D + (pb + 1) * 128, :]))
            nb = (pb + 8) % 16
            V1 = sy.dma(lambda e, nb=nb, th=th, b=b: e.dma_start(
                out=c1[:, b * TPC:(b + 1) * TPC],
                in_=t["Gxh"][th * D + nb * 128: th * D + (nb + 1) * 128, :]))
            ve.wait(gp, V0)
            ve.wait(sy, V1)
            ve.wait(ve, ve.v)
            ve.emit(lambda e, b=b: e.tensor_scalar(
                xf[:, b * TPC:(b + 1) * TPC], c0[:, b * TPC:(b + 1) * TPC],
                flg[:, 1:2], None, ALU.mult))
            ve.wait(ve, ve.v)
            ve.emit(lambda e, b=b: e.scalar_tensor_tensor(
                xf[:, b * TPC:(b + 1) * TPC], c1[:, b * TPC:(b + 1) * TPC],
                flg[:, 2:3], xf[:, b * TPC:(b + 1) * TPC], ALU.mult, ALU.add))
            sy.wait(ve, ve.v)
            sy.dma(lambda e, pb=pb, th=th, b=b: e.dma_start(
                out=t["xhP"][pb * 128:(pb + 1) * 128,
                             th * TPC:(th + 1) * TPC],
                in_=xf[:, b * TPC:(b + 1) * TPC]))
    sy.wait(sy, sy.v)


def emit_yT_glue(p, nc, ctx, t, flg):
    gp, te, sc, ve, sy = p.gp, p.te, p.sc, p.ve, p.sy
    c0 = ctx.enter_context(nc.sbuf_tensor("yt_c0", [128, 2 * TPC], F32))
    c1 = ctx.enter_context(nc.sbuf_tensor("yt_c1", [128, 2 * TPC], F32))
    yf = ctx.enter_context(nc.sbuf_tensor("yt_f", [128, 2 * TPC], F32))
    for gb in range(16):
        b = gb % 2
        gp.wait(ve, ve.v)
        sy.wait(ve, ve.v)
        V0 = gp.dma(lambda e, gb=gb, b=b: e.dma_start(
            out=c0[:, b * TPC:(b + 1) * TPC],
            in_=t["Gy"][gb * 128:(gb + 1) * 128, 0:TPC]))
        V1 = sy.dma(lambda e, gb=gb, b=b: e.dma_start(
            out=c1[:, b * TPC:(b + 1) * TPC],
            in_=t["Gy"][gb * 128:(gb + 1) * 128, TPC:2 * TPC]))
        ve.wait(gp, V0)
        ve.wait(sy, V1)
        ve.wait(ve, ve.v)
        ve.emit(lambda e, b=b: e.tensor_scalar(
            yf[:, b * TPC:(b + 1) * TPC], c0[:, b * TPC:(b + 1) * TPC],
            flg[:, 1:2], None, ALU.mult))
        ve.wait(ve, ve.v)
        ve.emit(lambda e, b=b: e.scalar_tensor_tensor(
            yf[:, b * TPC:(b + 1) * TPC], c1[:, b * TPC:(b + 1) * TPC],
            flg[:, 2:3], yf[:, b * TPC:(b + 1) * TPC], ALU.mult, ALU.add))
        sy.wait(ve, ve.v)
        sy.dma(lambda e, gb=gb, b=b: e.dma_start(
            out=t["yT"][gb * 128:(gb + 1) * 128, :],
            in_=yf[:, b * TPC:(b + 1) * TPC]))
    sy.wait(sy, sy.v)


def emit_kTf_copy(p, nc, ctx, t):
    gp, te, sc, ve, sy = p.gp, p.te, p.sc, p.ve, p.sy
    kb = ctx.enter_context(nc.sbuf_tensor("kc_b", [128, 2 * TPC], BF16))
    gp.wait(gp, gp.v)
    for r in range(2):
        gp.dma(lambda e, r=r: e.dma_start(
            out=kb[:, r * TPC:(r + 1) * TPC],
            in_=t["Gk"][r * 128:(r + 1) * 128, :]))
    V = gp.v
    sy.wait(gp, V)
    sy.dma(lambda e: e.dma_start(out=t["kTf"], in_=kb[:, :]))
    sy.wait(sy, sy.v)


def build_fused():
    nc = bass.Bass("TRN2", target_bir_lowering=False)
    t = {}

    def inp(name, shape, dt=F32):
        t[name] = nc.dram_tensor(name, shape, dt, kind="ExternalInput").ap()

    def intl(name, shape, dt=F32, handle=False):
        h = nc.dram_tensor(name, shape, dt)
        t[name] = h.ap()
        if handle:
            t[name + "_h"] = h

    inp("xg0", [D, TPC]); inp("xg1", [D, TPC]); inp("xTme", [D, TPC])
    inp("wqkvT", [D, 3 * D]); inp("maskA", [128, 128])
    inp("identb", [128, 128]); inp("onescolA", [128, 2])
    inp("onescol", [128, 4]); inp("onesrow", [1, 128]); inp("flags", [128, 4])
    inp("wprojT", [D, D]); inp("w1T", [D, D]); inp("w3T", [D, D])
    inp("w2T", [D, D]); inp("wrT", [D, 8])
    inp("wqTp", [D, D]); inp("wkTp", [D, D]); inp("wvT", [D, D])
    inp("woD", [D, D]); inp("tabQ", [128, 2 * TPC]); inp("tabK", [128, 2 * 1280])
    inp("maskB", [128, 16 * 256])
    inp("cw", [D, 4]); inp("cbI", [128, NCH]); inp("xpT", [D, 96])
    inp("dtwT", [64, TPC]); inp("dtb", [128, 8]); inp("Ain", [128, 128])
    inp("Dvin", [128, 8])
    inp("wr16T", [D, 16]); inp("sgD", [D, D]); inp("suD", [D, D])
    inp("sdD", [D, D]); inp("gwD", [16 * D, TPC]); inp("uwD", [16 * D, TPC])
    inp("dwD", [16 * TPC, D]); inp("wqD", [D, D]); inp("wkD", [D, 128])
    inp("wvD", [D, 128]); inp("woD2", [D, D])

    intl("att2", [2 * 16 * TPC, 128], BF16, handle=True)
    intl("dq", [TPC, D], BF16, handle=True)
    intl("dk", [TPC, D], BF16, handle=True)
    intl("dv", [TPC, D], BF16, handle=True)
    intl("rowsT", [D, TPC], BF16); intl("rowsHT", [D, 256], BF16)
    intl("x1T", [D, TPC]); intl("haloT", [D, 256])
    intl("part1", [8, 2048]); intl("scatR1", [8, 2048]); intl("scat8a", [D, 8])
    intl("dxx", [D, TPC]); intl("datt", [D, TPC], BF16)
    intl("x3T", [D, TPC]); intl("part2", [8, 2048])
    intl("scatR2", [8, 2048]); intl("scat8b", [D, 8])
    intl("x4T", [D, TPC]); intl("xhT", [D, TPC])
    intl("Gxh", [2 * D, TPC]); intl("xhP", [D, T])
    intl("yout", [TPC, T])
    intl("Gy", [2 * TPC, T]); intl("yT", [D, TPC])
    intl("x6T", [D, TPC]); intl("qTo", [128, 16 * TPC], BF16)
    intl("kTo", [128, TPC], BF16); intl("vO", [TPC, 128], BF16)
    intl("Gk", [256, TPC], BF16); intl("vf", [T, 128], BF16)
    intl("kTf", [128, T], BF16)
    t["x8T"] = nc.dram_tensor("x8T", [D, TPC], F32, kind="ExternalOutput").ap()

    with contextlib.ExitStack() as ctx:
        p = Prog(nc, ctx)
        gp, te, sc, ve, sy = p.gp, p.te, p.sc, p.ve, p.sy
        flg = ctx.enter_context(nc.sbuf_tensor("flgT", [128, 4], F32))
        idg = ctx.enter_context(nc.sbuf_tensor("idgT", [128, 128], F32))
        idgb = ctx.enter_context(nc.sbuf_tensor("idgbT", [128, 128], BF16))
        gp.dma(lambda e: e.dma_start(out=flg[:, :], in_=t["flags"]))
        V_ID = gp.dma(lambda e: e.dma_start(out=idg[:, :], in_=t["identb"]))
        ve.wait(gp, V_ID)
        ve.emit(lambda e: e.tensor_copy(idgb[:, :], idg[:, :]))

        def phase(fn, *args):
            c = contextlib.ExitStack()
            fn(p, nc, c, *args)
            c.close()
            emit_barrier(p)

        for r in (0, 1):
            phase(emit_A1, {**t, "XT_A": t[f"xg{r}"]}, r * 8, f"a{r}")
        phase(emit_rows_glue, t, flg, idgb)
        phase(emit_F1, t)
        phase(emit_halo, t)
        V = gp.emit(lambda e: e.collective_compute(
            "AllReduce", ALU.add, ins=[t["part1"]], outs=[t["scatR1"]],
            replica_groups=ALL8), 1)
        gp.wait(gp, V)
        phase(emit_scat_glue, t, "scatR1", "scat8a", flg, idg, "a")
        phase(emit_B1, t)
        V = gp.emit(lambda e: e.collective_compute(
            "AllReduce", ALU.add, ins=[t["part2"]], outs=[t["scatR2"]],
            replica_groups=ALL8), 1)
        gp.wait(gp, V)
        phase(emit_scat_glue, t, "scatR2", "scat8b", flg, idg, "b")
        phase(emit_M1, t)
        V = gp.emit(lambda e: e.collective_compute(
            "AllGather", ALU.bypass, ins=[t["xhT"]], outs=[t["Gxh"]],
            replica_groups=PAIRS), 1)
        gp.wait(gp, V)
        sy.wait(gp, V)
        phase(emit_xhP_glue, t, flg)
        phase(emit_M2, t)
        V = gp.emit(lambda e: e.collective_compute(
            "AllGather", ALU.bypass, ins=[t["yout"]], outs=[t["Gy"]],
            replica_groups=PAIRS), 1)
        gp.wait(gp, V)
        sy.wait(gp, V)
        phase(emit_yT_glue, t, flg)
        phase(emit_F2, t)
        V = gp.emit(lambda e: e.collective_compute(
            "AllGather", ALU.bypass, ins=[t["kTo"]], outs=[t["Gk"]],
            replica_groups=PAIRS), 1)
        V = gp.emit(lambda e: e.collective_compute(
            "AllGather", ALU.bypass, ins=[t["vO"]], outs=[t["vf"]],
            replica_groups=PAIRS), 1)
        gp.wait(gp, V)
        phase(emit_kTf_copy, t)
        c = contextlib.ExitStack()
        emit_Z(p, nc, c, t)
        c.close()
        p.finish()
    return nc



def _rope_tab(positions):
    inv = 1.0 / 10000.0 ** (np.arange(0, 128, 2, dtype=np.float32) / 128.0)
    fr = np.outer(positions.astype(np.float32), inv)
    W = len(positions)
    tab = np.zeros((128, 2 * W), np.float32)
    tab[0:64, 0:W] = np.cos(fr).T
    tab[64:128, 0:W] = np.cos(fr).T
    tab[0:64, W:] = np.sin(fr).T
    tab[64:128, W:] = np.sin(fr).T
    return tab


def _perm_headcols(Wm):
    WT = np.ascontiguousarray(np.asarray(Wm, np.float32).T)
    idx = np.zeros(2048, np.int64)
    for h in range(16):
        idx[h * 128:h * 128 + 64] = h * 128 + np.arange(0, 128, 2)
        idx[h * 128 + 64:(h + 1) * 128] = h * 128 + np.arange(1, 128, 2)
    return np.ascontiguousarray(WT[:, idx])


def _build_maskB(h):
    m = np.zeros((16, 128, 256), np.float32)
    for ci in range(4):
        gc = h * 4 + ci
        for kc in range(4):
            for jj in range(128):
                ww = kc * 128 + jj
                if ww < 256:
                    m[ci * 4 + kc, jj, :] = 0.0 if gc > 0 else -30000.0
                else:
                    qq = np.arange(256)
                    m[ci * 4 + kc, jj, :] = np.where(ww - 256 <= qq, 0.0,
                                                     -30000.0)
    return np.ascontiguousarray(m.transpose(1, 0, 2).reshape(128, 16 * 256))


def kernel(**inputs):
    inputs = {k: np.asarray(v) for k, v in inputs.items()}
    T4 = lambda a: np.ascontiguousarray(np.asarray(a, np.float32).T)
    DC = lambda a: np.ascontiguousarray(np.asarray(a, np.float32))
    onescol = np.ones((128, 4), np.float32)
    onescol[:, 1] = 1e-5
    onescol[:, 2] = -12.0
    onescolA = np.ones((128, 2), np.float32)
    onescolA[:, 1] = 1e-5
    onesrow = np.ones((1, 128), np.float32)
    ident = np.eye(128, dtype=np.float32)
    maskA = np.full((128, 128), -30000.0, np.float32)
    for tt in range(8):
        maskA[tt * 16:(tt + 1) * 16, tt * 16:(tt + 1) * 16] = 0.0

    x = np.asarray(inputs["L0_attn_a_x"], np.float32)
    shared = {
        "wqkvT": T4(inputs["L0_attn_a_w_qkv"]),
        "maskA": maskA, "identb": ident, "onescolA": onescolA,
        "onescol": onescol, "onesrow": onesrow,
        "wprojT": T4(inputs["L0_attn_a_w_proj"]),
        "w1T": T4(inputs["L0_ffn_w1"]), "w3T": T4(inputs["L0_ffn_w3"]),
        "w2T": T4(inputs["L0_ffn_w2"]), "wrT": T4(inputs["L0_ffn_wr"]),
        "wqTp": _perm_headcols(inputs["L0_attn_b_Wq"]),
        "wkTp": _perm_headcols(inputs["L0_attn_b_Wk"]),
        "wvT": T4(inputs["L0_attn_b_Wv"]),
        "woD": T4(inputs["L0_attn_b_Wo"]),
        "wr16T": T4(inputs["L1_ffn_router_w"]),
        "sgD": DC(inputs["L1_ffn_shared_gate_w"]),
        "suD": DC(inputs["L1_ffn_shared_up_w"]),
        "sdD": DC(inputs["L1_ffn_shared_down_w"]),
        "gwD": np.ascontiguousarray(np.asarray(
            inputs["L1_ffn_gate_w"], np.float32).reshape(16 * D, TPC)),
        "uwD": np.ascontiguousarray(np.asarray(
            inputs["L1_ffn_up_w"], np.float32).reshape(16 * D, TPC)),
        "dwD": np.ascontiguousarray(np.asarray(
            inputs["L1_ffn_down_w"], np.float32).reshape(16 * TPC, D)),
        "wqD": DC(inputs["L1_attn_b_Wq"]),
        "wkD": DC(inputs["L1_attn_b_Wk"]),
        "wvD": DC(inputs["L1_attn_b_Wv"]),
        "woD2": DC(inputs["L1_attn_b_Wo"]),
    }
    maps = []
    for c in range(NCORES):
        b, h = c // 2, c % 2
        xg0 = np.ascontiguousarray(x[b, 0:TPC, :].T)
        xg1 = np.ascontiguousarray(x[b, TPC:T, :].T)
        flags = np.zeros((128, 4), np.float32)
        flags[:, 0] = 1.0 if c == 0 else 0.0
        flags[:, 1] = 1.0 if h == 0 else 0.0
        flags[:, 2] = 1.0 if h == 1 else 0.0
        if h == 1:
            kpos = np.arange(768, 2048)
        else:
            kpos = np.concatenate([np.zeros(256, np.int64),
                                   np.arange(0, 1024)])
        eh = h
        permv = np.concatenate([np.arange(eh * 1024, (eh + 1) * 1024),
                                np.arange((1 - eh) * 1024, (2 - eh) * 1024)])
        own = permv[:1024]
        cwp = np.asarray(inputs["L1_attn_a_conv_w"], np.float32)[permv]
        cbp = np.asarray(inputs["L1_attn_a_conv_b"], np.float32)[permv]
        xp = np.asarray(inputs["L1_attn_a_x_proj_w"], np.float32)
        dtw = np.asarray(inputs["L1_attn_a_dt_proj_w"], np.float32)
        dtb = np.asarray(inputs["L1_attn_a_dt_proj_b"], np.float32)[own]
        Al = np.asarray(inputs["L1_attn_a_A_log"], np.float32)[own]
        Dv = np.asarray(inputs["L1_attn_a_D"], np.float32)[own]
        maps.append({
            **shared,
            "xg0": xg0, "xg1": xg1, "xTme": xg0 if h == 0 else xg1,
            "flags": flags,
            "tabQ": _rope_tab(np.arange(h * 1024, (h + 1) * 1024)),
            "tabK": _rope_tab(kpos), "maskB": _build_maskB(h),
            "cw": np.ascontiguousarray(cwp),
            "cbI": np.ascontiguousarray(cbp.reshape(16, 128).T),
            "xpT": np.ascontiguousarray(xp[:, permv].T),
            "dtwT": np.ascontiguousarray(dtw[own].T),
            "dtb": np.ascontiguousarray(dtb.reshape(8, 128).T),
            "Ain": np.ascontiguousarray(
                Al.reshape(8, 128, 16).transpose(1, 0, 2).reshape(128, 128)),
            "Dvin": np.ascontiguousarray(Dv.reshape(8, 128).T),
        })
    nc = build_fused()
    res = run_bass_kernel_spmd(nc, maps, core_ids=list(range(NCORES)))
    out = np.zeros((B, T, D), np.float32)
    for c in range(NCORES):
        b, h = c // 2, c % 2
        out[b, h * TPC:(h + 1) * TPC, :] = np.asarray(
            res.results[c]["x8T"]).T
    return out


# version 05 (final)
# speedup vs baseline: 1.5393x; 1.0159x over previous
import contextlib
import math
import numpy as np
import ml_dtypes
import concourse.bass as bass
import concourse.mybir as mybir
from concourse.bass_utils import run_bass_kernel_spmd

F32 = mybir.dt.float32
BF16 = mybir.dt.bfloat16
AF = mybir.ActivationFunctionType
ALU = mybir.AluOpType
ds = bass.ds
BF = ml_dtypes.bfloat16

NCORES = 8
B, T, D = 4, 2048, 2048
TPC = 1024
NCH = D // 128
RS = 1.0 / math.sqrt(128.0)


class EB:
    def __init__(self, ctx, nc, name):
        self.h = ctx.enter_context(nc.semaphore(f"ch_{name}"))
        self.v = 0
        self.ops = []
        self.name = name

    def emit(self, fn, n=1):
        self.v += n
        h, k = self.h, n

        def op(e):
            fn(e).then_inc(h, k)
        self.ops.append(op)
        return self.v

    def raw(self, fn):
        self.ops.append(fn)

    def wait(self, other, v):
        h = other.h
        self.ops.append(lambda e: e.wait_ge(h, v))

    def dma(self, fn):
        return self.emit(fn, 16)

    def run(self, e):
        for f in self.ops:
            f(e)


class Prog:
    def __init__(self, nc, ctx):
        self.nc = nc
        self.ctx = ctx
        self.sy = EB(ctx, nc, "sy")
        self.gp = EB(ctx, nc, "gp")
        self.ve = EB(ctx, nc, "ve")
        self.sc = EB(ctx, nc, "sc")
        self.te = EB(ctx, nc, "te")

    def finish(self):
        with self.nc.Block() as block:
            block.sync(self.sy.run)
            block.gpsimd(self.gp.run)
            block.vector(self.ve.run)
            block.scalar(self.sc.run)
            block.tensor(self.te.run)


def emit_ln(p, nc, xsb, W, consts, out_bf, deps, tag="", alloc_ctx=None,
            outW=None, out_off=0):
    if outW is None:
        outW = W
    wl = []
    off = 0
    while off < W:
        ln_ = min(512, W - off)
        wl.append((off, ln_))
        off += ln_
    NW = len(wl)
    sq = alloc_ctx.enter_context(nc.sbuf_tensor(f"ln_sq{tag}", [128, W], F32))
    st = alloc_ctx.enter_context(nc.sbuf_tensor(f"ln_st{tag}", [1, 2 * W], F32))
    abc = alloc_ctx.enter_context(nc.sbuf_tensor(f"ln_ab{tag}", [128, W], F32))
    bbc = alloc_ctx.enter_context(nc.sbuf_tensor(f"ln_bb{tag}", [128, W], F32))
    pctx = alloc_ctx.enter_context(contextlib.ExitStack())
    ps_sum = [pctx.enter_context(nc.psum_tensor(f"lnS{tag}{w}", [1, wl[w][1]],
                                                F32)) for w in range(NW)]
    ps_sq = [pctx.enter_context(nc.psum_tensor(f"lnQ{tag}{w}", [1, wl[w][1]],
                                               F32)) for w in range(NW)]
    ps_bc = [pctx.enter_context(nc.psum_tensor(f"lnB{tag}{w}",
                                               [128, wl[w // 2][1]], F32))
             for w in range(2 * NW)]
    ones_c = consts["ones_col"]
    ones_r = consts["ones_row"]
    eps = consts["eps"]
    ve, sc, te = p.ve, p.sc, p.te

    for (d, v) in deps:
        sc.wait(d, v)
        te.wait(d, v)

    TE0 = te.v
    V_SQ = []
    for cc in range(NCH):
        if cc >= 1:
            sc.wait(te, TE0 + 2 * NW * cc)
        V_SQ.append(sc.emit(lambda e, cc=cc: e.activation(
            sq[:, :], xsb[:, cc * W:(cc + 1) * W], AF.Square)))
    for cc in range(NCH):
        te.wait(sc, V_SQ[cc])
        for w, (wo, wn_) in enumerate(wl):
            te.emit(lambda e, cc=cc, w=w, wo=wo, wn_=wn_: e.matmul(
                ps_sum[w][:, :], ones_c[:, 0:1],
                xsb[:, cc * W + wo: cc * W + wo + wn_],
                start=(cc == 0), stop=(cc == NCH - 1)))
            te.emit(lambda e, cc=cc, w=w, wo=wo, wn_=wn_: e.matmul(
                ps_sq[w][:, :], ones_c[:, 0:1],
                sq[:, wo: wo + wn_],
                start=(cc == 0), stop=(cc == NCH - 1)))
    V_MM = te.v
    assert V_MM == TE0 + 2 * NW * NCH

    sc.wait(te, V_MM)
    V_MU = None
    for w, (wo, wn_) in enumerate(wl):
        V_MU = sc.emit(lambda e, w=w, wo=wo, wn_=wn_: e.mul(
            st[0:1, wo: wo + wn_], ps_sum[w][:, :], 1.0 / D))
    sc.wait(sc, V_MU)
    V_M2 = sc.emit(lambda e: e.activation(st[0:1, W:2 * W], st[0:1, 0:W],
                                          AF.Square))
    ve.wait(sc, V_M2)
    V_VAR = None
    for w, (wo, wn_) in enumerate(wl):
        if w >= 1:
            ve.wait(ve, V_VAR)
        V_VAR = ve.emit(lambda e, w=w, wo=wo, wn_=wn_: e.scalar_tensor_tensor(
            st[0:1, W + wo: W + wo + wn_],
            ps_sq[w][:, :], 1.0 / D,
            st[0:1, W + wo: W + wo + wn_],
            ALU.mult, ALU.subtract))
    sc.wait(ve, V_VAR)
    V_LN1 = sc.emit(lambda e: e.activation(st[0:1, W:2 * W],
                                           st[0:1, W:2 * W], AF.Ln,
                                           bias=eps[0:1, 0:1]))
    sc.wait(sc, V_LN1)
    V_RSTD = sc.emit(lambda e: e.activation(st[0:1, W:2 * W],
                                            st[0:1, W:2 * W],
                                            AF.Exp, scale=-0.5))
    ve.wait(sc, V_RSTD)
    V_BROW = ve.emit(lambda e: e.scalar_tensor_tensor(
        st[0:1, 0:W], st[0:1, 0:W], -1.0, st[0:1, W:2 * W],
        ALU.mult, ALU.mult))
    te.wait(ve, V_BROW)
    V_BC = None
    for w, (wo, wn_) in enumerate(wl):
        V_BC = te.emit(lambda e, w=w, wo=wo, wn_=wn_: e.matmul(
            ps_bc[2 * w][:, :], ones_r[0:1, :],
            st[0:1, W + wo: W + wo + wn_],
            start=True, stop=True))
        V_BC = te.emit(lambda e, w=w, wo=wo, wn_=wn_: e.matmul(
            ps_bc[2 * w + 1][:, :], ones_r[0:1, :],
            st[0:1, wo: wo + wn_],
            start=True, stop=True))
    ve.wait(te, V_BC)
    for w, (wo, wn_) in enumerate(wl):
        ve.emit(lambda e, w=w, wo=wo, wn_=wn_: e.tensor_copy(
            abc[:, wo: wo + wn_], ps_bc[2 * w][:, :]))
        ve.emit(lambda e, w=w, wo=wo, wn_=wn_: e.tensor_copy(
            bbc[:, wo: wo + wn_], ps_bc[2 * w + 1][:, :]))
    V_CP = ve.v
    V_DONE = None
    for cc in range(NCH):
        if cc == 0:
            ve.wait(ve, V_CP)
        else:
            ve.wait(ve, V_DONE)
        ve.emit(lambda e, cc=cc: e.tensor_mul(
            sq[:, :], xsb[:, cc * W:(cc + 1) * W], abc[:, :]))
        ve.wait(ve, ve.v)
        V_DONE = ve.emit(lambda e, cc=cc: e.tensor_add(
            out_bf[:, cc * outW + out_off: cc * outW + out_off + W],
            sq[:, :], bbc[:, :]))
    pctx.close()
    return V_DONE, st




def emit_ffcols(p, nc, ctx, src_bf, srcW, w_dram, out_bf, outW, act, nfc,
                ncc=NCH, tag="f", src_off=0, L=None, out_off=0, addto=None,
                addW=None, add_off=0, multo=False):
    if L is None:
        L = srcW
    wl = []
    off = 0
    while off < L:
        wn_ = min(512, L - off)
        wl.append((off, wn_))
        off += wn_
    NT = len(wl)
    NS = max(1, min(3 if NT <= 3 else 2, 7 // NT, nfc))
    NW = max(1, min(3, nfc))
    ictx = ctx.enter_context(contextlib.ExitStack())
    wcol = ictx.enter_context(nc.sbuf_tensor(f"wc{tag}", [128, NW * ncc * 128],
                                             BF16))
    pst = [ictx.enter_context(nc.psum_tensor(f"pf{tag}{i}",
                                             [128, wl[i % NT][1]], F32))
           for i in range(NS * NT)]
    gp, te, sc, ve = p.gp, p.te, p.sc, p.ve
    ev = ve if addto is not None else sc
    GP0 = gp.v
    TE0 = te.v
    EV0 = ev.v
    gp.wait(gp, GP0)
    gp.wait(te, TE0)
    te.wait(sc, sc.v)
    te.wait(ve, ve.v)

    def te_after(fc):
        return TE0 + (fc + 1) * NT * ncc

    for fc in range(nfc):
        if fc >= NW:
            gp.wait(te, te_after(fc - NW))
        gp.dma(lambda e, fc=fc: e.dma_start(
            out=wcol.rearrange("p (s cc f) -> p s cc f", s=NW, f=128)
                [:, fc % NW],
            in_=w_dram[:, fc * 128:(fc + 1) * 128].rearrange(
                "(cc p) f -> p cc f", p=128)))

    for fc in range(nfc):
        te.wait(gp, GP0 + 16 * min(fc + 2, nfc))
        if fc >= NS:
            te.wait(ev, EV0 + NT * (fc - NS + 1))
        for tt, (wo, wn_) in enumerate(wl):
            for cc in range(ncc):
                te.emit(lambda e, fc=fc, tt=tt, cc=cc, wo=wo, wn_=wn_:
                        e.matmul(
                    pst[(fc % NS) * NT + tt][:, :],
                    wcol[:, (fc % NW) * ncc * 128 + cc * 128:
                         (fc % NW) * ncc * 128 + (cc + 1) * 128],
                    src_bf[:, cc * srcW + src_off + wo:
                           cc * srcW + src_off + wo + wn_],
                    start=(cc == 0), stop=(cc == ncc - 1)))

    for fc in range(nfc):
        for tt, (wo, wn_) in enumerate(wl):
            ev.wait(te, TE0 + fc * NT * ncc + (tt + 1) * ncc)
            if addto is not None:
                ev.wait(ve, ve.v)
                op = ALU.mult if multo else ALU.add
                ve.emit(lambda e, fc=fc, tt=tt, wo=wo, wn_=wn_, op=op:
                        e.tensor_tensor(
                    addto[:, fc * addW + add_off + wo:
                          fc * addW + add_off + wo + wn_],
                    pst[(fc % NS) * NT + tt][:, :],
                    addto[:, fc * addW + add_off + wo:
                          fc * addW + add_off + wo + wn_], op))
            else:
                sc.emit(lambda e, fc=fc, tt=tt, wo=wo, wn_=wn_: e.activation(
                    out_bf[:, fc * outW + out_off + wo:
                           fc * outW + out_off + wo + wn_],
                    pst[(fc % NS) * NT + tt][:, :], act))
    assert ev.v == EV0 + nfc * NT
    ictx.close()
    return ev.v, NT


def emit_router(p, nc, ctx, xsb, stats, consts, E, wrT_dram, wn_out, tag="r"):
    W = TPC
    wrs = ctx.enter_context(nc.sbuf_tensor(f"wrs{tag}", [128, NCH * E], F32))
    cols = ctx.enter_context(nc.sbuf_tensor(f"cols{tag}", [128, 2 * 8], F32))
    csbc = ctx.enter_context(nc.sbuf_tensor(f"csbc{tag}", [128, E], F32))
    lrow = ctx.enter_context(nc.sbuf_tensor(f"lrow{tag}", [1, E], F32))
    sml = ctx.enter_context(nc.sbuf_tensor(f"sml{tag}", [128, 2 * E + 8], F32))
    pctx = ctx.enter_context(contextlib.ExitStack())
    ps_l = pctx.enter_context(nc.psum_tensor(f"psl{tag}", [128, E], F32))
    ps_cs = pctx.enter_context(nc.psum_tensor(f"pcs{tag}", [1, E], F32))
    ps_cb = pctx.enter_context(nc.psum_tensor(f"pcb{tag}", [128, E], F32))
    ps_t = pctx.enter_context(nc.psum_tensor(f"pst{tag}", [128, 2], F32))
    gp, te, sc, ve = p.gp, p.te, p.sc, p.ve
    ones_c, ones_r = consts["ones_col"], consts["ones_row"]

    gp.wait(gp, gp.v)
    V_WR = gp.dma(lambda e: e.dma_start(
        out=wrs.rearrange("p (cc f) -> p cc f", f=E),
        in_=wrT_dram.rearrange("(cc p) f -> p cc f", p=128)))

    te.wait(gp, V_WR)
    V_CS = None
    for cc in range(NCH):
        V_CS = te.emit(lambda e, cc=cc: e.matmul(
            ps_cs[:, :], ones_c[:, 0:1], wrs[:, cc * E:(cc + 1) * E],
            start=(cc == 0), stop=(cc == NCH - 1)))
    sc.wait(te, V_CS)
    V_CSR = sc.emit(lambda e: e.activation(lrow[0:1, :], ps_cs[:, :],
                                           AF.Identity))
    te.wait(sc, V_CSR)
    V_CB = te.emit(lambda e: e.matmul(ps_cb[:, :], ones_r[0:1, :],
                                      lrow[0:1, :], start=True, stop=True))
    ve.wait(te, V_CB)
    V_CBE = ve.emit(lambda e: e.tensor_copy(csbc[:, :], ps_cb[:, :]))

    V_TC = []
    for tc in range(8):
        V1 = te.emit(lambda e, tc=tc: e.transpose(
            ps_t[:, 0:1], stats[0:1, W + tc * 128: W + (tc + 1) * 128],
            consts["one1"][0:1, 0:1]))
        V1 = te.emit(lambda e, tc=tc: e.transpose(
            ps_t[:, 1:2], stats[0:1, tc * 128:(tc + 1) * 128],
            consts["one1"][0:1, 0:1]))
        ve.wait(te, V1)
        V_TC.append(ve.emit(lambda e, tc=tc: e.tensor_copy(
            cols[:, 2 * tc:2 * tc + 2], ps_t[:, 0:2])))
        if tc < 7:
            te.wait(ve, V_TC[-1])
    V_OUT = None
    V_TS1 = None
    for tc in range(8):
        if tc >= 1:
            te.wait(ve, V_TS1)
        V_L = None
        for cc in range(NCH):
            V_L = te.emit(lambda e, tc=tc, cc=cc: e.matmul(
                ps_l[:, :],
                xsb[:, cc * W + tc * 128: cc * W + (tc + 1) * 128],
                wrs[:, cc * E:(cc + 1) * E],
                start=(cc == 0), stop=(cc == NCH - 1)))
        ve.wait(te, V_L)
        ve.wait(ve, max(V_CBE, V_TC[tc]))
        if tc >= 1:
            ve.wait(ve, V_OUT)
        V_TS1 = ve.emit(lambda e, tc=tc: e.tensor_scalar(
            sml[:, 0:E], ps_l[:, :], cols[:, 2 * tc:2 * tc + 1], None,
            ALU.mult))
        ve.wait(ve, ve.v)
        ve.emit(lambda e, tc=tc: e.scalar_tensor_tensor(
            sml[:, 0:E], csbc[:, :], cols[:, 2 * tc + 1:2 * tc + 2],
            sml[:, 0:E], ALU.mult, ALU.add))
        ve.wait(ve, ve.v)
        ve.emit(lambda e: e.tensor_reduce(
            sml[:, 2 * E:2 * E + 1], sml[:, 0:E], mybir.AxisListType.X,
            ALU.max, negate=True))
        ve.wait(ve, ve.v)
        V_NM = ve.v
        sc.wait(ve, V_NM)
        V_EX = sc.emit(lambda e: e.activation(
            sml[:, E:2 * E], sml[:, 0:E], AF.Exp,
            bias=sml[:, 2 * E:2 * E + 1]))
        ve.wait(sc, V_EX)
        ve.emit(lambda e: e.tensor_scalar(
            sml[:, 0:E], sml[:, E:2 * E], 1.0, None, ALU.is_equal))
        ve.wait(ve, ve.v)
        ve.emit(lambda e: e.scalar_tensor_tensor(
            sml[:, 0:E], sml[:, 0:E], -30000.0, sml[:, E:2 * E],
            ALU.mult, ALU.add))
        ve.wait(ve, ve.v)
        ve.emit(lambda e: e.tensor_reduce(
            sml[:, 2 * E + 1:2 * E + 2], sml[:, 0:E], mybir.AxisListType.X,
            ALU.max))
        ve.wait(ve, ve.v)
        ve.emit(lambda e: e.tensor_scalar(
            sml[:, 0:E], sml[:, E:2 * E], sml[:, 2 * E + 1:2 * E + 2], None,
            ALU.is_ge))
        ve.wait(ve, ve.v)
        ve.emit(lambda e: e.scalar_tensor_tensor(
            sml[:, 0:E], sml[:, E:2 * E], 1.0, sml[:, 0:E],
            ALU.mult, ALU.mult, accum_out=sml[:, 2 * E + 2:2 * E + 3]))
        ve.wait(ve, ve.v)
        ve.emit(lambda e: e.reciprocal(sml[:, 2 * E + 3:2 * E + 4],
                                       sml[:, 2 * E + 2:2 * E + 3]))
        ve.wait(ve, ve.v)
        V_OUT = ve.emit(lambda e, tc=tc: e.tensor_scalar(
            wn_out[:, tc * E:(tc + 1) * E], sml[:, 0:E],
            sml[:, 2 * E + 3:2 * E + 4], None, ALU.mult))
    pctx.close()
    return V_OUT


def emit_ffn0(p, nc, ctx, xsb, consts, w1T, w3T, w2T, wrT, part_out,
              deps, tag="n"):
    gp, te, sc, ve, sy = p.gp, p.te, p.sc, p.ve, p.sy
    fctx = ctx.enter_context(contextlib.ExitStack())
    wn = fctx.enter_context(nc.sbuf_tensor(f"wn{tag}", [128, 8 * 8], BF16))
    p8 = fctx.enter_context(nc.sbuf_tensor(f"p8{tag}", [8, 2048], F32))
    h2 = fctx.enter_context(nc.sbuf_tensor(f"h2{tag}", [128, NCH * TPC], BF16))
    h3t = fctx.enter_context(nc.sbuf_tensor(f"h3{tag}", [128, 2048], BF16))
    ictx = fctx.enter_context(contextlib.ExitStack())
    h1 = ictx.enter_context(nc.sbuf_tensor(f"h1{tag}", [128, NCH * TPC], BF16))
    xctx = ictx.enter_context(contextlib.ExitStack())
    xh = xctx.enter_context(nc.sbuf_tensor(f"xh{tag}", [128, NCH * TPC], BF16))
    V_LN, stats = emit_ln(p, nc, xsb, TPC, consts, xh, deps, tag=tag,
                          alloc_ctx=xctx)
    for eng in (p.te, p.sc, p.gp):
        eng.wait(ve, V_LN)
    V_RT = emit_router(p, nc, xctx, xsb, stats, consts, 8, wrT, wn, tag=tag)
    p.te.wait(ve, V_RT)
    emit_ffcols(p, nc, xctx, xh, TPC, w1T, h1, TPC, AF.Silu, NCH,
                tag=tag + "1")
    V_H1S = sc.v
    xctx.close()
    for eng in (p.te, p.gp):
        eng.wait(sc, V_H1S)
    emit_ffcols(p, nc, ictx, h1, TPC, w3T, h2, TPC, AF.Silu, NCH,
                tag=tag + "3")
    V_H2S = sc.v
    ictx.close()

    w2r = fctx.enter_context(nc.sbuf_tensor(f"w2r{tag}", [128, NCH * 2048],
                                            BF16))
    pctx = fctx.enter_context(contextlib.ExitStack())
    ps3 = [pctx.enter_context(nc.psum_tensor(f"p3{tag}{i}", [128, 512], F32))
           for i in range(2)]
    ps8 = [pctx.enter_context(nc.psum_tensor(f"p8{tag}{i}", [8, 512], F32))
           for i in range(4)]
    gp.wait(gp, gp.v)
    gp.wait(sc, V_H2S)
    for fc in range(NCH):
        gp.dma(lambda e, fc=fc: e.dma_start(
            out=w2r.rearrange("p (fc c) -> p fc c", c=2048)[:, fc],
            in_=w2T[fc * 128:(fc + 1) * 128, :]))
    V_W2 = gp.v

    te.wait(gp, V_W2)
    te.wait(sc, V_H2S)
    TE3 = te.v
    SC3 = sc.v
    VE3 = ve.v

    for tc in range(8):
        te.wait(ve, V_RT)
        for cs in range(4):
            k = tc * 4 + cs
            if k >= 2:
                te.wait(sc, SC3 + (k - 1))
            for fc in range(NCH):
                te.emit(lambda e, tc=tc, cs=cs, fc=fc, k=k: e.matmul(
                    ps3[k % 2][:, :],
                    h2[:, fc * TPC + tc * 128: fc * TPC + (tc + 1) * 128],
                    w2r[:, fc * 2048 + cs * 512: fc * 2048 + (cs + 1) * 512],
                    start=(fc == 0), stop=(fc == NCH - 1)))
            sc.wait(te, te.v)
            sc.emit(lambda e, cs=cs, k=k: e.activation(
                h3t[:, cs * 512:(cs + 1) * 512], ps3[k % 2][:, :],
                AF.Identity))
            te.wait(sc, sc.v)
            te.emit(lambda e, tc=tc, cs=cs: e.matmul(
                ps8[cs][:, :], wn[:, tc * 8:(tc + 1) * 8],
                h3t[:, cs * 512:(cs + 1) * 512],
                start=(tc == 0), stop=(tc == 7)))
    V_TE8 = te.v
    ve.wait(te, V_TE8)
    V_P8 = None
    for cs in range(4):
        V_P8 = ve.emit(lambda e, cs=cs: e.tensor_copy(
            p8[:, cs * 512:(cs + 1) * 512], ps8[cs][:, :]))
    sy.wait(ve, V_P8)
    V_PS = sy.dma(lambda e: e.dma_start(out=part_out, in_=p8[:, :]))
    pctx.close()
    fctx.close()
    return V_PS




def emit_ffn1(p, nc, ctx, xsb, consts, wr16T, sgD, suD, sdD, gwD, uwD, dwD,
              deps, tag="M"):
    gp, te, sc, ve, sy = p.gp, p.te, p.sc, p.ve, p.sy
    E = 16
    fctx = ctx.enter_context(contextlib.ExitStack())
    wn = fctx.enter_context(nc.sbuf_tensor(f"wn{tag}", [128, 8 * E], BF16))
    wTr = fctx.enter_context(nc.sbuf_tensor(f"wTr{tag}", [16, TPC], BF16))
    wrow = fctx.enter_context(nc.sbuf_tensor(f"wrw{tag}", [1, TPC], BF16))
    onebr = fctx.enter_context(nc.sbuf_tensor(f"obr{tag}", [1, 128], BF16))
    wbc = fctx.enter_context(nc.sbuf_tensor(f"wbc{tag}", [128, TPC], F32))
    xh = fctx.enter_context(nc.sbuf_tensor(f"xh{tag}", [128, NCH * TPC], BF16))
    V_LN, stats = emit_ln(p, nc, xsb, TPC, consts, xh, deps, tag=tag,
                          alloc_ctx=fctx)
    for eng in (te, sc, gp):
        eng.wait(ve, V_LN)
    V_RT = emit_router(p, nc, fctx, xsb, stats, consts, E, wr16T, wn, tag=tag)
    rctx = fctx.enter_context(contextlib.ExitStack())
    ps_tw = rctx.enter_context(nc.psum_tensor(f"ptw{tag}", [16, 128], BF16))
    te.wait(ve, V_RT)
    ve.wait(ve, V_RT)
    V_OBR = ve.emit(lambda e: e.tensor_copy(onebr[0:1, :],
                                            consts["ones_row"][0:1, :]))
    for tc in range(8):
        te.wait(ve, ve.v)
        te.emit(lambda e, tc=tc: e.transpose(
            ps_tw[:, :], wn[:, tc * E:(tc + 1) * E], consts["idb"][:, :]))
        ve.wait(te, te.v)
        ve.emit(lambda e, tc=tc: e.tensor_copy(
            wTr[:, tc * 128:(tc + 1) * 128], ps_tw[:, :]))
    V_WTR = ve.v
    rctx.close()

    sctx = fctx.enter_context(contextlib.ExitStack())
    gsh = sctx.enter_context(nc.sbuf_tensor(f"gsh{tag}", [128, NCH * TPC],
                                            BF16))
    emit_ffcols(p, nc, sctx, xh, TPC, sgD, gsh, TPC, AF.Silu, NCH,
                tag=tag + "g")
    te.wait(sc, sc.v)
    emit_ffcols(p, nc, sctx, xh, TPC, suD, None, TPC, None, NCH,
                tag=tag + "u", addto=gsh, addW=TPC, multo=True)
    V_SH = ve.v
    for eng in (te, gp):
        eng.wait(ve, V_SH)
    emit_ffcols(p, nc, sctx, gsh, TPC, sdD, None, TPC, None, NCH,
                tag=tag + "d", addto=xsb, addW=TPC)
    V_SD = ve.v
    sctx.close()

    ectx = fctx.enter_context(contextlib.ExitStack())
    gE = ectx.enter_context(nc.sbuf_tensor(f"gE{tag}", [128, 8 * TPC], BF16))
    ps_wb = ectx.enter_context(nc.psum_tensor(f"pwb{tag}", [128, 512], F32))
    for e_i in range(16):
        gp.wait(gp, gp.v)
        gp.wait(te, te.v)
        gp.wait(ve, V_WTR)
        V_WRD = gp.dma(lambda e, e_i=e_i: e.dma_start(
            out=wrow[0:1, :], in_=wTr[e_i:e_i + 1, :]))
        for w2 in range(2):
            te.wait(gp, V_WRD)
            te.wait(ve, max(V_OBR, ve.v))
            te.emit(lambda e, w2=w2: e.matmul(
                ps_wb[:, :], onebr[0:1, :],
                wrow[0:1, w2 * 512:(w2 + 1) * 512], start=True, stop=True))
            ve.wait(te, te.v)
            ve.emit(lambda e, w2=w2: e.tensor_copy(
                wbc[:, w2 * 512:(w2 + 1) * 512], ps_wb[:, :]))
        V_WBC = ve.v
        emit_ffcols(p, nc, ectx, xh, TPC,
                    gwD[e_i * D:(e_i + 1) * D, :], gE, TPC, AF.Silu, 8,
                    tag=f"{tag}G{e_i}")
        te.wait(sc, sc.v)
        emit_ffcols(p, nc, ectx, xh, TPC,
                    uwD[e_i * D:(e_i + 1) * D, :], None, TPC, None, 8,
                    tag=f"{tag}U{e_i}", addto=gE, addW=TPC, multo=True)
        ve.wait(ve, ve.v)
        for fc in range(8):
            ve.wait(ve, ve.v)
            ve.emit(lambda e, fc=fc: e.tensor_mul(
                gE[:, fc * TPC:(fc + 1) * TPC],
                gE[:, fc * TPC:(fc + 1) * TPC], wbc[:, :]))
        V_HS = ve.v
        for eng in (te, gp):
            eng.wait(ve, V_HS)
        emit_ffcols(p, nc, ectx, gE, TPC,
                    dwD[e_i * TPC:(e_i + 1) * TPC, :], None, TPC, None, NCH,
                    ncc=8, tag=f"{tag}D{e_i}", addto=xsb, addW=TPC)
    V_OUT = ve.v
    ectx.close()
    fctx.close()
    return V_OUT




def emit_A1(p, nc, ctx, t, ro=0, PFX="a0"):
    xT = t["XT_A"]
    wqkvT = t["wqkvT"]
    mask = t["maskA"]
    identb = t["identb"]
    onescol = t["onescolA"]
    onesrow = t["onesrow"]
    att = t["att2"]
    dq = t["dq_h"]
    dk = t["dk_h"]
    dv = t["dv_h"]
    gp, te, sc, ve, sy = p.gp, p.te, p.sc, p.ve, p.sy
    sy, gp, ve, sc, te = p.sy, p.gp, p.ve, p.sc, p.te
    ones_c = ctx.enter_context(nc.sbuf_tensor(PFX + "onesc", [128, 2], F32))
    ones_r = ctx.enter_context(nc.sbuf_tensor(PFX + "onesr", [1, 128], F32))
    ident_bf = ctx.enter_context(nc.sbuf_tensor(PFX + "identbf", [128, 128], BF16))
    mask_sb = ctx.enter_context(nc.sbuf_tensor(PFX + "masksb", [128, 128], F32))
    xhat = ctx.enter_context(nc.sbuf_tensor(PFX + "xhat", [128, NCH * TPC], BF16))
    consts = {"ones_col": ones_c[:, 0:1], "ones_row": ones_r,
              "eps": ones_c[:, 1:2]}

    p1 = ctx.enter_context(contextlib.ExitStack())
    xsb = p1.enter_context(nc.sbuf_tensor(PFX + "xsb", [128, NCH * TPC], F32))

    gp.dma(lambda e: e.dma_start(
        out=xsb.rearrange("p (cc t) -> p cc t", cc=NCH),
        in_=xT.rearrange("(cc p) t -> p cc t", p=128)))
    gp.dma(lambda e: e.dma_start(out=ones_c[:, :], in_=onescol))
    gp.dma(lambda e: e.dma_start(out=ones_r[:, :], in_=onesrow))
    gp.dma(lambda e: e.dma_start(out=ident_bf[:, :], in_=identb))
    V_IN = gp.dma(lambda e: e.dma_start(out=mask_sb[:, :], in_=mask))

    V_LN, _ = emit_ln(p, nc, xsb, TPC, consts, xhat, [(gp, V_IN)],
                      tag=PFX, alloc_ctx=p1)
    p1.close()

    p2 = ctx.enter_context(contextlib.ExitStack())
    wbuf = p2.enter_context(nc.sbuf_tensor(PFX + "wbuf", [128, 2 * 512], BF16))
    stage = p2.enter_context(nc.sbuf_tensor(PFX + "stage", [128, 8 * 512], BF16))
    ps_qkv = [p2.enter_context(nc.psum_tensor(f"{PFX}psq{i}", [128, 512], F32))
              for i in range(8)]
    NFG = 12
    GP0, TE0, SC0, SY0 = gp.v, te.v, sc.v, sy.v
    def te_after(fg, cc):
        return TE0 + 8 * (fg * NCH + cc + 1)

    gp.wait(ve, V_LN)
    for fg in range(NFG):
        for cc in range(NCH):
            i = fg * NCH + cc
            if i >= 2:
                fg2, cc2 = divmod(i - 2, NCH)
                gp.wait(te, te_after(fg2, cc2))
            gp.dma(lambda e, fg=fg, cc=cc, i=i: e.dma_start(
                out=wbuf[:, (i % 2) * 512:(i % 2 + 1) * 512],
                in_=wqkvT[cc * 128:(cc + 1) * 128,
                          fg * 512:(fg + 1) * 512]))

    te.wait(ve, V_LN)
    for fg in range(NFG):
        if fg >= 1:
            te.wait(sc, SC0 + 8 * fg)
        for cc in range(NCH):
            i = fg * NCH + cc
            te.wait(gp, GP0 + 16 * min(i + 2, NFG * NCH))
            for tc in range(8):
                te.emit(lambda e, i=i, cc=cc, tc=tc: e.matmul(
                    ps_qkv[tc][:, :],
                    xhat[:, cc * TPC + tc * 128: cc * TPC + (tc + 1) * 128],
                    wbuf[:, (i % 2) * 512:(i % 2 + 1) * 512],
                    start=(cc == 0), stop=(cc == NCH - 1)))
            assert te.v == te_after(fg, cc)

    for fg in range(NFG):
        sc.wait(te, te_after(fg, NCH - 1))
        sc.wait(sy, SY0 + 16 * fg)
        for tc in range(8):
            sc.emit(lambda e, tc=tc: e.activation(
                stage[:, tc * 512:(tc + 1) * 512], ps_qkv[tc][:, :],
                AF.Identity))
        assert sc.v == SC0 + 8 * (fg + 1)

    for fg in range(NFG):
        sy.wait(sc, SC0 + 8 * (fg + 1))
        sy.wait(sy, SY0 + 16 * fg)
        tgt = [dq, dk, dv][fg // 4]
        col = (fg % 4) * 512
        sy.dma(lambda e, tgt=tgt, col=col: e.dma_start(
            out=tgt.ap()[0:TPC, col:col + 512].rearrange(
                "(tc p) f -> p tc f", p=128),
            in_=stage.rearrange("p (tc f) -> p tc f", f=512)))
    V_ST = sy.v
    assert V_ST == SY0 + 16 * NFG
    p2.close()

    p3 = ctx.enter_context(contextlib.ExitStack())
    qTs = p3.enter_context(nc.sbuf_tensor(PFX + "qTs", [128, 16 * TPC], BF16))
    kTs = p3.enter_context(nc.sbuf_tensor(PFX + "kTs", [128, 16 * TPC], BF16))
    vS = p3.enter_context(nc.sbuf_tensor(PFX + "vS", [128, 16 * TPC], BF16))
    ps_sc = p3.enter_context(nc.psum_tensor(PFX + "pssc", [128, 128], F32))
    ps_tr = p3.enter_context(nc.psum_tensor(PFX + "pstr", [128, 128], BF16))
    ps_o = p3.enter_context(nc.psum_tensor(PFX + "pso", [128, 128], F32))
    scf = p3.enter_context(nc.sbuf_tensor(PFX + "scf", [128, 128], F32))
    expb = p3.enter_context(nc.sbuf_tensor(PFX + "expb", [128, 128], BF16))
    expT = p3.enter_context(nc.sbuf_tensor(PFX + "expT", [128, 128], BF16))
    den = p3.enter_context(nc.sbuf_tensor(PFX + "den", [128, 2], F32))
    ostg = p3.enter_context(nc.sbuf_tensor(PFX + "ostg", [128, 16 * 128], BF16))
    NG = 16 * TPC // 128

    sy.wait(sy, V_ST)
    sy.dma(lambda e: e.dma_start_transpose(
        qTs[:, :], dq.ap().rearrange("t (i d) -> (t i) d", d=128)))
    sy.dma(lambda e: e.dma_start_transpose(
        kTs[:, :], dk.ap().rearrange("t (i d) -> (t i) d", d=128)))
    for tp in range(8):
        sy.dma(lambda e, tp=tp: e.dma_start(
            out=vS[tp * 16:(tp + 1) * 16, :].rearrange(
                "i (g d) -> i g d", d=128),
            in_=bass.AP(dv, tp * D, [[128, 16], [8 * D, 128], [1, 128]])))
    V_TR = sy.v

    TE1, VE1, SC1, SY1 = te.v, ve.v, sc.v, sy.v
    te.wait(sy, V_TR)
    for g in range(NG):
        if g >= 1:
            te.wait(ve, VE1 + 3 * g)
        te.emit(lambda e, g=g: e.matmul(
            ps_sc[:, :], qTs[:, g * 128:(g + 1) * 128],
            kTs[:, g * 128:(g + 1) * 128], start=True, stop=True))
        te.wait(sc, SC1 + 2 * g + 1)
        te.emit(lambda e: e.transpose(ps_tr[:, :], expb[:, :],
                                      ident_bf[:, :]))
        te.wait(ve, VE1 + 3 * g + 2)
        te.emit(lambda e, g=g: e.matmul(
            ps_o[:, :], expT[:, :], vS[:, g * 128:(g + 1) * 128],
            start=True, stop=True))

    for g in range(NG):
        ve.wait(te, TE1 + 3 * g + 1)
        if g >= 1:
            ve.wait(sc, SC1 + 2 * g)
        ve.emit(lambda e: e.scalar_tensor_tensor(
            scf[:, :], ps_sc[:, :], RS, mask_sb[:, :], ALU.mult, ALU.add))
        ve.wait(te, TE1 + 3 * g + 2)
        ve.emit(lambda e: e.tensor_copy(expT[:, :], ps_tr[:, :]))
        ve.wait(ve, VE1 + 3 * g + 2)
        ve.wait(sc, SC1 + 2 * g + 1)
        ve.emit(lambda e: e.reciprocal(den[:, 1:2], den[:, 0:1]))

    for g in range(NG):
        sc.wait(ve, VE1 + 3 * g + 1)
        if g >= 1:
            sc.wait(te, TE1 + 3 * g - 1)
            sc.wait(ve, VE1 + 3 * g)
        sc.emit(lambda e: e.activation(expb[:, :], scf[:, :], AF.Exp,
                                       accum_out=den[:, 0:1]))
        sc.wait(te, TE1 + 3 * g + 3)
        sc.wait(ve, VE1 + 3 * g + 3)
        if g >= 16:
            sc.wait(sy, SY1 + 16 * (g // 16))
        sc.emit(lambda e, g=g: e.activation(
            ostg[:, (g % 16) * 128:(g % 16 + 1) * 128], ps_o[:, :],
            AF.Identity, scale=den[:, 1:2]))

    for blk in range(NG // 16):
        sy.wait(sc, SC1 + 2 * 16 * (blk + 1))
        sy.wait(sy, V_TR + 16 * blk)
        sy.dma(lambda e, blk=blk: e.dma_start(
            out=att[(ro + blk) * 2048:(ro + blk + 1) * 2048, :].rearrange(
                "(g p) d -> p g d", p=128),
            in_=ostg.rearrange("p (g d) -> p g d", d=128)))
    sy.wait(sy, sy.v)
    p3.close()



def emit_F1(p, nc, ctx, t):
    xT = t["xTme"]
    rowsT = t["rowsT"]
    wprojT = t["wprojT"]
    w1T = t["w1T"]
    w3T = t["w3T"]
    w2T = t["w2T"]
    wrT = t["wrT"]
    onescol = t["onescol"]
    onesrow = t["onesrow"]
    x1T = t["x1T"]
    part = t["part1"]
    gp, te, sc, ve, sy = p.gp, p.te, p.sc, p.ve, p.sy
    gp, te, sc, ve, sy = p.gp, p.te, p.sc, p.ve, p.sy
    ones_c = ctx.enter_context(nc.sbuf_tensor("f1onesc", [128, 4], F32))
    ones_r = ctx.enter_context(nc.sbuf_tensor("f1onesr", [1, 128], F32))
    xsb = ctx.enter_context(nc.sbuf_tensor("f1xsb", [128, NCH * TPC], F32))
    consts = {"ones_col": ones_c[:, 0:1], "ones_row": ones_r,
              "eps": ones_c[:, 1:2], "one1": ones_c,
              "ones_bf": None}

    gp.dma(lambda e: e.dma_start(out=ones_c[:, :], in_=onescol))
    gp.dma(lambda e: e.dma_start(out=ones_r[:, :], in_=onesrow))
    gp.dma(lambda e: e.dma_start(
        out=xsb.rearrange("p (cc t) -> p cc t", cc=NCH),
        in_=xT.rearrange("(cc p) t -> p cc t", p=128)))
    V_IN = gp.dma(lambda e: e.dma_start(
        out=rsb.rearrange("p (cc t) -> p cc t", cc=NCH),
        in_=rowsT.rearrange("(cc p) t -> p cc t", p=128)))

    prctx = ctx.enter_context(contextlib.ExitStack())
    rsb = prctx.enter_context(nc.sbuf_tensor("f1rsb", [128, NCH * TPC],
                                             BF16))
    wcol = prctx.enter_context(nc.sbuf_tensor("f1wcp", [128, 2 * NCH * 128],
                                              BF16))
    psp = [prctx.enter_context(nc.psum_tensor(f"f1psp{i}", [128, 512], F32))
           for i in range(8)]
    GP0, TE0, VE0 = gp.v, te.v, ve.v

    def te_after(fc):
        return TE0 + (fc + 1) * 2 * NCH

    gp.wait(gp, V_IN)
    for fc in range(NCH):
        if fc >= 2:
            gp.wait(te, te_after(fc - 2))
        gp.dma(lambda e, fc=fc: e.dma_start(
            out=wcol.rearrange("p (s cc f) -> p s cc f", s=2, f=128)
                [:, fc % 2],
            in_=wprojT[:, fc * 128:(fc + 1) * 128].rearrange(
                "(cc p) f -> p cc f", p=128)))
    te.wait(gp, V_IN)
    for fc in range(NCH):
        te.wait(gp, GP0 + 16 * min(fc + 2, NCH))
        if fc >= 4:
            te.wait(ve, VE0 + 2 * (fc - 3))
        for tt in range(2):
            for cc in range(NCH):
                te.emit(lambda e, fc=fc, tt=tt, cc=cc: e.matmul(
                    psp[(fc % 4) * 2 + tt][:, :],
                    wcol[:, (fc % 2) * NCH * 128 + cc * 128:
                         (fc % 2) * NCH * 128 + (cc + 1) * 128],
                    rsb[:, cc * TPC + tt * 512: cc * TPC + (tt + 1) * 512],
                    start=(cc == 0), stop=(cc == NCH - 1)))
    for fc in range(NCH):
        ve.wait(te, TE0 + fc * 2 * NCH + NCH)
        ve.emit(lambda e, fc=fc: e.tensor_add(
            xsb[:, fc * TPC: fc * TPC + 512], psp[(fc % 4) * 2][:, :],
            xsb[:, fc * TPC: fc * TPC + 512]))
        ve.wait(te, TE0 + (fc + 1) * 2 * NCH)
        ve.emit(lambda e, fc=fc: e.tensor_add(
            xsb[:, fc * TPC + 512: fc * TPC + 1024],
            psp[(fc % 4) * 2 + 1][:, :],
            xsb[:, fc * TPC + 512: fc * TPC + 1024]))
    V_X1 = ve.v
    assert V_X1 == VE0 + 2 * NCH
    prctx.close()

    sy.wait(ve, V_X1)
    V_X1ST = sy.dma(lambda e: e.dma_start(
        out=x1T.rearrange("(cc p) t -> p cc t", p=128),
        in_=xsb.rearrange("p (cc t) -> p cc t", cc=NCH)))

    emit_ffn0(p, nc, ctx, xsb, consts, w1T, w3T, w2T, wrT, part,
              [(ve, V_X1), (gp, V_IN)], tag="n")
    sy.wait(sy, sy.v)


def emit_B1(p, nc, ctx, t):
    xT = t["x1T"]
    scat = t["scat8a"]
    haloT = t["haloT"]
    wqTp = t["wqTp"]
    wkTp = t["wkTp"]
    wvT = t["wvT"]
    woD = t["woD"]
    tabQ = t["tabQ"]
    tabK = t["tabK"]
    maskB = t["maskB"]
    identb = t["identb"]
    w1T = t["w1T"]
    w3T = t["w3T"]
    w2T = t["w2T"]
    wrT = t["wrT"]
    onescol = t["onescol"]
    onesrow = t["onesrow"]
    x3T = t["x3T"]
    part = t["part2"]
    dxx = t["dxx"]
    datt = t["datt"]
    KL = 1280
    gp, te, sc, ve, sy = p.gp, p.te, p.sc, p.ve, p.sy
    gp, te, sc, ve, sy = p.gp, p.te, p.sc, p.ve, p.sy
    ones_c = ctx.enter_context(nc.sbuf_tensor("b1onesc", [128, 4], F32))
    ones_r = ctx.enter_context(nc.sbuf_tensor("b1onesr", [1, 128], F32))
    ones_b = ctx.enter_context(nc.sbuf_tensor("b1onesb", [128, 1], BF16))
    idb = ctx.enter_context(nc.sbuf_tensor("b1idb", [128, 128], BF16))
    consts = {"ones_col": ones_c[:, 0:1], "ones_row": ones_r,
              "eps": ones_c[:, 1:2], "one1": ones_c}
    gp.dma(lambda e: e.dma_start(out=ones_c[:, :], in_=onescol))
    gp.dma(lambda e: e.dma_start(out=ones_r[:, :], in_=onesrow))
    V_C = gp.dma(lambda e: e.dma_start(out=idb[:, :], in_=identb))
    ve.wait(gp, V_C)
    V_OB = ve.emit(lambda e: e.tensor_copy(ones_b[:, :], ones_c[:, 0:1]))

    ph1 = ctx.enter_context(contextlib.ExitStack())
    xhB = ph1.enter_context(nc.sbuf_tensor("b1xhB", [128, NCH * KL], BF16))
    ph1b = ph1.enter_context(contextlib.ExitStack())
    xsb = ph1b.enter_context(nc.sbuf_tensor("b1xsb", [128, NCH * TPC], F32))
    xhalo = ph1b.enter_context(nc.sbuf_tensor("b1xhalo", [128, NCH * 256],
                                              F32))
    ssb = ph1b.enter_context(nc.sbuf_tensor("b1ssb", [128, NCH * 8], F32))
    sred = ph1b.enter_context(nc.sbuf_tensor("b1sred", [128, 8], F32))
    gp.wait(gp, gp.v)
    gp.dma(lambda e: e.dma_start(
        out=xsb.rearrange("p (cc t) -> p cc t", cc=NCH),
        in_=xT.rearrange("(cc p) t -> p cc t", p=128)))
    gp.dma(lambda e: e.dma_start(
        out=xhalo.rearrange("p (cc t) -> p cc t", cc=NCH),
        in_=haloT.rearrange("(cc p) t -> p cc t", p=128)))
    V_XIN = gp.dma(lambda e: e.dma_start(
        out=ssb.rearrange("p (cc t) -> p cc t", cc=NCH),
        in_=scat.rearrange("(cc p) t -> p cc t", p=128)))
    ve.wait(gp, V_XIN)
    V_SC = None
    for cc in range(NCH):
        ve.wait(ve, ve.v)
        ve.emit(lambda e, cc=cc: e.tensor_copy(
            sred[:, :],
            ssb.rearrange("p (cc e) -> p cc e", cc=NCH)[:, cc]))
        ve.wait(ve, ve.v)
        V_SC = ve.emit(lambda e, cc=cc: e.tensor_add(
            xsb[:, cc * TPC: cc * TPC + 8], sred[:, :],
            xsb[:, cc * TPC: cc * TPC + 8]))
    sy.wait(ve, V_SC)
    V_X2ST = sy.dma(lambda e: e.dma_start(
        out=dxx.rearrange("(cc p) t -> p cc t", p=128),
        in_=xsb.rearrange("p (cc t) -> p cc t", cc=NCH)))
    V_LN, _ = emit_ln(p, nc, xsb, TPC, consts, xhB,
                      [(ve, V_SC), (gp, V_XIN)], tag="bo",
                      alloc_ctx=ph1b, outW=KL, out_off=256)
    V_LNH, _ = emit_ln(p, nc, xhalo, 256, consts, xhB,
                       [(gp, V_XIN)], tag="bh",
                       alloc_ctx=ph1b, outW=KL, out_off=0)
    for eng in (te, sc, gp, sy):
        eng.wait(ve, V_LN)
        eng.wait(ve, V_LNH)
    sy.wait(sy, V_X2ST)
    for eng in (te, sc, gp, ve):
        eng.wait(sy, V_X2ST)
    ph1b.close()

    att_r = ctx.enter_context(contextlib.ExitStack())
    ySB = att_r.enter_context(nc.sbuf_tensor("b1ySB", [128, 16 * TPC], BF16,
                                             side="right"))
    att_r2 = att_r.enter_context(contextlib.ExitStack())
    qT = att_r2.enter_context(nc.sbuf_tensor("b1qT", [128, 16 * TPC], BF16,
                                             side="right"))
    kT = att_r2.enter_context(nc.sbuf_tensor("b1kT", [128, 16 * KL], BF16,
                                             side="right"))
    vS = att_r2.enter_context(nc.sbuf_tensor("b1vS", [128, 160 * 128], BF16,
                                             side="right"))
    msk = att_r2.enter_context(nc.sbuf_tensor("b1msk", [128, 16 * 256], BF16,
                                              side="right"))
    gp.wait(gp, gp.v)
    V_MSK = gp.dma(lambda e: e.dma_start(out=msk[:, :], in_=maskB))

    ph2 = ctx.enter_context(contextlib.ExitStack())
    emit_ffcols(p, nc, ph2, xhB, KL, wqTp, qT, TPC, AF.Identity, NCH,
                tag="bq", src_off=256, L=TPC)
    emit_ffcols(p, nc, ph2, xhB, KL, wkTp, kT, KL, AF.Identity, NCH,
                tag="bk", L=KL)
    for h in range(16):
        hctx = ph2.enter_context(contextlib.ExitStack())
        vTh = hctx.enter_context(nc.sbuf_tensor(f"b1vTh{h}", [128, KL],
                                                BF16))
        ps_vt = hctx.enter_context(nc.psum_tensor(f"b1pvt{h}", [128, 128],
                                                  BF16))
        emit_ffcols(p, nc, hctx, xhB, KL,
                    wvT[:, h * 128:(h + 1) * 128], vTh, KL,
                    AF.Identity, 1, tag=f"bv{h}", L=KL)
        te.wait(sc, sc.v)
        te.wait(ve, V_OB)
        for tc in range(10):
            te.wait(ve, ve.v)
            te.emit(lambda e, tc=tc, ps_vt=ps_vt, vTh=vTh: e.transpose(
                ps_vt[:, :], vTh[:, tc * 128:(tc + 1) * 128], idb[:, :]))
            ve.wait(te, te.v)
            ve.emit(lambda e, h=h, tc=tc, ps_vt=ps_vt: e.tensor_copy(
                vS[:, (tc * 16 + h) * 128:(tc * 16 + h) * 128 + 128],
                ps_vt[:, :]))
        te.wait(ve, ve.v)
        hctx.close()
    V_VS = ve.v
    V_TEP2 = te.v
    ph2.close()
    ph1.close()

    ph3 = ctx.enter_context(contextlib.ExitStack())
    tq = ph3.enter_context(nc.sbuf_tensor("b1tq", [128, 2 * TPC], F32))
    tk = ph3.enter_context(nc.sbuf_tensor("b1tk", [128, 2 * KL], F32))
    scr = ph3.enter_context(nc.sbuf_tensor("b1scr", [64, 3 * KL], F32))
    gp.wait(gp, gp.v)
    gp.wait(te, V_TEP2)
    gp.wait(ve, V_VS)
    gp.wait(sc, sc.v)
    gp.dma(lambda e: e.dma_start(out=tq[:, :], in_=tabQ))
    V_TAB = gp.dma(lambda e: e.dma_start(out=tk[:, :], in_=tabK))
    ve.wait(gp, V_TAB)
    ve.wait(sc, sc.v)

    def rope(buf, tab, W):
        cos0 = tab[0:64, 0:W]
        cos1 = tab[64:128, 0:W]
        sin0 = tab[0:64, W:2 * W]
        sin1 = tab[64:128, W:2 * W]
        for h in range(16):
            qe = buf[:, h * W:(h + 1) * W][0:64, :]
            qo = buf[:, h * W:(h + 1) * W][64:128, :]
            s1 = scr[0:64, 0:W]
            s2 = scr[0:64, KL:KL + W]
            s3 = scr[0:64, 2 * KL:2 * KL + W]
            ve.wait(ve, ve.v)
            ve.emit(lambda e, qe=qe, c=cos0, s1=s1: e.tensor_mul(
                s1, qe, c))
            ve.emit(lambda e, qo=qo, sn=sin1, s2=s2: e.tensor_mul(
                s2, qo, sn))
            ve.emit(lambda e, qe=qe, sn=sin0, s3=s3: e.tensor_mul(
                s3, qe, sn))
            ve.wait(ve, ve.v)
            ve.emit(lambda e, qe=qe, s1=s1, s2=s2: e.tensor_sub(
                qe, s1, s2))
            ve.wait(ve, ve.v)
            ve.emit(lambda e, qo=qo, c=cos1, s2=s2: e.tensor_mul(
                s2, qo, c))
            ve.wait(ve, ve.v)
            ve.emit(lambda e, qo=qo, s2=s2, s3=s3: e.tensor_add(
                qo, s3, s2))
    rope(qT, tq, TPC)
    rope(kT, tk, KL)
    V_ROPE = ve.v
    ph3.close()

    ph4 = ctx.enter_context(contextlib.ExitStack())
    exB = ph4.enter_context(nc.sbuf_tensor("b1exB", [128, 4 * 256], BF16))
    rbc = ph4.enter_context(nc.sbuf_tensor("b1rbc", [128, 256], F32))
    rro = ph4.enter_context(nc.sbuf_tensor("b1rro", [1, 256], F32))
    ps_s = [ph4.enter_context(nc.psum_tensor(f"b1psbs{i}", [128, 256], F32))
            for i in range(2)]
    ps_d = ph4.enter_context(nc.psum_tensor("b1psbd", [1, 256], F32))
    ps_y = ph4.enter_context(nc.psum_tensor("b1psby", [128, 256], F32))
    ps_rb = ph4.enter_context(nc.psum_tensor("b1psrb", [128, 256], F32))
    te.wait(ve, V_ROPE)
    te.wait(gp, V_MSK)
    for ci in range(4):
        for h in range(16):
            VE_R = ve.v
            for kc in range(4):
                if kc >= 2:
                    te.wait(ve, VE_R + kc - 1)
                elif kc == 0:
                    te.wait(ve, ve.v)
                te.emit(lambda e, ci=ci, h=h, kc=kc: e.matmul(
                    ps_s[kc % 2][:, :],
                    kT[:, h * KL + ci * 256 + kc * 128:
                       h * KL + ci * 256 + (kc + 1) * 128],
                    qT[:, h * TPC + ci * 256: h * TPC + (ci + 1) * 256],
                    start=True, stop=True))
                ve.wait(te, te.v)
                ve.emit(lambda e, ci=ci, kc=kc: e.scalar_tensor_tensor(
                    exB[:, kc * 256:(kc + 1) * 256], ps_s[kc % 2][:, :], RS,
                    msk[:, (ci * 4 + kc) * 256:(ci * 4 + kc + 1) * 256],
                    ALU.mult, ALU.add))
                sc.wait(ve, ve.v)
                sc.emit(lambda e, kc=kc: e.activation(
                    exB[:, kc * 256:(kc + 1) * 256],
                    exB[:, kc * 256:(kc + 1) * 256], AF.Exp))
            te.wait(sc, sc.v)
            for kc in range(4):
                te.emit(lambda e, kc=kc: e.matmul(
                    ps_d[:, :], ones_b[:, 0:1],
                    exB[:, kc * 256:(kc + 1) * 256],
                    start=(kc == 0), stop=(kc == 3)))
            for kc in range(4):
                te.emit(lambda e, ci=ci, h=h, kc=kc: e.matmul(
                    ps_y[:, :],
                    vS[:, ((ci * 2 + kc) * 16 + h) * 128:
                       ((ci * 2 + kc) * 16 + h) * 128 + 128],
                    exB[:, kc * 256:(kc + 1) * 256],
                    start=(kc == 0), stop=(kc == 3)))
            V_YMM = te.v
            ve.wait(te, V_YMM)
            ve.emit(lambda e: e.reciprocal(rro[0:1, :], ps_d[:, :]))
            te.wait(ve, ve.v)
            V_RB = te.emit(lambda e: e.matmul(
                ps_rb[:, :], ones_r[0:1, :], rro[0:1, :],
                start=True, stop=True))
            ve.wait(te, V_RB)
            ve.emit(lambda e: e.tensor_copy(rbc[:, :], ps_rb[:, :]))
            ve.wait(ve, ve.v)
            ve.emit(lambda e, ci=ci, h=h: e.tensor_mul(
                ySB[:, h * TPC + ci * 256: h * TPC + (ci + 1) * 256],
                ps_y[:, :], rbc[:, :]))
    V_ATT = ve.v
    ph4.close()

    ph5a = ctx.enter_context(contextlib.ExitStack())
    atout = ph5a.enter_context(nc.sbuf_tensor("b1atout", [128, NCH * TPC],
                                              BF16))
    for eng in (te, sc, gp):
        eng.wait(ve, V_ATT)
    emit_ffcols(p, nc, ph5a, ySB, TPC, woD, atout, TPC, AF.Identity, NCH,
                tag="bw")
    V_WOS = sc.v
    sy.wait(sc, V_WOS)
    sy.wait(sy, sy.v)
    V_ATST = sy.dma(lambda e: e.dma_start(
        out=datt.rearrange("(cc p) t -> p cc t", p=128),
        in_=atout.rearrange("p (cc t) -> p cc t", cc=NCH)))
    ph5a.close()
    att_r2.close()
    att_r.close()

    ph5 = ctx.enter_context(contextlib.ExitStack())
    xsb2 = ph5.enter_context(nc.sbuf_tensor("b1xsb2", [128, NCH * TPC], F32))
    ph5c = ph5.enter_context(contextlib.ExitStack())
    attin = ph5c.enter_context(nc.sbuf_tensor("b1attin", [128, NCH * TPC],
                                              BF16))
    gp.wait(gp, gp.v)
    gp.wait(sy, V_ATST)
    gp.wait(sc, sc.v)
    gp.wait(te, te.v)
    gp.dma(lambda e: e.dma_start(
        out=xsb2.rearrange("p (cc t) -> p cc t", cc=NCH),
        in_=dxx.rearrange("(cc p) t -> p cc t", p=128)))
    V_RL = gp.dma(lambda e: e.dma_start(
        out=attin.rearrange("p (cc t) -> p cc t", cc=NCH),
        in_=datt.rearrange("(cc p) t -> p cc t", p=128)))
    ve.wait(gp, V_RL)
    V_X3 = None
    for cc in range(NCH):
        ve.wait(ve, ve.v)
        V_X3 = ve.emit(lambda e, cc=cc: e.tensor_add(
            xsb2[:, cc * TPC:(cc + 1) * TPC],
            attin[:, cc * TPC:(cc + 1) * TPC],
            xsb2[:, cc * TPC:(cc + 1) * TPC]))
    ph5c.close()
    sy.wait(ve, V_X3)
    sy.dma(lambda e: e.dma_start(
        out=x3T.rearrange("(cc p) t -> p cc t", p=128),
        in_=xsb2.rearrange("p (cc t) -> p cc t", cc=NCH)))
    emit_ffn0(p, nc, ph5, xsb2, consts, w1T, w3T, w2T, wrT, part,
              [(ve, V_X3)], tag="c")
    sy.wait(sy, sy.v)
    ph5.close()


def emit_M1(p, nc, ctx, t):
    xT = t["x3T"]
    scat = t["scat8b"]
    onescol = t["onescol"]
    onesrow = t["onesrow"]
    x4T = t["x4T"]
    xhT = t["xhT"]
    gp, te, sc, ve, sy = p.gp, p.te, p.sc, p.ve, p.sy
    gp, te, sc, ve, sy = p.gp, p.te, p.sc, p.ve, p.sy
    ones_c = ctx.enter_context(nc.sbuf_tensor("m1onesc", [128, 4], F32))
    ones_r = ctx.enter_context(nc.sbuf_tensor("m1onesr", [1, 128], F32))
    consts = {"ones_col": ones_c[:, 0:1], "ones_row": ones_r,
              "eps": ones_c[:, 1:2], "one1": ones_c}
    xsb = ctx.enter_context(nc.sbuf_tensor("m1xsb", [128, NCH * TPC], F32))
    xh = ctx.enter_context(nc.sbuf_tensor("m1xh", [128, NCH * TPC], F32))
    ssb = ctx.enter_context(nc.sbuf_tensor("m1ssb", [128, NCH * 8], F32))
    sred = ctx.enter_context(nc.sbuf_tensor("m1sred", [128, 8], F32))
    gp.dma(lambda e: e.dma_start(out=ones_c[:, :], in_=onescol))
    gp.dma(lambda e: e.dma_start(out=ones_r[:, :], in_=onesrow))
    gp.dma(lambda e: e.dma_start(
        out=xsb.rearrange("p (cc t) -> p cc t", cc=NCH),
        in_=xT.rearrange("(cc p) t -> p cc t", p=128)))
    V_XIN = gp.dma(lambda e: e.dma_start(
        out=ssb.rearrange("p (cc t) -> p cc t", cc=NCH),
        in_=scat.rearrange("(cc p) t -> p cc t", p=128)))
    ve.wait(gp, V_XIN)
    V_SC = None
    for cc in range(NCH):
        ve.wait(ve, ve.v)
        ve.emit(lambda e, cc=cc: e.tensor_copy(
            sred[:, :],
            ssb.rearrange("p (cc e) -> p cc e", cc=NCH)[:, cc]))
        ve.wait(ve, ve.v)
        V_SC = ve.emit(lambda e, cc=cc: e.tensor_add(
            xsb[:, cc * TPC: cc * TPC + 8], sred[:, :],
            xsb[:, cc * TPC: cc * TPC + 8]))
    sy.wait(ve, V_SC)
    sy.dma(lambda e: e.dma_start(
        out=x4T.rearrange("(cc p) t -> p cc t", p=128),
        in_=xsb.rearrange("p (cc t) -> p cc t", cc=NCH)))
    V_LN, _ = emit_ln(p, nc, xsb, TPC, consts, xh,
                      [(ve, V_SC), (gp, V_XIN)], tag="m", alloc_ctx=ctx)
    sy.wait(ve, V_LN)
    sy.dma(lambda e: e.dma_start(
        out=xhT.rearrange("(cc p) t -> p cc t", p=128),
        in_=xh.rearrange("p (cc t) -> p cc t", cc=NCH)))
    sy.wait(sy, sy.v)


def emit_M2(p, nc, ctx, t):
    xh = t["xhP"]
    cw = t["cw"]
    cb = t["cbI"]
    xpT = t["xpT"]
    dtwT = t["dtwT"]
    dtb = t["dtb"]
    Ain = t["Ain"]
    Dvin = t["Dvin"]
    onescol = t["onescol"]
    onesrow = t["onesrow"]
    yout = t["yout"]
    TH = 1024
    gp, te, sc, ve, sy = p.gp, p.te, p.sc, p.ve, p.sy
    gp, te, sc, ve, sy = p.gp, p.te, p.sc, p.ve, p.sy
    ones_c = ctx.enter_context(nc.sbuf_tensor("m2onesc", [128, 4], F32))
    ones_r = ctx.enter_context(nc.sbuf_tensor("m2onesr", [1, 128], F32))
    bst = ctx.enter_context(nc.sbuf_tensor("m2bst", [1, 2 * TH], F32))
    cwsb = ctx.enter_context(nc.sbuf_tensor("m2cwsb", [128, NCH * 4], F32))
    cbsb = ctx.enter_context(nc.sbuf_tensor("m2cbsb", [128, NCH], F32))
    xps = ctx.enter_context(nc.sbuf_tensor("m2xps", [128, NCH * 96], BF16))
    dtw = ctx.enter_context(nc.sbuf_tensor("m2dtw", [64, TPC], BF16))
    dtbs = ctx.enter_context(nc.sbuf_tensor("m2dtbs", [128, 8], F32))
    Acol = ctx.enter_context(nc.sbuf_tensor("m2Acol", [128, 8 * 16], F32))
    Dcol = ctx.enter_context(nc.sbuf_tensor("m2Dcol", [128, 8], F32))
    carry = ctx.enter_context(nc.sbuf_tensor("m2carry", [128, 128], F32))
    xcN = ctx.enter_context(nc.sbuf_tensor("m2xcN", [128, 8 * TH], F32))
    dltN = ctx.enter_context(nc.sbuf_tensor("m2dltN", [128, 8 * TH], F32))
    duN = ctx.enter_context(nc.sbuf_tensor("m2duN", [128, 8 * TH], F32))
    yac = ctx.enter_context(nc.sbuf_tensor("m2yac", [128, 8 * TH], F32))
    Bbc = ctx.enter_context(nc.sbuf_tensor("m2Bbc", [128, TH], F32))
    Cbc = ctx.enter_context(nc.sbuf_tensor("m2Cbc", [128, TH], F32))
    htl = ctx.enter_context(nc.sbuf_tensor("m2htl", [128, TH], F32))
    ttl = ctx.enter_context(nc.sbuf_tensor("m2ttl", [128, TH], F32))
    atl = ctx.enter_context(nc.sbuf_tensor("m2atl", [128, TH], F32))
    atl2 = ctx.enter_context(nc.sbuf_tensor("m2atl2", [128, 2 * TH], F32))
    t2b = ctx.enter_context(nc.sbuf_tensor("m2t2b", [128, 2 * TH], F32))
    xdb = ctx.enter_context(nc.sbuf_tensor("m2xdb", [96, TH], F32))
    xdbb = ctx.enter_context(nc.sbuf_tensor("m2xdbb", [64, TH], BF16))
    pad = ctx.enter_context(nc.sbuf_tensor("m2pad", [128, TH + 3], F32))
    xct = ctx.enter_context(nc.sbuf_tensor("m2xct", [128, TH], F32))
    xcb = ctx.enter_context(nc.sbuf_tensor("m2xcb", [128, TH], BF16))
    ps_xd = [ctx.enter_context(nc.psum_tensor(f"m2pxd{i}", [96, 512], F32))
             for i in range(2)]
    ps_dl = [ctx.enter_context(nc.psum_tensor(f"m2pdl{i}", [128, 512], F32))
             for i in range(2)]
    ps_bc = [ctx.enter_context(nc.psum_tensor(f"m2pbc{i}", [128, 512], F32))
             for i in range(4)]
    spl = ctx.enter_context(nc.sbuf_tensor("m2spl", [128, 2 * 512], F32))

    gp.dma(lambda e: e.dma_start(out=ones_c[:, :], in_=onescol))
    gp.dma(lambda e: e.dma_start(out=ones_r[:, :], in_=onesrow))
    gp.dma(lambda e: e.dma_start(
        out=cwsb.rearrange("p (cc k) -> p cc k", k=4),
        in_=cw.rearrange("(cc p) k -> p cc k", p=128)))
    gp.dma(lambda e: e.dma_start(out=cbsb[:, :], in_=cb))
    gp.dma(lambda e: e.dma_start(
        out=xps.rearrange("p (cc f) -> p cc f", f=96),
        in_=xpT.rearrange("(cc p) f -> p cc f", p=128)))
    gp.dma(lambda e: e.dma_start(out=dtw[:, :], in_=dtwT))
    gp.dma(lambda e: e.dma_start(out=dtbs[:, :], in_=dtb))
    gp.dma(lambda e: e.dma_start(out=Acol[:, :], in_=Ain))
    V_W = gp.dma(lambda e: e.dma_start(out=Dcol[:, :], in_=Dvin))
    ve.wait(gp, V_W)
    V_CZ = ve.emit(lambda e: e.memset(carry[:, :], 0.0))
    sc.wait(gp, V_W)
    V_AE = sc.emit(lambda e: e.activation(Acol[:, :], Acol[:, :], AF.Exp))
    ve.wait(sc, V_AE)
    V_AN = ve.emit(lambda e: e.tensor_scalar(Acol[:, :], Acol[:, :], -1.0,
                                             None, ALU.mult))
    sc.wait(ve, V_AN)

    for th in range(2):
        GP_TH = gp.v
        for cc in range(NCH):
            tgt = xcN[:, cc * TH:(cc + 1) * TH] if cc < 8 else xct[:, :]
            gp.wait(ve, ve.v)
            gp.wait(sc, sc.v)
            gp.wait(gp, gp.v)
            if th == 0:
                ve.wait(gp, gp.v)
                ve.emit(lambda e: e.memset(pad[:, 0:3], 0.0))
                V_PD = gp.dma(lambda e, cc=cc: e.dma_start(
                    out=pad[:, 3:3 + TH],
                    in_=xh[cc * 128:(cc + 1) * 128, 0:TH]))
            else:
                V_PD = gp.dma(lambda e, cc=cc: e.dma_start(
                    out=pad[:, 0:3 + TH],
                    in_=xh[cc * 128:(cc + 1) * 128, TH - 3:T]))
            ve.wait(gp, V_PD)
            ve.wait(ve, ve.v)
            ve.wait(sc, sc.v)
            ve.wait(te, te.v)
            ve.emit(lambda e, cc=cc: e.tensor_scalar(
                ttl[:, :], pad[:, 0:TH], cwsb[:, 4 * cc:4 * cc + 1], None,
                ALU.mult))
            for k in range(1, 4):
                ve.wait(ve, ve.v)
                ve.emit(lambda e, cc=cc, k=k: e.scalar_tensor_tensor(
                    ttl[:, :], pad[:, k:k + TH],
                    cwsb[:, 4 * cc + k:4 * cc + k + 1],
                    ttl[:, :], ALU.mult, ALU.add))
            sc.wait(ve, ve.v)
            sc.wait(te, te.v)
            V_SIL = sc.emit(lambda e, cc=cc, tgt=tgt: e.activation(
                tgt, ttl[:, :], AF.Silu, bias=cbsb[:, cc:cc + 1]))
            sc.wait(sc, V_SIL)
            V_XCB = sc.emit(lambda e, cc=cc, tgt=tgt: e.activation(
                xcb[:, :], tgt, AF.Identity))
            te.wait(sc, V_XCB)
            for w2 in range(2):
                te.emit(lambda e, cc=cc, w2=w2: e.matmul(
                    ps_xd[w2][:, :], xps[:, cc * 96:(cc + 1) * 96],
                    xcb[:, w2 * 512:(w2 + 1) * 512],
                    start=(cc == 0), stop=(cc == NCH - 1)))
        V_XD = te.v
        ve.wait(te, V_XD)
        for w2 in range(2):
            ve.wait(ve, ve.v)
            ve.emit(lambda e, w2=w2: e.tensor_copy(
                xdb[:, w2 * 512:(w2 + 1) * 512], ps_xd[w2][:, :]))
        V_XDB = ve.v
        ve.wait(ve, V_XDB)
        V_XBB = ve.emit(lambda e: e.tensor_copy(xdbb[:, :], xdb[0:64, :]))
        te.wait(ve, V_XBB)
        for ec in range(8):
            if ec >= 1:
                te.wait(sc, sc.v)
            for w2 in range(2):
                te.emit(lambda e, ec=ec, w2=w2: e.matmul(
                    ps_dl[w2][:, :], dtw[:, ec * 128:(ec + 1) * 128],
                    xdbb[:, w2 * 512:(w2 + 1) * 512],
                    start=True, stop=True))
            sc.wait(te, te.v)
            for w2 in range(2):
                sc.wait(sc, sc.v)
                sc.wait(ve, ve.v)
                sc.emit(lambda e, ec=ec, w2=w2: e.activation(
                    spl[:, 0:512], ps_dl[w2][:, :], AF.Abs,
                    bias=dtbs[:, ec:ec + 1]))
                sc.wait(sc, sc.v)
                sc.emit(lambda e: e.activation(
                    spl[:, 0:512], spl[:, 0:512], AF.Exp, scale=-1.0))
                sc.wait(sc, sc.v)
                sc.emit(lambda e: e.activation(
                    spl[:, 0:512], spl[:, 0:512], AF.Ln,
                    bias=ones_c[:, 0:1]))
                sc.wait(sc, sc.v)
                sc.emit(lambda e, ec=ec, w2=w2: e.activation(
                    spl[:, 512:1024], ps_dl[w2][:, :], AF.Relu,
                    bias=dtbs[:, ec:ec + 1]))
                ve.wait(sc, sc.v)
                ve.wait(ve, ve.v)
                ve.emit(lambda e, ec=ec, w2=w2: e.tensor_add(
                    dltN[:, ec * TH + w2 * 512: ec * TH + (w2 + 1) * 512],
                    spl[:, 0:512], spl[:, 512:1024]))
        V_DLT = ve.v
        for ec in range(8):
            ve.wait(ve, ve.v)
            ve.emit(lambda e, ec=ec: e.tensor_mul(
                duN[:, ec * TH:(ec + 1) * TH],
                dltN[:, ec * TH:(ec + 1) * TH],
                xcN[:, ec * TH:(ec + 1) * TH]))
        V_DU = ve.v

        V_scb = [0, 0]
        V_sclast = 0
        V_t2gp = [0, 0]
        V_cw = [0] * 128
        for n in range(16):
            gp.wait(gp, gp.v)
            gp.wait(te, te.v)
            gp.wait(ve, V_XDB)
            gp.dma(lambda e, n=n: e.dma_start(
                out=bst[0:1, 0:TH], in_=xdb[64 + n:65 + n, 0:TH]))
            V_BST = gp.dma(lambda e, n=n: e.dma_start(
                out=bst[0:1, TH:2 * TH], in_=xdb[80 + n:81 + n, 0:TH]))
            te.wait(ve, ve.v)
            te.wait(gp, V_BST)
            for w2 in range(2):
                te.emit(lambda e, w2=w2: e.matmul(
                    ps_bc[w2][:, :], ones_r[0:1, :],
                    bst[0:1, w2 * 512:(w2 + 1) * 512],
                    start=True, stop=True))
                te.emit(lambda e, w2=w2: e.matmul(
                    ps_bc[2 + w2][:, :], ones_r[0:1, :],
                    bst[0:1, TH + w2 * 512: TH + (w2 + 1) * 512],
                    start=True, stop=True))
            ve.wait(te, te.v)
            for w2 in range(2):
                ve.wait(ve, ve.v)
                ve.emit(lambda e, w2=w2: e.tensor_copy(
                    Bbc[:, w2 * 512:(w2 + 1) * 512], ps_bc[w2][:, :]))
                ve.wait(ve, ve.v)
                ve.emit(lambda e, w2=w2: e.tensor_copy(
                    Cbc[:, w2 * 512:(w2 + 1) * 512], ps_bc[2 + w2][:, :]))
            V_BCC = ve.v
            for ec in range(8):
                b = ec % 2
                idx = ec * 16 + n
                sc.wait(ve, V_scb[b])
                sc.wait(sc, sc.v)
                V_A = sc.emit(lambda e, ec=ec, n=n, b=b: e.activation(
                    atl2[:, b * TH:(b + 1) * TH],
                    dltN[:, ec * TH:(ec + 1) * TH], AF.Exp,
                    scale=Acol[:, ec * 16 + n: ec * 16 + n + 1]))
                ve.wait(ve, V_sclast)
                V_T1 = ve.emit(lambda e, ec=ec: e.tensor_mul(
                    ttl[:, :], duN[:, ec * TH:(ec + 1) * TH], Bbc[:, :]))
                ve.wait(sc, V_A)
                ve.wait(ve, V_T1)
                ve.wait(ve, V_cw[idx])
                V_SCN = ve.emit(lambda e, idx=idx, b=b: e.tensor_tensor_scan(
                    htl[:, :], atl2[:, b * TH:(b + 1) * TH], ttl[:, :],
                    carry[:, idx:idx + 1], ALU.mult, ALU.add))
                V_scb[b] = V_SCN
                V_sclast = V_SCN
                ve.wait(ve, V_SCN)
                V_cw[idx] = ve.emit(lambda e, idx=idx: e.tensor_copy(
                    carry[:, idx:idx + 1], htl[:, TH - 1:TH]))
                ve.wait(ve, ve.v)
                ve.wait(gp, V_t2gp[b])
                V_T2 = ve.emit(lambda e, b=b: e.tensor_mul(
                    t2b[:, b * TH:(b + 1) * TH], htl[:, :], Cbc[:, :]))
                gp.wait(ve, V_T2)
                gp.wait(gp, gp.v)
                if n == 0:
                    V_t2gp[b] = gp.emit(lambda e, ec=ec, b=b: e.tensor_copy(
                        yac[:, ec * TH:(ec + 1) * TH],
                        t2b[:, b * TH:(b + 1) * TH]))
                else:
                    V_t2gp[b] = gp.emit(lambda e, ec=ec, b=b: e.tensor_add(
                        yac[:, ec * TH:(ec + 1) * TH],
                        t2b[:, b * TH:(b + 1) * TH],
                        yac[:, ec * TH:(ec + 1) * TH]))
        ve.wait(gp, gp.v)
        for ec in range(8):
            ve.wait(ve, ve.v)
            ve.emit(lambda e, ec=ec: e.scalar_tensor_tensor(
                yac[:, ec * TH:(ec + 1) * TH],
                xcN[:, ec * TH:(ec + 1) * TH], Dcol[:, ec:ec + 1],
                yac[:, ec * TH:(ec + 1) * TH], ALU.mult, ALU.add))
        V_Y = ve.v
        sy.wait(ve, V_Y)
        sy.wait(sy, sy.v)
        sy.dma(lambda e, th=th: e.dma_start(
            out=yout.rearrange("(ec p) t -> p ec t", p=128)
                [:, :, th * TH:(th + 1) * TH],
            in_=yac.rearrange("p (ec t) -> p ec t", t=TH)))
        ve.wait(sy, sy.v)
        gp.wait(sy, sy.v)
    sy.wait(sy, sy.v)


def emit_F2(p, nc, ctx, t):
    xT = t["x4T"]
    yT = t["yT"]
    wr16T = t["wr16T"]
    sgD = t["sgD"]
    suD = t["suD"]
    sdD = t["sdD"]
    gwD = t["gwD"]
    uwD = t["uwD"]
    dwD = t["dwD"]
    wqD = t["wqD"]
    wkD = t["wkD"]
    wvD = t["wvD"]
    identb = t["identb"]
    onescol = t["onescol"]
    onesrow = t["onesrow"]
    x6T = t["x6T"]
    qTo = t["qTo"]
    kTo = t["kTo"]
    vO = t["vO"]
    gp, te, sc, ve, sy = p.gp, p.te, p.sc, p.ve, p.sy
    gp, te, sc, ve, sy = p.gp, p.te, p.sc, p.ve, p.sy
    ones_c = ctx.enter_context(nc.sbuf_tensor("f2onesc", [128, 4], F32))
    ones_r = ctx.enter_context(nc.sbuf_tensor("f2onesr", [1, 128], F32))
    idb = ctx.enter_context(nc.sbuf_tensor("f2idb", [128, 128], BF16))
    consts = {"ones_col": ones_c[:, 0:1], "ones_row": ones_r,
              "eps": ones_c[:, 1:2], "one1": ones_c, "idb": idb}
    xsb = ctx.enter_context(nc.sbuf_tensor("f2xsb", [128, NCH * TPC], F32))
    gp.dma(lambda e: e.dma_start(out=ones_c[:, :], in_=onescol))
    gp.dma(lambda e: e.dma_start(out=ones_r[:, :], in_=onesrow))
    gp.dma(lambda e: e.dma_start(out=idb[:, :], in_=identb))
    gp.dma(lambda e: e.dma_start(
        out=xsb.rearrange("p (cc t) -> p cc t", cc=NCH),
        in_=xT.rearrange("(cc p) t -> p cc t", p=128)))
    ph0 = ctx.enter_context(contextlib.ExitStack())
    ysb = ph0.enter_context(nc.sbuf_tensor("f2ysb", [128, NCH * TPC], F32))
    V_IN = gp.dma(lambda e: e.dma_start(
        out=ysb.rearrange("p (cc t) -> p cc t", cc=NCH),
        in_=yT.rearrange("(cc p) t -> p cc t", p=128)))
    ve.wait(gp, V_IN)
    V_X5 = None
    for cc in range(NCH):
        ve.wait(ve, ve.v)
        V_X5 = ve.emit(lambda e, cc=cc: e.tensor_add(
            xsb[:, cc * TPC:(cc + 1) * TPC],
            ysb[:, cc * TPC:(cc + 1) * TPC],
            xsb[:, cc * TPC:(cc + 1) * TPC]))
    ph0.close()

    V_X6 = emit_ffn1(p, nc, ctx, xsb, consts, wr16T, sgD, suD, sdD,
                     gwD, uwD, dwD, [(ve, V_X5), (gp, V_IN)], tag="M")
    sy.wait(ve, V_X6)
    sy.dma(lambda e: e.dma_start(
        out=x6T.rearrange("(cc p) t -> p cc t", p=128),
        in_=xsb.rearrange("p (cc t) -> p cc t", cc=NCH)))

    ph2 = ctx.enter_context(contextlib.ExitStack())
    xh2 = ph2.enter_context(nc.sbuf_tensor("f2xh2", [128, NCH * TPC], BF16))
    V_LN2, _ = emit_ln(p, nc, xsb, TPC, consts, xh2, [(ve, V_X6)],
                       tag="q", alloc_ctx=ph2)
    for eng in (te, sc, gp):
        eng.wait(ve, V_LN2)
    qTt = ph2.enter_context(nc.sbuf_tensor("f2qTt", [128, NCH * TPC], BF16))
    emit_ffcols(p, nc, ph2, xh2, TPC, wqD, qTt, TPC, AF.Identity, NCH,
                tag="q2")
    V_QS = sc.v
    kTt = ph2.enter_context(nc.sbuf_tensor("f2kTt", [128, TPC], BF16))
    emit_ffcols(p, nc, ph2, xh2, TPC, wkD, kTt, TPC, AF.Identity, 1,
                tag="k2")
    V_KS = sc.v
    wvs = ph2.enter_context(nc.sbuf_tensor("f2wvs", [128, NCH * 128], BF16))
    vOt = ph2.enter_context(nc.sbuf_tensor("f2vOt", [128, 8 * 128], BF16))
    ps_vv = [ph2.enter_context(nc.psum_tensor(f"f2pvv{i}", [128, 128], F32))
             for i in range(2)]
    gp.wait(gp, gp.v)
    gp.wait(te, te.v)
    gp.wait(sc, sc.v)
    V_WV = gp.dma(lambda e: e.dma_start(
        out=wvs.rearrange("p (cc f) -> p cc f", f=128),
        in_=wvD.rearrange("(cc p) f -> p cc f", p=128)))
    te.wait(gp, V_WV)
    te.wait(sc, V_KS)
    for tc in range(8):
        if tc >= 2:
            te.wait(sc, sc.v)
        for cc in range(NCH):
            te.emit(lambda e, tc=tc, cc=cc: e.matmul(
                ps_vv[tc % 2][:, :],
                xh2[:, cc * TPC + tc * 128: cc * TPC + (tc + 1) * 128],
                wvs[:, cc * 128:(cc + 1) * 128],
                start=(cc == 0), stop=(cc == NCH - 1)))
        sc.wait(te, te.v)
        sc.emit(lambda e, tc=tc: e.activation(
            vOt[:, tc * 128:(tc + 1) * 128], ps_vv[tc % 2][:, :],
            AF.Identity))
    V_VO = sc.v
    sy.wait(sc, V_QS)
    sy.wait(sy, sy.v)
    sy.dma(lambda e: e.dma_start(out=qTo, in_=qTt[:, :]))
    sy.wait(sc, V_KS)
    sy.wait(sy, sy.v)
    sy.dma(lambda e: e.dma_start(out=kTo, in_=kTt[:, :]))
    sy.wait(sc, V_VO)
    sy.wait(sy, sy.v)
    sy.dma(lambda e: e.dma_start(
        out=vO.rearrange("(tc p) d -> p tc d", p=128),
        in_=vOt.rearrange("p (tc d) -> p tc d", d=128)))
    sy.wait(sy, sy.v)
    ph2.close()


def emit_Z(p, nc, ctx, t):
    xT = t["x6T"]
    qTd = t["qTo"]
    kTf = t["kTf"]
    vf = t["vf"]
    woD2 = t["woD2"]
    wr16T = t["wr16T"]
    sgD = t["sgD"]
    suD = t["suD"]
    sdD = t["sdD"]
    gwD = t["gwD"]
    uwD = t["uwD"]
    dwD = t["dwD"]
    identb = t["identb"]
    onescol = t["onescol"]
    onesrow = t["onesrow"]
    x8T = t["x8T"]
    gp, te, sc, ve, sy = p.gp, p.te, p.sc, p.ve, p.sy
    gp, te, sc, ve, sy = p.gp, p.te, p.sc, p.ve, p.sy
    ones_c = ctx.enter_context(nc.sbuf_tensor("zonesc", [128, 4], F32))
    ones_r = ctx.enter_context(nc.sbuf_tensor("zonesr", [1, 128], F32))
    ones_b = ctx.enter_context(nc.sbuf_tensor("zonesb", [128, 1], BF16))
    idb = ctx.enter_context(nc.sbuf_tensor("zidb", [128, 128], BF16))
    consts = {"ones_col": ones_c[:, 0:1], "ones_row": ones_r,
              "eps": ones_c[:, 1:2], "one1": ones_c, "idb": idb}
    xsb = ctx.enter_context(nc.sbuf_tensor("zxsb", [128, NCH * TPC], F32))
    gp.dma(lambda e: e.dma_start(out=ones_c[:, :], in_=onescol))
    gp.dma(lambda e: e.dma_start(out=ones_r[:, :], in_=onesrow))
    gp.dma(lambda e: e.dma_start(out=idb[:, :], in_=identb))
    V_C = gp.dma(lambda e: e.dma_start(
        out=xsb.rearrange("p (cc t) -> p cc t", cc=NCH),
        in_=xT.rearrange("(cc p) t -> p cc t", p=128)))
    ve.wait(gp, V_C)
    V_OB = ve.emit(lambda e: e.tensor_copy(ones_b[:, :], ones_c[:, 0:1]))

    pha = ctx.enter_context(contextlib.ExitStack())
    ySB = pha.enter_context(nc.sbuf_tensor("zySB", [128, NCH * TPC], BF16))
    pha2 = pha.enter_context(contextlib.ExitStack())
    qTs = pha2.enter_context(nc.sbuf_tensor("zqTs", [128, 16 * TPC], BF16))
    kTs = pha2.enter_context(nc.sbuf_tensor("zkTs", [128, T], BF16))
    vSf = pha2.enter_context(nc.sbuf_tensor("zvSf", [128, 16 * 128], BF16))
    exQ = pha2.enter_context(nc.sbuf_tensor("zexQ", [128, 16 * 512], BF16))
    rbc = pha2.enter_context(nc.sbuf_tensor("zrbc", [128, 512], F32))
    rro = pha2.enter_context(nc.sbuf_tensor("zrro", [1, 512], F32))
    ps_s = [pha2.enter_context(nc.psum_tensor(f"zpzs{i}", [128, 512], F32))
            for i in range(4)]
    ps_d = pha2.enter_context(nc.psum_tensor("zpzd", [1, 512], F32))
    ps_y = pha2.enter_context(nc.psum_tensor("zpzy", [128, 512], F32))
    ps_rb = pha2.enter_context(nc.psum_tensor("zpzrb", [128, 512], F32))
    gp.wait(gp, gp.v)
    gp.dma(lambda e: e.dma_start(out=qTs[:, :], in_=qTd))
    gp.dma(lambda e: e.dma_start(out=kTs[:, :], in_=kTf))
    V_KV = gp.dma(lambda e: e.dma_start(
        out=vSf.rearrange("p (g d) -> p g d", d=128),
        in_=vf.rearrange("(g p) d -> p g d", p=128)))
    te.wait(gp, V_KV)
    te.wait(ve, V_OB)
    for h in range(16):
        for qh in range(2):
            SC_R = sc.v
            for kc in range(16):
                if kc >= 4:
                    te.wait(sc, SC_R + kc - 3)
                elif kc == 0:
                    te.wait(sc, sc.v)
                te.emit(lambda e, h=h, qh=qh, kc=kc: e.matmul(
                    ps_s[kc % 4][:, :], kTs[:, kc * 128:(kc + 1) * 128],
                    qTs[:, h * TPC + qh * 512: h * TPC + qh * 512 + 512],
                    start=True, stop=True))
                sc.wait(te, te.v)
                if kc == 0:
                    sc.wait(ve, ve.v)
                sc.emit(lambda e, kc=kc: e.activation(
                    exQ[:, kc * 512:(kc + 1) * 512], ps_s[kc % 4][:, :],
                    AF.Exp, scale=RS, bias=ones_c[:, 2:3]))
            te.wait(sc, sc.v)
            for kc in range(16):
                te.emit(lambda e, kc=kc: e.matmul(
                    ps_d[:, :], ones_b[:, 0:1],
                    exQ[:, kc * 512:(kc + 1) * 512],
                    start=(kc == 0), stop=(kc == 15)))
            for kc in range(16):
                te.emit(lambda e, kc=kc: e.matmul(
                    ps_y[:, :], vSf[:, kc * 128:(kc + 1) * 128],
                    exQ[:, kc * 512:(kc + 1) * 512],
                    start=(kc == 0), stop=(kc == 15)))
            V_YMM = te.v
            ve.wait(te, V_YMM)
            ve.wait(ve, ve.v)
            ve.emit(lambda e: e.reciprocal(rro[0:1, :], ps_d[:, :]))
            te.wait(ve, ve.v)
            V_RB = te.emit(lambda e: e.matmul(
                ps_rb[:, :], ones_r[0:1, :], rro[0:1, :],
                start=True, stop=True))
            ve.wait(te, V_RB)
            ve.emit(lambda e: e.tensor_copy(rbc[:, :], ps_rb[:, :]))
            ve.wait(ve, ve.v)
            ve.emit(lambda e, h=h, qh=qh: e.tensor_mul(
                ySB[:, h * TPC + qh * 512: h * TPC + qh * 512 + 512],
                ps_y[:, :], rbc[:, :]))
    V_ATT = ve.v
    pha2.close()

    for eng in (te, sc, gp):
        eng.wait(ve, V_ATT)
    emit_ffcols(p, nc, pha, ySB, TPC, woD2, None, TPC, None, NCH,
                tag="zw", addto=xsb, addW=TPC)
    V_X7 = ve.v
    pha.close()

    V_X8 = emit_ffn1(p, nc, ctx, xsb, consts, wr16T, sgD, suD, sdD,
                     gwD, uwD, dwD, [(ve, V_X7)], tag="Z")
    sy.wait(ve, V_X8)
    sy.dma(lambda e: e.dma_start(
        out=x8T.rearrange("(cc p) t -> p cc t", p=128),
        in_=xsb.rearrange("p (cc t) -> p cc t", cc=NCH)))
    sy.wait(sy, sy.v)



PAIRS = [[0, 1], [2, 3], [4, 5], [6, 7]]
ALL8 = [[0, 1, 2, 3, 4, 5, 6, 7]]


def emit_barrier(p):
    es = [p.sy, p.gp, p.ve, p.sc, p.te]
    vs = [e.v for e in es]
    for e in es:
        for o, v in zip(es, vs):
            if o is not e and v > 0:
                e.wait(o, v)


def emit_rows_glue(p, nc, ctx, t, flg, idgb):
    gp, te, sc, ve, sy = p.gp, p.te, p.sc, p.ve, p.sy
    ah = t["att2_h"]
    c0 = ctx.enter_context(nc.sbuf_tensor("rg_c0", [128, 2 * 512], BF16))
    c1 = ctx.enter_context(nc.sbuf_tensor("rg_c1", [128, 2 * 512], BF16))
    bt = ctx.enter_context(nc.sbuf_tensor("rg_bt", [128, 2 * 512], BF16))
    rstg = [ctx.enter_context(nc.sbuf_tensor(f"rg_s{i}", [128, TPC], BF16))
            for i in range(4)]
    pctx = ctx.enter_context(contextlib.ExitStack())
    ps_tr = [pctx.enter_context(nc.psum_tensor(f"rg_pt{i}", [128, 128], BF16))
             for i in range(2)]

    def gath(j0, head):
        return bass.AP(ah, (16 * j0 + head) * 128,
                       [[256 * 128, 128], [16 * 128, 4], [1, 128]])

    for j4 in range(4):
        for hl in range(8):
            it = j4 * 8 + hl
            b = it % 2
            gp.wait(ve, ve.v)
            sy.wait(ve, ve.v)
            V0 = gp.dma(lambda e, j4=j4, hl=hl, b=b: e.dma_start(
                out=c0[:, b * 512:(b + 1) * 512].rearrange(
                    "p (j d) -> p j d", d=128),
                in_=gath(j4 * 4, hl)))
            V1 = sy.dma(lambda e, j4=j4, hl=hl, b=b: e.dma_start(
                out=c1[:, b * 512:(b + 1) * 512].rearrange(
                    "p (j d) -> p j d", d=128),
                in_=gath(j4 * 4, 8 + hl)))
            ve.wait(gp, V0)
            ve.wait(sy, V1)
            ve.wait(ve, ve.v)
            ve.emit(lambda e, b=b: e.tensor_scalar(
                bt[:, b * 512:(b + 1) * 512], c0[:, b * 512:(b + 1) * 512],
                flg[:, 1:2], None, ALU.mult))
            ve.wait(ve, ve.v)
            ve.emit(lambda e, b=b: e.scalar_tensor_tensor(
                bt[:, b * 512:(b + 1) * 512], c1[:, b * 512:(b + 1) * 512],
                flg[:, 2:3], bt[:, b * 512:(b + 1) * 512],
                ALU.mult, ALU.add))
            for jq in range(4):
                te.wait(ve, ve.v)
                te.emit(lambda e, b=b, jq=jq: e.transpose(
                    ps_tr[jq % 2][:, :],
                    bt[:, b * 512 + jq * 128: b * 512 + (jq + 1) * 128],
                    idgb[:, :]))
                ve.wait(te, te.v)
                ve.emit(lambda e, jq=jq, hl=hl: e.tensor_copy(
                    rstg[jq][:, hl * 128:(hl + 1) * 128], ps_tr[jq % 2][:, :]))
        sy.wait(ve, ve.v)
        for jq in range(4):
            sy.dma(lambda e, j4=j4, jq=jq: e.dma_start(
                out=t["rowsT"][(j4 * 4 + jq) * 128:(j4 * 4 + jq + 1) * 128, :],
                in_=rstg[jq][:, :]))
    V_RST = sy.v
    for j4 in range(4):
        for hb in range(2):
            b = hb
            gp.wait(ve, ve.v)
            gp.wait(sy, V_RST)
            V0 = gp.dma(lambda e, j4=j4, hb=hb, b=b: e.dma_start(
                out=c0[:, b * 512:(b + 1) * 512].rearrange(
                    "p (j d) -> p j d", d=128),
                in_=gath(j4 * 4, 6 + hb)))
            for jq in range(4):
                te.wait(gp, V0)
                te.wait(ve, ve.v)
                te.emit(lambda e, b=b, jq=jq: e.transpose(
                    ps_tr[jq % 2][:, :],
                    c0[:, b * 512 + jq * 128: b * 512 + (jq + 1) * 128],
                    idgb[:, :]))
                ve.wait(te, te.v)
                ve.emit(lambda e, jq=jq, hb=hb: e.tensor_copy(
                    rstg[jq][:, hb * 128:(hb + 1) * 128], ps_tr[jq % 2][:, :]))
        sy.wait(ve, ve.v)
        for jq in range(4):
            sy.dma(lambda e, j4=j4, jq=jq: e.dma_start(
                out=t["rowsHT"][(j4 * 4 + jq) * 128:
                                (j4 * 4 + jq + 1) * 128, :],
                in_=rstg[jq][:, 0:256]))
    sy.wait(sy, sy.v)
    pctx.close()


def emit_halo(p, nc, ctx, t):
    gp, te, sc, ve, sy = p.gp, p.te, p.sc, p.ve, p.sy
    xhalo = ctx.enter_context(nc.sbuf_tensor("hp_x", [128, NCH * 256], F32))
    rh = ctx.enter_context(nc.sbuf_tensor("hp_r", [128, NCH * 256], BF16))
    gp.wait(gp, gp.v)
    gp.dma(lambda e: e.dma_start(
        out=xhalo.rearrange("p (cc t) -> p cc t", cc=NCH),
        in_=t["xg0"][:, 768:1024].rearrange("(cc p) t -> p cc t", p=128)))
    V_IN = gp.dma(lambda e: e.dma_start(
        out=rh.rearrange("p (cc t) -> p cc t", cc=NCH),
        in_=t["rowsHT"].rearrange("(cc p) t -> p cc t", p=128)))
    for eng in (te, sc, ve):
        eng.wait(gp, V_IN)
    emit_ffcols(p, nc, ctx, rh, 256, t["wprojT"], None, 256, None, NCH,
                tag="hp", addto=xhalo, addW=256)
    sy.wait(ve, ve.v)
    sy.dma(lambda e: e.dma_start(
        out=t["haloT"].rearrange("(cc p) t -> p cc t", p=128),
        in_=xhalo.rearrange("p (cc t) -> p cc t", cc=NCH)))
    sy.wait(sy, sy.v)


def emit_scat_glue(p, nc, ctx, t, inkey, outkey, flg, idg, tag):
    gp, te, sc, ve, sy = p.gp, p.te, p.sc, p.ve, p.sy
    srB = ctx.enter_context(nc.sbuf_tensor(f"sg_r{tag}", [8, 2048], F32))
    sc8 = ctx.enter_context(nc.sbuf_tensor(f"sg_o{tag}", [128, NCH * 8], F32))
    pctx = ctx.enter_context(contextlib.ExitStack())
    ps8 = pctx.enter_context(nc.psum_tensor(f"sg_p{tag}", [128, 8], F32))
    gp.wait(gp, gp.v)
    V = gp.dma(lambda e: e.dma_start(out=srB[:, :], in_=t[inkey]))
    te.wait(gp, V)
    for cc in range(NCH):
        te.wait(ve, ve.v)
        te.emit(lambda e, cc=cc: e.transpose(
            ps8[:, :], srB[0:8, cc * 128:(cc + 1) * 128], idg[0:8, 0:8]))
        ve.wait(te, te.v)
        ve.emit(lambda e, cc=cc: e.tensor_scalar(
            sc8[:, cc * 8:(cc + 1) * 8], ps8[:, :], flg[:, 0:1], None,
            ALU.mult))
    sy.wait(ve, ve.v)
    sy.dma(lambda e: e.dma_start(
        out=t[outkey].rearrange("(cc p) e -> p cc e", p=128),
        in_=sc8.rearrange("p (cc e) -> p cc e", e=8)))
    sy.wait(sy, sy.v)
    pctx.close()


def emit_cvt(p, nc, ctx, t, inkey, outkey, nrb, C, tag):
    gp, te, sc, ve, sy = p.gp, p.te, p.sc, p.ve, p.sy
    af = ctx.enter_context(nc.sbuf_tensor(f"cv_f{tag}", [128, 2 * C], F32))
    ab = ctx.enter_context(nc.sbuf_tensor(f"cv_b{tag}", [128, 2 * C], BF16))
    for rb in range(nrb):
        b = rb % 2
        gp.wait(sc, sc.v)
        V = gp.dma(lambda e, rb=rb, b=b: e.dma_start(
            out=af[:, b * C:(b + 1) * C],
            in_=t[inkey][rb * 128:(rb + 1) * 128, :]))
        sc.wait(gp, V)
        sc.wait(sy, sy.v)
        sc.emit(lambda e, b=b: e.activation(
            ab[:, b * C:(b + 1) * C], af[:, b * C:(b + 1) * C], AF.Identity))
        sy.wait(sc, sc.v)
        sy.dma(lambda e, rb=rb, b=b: e.dma_start(
            out=t[outkey][rb * 128:(rb + 1) * 128, :],
            in_=ab[:, b * C:(b + 1) * C]))
    sy.wait(sy, sy.v)


def emit_xhP_glue(p, nc, ctx, t, flg):
    gp, te, sc, ve, sy = p.gp, p.te, p.sc, p.ve, p.sy
    c0 = ctx.enter_context(nc.sbuf_tensor("xp_c0", [128, 2 * TPC], F32))
    c1 = ctx.enter_context(nc.sbuf_tensor("xp_c1", [128, 2 * TPC], F32))
    xf = ctx.enter_context(nc.sbuf_tensor("xp_f", [128, 2 * TPC], F32))
    for pb in range(16):
        for th in range(2):
            it = pb * 2 + th
            b = it % 2
            gp.wait(ve, ve.v)
            sy.wait(ve, ve.v)
            V0 = gp.dma(lambda e, pb=pb, th=th, b=b: e.dma_start(
                out=c0[:, b * TPC:(b + 1) * TPC],
                in_=t["Gxh"][th * D + pb * 128: th * D + (pb + 1) * 128, :]))
            nb = (pb + 8) % 16
            V1 = sy.dma(lambda e, nb=nb, th=th, b=b: e.dma_start(
                out=c1[:, b * TPC:(b + 1) * TPC],
                in_=t["Gxh"][th * D + nb * 128: th * D + (nb + 1) * 128, :]))
            ve.wait(gp, V0)
            ve.wait(sy, V1)
            ve.wait(ve, ve.v)
            ve.emit(lambda e, b=b: e.tensor_scalar(
                xf[:, b * TPC:(b + 1) * TPC], c0[:, b * TPC:(b + 1) * TPC],
                flg[:, 1:2], None, ALU.mult))
            ve.wait(ve, ve.v)
            ve.emit(lambda e, b=b: e.scalar_tensor_tensor(
                xf[:, b * TPC:(b + 1) * TPC], c1[:, b * TPC:(b + 1) * TPC],
                flg[:, 2:3], xf[:, b * TPC:(b + 1) * TPC], ALU.mult, ALU.add))
            sy.wait(ve, ve.v)
            sy.dma(lambda e, pb=pb, th=th, b=b: e.dma_start(
                out=t["xhP"][pb * 128:(pb + 1) * 128,
                             th * TPC:(th + 1) * TPC],
                in_=xf[:, b * TPC:(b + 1) * TPC]))
    sy.wait(sy, sy.v)


def emit_yT_glue(p, nc, ctx, t, flg):
    gp, te, sc, ve, sy = p.gp, p.te, p.sc, p.ve, p.sy
    c0 = ctx.enter_context(nc.sbuf_tensor("yt_c0", [128, 2 * TPC], F32))
    c1 = ctx.enter_context(nc.sbuf_tensor("yt_c1", [128, 2 * TPC], F32))
    yf = ctx.enter_context(nc.sbuf_tensor("yt_f", [128, 2 * TPC], F32))
    for gb in range(16):
        b = gb % 2
        gp.wait(ve, ve.v)
        sy.wait(ve, ve.v)
        V0 = gp.dma(lambda e, gb=gb, b=b: e.dma_start(
            out=c0[:, b * TPC:(b + 1) * TPC],
            in_=t["Gy"][gb * 128:(gb + 1) * 128, 0:TPC]))
        V1 = sy.dma(lambda e, gb=gb, b=b: e.dma_start(
            out=c1[:, b * TPC:(b + 1) * TPC],
            in_=t["Gy"][gb * 128:(gb + 1) * 128, TPC:2 * TPC]))
        ve.wait(gp, V0)
        ve.wait(sy, V1)
        ve.wait(ve, ve.v)
        ve.emit(lambda e, b=b: e.tensor_scalar(
            yf[:, b * TPC:(b + 1) * TPC], c0[:, b * TPC:(b + 1) * TPC],
            flg[:, 1:2], None, ALU.mult))
        ve.wait(ve, ve.v)
        ve.emit(lambda e, b=b: e.scalar_tensor_tensor(
            yf[:, b * TPC:(b + 1) * TPC], c1[:, b * TPC:(b + 1) * TPC],
            flg[:, 2:3], yf[:, b * TPC:(b + 1) * TPC], ALU.mult, ALU.add))
        sy.wait(ve, ve.v)
        sy.dma(lambda e, gb=gb, b=b: e.dma_start(
            out=t["yT"][gb * 128:(gb + 1) * 128, :],
            in_=yf[:, b * TPC:(b + 1) * TPC]))
    sy.wait(sy, sy.v)


def emit_kTf_copy(p, nc, ctx, t):
    gp, te, sc, ve, sy = p.gp, p.te, p.sc, p.ve, p.sy
    kb = ctx.enter_context(nc.sbuf_tensor("kc_b", [128, 2 * TPC], BF16))
    gp.wait(gp, gp.v)
    for r in range(2):
        gp.dma(lambda e, r=r: e.dma_start(
            out=kb[:, r * TPC:(r + 1) * TPC],
            in_=t["Gk"][r * 128:(r + 1) * 128, :]))
    V = gp.v
    sy.wait(gp, V)
    sy.dma(lambda e: e.dma_start(out=t["kTf"], in_=kb[:, :]))
    sy.wait(sy, sy.v)


def build_fused():
    nc = bass.Bass("TRN2", target_bir_lowering=False)
    t = {}

    def inp(name, shape, dt=F32):
        t[name] = nc.dram_tensor(name, shape, dt, kind="ExternalInput").ap()

    def intl(name, shape, dt=F32, handle=False):
        h = nc.dram_tensor(name, shape, dt)
        t[name] = h.ap()
        if handle:
            t[name + "_h"] = h

    inp("xg0", [D, TPC]); inp("xg1", [D, TPC]); inp("xTme", [D, TPC])
    inp("wqkvT", [D, 3 * D]); inp("maskA", [128, 128])
    inp("identb", [128, 128]); inp("onescolA", [128, 2])
    inp("onescol", [128, 4]); inp("onesrow", [1, 128]); inp("flags", [128, 4])
    inp("wprojT", [D, D]); inp("w1T", [D, D]); inp("w3T", [D, D])
    inp("w2T", [D, D]); inp("wrT", [D, 8])
    inp("wqTp", [D, D]); inp("wkTp", [D, D]); inp("wvT", [D, D])
    inp("woD", [D, D]); inp("tabQ", [128, 2 * TPC]); inp("tabK", [128, 2 * 1280])
    inp("maskB", [128, 16 * 256])
    inp("cw", [D, 4]); inp("cbI", [128, NCH]); inp("xpT", [D, 96])
    inp("dtwT", [64, TPC]); inp("dtb", [128, 8]); inp("Ain", [128, 128])
    inp("Dvin", [128, 8])
    inp("wr16T", [D, 16]); inp("sgD", [D, D]); inp("suD", [D, D])
    inp("sdD", [D, D]); inp("gwD", [16 * D, TPC]); inp("uwD", [16 * D, TPC])
    inp("dwD", [16 * TPC, D]); inp("wqD", [D, D]); inp("wkD", [D, 128])
    inp("wvD", [D, 128]); inp("woD2", [D, D])

    intl("att2", [2 * 16 * TPC, 128], BF16, handle=True)
    intl("dq", [TPC, D], BF16, handle=True)
    intl("dk", [TPC, D], BF16, handle=True)
    intl("dv", [TPC, D], BF16, handle=True)
    intl("rowsT", [D, TPC], BF16); intl("rowsHT", [D, 256], BF16)
    intl("x1T", [D, TPC]); intl("haloT", [D, 256])
    intl("part1", [8, 2048]); intl("scatR1", [8, 2048]); intl("scat8a", [D, 8])
    intl("dxx", [D, TPC]); intl("datt", [D, TPC], BF16)
    intl("x3T", [D, TPC]); intl("part2", [8, 2048])
    intl("scatR2", [8, 2048]); intl("scat8b", [D, 8])
    intl("x4T", [D, TPC]); intl("xhT", [D, TPC])
    intl("Gxh", [2 * D, TPC]); intl("xhP", [D, T])
    intl("yout", [TPC, T])
    intl("Gy", [2 * TPC, T]); intl("yT", [D, TPC])
    intl("x6T", [D, TPC]); intl("qTo", [128, 16 * TPC], BF16)
    intl("kTo", [128, TPC], BF16); intl("vO", [TPC, 128], BF16)
    intl("Gk", [256, TPC], BF16); intl("vf", [T, 128], BF16)
    intl("kTf", [128, T], BF16)
    t["x8T"] = nc.dram_tensor("x8T", [D, TPC], F32, kind="ExternalOutput").ap()

    with contextlib.ExitStack() as ctx:
        p = Prog(nc, ctx)
        gp, te, sc, ve, sy = p.gp, p.te, p.sc, p.ve, p.sy
        flg = ctx.enter_context(nc.sbuf_tensor("flgT", [128, 4], F32))
        idg = ctx.enter_context(nc.sbuf_tensor("idgT", [128, 128], F32))
        idgb = ctx.enter_context(nc.sbuf_tensor("idgbT", [128, 128], BF16))
        gp.dma(lambda e: e.dma_start(out=flg[:, :], in_=t["flags"]))
        V_ID = gp.dma(lambda e: e.dma_start(out=idg[:, :], in_=t["identb"]))
        ve.wait(gp, V_ID)
        ve.emit(lambda e: e.tensor_copy(idgb[:, :], idg[:, :]))

        def phase(fn, *args):
            c = contextlib.ExitStack()
            fn(p, nc, c, *args)
            c.close()
            emit_barrier(p)

        for r in (0, 1):
            phase(emit_A1, {**t, "XT_A": t[f"xg{r}"]}, r * 8, f"a{r}")
        phase(emit_rows_glue, t, flg, idgb)
        phase(emit_F1, t)
        phase(emit_halo, t)
        V = gp.emit(lambda e: e.collective_compute(
            "AllReduce", ALU.add, ins=[t["part1"]], outs=[t["scatR1"]],
            replica_groups=ALL8), 1)
        gp.wait(gp, V)
        phase(emit_scat_glue, t, "scatR1", "scat8a", flg, idg, "a")
        phase(emit_B1, t)
        V = gp.emit(lambda e: e.collective_compute(
            "AllReduce", ALU.add, ins=[t["part2"]], outs=[t["scatR2"]],
            replica_groups=ALL8), 1)
        gp.wait(gp, V)
        phase(emit_scat_glue, t, "scatR2", "scat8b", flg, idg, "b")
        phase(emit_M1, t)
        V = gp.emit(lambda e: e.collective_compute(
            "AllGather", ALU.bypass, ins=[t["xhT"]], outs=[t["Gxh"]],
            replica_groups=PAIRS), 1)
        gp.wait(gp, V)
        sy.wait(gp, V)
        phase(emit_xhP_glue, t, flg)
        phase(emit_M2, t)
        V = gp.emit(lambda e: e.collective_compute(
            "AllGather", ALU.bypass, ins=[t["yout"]], outs=[t["Gy"]],
            replica_groups=PAIRS), 1)
        gp.wait(gp, V)
        sy.wait(gp, V)
        phase(emit_yT_glue, t, flg)
        phase(emit_F2, t)
        V = gp.emit(lambda e: e.collective_compute(
            "AllGather", ALU.bypass, ins=[t["kTo"]], outs=[t["Gk"]],
            replica_groups=PAIRS), 1)
        V = gp.emit(lambda e: e.collective_compute(
            "AllGather", ALU.bypass, ins=[t["vO"]], outs=[t["vf"]],
            replica_groups=PAIRS), 1)
        gp.wait(gp, V)
        phase(emit_kTf_copy, t)
        c = contextlib.ExitStack()
        emit_Z(p, nc, c, t)
        c.close()
        p.finish()
    return nc



def _rope_tab(positions):
    inv = 1.0 / 10000.0 ** (np.arange(0, 128, 2, dtype=np.float32) / 128.0)
    fr = np.outer(positions.astype(np.float32), inv)
    W = len(positions)
    tab = np.zeros((128, 2 * W), np.float32)
    tab[0:64, 0:W] = np.cos(fr).T
    tab[64:128, 0:W] = np.cos(fr).T
    tab[0:64, W:] = np.sin(fr).T
    tab[64:128, W:] = np.sin(fr).T
    return tab


def _perm_headcols(Wm):
    WT = np.ascontiguousarray(np.asarray(Wm, np.float32).T)
    idx = np.zeros(2048, np.int64)
    for h in range(16):
        idx[h * 128:h * 128 + 64] = h * 128 + np.arange(0, 128, 2)
        idx[h * 128 + 64:(h + 1) * 128] = h * 128 + np.arange(1, 128, 2)
    return np.ascontiguousarray(WT[:, idx])


def _build_maskB(h):
    m = np.zeros((16, 128, 256), np.float32)
    for ci in range(4):
        gc = h * 4 + ci
        for kc in range(4):
            for jj in range(128):
                ww = kc * 128 + jj
                if ww < 256:
                    m[ci * 4 + kc, jj, :] = 0.0 if gc > 0 else -30000.0
                else:
                    qq = np.arange(256)
                    m[ci * 4 + kc, jj, :] = np.where(ww - 256 <= qq, 0.0,
                                                     -30000.0)
    return np.ascontiguousarray(m.transpose(1, 0, 2).reshape(128, 16 * 256))


def kernel(**inputs):
    inputs = {k: np.asarray(v) for k, v in inputs.items()}
    T4 = lambda a: np.ascontiguousarray(np.asarray(a, np.float32).T)
    DC = lambda a: np.ascontiguousarray(np.asarray(a, np.float32))
    onescol = np.ones((128, 4), np.float32)
    onescol[:, 1] = 1e-5
    onescol[:, 2] = -12.0
    onescolA = np.ones((128, 2), np.float32)
    onescolA[:, 1] = 1e-5
    onesrow = np.ones((1, 128), np.float32)
    ident = np.eye(128, dtype=np.float32)
    maskA = np.full((128, 128), -30000.0, np.float32)
    for tt in range(8):
        maskA[tt * 16:(tt + 1) * 16, tt * 16:(tt + 1) * 16] = 0.0

    x = np.asarray(inputs["L0_attn_a_x"], np.float32)
    shared = {
        "wqkvT": T4(inputs["L0_attn_a_w_qkv"]),
        "maskA": maskA, "identb": ident, "onescolA": onescolA,
        "onescol": onescol, "onesrow": onesrow,
        "wprojT": T4(inputs["L0_attn_a_w_proj"]),
        "w1T": T4(inputs["L0_ffn_w1"]), "w3T": T4(inputs["L0_ffn_w3"]),
        "w2T": T4(inputs["L0_ffn_w2"]), "wrT": T4(inputs["L0_ffn_wr"]),
        "wqTp": _perm_headcols(inputs["L0_attn_b_Wq"]),
        "wkTp": _perm_headcols(inputs["L0_attn_b_Wk"]),
        "wvT": T4(inputs["L0_attn_b_Wv"]),
        "woD": T4(inputs["L0_attn_b_Wo"]),
        "wr16T": T4(inputs["L1_ffn_router_w"]),
        "sgD": DC(inputs["L1_ffn_shared_gate_w"]),
        "suD": DC(inputs["L1_ffn_shared_up_w"]),
        "sdD": DC(inputs["L1_ffn_shared_down_w"]),
        "gwD": np.ascontiguousarray(np.asarray(
            inputs["L1_ffn_gate_w"], np.float32).reshape(16 * D, TPC)),
        "uwD": np.ascontiguousarray(np.asarray(
            inputs["L1_ffn_up_w"], np.float32).reshape(16 * D, TPC)),
        "dwD": np.ascontiguousarray(np.asarray(
            inputs["L1_ffn_down_w"], np.float32).reshape(16 * TPC, D)),
        "wqD": DC(inputs["L1_attn_b_Wq"]),
        "wkD": DC(inputs["L1_attn_b_Wk"]),
        "wvD": DC(inputs["L1_attn_b_Wv"]),
        "woD2": DC(inputs["L1_attn_b_Wo"]),
    }
    maps = []
    for c in range(NCORES):
        b, h = c // 2, c % 2
        xg0 = np.ascontiguousarray(x[b, 0:TPC, :].T)
        xg1 = np.ascontiguousarray(x[b, TPC:T, :].T)
        flags = np.zeros((128, 4), np.float32)
        flags[:, 0] = 1.0 if c == 0 else 0.0
        flags[:, 1] = 1.0 if h == 0 else 0.0
        flags[:, 2] = 1.0 if h == 1 else 0.0
        if h == 1:
            kpos = np.arange(768, 2048)
        else:
            kpos = np.concatenate([np.zeros(256, np.int64),
                                   np.arange(0, 1024)])
        eh = h
        permv = np.concatenate([np.arange(eh * 1024, (eh + 1) * 1024),
                                np.arange((1 - eh) * 1024, (2 - eh) * 1024)])
        own = permv[:1024]
        cwp = np.asarray(inputs["L1_attn_a_conv_w"], np.float32)[permv]
        cbp = np.asarray(inputs["L1_attn_a_conv_b"], np.float32)[permv]
        xp = np.asarray(inputs["L1_attn_a_x_proj_w"], np.float32)
        dtw = np.asarray(inputs["L1_attn_a_dt_proj_w"], np.float32)
        dtb = np.asarray(inputs["L1_attn_a_dt_proj_b"], np.float32)[own]
        Al = np.asarray(inputs["L1_attn_a_A_log"], np.float32)[own]
        Dv = np.asarray(inputs["L1_attn_a_D"], np.float32)[own]
        maps.append({
            **shared,
            "xg0": xg0, "xg1": xg1, "xTme": xg0 if h == 0 else xg1,
            "flags": flags,
            "tabQ": _rope_tab(np.arange(h * 1024, (h + 1) * 1024)),
            "tabK": _rope_tab(kpos), "maskB": _build_maskB(h),
            "cw": np.ascontiguousarray(cwp),
            "cbI": np.ascontiguousarray(cbp.reshape(16, 128).T),
            "xpT": np.ascontiguousarray(xp[:, permv].T),
            "dtwT": np.ascontiguousarray(dtw[own].T),
            "dtb": np.ascontiguousarray(dtb.reshape(8, 128).T),
            "Ain": np.ascontiguousarray(
                Al.reshape(8, 128, 16).transpose(1, 0, 2).reshape(128, 128)),
            "Dvin": np.ascontiguousarray(Dv.reshape(8, 128).T),
        })
    nc = build_fused()
    res = run_bass_kernel_spmd(nc, maps, core_ids=list(range(NCORES)))
    out = np.zeros((B, T, D), np.float32)
    for c in range(NCORES):
        b, h = c // 2, c % 2
        out[b, h * TPC:(h + 1) * TPC, :] = np.asarray(
            res.results[c]["x8T"]).T
    return out
